# version 05
# baseline (speedup 1.0000x reference)
import math
from contextlib import ExitStack

import numpy as np

import concourse.bass as bass
import concourse.mybir as mybir
from concourse.bass_utils import run_bass_kernel_spmd

F32 = mybir.dt.float32
BF16 = mybir.dt.bfloat16
AF = mybir.ActivationFunctionType
ALU = mybir.AluOpType
AX = mybir.AxisListType

T = 4096
D = 1024
TB = 512
NTB = T // TB
NFC = D // 128
EPS_LN = 1e-5
NEG = -30000.0


ANNOTATE = False
DMA_Q = {"pool": "q_pool", "sp": "q_sp", "act": "q_act", "dve": "q_dve"}


class Reg:
    __slots__ = ("key", "lo", "hi")

    def __init__(self, key, lo, hi):
        self.key, self.lo, self.hi = key, lo, hi


def _ap_region(ap):
    t = ap.tensor
    name = t.name
    if type(t).__name__.startswith("PSum"):
        return Reg(name, 0, 1 << 60)
    shape = list(t.shape)
    pairs = list(ap.ap)
    off = ap.offset
    if len(shape) == 2 and shape[0] > 1:
        n = shape[1]
        free_span = sum(s * (c - 1) for (s, c) in pairs if s != n)
        col = off % n
        return Reg(name, col, col + free_span + 1)
    span = sum(s * (c - 1) for (s, c) in pairs)
    return Reg(name, off, off + span + 1)


class Op:
    __slots__ = ("idx", "veng", "ieng", "method", "args", "kwargs", "waits",
                 "incs", "seq", "tag")

    def __init__(self, idx, veng, ieng, method, args, kwargs, tag=""):
        self.idx = idx
        self.veng = veng
        self.ieng = ieng
        self.method = method
        self.args = args
        self.kwargs = kwargs
        self.waits = []
        self.incs = 0
        self.seq = 0
        self.tag = tag




def op_wave_target(dep_op):
    return dep_op.kwargs.get("_wave_cum", dep_op.seq + 1)

class Builder:
    def __init__(self, name):
        self.nc = bass.Bass("TRN2", target_bir_lowering=False, debug=False)
        self.name = name
        self.ops = []
        self.per_veng = {}
        self.marks = {}
        self.records = {}
        self.qstate = {}
        self.scopes = []
        self.ctx = ExitStack()
        self._uid = 0

    def dram(self, name, shape, dtype, kind="Internal"):
        return self.nc.dram_tensor(name, list(shape), dtype, kind=kind).ap()

    def sbuf(self, name, shape, dtype):
        tgt = self.scopes[-1] if self.scopes else self.ctx
        return tgt.enter_context(self.nc.sbuf_tensor(name, list(shape), dtype))

    def push_scope(self):
        self.scopes.append(ExitStack())

    def pop_scope(self):
        self.barrier()
        st = self.scopes.pop()
        st.close()

    def psum(self, name, shape, dtype):
        return self.ctx.enter_context(self.nc.psum_tensor(name, list(shape), dtype))

    def uid(self):
        self._uid += 1
        return self._uid

    def _dep(self, op, dep_op):
        if dep_op is op:
            return
        if dep_op.ieng == "pe" and op.ieng == "pe":
            return
        veng = dep_op.veng
        if veng.startswith("q_") or veng.startswith("sq_"):
            self._close_wave(veng)
            val = 16 * op_wave_target(dep_op)
            for i, (v, w) in enumerate(op.waits):
                if v == veng:
                    if val > w:
                        op.waits[i] = (veng, val)
                    return
            op.waits.append((veng, val))
            return
        marks = self.marks.setdefault(veng, [])
        pos = None
        for i in range(len(marks) - 1, -1, -1):
            if marks[i] >= dep_op.seq:
                pos = i
            else:
                break
        if pos is None:
            marks.append(dep_op.seq)
            self.per_veng[veng][dep_op.seq].incs += 1
            pos = len(marks) - 1
        val = pos + 1
        for i, (v, w) in enumerate(op.waits):
            if v == veng:
                if val > w:
                    op.waits[i] = (veng, val)
                return
        op.waits.append((veng, val))

    def _access(self, op, regs, is_write):
        for r in regs:
            recs = self.records.setdefault(r.key, [])
            if is_write:
                newrecs = []
                for rec in recs:
                    if rec[0] < r.hi and r.lo < rec[1]:
                        if rec[2] is not None:
                            self._dep(op, rec[2])
                        for rd in rec[3]:
                            self._dep(op, rd)
                        if rec[0] < r.lo or rec[1] > r.hi:
                            newrecs.append(rec)
                    else:
                        newrecs.append(rec)
                newrecs.append([r.lo, r.hi, op, []])
                self.records[r.key] = newrecs
            else:
                for rec in recs:
                    if rec[0] < r.hi and r.lo < rec[1]:
                        if rec[2] is not None:
                            self._dep(op, rec[2])
                        rec[3].append(op)

    def _mk(self, veng, ieng, method, args, kwargs, tag):
        op = Op(len(self.ops), veng, ieng, method, args, kwargs, tag)
        lst = self.per_veng.setdefault(veng, [])
        op.seq = len(lst)
        lst.append(op)
        if veng != ieng:
            pass
        self.ops.append(op)
        return op

    def barrier(self):
        for q in list(self.qstate):
            st = self.qstate[q]
            sem_i = st["wave"] % self.R_WAVES
            self._close_wave(f"{q}#{sem_i}")
        qwaits = []
        for q, st in self.qstate.items():
            for i in range(self.R_WAVES):
                if st["cum"][i] > 0:
                    qwaits.append((f"{q}#{i}", 16 * st["cum"][i]))
        vals = {}
        for e in ("pe", "act", "dve", "pool", "sp"):
            lst = self.per_veng.get(e, [])
            last = None
            for o in reversed(lst):
                if o.method is not None:
                    last = o
                    break
            if last is not None:
                marks = self.marks.setdefault(e, [])
                if not marks or marks[-1] < last.seq:
                    marks.append(last.seq)
                    last.incs = 1
                vals[e] = len(marks)
        spop = self._mk("sp", "sp", "nop", (), {}, "barrier")
        for e, v in vals.items():
            if e != "sp":
                spop.waits.append((e, v))
        spop.waits.extend(qwaits)
        marks_sp = self.marks.setdefault("sp", [])
        marks_sp.append(spop.seq)
        spop.incs = 1
        spv = len(marks_sp)
        for e in ("pe", "act", "dve", "pool"):
            w = self._mk(e, e, None, (), {}, "barwait")
            w.waits.append(("sp", spv))

    def op(self, eng, method, args, kwargs=None, reads=(), writes=(), tag=""):
        op = self._mk(eng, eng, method, args, dict(kwargs or {}), tag)
        self._access(op, [r if isinstance(r, Reg) else _ap_region(r) for r in reads], False)
        self._access(op, [r if isinstance(r, Reg) else _ap_region(r) for r in writes], True)
        return op

    R_WAVES = 4

    def dma(self, eng, out_ap, in_ap, tag="", **kw):
        small = False
        for a in (out_ap, in_ap):
            tn = type(a.tensor).__name__
            if tn.startswith("SB") and a.tensor.shape[0] < 128:
                small = True
        kw["out"] = out_ap
        kw["in_"] = in_ap
        q = ("s" + DMA_Q[eng]) if small else DMA_Q[eng]
        st = self.qstate.setdefault(q, {"wave": 0, "open": 0, "cum": [0] * self.R_WAVES})
        sem_i = st["wave"] % self.R_WAVES
        veng = f"{q}#{sem_i}"
        if st["open"] == 0 and st["cum"][sem_i] > 0:
            gate = Op(len(self.ops), eng, eng, None, (), {}, "gate")
            glst = self.per_veng.setdefault(eng, [])
            gate.seq = len(glst)
            glst.append(gate)
            self.ops.append(gate)
            gate.waits.append((veng, 16 * st["cum"][sem_i]))
        op = self._mk(veng, eng, "dma_start", (), kw, tag)
        op.incs = 1
        st["open"] += 1
        st.setdefault("open_ops", []).append(op)
        op.kwargs["_wave_cum"] = st["cum"][sem_i] + st["open"]
        self._access(op, [_ap_region(in_ap)], False)
        self._access(op, [_ap_region(out_ap)], True)
        return op

    def _close_wave(self, veng):
        q, _, si = veng.partition("#")
        st = self.qstate.get(q)
        if st is None:
            return
        sem_i = st["wave"] % self.R_WAVES
        if f"{q}#{sem_i}" == veng and st["open"] > 0:
            st["cum"][sem_i] += st["open"]
            final = st["cum"][sem_i]
            for o in st.get("open_ops", []):
                o.kwargs["_wave_cum"] = final
            st["open_ops"] = []
            st["open"] = 0
            st["wave"] += 1

    def mm(self, out_ap, lhsT_ap, rhs_ap, start, stop, tile_position=None, tag=""):
        kw = dict(start=start, stop=stop)
        if tile_position is not None:
            kw["tile_position"] = tile_position
        return self.op("pe", "matmul", (out_ap, lhsT_ap, rhs_ap), kw,
                       reads=(lhsT_ap, rhs_ap), writes=(out_ap,), tag=tag)

    def transpose(self, out_ap, in_ap, ident_ap, tag=""):
        return self.op("pe", "transpose", (out_ap, in_ap, ident_ap), {},
                       reads=(in_ap, ident_ap), writes=(out_ap,), tag=tag)

    def act(self, out_ap, in_ap, func, bias=0.0, scale=1.0, tag=""):
        kw = dict(bias=bias, scale=scale)
        reads = [in_ap] + [b for b in (bias, scale) if hasattr(b, "tensor")]
        return self.op("act", "activation", (out_ap, in_ap, func), kw,
                       reads=reads, writes=(out_ap,), tag=tag)

    def finalize(self, final_writes=()):
        end_deps = []
        for ap in final_writes:
            r = ap if isinstance(ap, Reg) else _ap_region(ap)
            for rec in self.records.get(r.key, []):
                if rec[2] is not None:
                    end_deps.append(rec[2])
        term = self._mk("sp", "sp", None, (), {}, "terminal")
        for d in end_deps:
            self._dep(term, d)

        nc = self.nc
        with self.ctx:
            sems = {}
            for veng in self.per_veng:
                sems[veng] = self.ctx.enter_context(nc.semaphore("sem_" + veng))
            block = self.ctx.enter_context(nc.Block())

            def run_engine(eng_obj, eng_name):
                for op in self.ops:
                    if op.ieng != eng_name:
                        continue
                    for (veng, val) in op.waits:
                        eng_obj.wait_ge(sems[veng], val)
                    if op.method is None:
                        continue
                    kwargs = {k: v for k, v in op.kwargs.items() if not k.startswith("_")}
                    inst = getattr(eng_obj, op.method)(*op.args, **kwargs)
                    if op.tag and ANNOTATE:
                        try:
                            inst.annotate(op.tag)
                        except Exception:
                            pass
                    if op.incs:
                        amt = 16 if ("q_" in op.veng) else 1
                        inst.then_inc(sems[op.veng], amt)

            @block.tensor
            def _(t):
                run_engine(t, "pe")

            @block.scalar
            def _(s):
                run_engine(s, "act")

            @block.vector
            def _(v):
                run_engine(v, "dve")

            @block.gpsimd
            def _(g):
                run_engine(g, "pool")

            @block.sync
            def _(sy):
                run_engine(sy, "sp")

        return nc



class ConstReg:

    def __init__(self):
        self.pvt_cols = []
        self.pvt_index = {}
        self.row_vals = []
        self.row_index = {}

    def pvec(self, name, vec):
        v = np.asarray(vec, np.float32).reshape(-1)
        assert v.size % 128 == 0
        if name in self.pvt_index:
            return self.pvt_index[name]
        start = len(self.pvt_cols)
        for i in range(v.size // 128):
            self.pvt_cols.append(v[i * 128:(i + 1) * 128])
        self.pvt_index[name] = start
        return start

    def rowvec(self, name, vec):
        v = np.asarray(vec, np.float32).reshape(-1)
        if name in self.row_index:
            return self.row_index[name]
        start = len(self.row_vals)
        self.row_vals.extend(v.tolist())
        self.row_index[name] = start
        return start

    def pvt_array(self):
        if not self.pvt_cols:
            return np.zeros((128, 1), np.float32)
        return np.ascontiguousarray(np.stack(self.pvt_cols, axis=1).astype(np.float32))

    def rows_array(self):
        if not self.row_vals:
            return np.zeros((1, 4), np.float32)
        return np.asarray(self.row_vals, np.float32).reshape(1, -1)


def make_trimask_T():
    kt = np.arange(128)[:, None]
    qt = np.arange(128)[None, :]
    b = np.where(kt <= qt, 0.0, NEG).astype(np.float32)
    return np.ascontiguousarray(b.T)


def fm_layout(x):
    return np.ascontiguousarray(np.asarray(x).T.reshape(NFC, 128, -1))


def fm_unlayout(xfm):
    return np.ascontiguousarray(np.asarray(xfm).reshape(D, -1).T)



class Ctx:
    def __init__(self, B, n_pvt, n_rows, cr=None):
        self.B = B
        self.cr = cr
        self.identf_d = B.dram("c_identf", [128, 128], F32, kind="ExternalInput")
        self.pvt_d = B.dram("c_pvt", [128, max(n_pvt, 1)], F32, kind="ExternalInput")
        self.rows_d = B.dram("c_rows", [1, max(n_rows, 4)], F32, kind="ExternalInput")
        self.trimask_d = B.dram("c_trimask", [128, 128], F32, kind="ExternalInput")
        self.esel_d = B.dram("c_esel", [8, 8 * 128], F32, kind="ExternalInput")
        self.identf = B.sbuf("s_identf", [128, 128], F32)
        self.identb = B.sbuf("s_identb", [128, 128], BF16)
        self.pvt = B.sbuf("s_pvt", [128, max(n_pvt, 1)], F32)
        self.trimask = B.sbuf("s_trimask", [128, 128], BF16)
        self.onesrowf = B.sbuf("s_onesrowf", [1, 128], F32)
        self.onestb = B.sbuf("s_onestb", [1, TB], F32)
        B.dma("pool", self.identf[:, :], self.identf_d[:, :], tag="ldc")
        B.dma("pool", self.identb[:, :], self.identf_d[:, :], tag="ldc")
        B.dma("pool", self.pvt[:, :], self.pvt_d[:, :], tag="ldc")
        B.dma("pool", self.trimask[:, :], self.trimask_d[:, :], tag="ldc")
        B.op("pool", "memset", (self.onesrowf[:, :], 1.0), {}, writes=(self.onesrowf[:, :],))
        B.op("pool", "memset", (self.onestb[:, :], 1.0), {}, writes=(self.onestb[:, :],))
        self.bank = [B.psum(f"ps{i}", [128, 512], F32) for i in range(7)]
        self.bankb = B.psum("ps7b", [128, 1024], BF16)

    def pv(self, col, n=1):
        return self.pvt[:, col:col + n]

    def eps(self):
        return self.pv(self.cr.pvt_index["eps_ln"])





def ln_full(B, C, x_d, out_d, tag, g_rowcol=None, b_rowcol=None, rms=False,
            src_dtype=F32):
    u = B.uid()
    xs = B.sbuf(f"{tag}_x{u}", [128, NFC * TB], src_dtype)
    sq = B.sbuf(f"{tag}_sq{u}", [128, TB], F32)
    mrow = B.sbuf(f"{tag}_m{u}", [1, TB], F32)
    qrow = B.sbuf(f"{tag}_q{u}", [1, TB], F32)
    rrow = B.sbuf(f"{tag}_r{u}", [1, TB], F32)
    mrrow = B.sbuf(f"{tag}_mr{u}", [1, TB], F32)
    trow = B.sbuf(f"{tag}_tr{u}", [1, TB], F32)
    outs = B.sbuf(f"{tag}_o{u}", [128, NFC * TB], BF16)
    onesc = B.sbuf(f"{tag}_1c{u}", [128, 1], src_dtype)
    onescf = B.sbuf(f"{tag}_1f{u}", [128, 1], F32)
    grows = None
    brows = None
    if g_rowcol is not None:
        grows = B.sbuf(f"{tag}_gr{u}", [1, D], F32)
        B.dma("sp", grows[0:1, :], C.rows_d[0:1, g_rowcol:g_rowcol + D], tag=f"{tag}lg")
    if b_rowcol is not None:
        brows = B.sbuf(f"{tag}_br{u}", [1, D], F32)
        B.dma("sp", brows[0:1, :], C.rows_d[0:1, b_rowcol:b_rowcol + D], tag=f"{tag}lb")
    tmp = B.sbuf(f"{tag}_t{u}", [128, TB], F32)
    B.op("pool", "memset", (onesc[:, :], 1.0), {}, writes=(onesc[:, :],))
    B.op("pool", "memset", (onescf[:, :], 1.0), {}, writes=(onescf[:, :],))
    bank = C.bank
    inv_d = 1.0 / D
    for tb in range(NTB):
        t0 = tb * TB
        for fc in range(NFC):
            B.dma("sp", xs[:, fc * TB:(fc + 1) * TB], x_d[fc, :, t0:t0 + TB], tag=f"{tag}ldx")
        ps1 = bank[0][0:1, 0:TB]
        ps2 = bank[1][0:1, 0:TB]
        for fc in range(NFC):
            xc = xs[:, fc * TB:(fc + 1) * TB]
            if not rms:
                B.mm(ps1, onesc[:, 0:1], xc, start=(fc == 0), stop=(fc == NFC - 1), tag=f"{tag}s1")
            B.act(sq[:, 0:TB], xc, AF.Square, tag=f"{tag}sq")
            B.mm(ps2, onescf[:, 0:1], sq[:, 0:TB], start=(fc == 0), stop=(fc == NFC - 1), tag=f"{tag}s2")
        B.op("dve", "tensor_scalar", (qrow[0:1, :], ps2, inv_d, None, ALU.mult), {},
             reads=(ps2,), writes=(qrow[0:1, :],), tag=f"{tag}rq")
        if not rms:
            B.op("dve", "tensor_scalar", (mrow[0:1, :], ps1, inv_d, None, ALU.mult), {},
                 reads=(ps1,), writes=(mrow[0:1, :],), tag=f"{tag}rm")
            B.op("dve", "scalar_tensor_tensor",
                 (trow[0:1, :], mrow[0:1, :], 1.0, mrow[0:1, :], ALU.mult, ALU.mult), {},
                 reads=(mrow[0:1, :],), writes=(trow[0:1, :],), tag=f"{tag}m2")
            B.op("dve", "tensor_sub", (qrow[0:1, :], qrow[0:1, :], trow[0:1, :]), {},
                 reads=(qrow[0:1, :], trow[0:1, :]), writes=(qrow[0:1, :],), tag=f"{tag}var")
        B.act(rrow[0:1, :], qrow[0:1, :], AF.Ln, bias=C.eps()[0:1, 0:1], tag=f"{tag}ln")
        B.act(rrow[0:1, :], rrow[0:1, :], AF.Exp, scale=-0.5, tag=f"{tag}exp")
        if not rms:
            B.op("dve", "scalar_tensor_tensor",
                 (mrrow[0:1, :], mrow[0:1, :], -1.0, rrow[0:1, :], ALU.mult, ALU.mult), {},
                 reads=(mrow[0:1, :], rrow[0:1, :]), writes=(mrrow[0:1, :],), tag=f"{tag}mr")
        for fc in range(NFC):
            xc = xs[:, fc * TB:(fc + 1) * TB]
            oc = outs[:, fc * TB:(fc + 1) * TB]
            pg = bank[2 + (fc % 2) * 2][:, 0:TB]
            pb = bank[3 + (fc % 2) * 2][:, 0:TB]
            grow = C.onesrowf[0:1, :] if g_rowcol is None else grows[0:1, fc * 128:(fc + 1) * 128]
            B.mm(pg, grow, rrow[0:1, :], start=True, stop=True, tag=f"{tag}bg")
            use_pb = (not rms) or (b_rowcol is not None)
            if use_pb:
                first = True
                if not rms:
                    B.mm(pb, grow, mrrow[0:1, :], start=True, stop=(b_rowcol is None), tag=f"{tag}bb")
                    first = False
                if b_rowcol is not None:
                    B.mm(pb, brows[0:1, fc * 128:(fc + 1) * 128], C.onestb[0:1, 0:TB],
                         start=first, stop=True, tag=f"{tag}bb2")
            B.op("dve", "tensor_mul", (tmp[:, 0:TB], xc, pg), {},
                 reads=(xc, pg), writes=(tmp[:, 0:TB],), tag=f"{tag}ap1")
            if use_pb:
                B.op("dve", "tensor_add", (oc, tmp[:, 0:TB], pb), {},
                     reads=(tmp[:, 0:TB], pb), writes=(oc,), tag=f"{tag}ap2")
            else:
                B.op("dve", "tensor_copy", (oc, tmp[:, 0:TB]), {},
                     reads=(tmp[:, 0:TB],), writes=(oc,), tag=f"{tag}ap2")
        if out_d is not None:
            for fc in range(NFC):
                B.dma("act", out_d[fc, :, t0:t0 + TB], outs[:, fc * TB:(fc + 1) * TB],
                      tag=f"{tag}st")



def proj_fm(B, C, h_d, w_d, out_d, dout, tag, bias_pvcol=None, scale=None,
            oc_group=4, col_off=0):
    u = B.uid()
    noc = dout // 128
    hs = B.sbuf(f"{tag}_h{u}", [128, NFC * TB], BF16)
    og = min(oc_group, noc)
    ws = B.sbuf(f"{tag}_w{u}", [128, 2 * NFC * og * 128], BF16)
    os_ = B.sbuf(f"{tag}_o{u}", [128, og * TB], BF16)
    bank = C.bank
    w_view = w_d.rearrange("(kc kp) n -> kc kp n", kp=128)
    ngrp = (noc + og - 1) // og
    for g in range(ngrp):
        ocs = list(range(g * og, min((g + 1) * og, noc)))
        wbuf = (g % 2) * NFC * og * 128
        for kc in range(NFC):
            B.dma("pool",
                  ws[:, wbuf + kc * og * 128: wbuf + kc * og * 128 + len(ocs) * 128],
                  w_view[kc, :, col_off + ocs[0] * 128: col_off + ocs[0] * 128 + len(ocs) * 128],
                  tag=f"{tag}ldw")
        for tb in range(NTB):
            t0 = tb * TB
            for kc in range(NFC):
                B.dma("sp", hs[:, kc * TB:(kc + 1) * TB], h_d[kc, :, t0:t0 + TB],
                      tag=f"{tag}ldh")
            for j, oc in enumerate(ocs):
                pb = bank[j % 4][:, 0:TB]
                for kc in range(NFC):
                    B.mm(pb, ws[:, wbuf + kc * og * 128 + j * 128: wbuf + kc * og * 128 + (j + 1) * 128],
                         hs[:, kc * TB:(kc + 1) * TB],
                         start=(kc == 0), stop=(kc == NFC - 1), tag=f"{tag}mm")
                oc_sb = os_[:, j * TB:(j + 1) * TB]
                if bias_pvcol is not None and scale is not None:
                    B.op("dve", "tensor_scalar",
                         (oc_sb, pb, C.pv(bias_pvcol + oc), scale, ALU.add, ALU.mult), {},
                         reads=(pb, C.pv(bias_pvcol + oc)), writes=(oc_sb,), tag=f"{tag}ev")
                elif bias_pvcol is not None:
                    B.op("dve", "tensor_scalar",
                         (oc_sb, pb, C.pv(bias_pvcol + oc), None, ALU.add), {},
                         reads=(pb, C.pv(bias_pvcol + oc)), writes=(oc_sb,), tag=f"{tag}ev")
                elif scale is not None:
                    B.op("dve", "tensor_scalar", (oc_sb, pb, scale, None, ALU.mult), {},
                         reads=(pb,), writes=(oc_sb,), tag=f"{tag}ev")
                else:
                    B.op("dve", "tensor_copy", (oc_sb, pb), {},
                         reads=(pb,), writes=(oc_sb,), tag=f"{tag}ev")
                B.dma("act", out_d[oc, :, t0:t0 + TB], oc_sb, tag=f"{tag}st")



def proj_tm_v(B, C, h_d, w_d, out_d, dout, nheads, dv, tag, vbias_d=None,
               scale=None, col_off=0):
    u = B.uid()
    ws = B.sbuf(f"{tag}_w{u}", [128, NFC * dout], BF16)
    hs = B.sbuf(f"{tag}_h{u}", [128, NFC * 128], BF16)
    width = nheads * (dv + 1)
    os_ = B.sbuf(f"{tag}_o{u}", [128, width], BF16)
    vb = None
    if vbias_d is not None:
        vb = B.sbuf(f"{tag}_b{u}", [128, dout], F32)
        B.dma("pool", vb[:, :], vbias_d[:, :], tag=f"{tag}ldb")
    w_view = w_d.rearrange("(kc kp) n -> kc kp n", kp=128)
    for kc in range(NFC):
        B.dma("pool", ws[:, kc * dout:(kc + 1) * dout], w_view[kc, :, col_off:col_off + dout], tag=f"{tag}ldw")
    bank = C.bank
    nnc = (dout + 511) // 512
    for tt in range(T // 128):
        t0 = tt * 128
        for kc in range(NFC):
            B.dma("sp", hs[:, kc * 128:(kc + 1) * 128], h_d[kc, :, t0:t0 + 128],
                  tag=f"{tag}ldh")
        for nc_ in range(nnc):
            n0 = nc_ * 512
            nw = min(512, dout - n0)
            pb = bank[nc_ % 4][:, 0:nw]
            for kc in range(NFC):
                B.mm(pb, hs[:, kc * 128:(kc + 1) * 128], ws[:, kc * dout + n0: kc * dout + n0 + nw],
                     start=(kc == 0), stop=(kc == NFC - 1), tag=f"{tag}mm")
            h0 = n0 // dv
            nh = nw // dv
            for hh in range(nh):
                if dv > 64:
                    base = (h0 + hh) * (dv + 1)
                    for (po, do, w_) in ((0, 0, 64), (64, 65, 64)):
                        dsth = os_[:, base + do: base + do + w_]
                        srch = pb[:, hh * dv + po: hh * dv + po + w_]
                        if vb is not None:
                            B.op("dve", "tensor_add", (dsth, srch,
                                 vb[:, n0 + hh * dv + po: n0 + hh * dv + po + w_]), {},
                                 reads=(srch, vb[:, n0 + hh * dv + po: n0 + hh * dv + po + w_]),
                                 writes=(dsth,), tag=f"{tag}ev")
                        elif scale is not None:
                            B.op("dve", "tensor_scalar", (dsth, srch, scale, None, ALU.mult),
                                 {}, reads=(srch,), writes=(dsth,), tag=f"{tag}ev")
                        else:
                            B.op("dve", "tensor_copy", (dsth, srch), {},
                                 reads=(srch,), writes=(dsth,), tag=f"{tag}ev")
                    continue
                dst = os_[:, (h0 + hh) * (dv + 1): (h0 + hh) * (dv + 1) + dv]
                src = pb[:, hh * dv: (hh + 1) * dv]
                if vb is not None:
                    B.op("dve", "tensor_add", (dst, src, vb[:, n0 + hh * dv: n0 + (hh + 1) * dv]), {},
                         reads=(src, vb[:, n0 + hh * dv: n0 + (hh + 1) * dv]), writes=(dst,),
                         tag=f"{tag}ev")
                elif scale is not None:
                    B.op("dve", "tensor_scalar", (dst, src, scale, None, ALU.mult), {},
                         reads=(src,), writes=(dst,), tag=f"{tag}ev")
                else:
                    B.op("dve", "tensor_copy", (dst, src), {},
                         reads=(src,), writes=(dst,), tag=f"{tag}ev")
        one_col = 64 if dv > 64 else dv
        ones_view = os_.rearrange("p (h e) -> p h e", e=dv + 1)[:, :, one_col:one_col + 1]
        B.op("dve", "memset", (ones_view, 1.0), {}, writes=(os_[:, 0:width],), tag=f"{tag}on")
        B.dma("act", out_d[t0:t0 + 128, 0:width], os_[:, 0:width], tag=f"{tag}st")



FM = lambda: ([NFC, 128, T], F32)
FMB = lambda: ([NFC, 128, T], BF16)

TSPEC = {
    "x0": FM(),
    "u_l0": FMB(),
    "Q1": FMB(), "K1": FMB(),
    "V1": ([T, 16 * 65], BF16),
    "O1": ([8, 128, T], BF16),
    "x1": FM(),
    "u1": FMB(), "h1": FMB(),
    "Q2": FMB(), "K2": FMB(),
    "V2": ([T, 16 * 65], BF16),
    "O2": ([8, 128, T], BF16),
    "xr": FMB(), "h2": FMB(),
    "G1": ([T, 8], F32),
    "x2": FM(),
    "u2": FMB(),
    "q3t": ([T, D], BF16), "k3t": ([T, D], BF16),
    "Q3a": ([4, 128, T], BF16), "Q3b": ([4, 128, T], BF16),
    "K3a": ([4, 128, T], BF16), "K3b": ([4, 128, T], BF16),
    "V3": ([T, 8 * 129], BF16),
    "O3a": ([8, 128, T], BF16), "O3b": ([8, 128, T], BF16),
    "Y3": ([8, 128, T], BF16),
    "x3": FM(),
    "u3": FMB(), "h3": FMB(),
    "q4t": ([T, D], BF16), "ckvt": ([T, 160], BF16),
    "Q4": ([8, 128, T], BF16), "K4": ([8, 128, T], BF16),
    "V4": ([T, 16 * 33], BF16),
    "O4": ([4, 128, T], BF16),
    "xi": FMB(), "h4": FMB(),
    "G2": ([T, 8], F32),
    "x4": FM(),
}

WSPEC = {
    "L0_attn_W_qkv": ([D, 3 * D], F32),
    "L0_attn_W_out": ([D, D], F32),
    "L0_ffn_w_qkv": ([D, 3 * D], F32),
    "L0_ffn_w_o": ([D, D], F32),
    "L0_ffn_w_gate": ([D, 8], F32),
    "L0_ffn_w1": ([8, D, 4096], F32),
    "L0_ffn_w2": ([8, 4096, D], F32),
    "L1_attn_WqT": ([D, D], F32),
    "L1_attn_WkT": ([D, D], F32),
    "L1_attn_WvT": ([D, D], F32),
    "L1_attn_WoT": ([D, D], F32),
    "L1_ffn_Wq": ([D, D], F32),
    "L1_ffn_Wkva": ([D, 160], F32),
    "L1_ffn_Wkvb": ([128, 512], F32),
    "L1f_Wo_sub": ([512, D], F32),
    "L1_ffn_Wgate": ([D, 8], F32),
    "L1_ffn_Wg": ([8, D, 512], F32),
    "L1_ffn_Wu": ([8, D, 512], F32),
    "L1_ffn_Wd": ([8, 512, D], F32),
    "L1_ffn_sWg": ([1, D, 512], F32),
    "L1_ffn_sWu": ([1, D, 512], F32),
    "L1_ffn_sWd": ([1, 512, D], F32),
}

EXTRA_IN = {
    "vbias_l0": ([128, D], F32),
    "rope64_cos": ([T, 32], F32), "rope64_sin": ([T, 32], F32),
    "rope32_cos": ([T, 16], F32), "rope32_sin": ([T, 16], F32),
    "lnw_col": ([128, 1], F32), "lnb_col": ([128, 1], F32),
    "lam_cat": ([1, 256], F32),
    "ebias_bc": ([128, 8], F32),
}


def register_consts(cr: "ConstReg", inp=None):
    z = lambda n: np.zeros(n, np.float32)
    g = (lambda k: np.asarray(inp[k], np.float32)) if inp is not None else (lambda k: None)
    def pv(name, n, val):
        cr.pvec(name, z(n * 128) if val is None else val)
    def rv(name, n, val):
        cr.rowvec(name, z(n) if val is None else val)
    pv("eps_ln", 1, np.full(128, EPS_LN, np.float32))
    bqkv = g("L0_attn_b_qkv")
    pv("l0a_bq", 8, None if bqkv is None else bqkv[0:1024])
    pv("l0a_bk", 8, None if bqkv is None else bqkv[1024:2048])
    bout = g("L0_attn_b_out")
    pv("l0a_bout", 8, bout)
    rv("l0f_g1", 1024, g("L0_ffn_ln1_g"))
    rv("l0f_b1", 1024, g("L0_ffn_ln1_b"))
    rv("l0f_g2", 1024, g("L0_ffn_ln2_g"))
    rv("l0f_b2", 1024, g("L0_ffn_ln2_b"))
    b1 = g("L0_ffn_b1")
    b2 = g("L0_ffn_b2")
    for e in range(8):
        pv(f"l0f_b1_{e}", 32, None if b1 is None else b1[e])
        pv(f"l0f_b2_{e}", 8, None if b2 is None else b2[e])
    pv("eps_a8", 1, np.full(128, 1e-8, np.float32))
    rv("l1f_g1", 1024, g("L1_ffn_w_ln1"))
    rv("l1f_g2", 1024, g("L1_ffn_w_ln2"))
    return cr



def phase_l0attn_qkv(B, C, Tn, cr):
    ln_full(B, C, Tn["x0"], Tn["u_l0"], "p1ln")
    proj_fm(B, C, Tn["u_l0"], Tn["L0_attn_W_qkv"], Tn["Q1"], D, "p1q",
            bias_pvcol=cr.pvt_index["l0a_bq"], scale=0.125, col_off=0)
    proj_fm(B, C, Tn["u_l0"], Tn["L0_attn_W_qkv"], Tn["K1"], D, "p1k",
            bias_pvcol=cr.pvt_index["l0a_bk"], col_off=D)
    proj_tm_v(B, C, Tn["u_l0"], Tn["L0_attn_W_qkv"], Tn["V1"], D, 16, 64, "p1v",
              vbias_d=Tn["vbias_l0"], col_off=2 * D)


PHASES = []



def build_phase_program(idx):
    name, needs, makes, wneeds, extra, fn = PHASES[idx]
    cr = register_consts(ConstReg())
    B = Builder(name)
    Tn = {}
    for t in needs:
        shape, dt = TSPEC[t]
        Tn[t] = B.dram(t, shape, dt, kind="ExternalInput")
    for t in makes:
        shape, dt = TSPEC[t]
        Tn[t] = B.dram(t, shape, dt, kind="ExternalOutput")
    for t in wneeds:
        shape, dt = WSPEC[t]
        Tn[t] = B.dram(t, shape, dt, kind="ExternalInput")
    for t in extra:
        shape, dt = EXTRA_IN[t]
        Tn[t] = B.dram(t, shape, dt, kind="ExternalInput")
    C = Ctx(B, len(cr.pvt_cols), len(cr.row_vals), cr)
    fn(B, C, Tn, cr)
    outs = [Tn[t] for t in makes]
    nc = B.finalize(final_writes=outs)
    in_names = needs + wneeds + extra + ["c_identf", "c_pvt", "c_rows", "c_trimask", "c_esel"]
    return nc, in_names, makes


def host_all_arrays(inputs, seq):
    cr = register_consts(ConstReg(), inputs)
    arrs = {
        "x0": fm_layout(np.asarray(inputs["L0_attn_x"][seq], np.float32)),
        "c_identf": np.eye(128, dtype=np.float32),
        "c_pvt": cr.pvt_array(),
        "c_rows": cr.rows_array(),
        "c_trimask": make_trimask_T(),
        "vbias_l0": np.ascontiguousarray(
            np.broadcast_to(np.asarray(inputs["L0_attn_b_qkv"][2048:3072], np.float32)[None, :],
                            (128, 1024))),
    }
    pos = np.arange(T, dtype=np.float64)
    f64 = 1.0 / (10000.0 ** (np.arange(32, dtype=np.float64) / 32.0))
    ang = pos[:, None] * f64[None, :]
    arrs["rope64_cos"] = np.cos(ang).astype(np.float32)
    arrs["rope64_sin"] = np.sin(ang).astype(np.float32)
    f32_ = 1.0 / (100000.0 ** (np.arange(16, dtype=np.float64) / 16.0))
    ang2 = pos[:, None] * f32_[None, :]
    arrs["rope32_cos"] = np.cos(ang2).astype(np.float32)
    arrs["rope32_sin"] = np.sin(ang2).astype(np.float32)
    arrs["lnw_col"] = np.asarray(inputs["L1_attn_ln_w"], np.float32).reshape(128, 1).copy()
    arrs["lnb_col"] = np.asarray(inputs["L1_attn_ln_b"], np.float32).reshape(128, 1).copy()
    arrs["lam_cat"] = np.concatenate([
        np.asarray(inputs["L1_attn_lambda_q1"], np.float32),
        np.asarray(inputs["L1_attn_lambda_k1"], np.float32),
        np.asarray(inputs["L1_attn_lambda_q2"], np.float32),
        np.asarray(inputs["L1_attn_lambda_k2"], np.float32)]).reshape(1, 256)
    arrs["ebias_bc"] = np.ascontiguousarray(np.broadcast_to(
        np.asarray(inputs["L1_ffn_expert_bias"], np.float32)[None, :], (128, 8)))
    esel = np.zeros((8, 8 * 128), np.float32)
    for e in range(8):
        esel[e, e * 128:(e + 1) * 128] = 1.0
    arrs["c_esel"] = esel
    for w in WSPEC:
        if w.endswith("T") and w.startswith("L1_attn"):
            base = w[:-1]
            arrs[w] = np.ascontiguousarray(np.asarray(inputs[base], np.float32).T)
        elif w == "L1f_Wo_sub":
            Wo = np.asarray(inputs["L1_ffn_Wo"], np.float32)
            rows = np.concatenate([np.arange(64 * h, 64 * h + 32) for h in range(16)])
            arrs[w] = np.ascontiguousarray(Wo[rows, :])
        else:
            arrs[w] = np.ascontiguousarray(np.asarray(inputs[w], np.float32))
    return arrs


def run_phases(inputs, upto=None, state=None):
    if state is None:
        state = [host_all_arrays(inputs, b) for b in range(4)]
    n = len(PHASES) if upto is None else upto
    for i in range(n):
        if state[0].get("_done_" + PHASES[i][0]) is not None:
            continue
        import sys as _sys
        print(f"[phase {i}: {PHASES[i][0]}]", flush=True)
        nc, in_names, out_names = build_phase_program(i)
        in_maps = []
        for c in range(8):
            st = state[c // 2]
            in_maps.append({k: st[k] for k in in_names})
        res = run_bass_kernel_spmd(nc, in_maps, core_ids=list(range(8)))
        for b in range(4):
            for k in out_names:
                state[b][k] = np.asarray(res.results[2 * b][k])
            state[b]["_done_" + PHASES[i][0]] = True
    return state


FUSED = True
ANNOTATE = False


def build_fused():
    cr = register_consts(ConstReg())
    B = Builder("fused")
    Tn = {}
    ext_ins = []
    sx, dx = TSPEC["x0"]
    Tn["x0"] = B.dram("x0", sx, dx, kind="ExternalInput")
    ext_ins.append("x0")
    s4, d4 = TSPEC["x4"]
    Tn["x4"] = B.dram("x4", s4, d4, kind="ExternalOutput")
    for t, (s, d) in TSPEC.items():
        if t in ("x0", "x4"):
            continue
        Tn[t] = B.dram(t, s, d, kind="Internal")
    for t, (s, d) in WSPEC.items():
        Tn[t] = B.dram(t, s, d, kind="ExternalInput")
        ext_ins.append(t)
    for t, (s, d) in EXTRA_IN.items():
        Tn[t] = B.dram(t, s, d, kind="ExternalInput")
        ext_ins.append(t)
    C = Ctx(B, len(cr.pvt_cols), len(cr.row_vals), cr)
    for (name, needs, makes, wneeds, extra, fn) in PHASES:
        B.push_scope()
        fn(B, C, Tn, cr)
        B.pop_scope()
    nc = B.finalize(final_writes=[Tn["x4"]])
    ext_ins += ["c_identf", "c_pvt", "c_rows", "c_trimask", "c_esel"]
    return nc, ext_ins


def kernel(**inputs):
    if not FUSED:
        state = run_phases(inputs)
        out = np.stack([fm_unlayout(np.asarray(state[b]["x4"], np.float32))
                        for b in range(4)])
        return out.reshape(4, T, D)
    nc, ext_ins = build_fused()
    arrs = [host_all_arrays(inputs, b) for b in range(4)]
    in_maps = [{k: arrs[c // 2][k] for k in ext_ins} for c in range(8)]
    res = run_bass_kernel_spmd(nc, in_maps, core_ids=list(range(8)))
    out = np.stack([fm_unlayout(np.asarray(res.results[2 * b]["x4"], np.float32))
                    for b in range(4)])
    return out.reshape(4, T, D)



def attention(B, C, Q_d, K_d, V_d, O_d, nheads, dv, tag, causal=True,
              margin=0.5):
    u = B.uid()
    npair = nheads // 2
    vstride = dv + 1
    big_v = dv > 64
    ks = B.sbuf(f"{tag}_k{u}", [128, npair * T], BF16)
    vs = B.sbuf(f"{tag}_v{u}", [128, (T // 128) * nheads * vstride], BF16)
    qs = B.sbuf(f"{tag}_q{u}", [128, TB], BF16)
    qb_sq = B.sbuf(f"{tag}_sq{u}", [128, TB], F32)
    pt = B.sbuf(f"{tag}_pt{u}", [128, 4 * TB], BF16)
    maxr = B.sbuf(f"{tag}_mx{u}", [1, 2 * nheads + 4], F32)
    tmr = B.sbuf(f"{tag}_tm{u}", [1, 4], F32)
    negc = B.sbuf(f"{tag}_nc{u}", [128, nheads], F32)
    ones64 = B.sbuf(f"{tag}_o64{u}", [128, 1], F32)
    rrow = B.sbuf(f"{tag}_rr{u}", [1, TB], F32)
    bcf = B.sbuf(f"{tag}_bc{u}", [128, TB], F32)
    ost = B.sbuf(f"{tag}_os{u}", [128, TB], BF16)
    B.op("pool", "memset", (ones64[:, :], 1.0), {}, writes=(ones64[:, :],))
    B.op("pool", "memset", (maxr[0:1, :], -1.0), {}, writes=(maxr[0:1, :],))
    bank = C.bank
    NKT = T // 128

    def sbank_of(hh, kc):
        if big_v:
            return bank[kc % 2] if hh == 0 else bank[2]
        return bank[hh * 2 + kc % 2]

    def obanks_of(hh):
        if big_v:
            return [(bank[3 + 2 * hh], 65), (bank[4 + 2 * hh], 64)]
        return [(bank[4 + hh], dv + 1)]

    for p in range(npair):
        for tb in range(NTB):
            B.dma("sp", ks[:, p * T + tb * TB: p * T + (tb + 1) * TB],
                  K_d[p, :, tb * TB:(tb + 1) * TB], tag=f"{tag}ldk")
    vw = nheads * vstride
    for tt in range(NKT):
        B.dma("sp", vs[:, tt * vw:(tt + 1) * vw],
              V_d[tt * 128:(tt + 1) * 128, 0:vw], tag=f"{tag}ldv")

    def bounds_for(src_d, out_col0):
        for p in range(npair):
            for tb in range(NTB):
                B.dma("sp", qs[:, 0:TB], src_d[p, :, tb * TB:(tb + 1) * TB],
                      tag=f"{tag}ldb")
                B.act(qb_sq[:, 0:TB], qs[:, 0:TB], AF.Square, tag=f"{tag}bsq")
                pa = bank[0][0:1, 0:TB]
                pb = bank[1][0:1, 0:TB]
                B.mm(pa, ones64[0:64, 0:1], qb_sq[0:64, 0:TB], start=True, stop=True,
                     tile_position=(0, 0), tag=f"{tag}bn")
                B.mm(pb, ones64[64:128, 0:1], qb_sq[64:128, 0:TB], start=True, stop=True,
                     tile_position=(64, 0), tag=f"{tag}bn")
                for hh, pp in ((0, pa), (1, pb)):
                    col = out_col0 + 2 * p + hh
                    B.op("dve", "reduce_max", (tmr[0:1, hh:hh + 1], pp), dict(axis=AX.X),
                         reads=(pp,), writes=(tmr[0:1, hh:hh + 1],), tag=f"{tag}brd")
                    B.op("dve", "tensor_max",
                         (maxr[0:1, col:col + 1], maxr[0:1, col:col + 1], tmr[0:1, hh:hh + 1]),
                         {}, reads=(maxr[0:1, col:col + 1], tmr[0:1, hh:hh + 1]),
                         writes=(maxr[0:1, col:col + 1],), tag=f"{tag}bmx")

    bounds_for(Q_d, 0)
    bounds_for(K_d, nheads)
    B.act(maxr[0:1, 0:2 * nheads], maxr[0:1, 0:2 * nheads], AF.Sqrt, tag=f"{tag}bsqrt")
    B.op("dve", "tensor_mul",
         (maxr[0:1, 0:nheads], maxr[0:1, 0:nheads], maxr[0:1, nheads:2 * nheads]),
         {}, reads=(maxr[0:1, 0:2 * nheads],), writes=(maxr[0:1, 0:nheads],), tag=f"{tag}bc")
    B.op("dve", "tensor_scalar",
         (maxr[0:1, 0:nheads], maxr[0:1, 0:nheads], margin, -1.0, ALU.add, ALU.mult),
         {}, reads=(maxr[0:1, 0:nheads],), writes=(maxr[0:1, 0:nheads],), tag=f"{tag}bneg")
    for h in range(nheads):
        pb = bank[0][:, 0:1]
        B.mm(pb, C.onesrowf[0:1, :], maxr[0:1, h:h + 1], start=True, stop=True,
             tag=f"{tag}brep")
        B.op("dve", "tensor_copy", (negc[:, h:h + 1], pb), {},
             reads=(pb,), writes=(negc[:, h:h + 1],), tag=f"{tag}bcp")

    for p in range(npair):
        for qb in range(NTB):
            nkc = (qb + 1) * 4 if causal else NKT
            diag0 = qb * 4 if causal else None
            B.dma("sp", qs[:, 0:TB], Q_d[p, :, qb * TB:(qb + 1) * TB], tag=f"{tag}ldq")
            started = set()

            def emit_scores(kc):
                a = (kc - diag0) if (causal and kc >= diag0) else -1
                q0 = a * 128 if a > 0 else 0
                nq = TB - q0
                sA = sbank_of(0, kc)[:, 0:nq]
                sB = sbank_of(1, kc)[:, 0:nq]
                kslice = ks[:, p * T + kc * 128: p * T + (kc + 1) * 128]
                B.mm(sA, kslice[0:64, :], qs[0:64, q0:TB], start=True, stop=(a < 0),
                     tile_position=(0, 0), tag=f"{tag}sA")
                B.mm(sB, kslice[64:128, :], qs[64:128, q0:TB], start=True, stop=(a < 0),
                     tile_position=(64, 0), tag=f"{tag}sB")
                if a >= 0:
                    B.mm(sbank_of(0, kc)[:, 0:128], C.trimask[:, :], C.identb[:, :],
                         start=False, stop=True, tag=f"{tag}mk")
                    B.mm(sbank_of(1, kc)[:, 0:128], C.trimask[:, :], C.identb[:, :],
                         start=False, stop=True, tag=f"{tag}mk")

            def emit_exp(kc):
                a = (kc - diag0) if (causal and kc >= diag0) else -1
                q0 = a * 128 if a > 0 else 0
                nq = TB - q0
                for hh in range(2):
                    h = 2 * p + hh
                    sbank = sbank_of(hh, kc)[:, 0:nq]
                    slot = pt[:, (kc % 2) * 2 * TB + hh * TB: (kc % 2) * 2 * TB + hh * TB + nq]
                    B.act(slot, sbank, AF.Exp, bias=negc[:, h:h + 1], tag=f"{tag}exp")

            def emit_av(kc):
                a = (kc - diag0) if (causal and kc >= diag0) else -1
                q0 = a * 128 if a > 0 else 0
                nq = TB - q0
                for hh in range(2):
                    h = 2 * p + hh
                    slot = pt[:, (kc % 2) * 2 * TB + hh * TB: (kc % 2) * 2 * TB + hh * TB + nq]
                    vbase = kc * vw + h * vstride
                    c0 = 0
                    for (obk, M) in obanks_of(hh):
                        st_ = id(obk) not in started
                        started.add(id(obk))
                        B.mm(obk[0:M, q0:TB], vs[:, vbase + c0: vbase + c0 + M], slot,
                             start=st_, stop=(kc == nkc - 1), tag=f"{tag}av")
                        c0 += M

            emit_scores(0)
            emit_exp(0)
            for kc in range(1, nkc):
                emit_scores(kc)
                emit_exp(kc)
                emit_av(kc - 1)
            emit_av(nkc - 1)
            for hh in range(2):
                h = 2 * p + hh
                obs = obanks_of(hh)
                dbank = obs[0][0]
                drow_p = 64 if big_v else dv
                B.op("dve", "reciprocal", (rrow[0:1, 0:TB],
                                           dbank[drow_p:drow_p + 1, 0:TB]), {},
                     reads=(dbank[drow_p:drow_p + 1, 0:TB],), writes=(rrow[0:1, 0:TB],),
                     tag=f"{tag}rc")
                pbc = (bank[2] if big_v else bank[6])[:, 0:TB]
                B.mm(pbc, C.onesrowf[0:1, :], rrow[0:1, 0:TB], start=True, stop=True,
                     tag=f"{tag}bcm")
                B.op("dve", "tensor_copy", (bcf[:, 0:TB], pbc), {},
                     reads=(pbc,), writes=(bcf[:, 0:TB],), tag=f"{tag}bce")
                if big_v:
                    B.op("dve", "tensor_mul", (ost[0:64, 0:TB], obs[0][0][0:64, 0:TB],
                                               bcf[0:64, 0:TB]), {},
                         reads=(obs[0][0][0:64, 0:TB], bcf[0:64, 0:TB]),
                         writes=(ost[0:64, 0:TB],), tag=f"{tag}nr1")
                    B.op("dve", "tensor_mul", (ost[64:128, 0:TB], obs[1][0][0:64, 0:TB],
                                               bcf[64:128, 0:TB]), {},
                         reads=(obs[1][0][0:64, 0:TB], bcf[64:128, 0:TB]),
                         writes=(ost[64:128, 0:TB],), tag=f"{tag}nr2")
                    B.dma("act", O_d[h, :, qb * TB:(qb + 1) * TB], ost[:, 0:TB],
                          tag=f"{tag}sto")
                else:
                    r0 = (h * dv) % 128
                    B.op("dve", "tensor_mul", (ost[0:dv, 0:TB],
                                               obs[0][0][0:dv, 0:TB], bcf[0:dv, 0:TB]),
                         {}, reads=(obs[0][0][0:dv, 0:TB], bcf[0:dv, 0:TB]),
                         writes=(ost[0:dv, 0:TB],), tag=f"{tag}nr")
                    B.dma("act", O_d[(h * dv) // 128, r0:r0 + dv, qb * TB:(qb + 1) * TB],
                          ost[0:dv, 0:TB], tag=f"{tag}sto")



def attn_out_proj(B, C, O_d, W_d, xin_d, xout_d, tag, bias_pvcol=None,
                  extra_add_d=None, xin_dtype=F32, out_dtype=F32, ncol=D):
    u = B.uid()
    nic = ncol // 128
    ofm = B.sbuf(f"{tag}_of{u}", [128, nic * TB], BF16)
    ws = B.sbuf(f"{tag}_w{u}", [128, nic * D], BF16)
    xs = B.sbuf(f"{tag}_xs{u}", [128, TB], xin_dtype)
    es = B.sbuf(f"{tag}_es{u}", [128, TB], BF16) if extra_add_d is not None else None
    outs = B.sbuf(f"{tag}_os{u}", [128, TB], out_dtype)
    w_view = W_d.rearrange("(kc kp) n -> kc kp n", kp=128)
    for kc in range(nic):
        B.dma("pool", ws[:, kc * D:(kc + 1) * D], w_view[kc, :, 0:D], tag=f"{tag}ldw")
    bank = C.bank
    bkb = C.bankb
    for tb in range(NTB):
        t0 = tb * TB
        for fc in range(nic):
            B.dma("sp", ofm[:, fc * TB:(fc + 1) * TB], O_d[fc, :, t0:t0 + TB],
                  tag=f"{tag}ldo")
        for oc in range(NFC):
            pb = bank[oc % 4][:, 0:TB]
            for kc in range(nic):
                B.mm(pb, ws[:, kc * D + oc * 128: kc * D + (oc + 1) * 128],
                     ofm[:, kc * TB:(kc + 1) * TB],
                     start=(kc == 0), stop=(kc == nic - 1), tag=f"{tag}mm")
            B.dma("sp", xs[:, 0:TB], xin_d[oc, :, t0:t0 + TB], tag=f"{tag}ldx")
            if bias_pvcol is not None:
                B.op("dve", "scalar_tensor_tensor",
                     (outs[:, 0:TB], pb, C.pv(bias_pvcol + oc), xs[:, 0:TB],
                      ALU.add, ALU.add), {},
                     reads=(pb, C.pv(bias_pvcol + oc), xs[:, 0:TB]),
                     writes=(outs[:, 0:TB],), tag=f"{tag}ev")
            else:
                B.op("dve", "tensor_add", (outs[:, 0:TB], pb, xs[:, 0:TB]), {},
                     reads=(pb, xs[:, 0:TB]), writes=(outs[:, 0:TB],), tag=f"{tag}ev")
            if extra_add_d is not None:
                B.dma("sp", es[:, 0:TB], extra_add_d[oc, :, t0:t0 + TB], tag=f"{tag}lde")
                B.op("dve", "tensor_add", (outs[:, 0:TB], outs[:, 0:TB], es[:, 0:TB]), {},
                     reads=(outs[:, 0:TB], es[:, 0:TB]), writes=(outs[:, 0:TB],),
                     tag=f"{tag}ev2")
            B.dma("act", xout_d[oc, :, t0:t0 + TB], outs[:, 0:TB], tag=f"{tag}st")


def phase_l0attn_av(B, C, Tn, cr):
    attention(B, C, Tn["Q1"], Tn["K1"], Tn["V1"], Tn["O1"], 16, 64, "p2")


def phase_l0attn_out(B, C, Tn, cr):
    attn_out_proj(B, C, Tn["O1"], Tn["L0_attn_W_out"], Tn["x0"], Tn["x1"], "p3",
                  bias_pvcol=cr.pvt_index["l0a_bout"])





def moe_gate(B, C, h_d, wgate_d, G_d, tag, ebias_bc_d=None):
    u = B.uid()
    ws = B.sbuf(f"{tag}_w{u}", [128, NFC * 8], BF16)
    hs = B.sbuf(f"{tag}_h{u}", [128, NFC * 128], BF16)
    pr = B.sbuf(f"{tag}_p{u}", [128, 8], F32)
    q2 = B.sbuf(f"{tag}_q2{u}", [128, 8], F32)
    eq1 = B.sbuf(f"{tag}_e1{u}", [128, 8], F32)
    eq2 = B.sbuf(f"{tag}_e2{u}", [128, 8], F32)
    gout = B.sbuf(f"{tag}_g{u}", [128, 8], F32)
    col = B.sbuf(f"{tag}_c{u}", [128, 6], F32)
    eb = None
    if ebias_bc_d is not None:
        eb = B.sbuf(f"{tag}_eb{u}", [128, 8], F32)
        B.dma("pool", eb[:, :], ebias_bc_d[:, :], tag=f"{tag}ldeb")
    lg = B.sbuf(f"{tag}_lg{u}", [128, 8], F32)
    w_view = wgate_d.rearrange("(kc kp) n -> kc kp n", kp=128)
    for kc in range(NFC):
        B.dma("pool", ws[:, kc * 8:(kc + 1) * 8], w_view[kc, :, 0:8], tag=f"{tag}ldw")
    bank = C.bank
    for tt in range(T // 128):
        t0 = tt * 128
        for kc in range(NFC):
            B.dma("sp", hs[:, kc * 128:(kc + 1) * 128], h_d[kc, :, t0:t0 + 128],
                  tag=f"{tag}ldh")
        pb = bank[tt % 2][:, 0:8]
        for kc in range(NFC):
            B.mm(pb, hs[:, kc * 128:(kc + 1) * 128], ws[:, kc * 8:(kc + 1) * 8],
                 start=(kc == 0), stop=(kc == NFC - 1), tag=f"{tag}mm")
        if eb is not None:
            B.op("dve", "tensor_add", (lg[:, :], pb, eb[:, :]), {},
                 reads=(pb, eb[:, :]), writes=(lg[:, :],), tag=f"{tag}eb")
            lsrc = lg[:, :]
        else:
            B.op("dve", "tensor_copy", (lg[:, :], pb), {},
                 reads=(pb,), writes=(lg[:, :],), tag=f"{tag}cp")
            lsrc = lg[:, :]
        B.op("dve", "reduce_max", (col[:, 0:1], lsrc), dict(axis=AX.X, negate=True),
             reads=(lsrc,), writes=(col[:, 0:1],), tag=f"{tag}mx")
        B.act(pr[:, :], lsrc, AF.Exp, bias=col[:, 0:1], tag=f"{tag}ex")
        B.op("dve", "reduce_sum", (col[:, 1:2], pr[:, :]), dict(axis=AX.X),
             reads=(pr[:, :],), writes=(col[:, 1:2],), tag=f"{tag}sm")
        B.op("dve", "reciprocal", (col[:, 2:3], col[:, 1:2]), {},
             reads=(col[:, 1:2],), writes=(col[:, 2:3],), tag=f"{tag}rc")
        B.op("dve", "tensor_scalar", (pr[:, :], pr[:, :], col[:, 2:3], None, ALU.mult),
             {}, reads=(pr[:, :], col[:, 2:3]), writes=(pr[:, :],), tag=f"{tag}pn")
        B.op("dve", "reduce_max", (col[:, 3:4], pr[:, :]), dict(axis=AX.X),
             reads=(pr[:, :],), writes=(col[:, 3:4],), tag=f"{tag}m1")
        B.op("dve", "tensor_scalar", (eq1[:, :], pr[:, :], col[:, 3:4], None, ALU.is_equal),
             {}, reads=(pr[:, :], col[:, 3:4]), writes=(eq1[:, :],), tag=f"{tag}e1")
        B.op("dve", "scalar_tensor_tensor", (q2[:, :], eq1[:, :], -2.0, pr[:, :],
                                             ALU.mult, ALU.add), {},
             reads=(eq1[:, :], pr[:, :]), writes=(q2[:, :],), tag=f"{tag}q2")
        B.op("dve", "reduce_max", (col[:, 4:5], q2[:, :]), dict(axis=AX.X),
             reads=(q2[:, :],), writes=(col[:, 4:5],), tag=f"{tag}m2")
        B.op("dve", "tensor_scalar", (eq2[:, :], q2[:, :], col[:, 4:5], None, ALU.is_equal),
             {}, reads=(q2[:, :], col[:, 4:5]), writes=(eq2[:, :],), tag=f"{tag}e2")
        B.op("dve", "tensor_add", (col[:, 5:6], col[:, 3:4], col[:, 4:5]), {},
             reads=(col[:, 3:5],), writes=(col[:, 5:6],), tag=f"{tag}dn")
        B.op("dve", "reciprocal", (col[:, 5:6], col[:, 5:6]), {},
             reads=(col[:, 5:6],), writes=(col[:, 5:6],), tag=f"{tag}rd")
        B.op("dve", "tensor_mul", (col[:, 3:4], col[:, 3:4], col[:, 5:6]), {},
             reads=(col[:, 3:4], col[:, 5:6]), writes=(col[:, 3:4],), tag=f"{tag}a1")
        B.op("dve", "tensor_mul", (col[:, 4:5], col[:, 4:5], col[:, 5:6]), {},
             reads=(col[:, 4:5], col[:, 5:6]), writes=(col[:, 4:5],), tag=f"{tag}a2")
        B.op("dve", "tensor_scalar", (gout[:, :], eq1[:, :], col[:, 3:4], None, ALU.mult),
             {}, reads=(eq1[:, :], col[:, 3:4]), writes=(gout[:, :],), tag=f"{tag}g1")
        B.op("dve", "scalar_tensor_tensor", (gout[:, :], eq2[:, :], col[:, 4:5],
                                             gout[:, :], ALU.mult, ALU.add), {},
             reads=(eq2[:, :], col[:, 4:5], gout[:, :]), writes=(gout[:, :],), tag=f"{tag}g2")
        B.dma("act", G_d[t0:t0 + 128, 0:8], gout[:, :], tag=f"{tag}st")



def add_streams(B, C, a_d, b_d, out_d, tag, a_dtype=F32):
    u = B.uid()
    xa = B.sbuf(f"{tag}_a{u}", [128, TB], a_dtype)
    xb = B.sbuf(f"{tag}_b{u}", [128, TB], BF16)
    xo = B.sbuf(f"{tag}_o{u}", [128, TB], F32)
    for fc in range(NFC):
        for tb in range(NTB):
            t0 = tb * TB
            B.dma("sp", xa[:, 0:TB], a_d[fc, :, t0:t0 + TB], tag=f"{tag}la")
            B.dma("sp", xb[:, 0:TB], b_d[fc, :, t0:t0 + TB], tag=f"{tag}lb")
            B.op("dve", "tensor_add", (xo[:, 0:TB], xa[:, 0:TB], xb[:, 0:TB]), {},
                 reads=(xa[:, 0:TB], xb[:, 0:TB]), writes=(xo[:, 0:TB],), tag=f"{tag}ad")
            B.dma("act", out_d[fc, :, t0:t0 + TB], xo[:, 0:TB], tag=f"{tag}st")



def moe_dense_l0(B, C, h_d, w1_d, w2_d, G_d, x2_d, tag, b1cols, b2cols,
                 dff=4096, nexp=8):
    u = B.uid()
    TQ = 512
    ndf = dff // 128
    ws1 = B.sbuf(f"{tag}_w1{u}", [128, NFC * dff], BF16)
    ws2 = B.sbuf(f"{tag}_w2{u}", [128, ndf * D], BF16)
    hs = B.sbuf(f"{tag}_h{u}", [128, NFC * TQ], BF16)
    he = B.sbuf(f"{tag}_he{u}", [128, ndf * TQ], BF16)
    gt = B.sbuf(f"{tag}_gt{u}", [128, 8], F32)
    gtr = B.sbuf(f"{tag}_gr{u}", [8, TQ], F32)
    gbc = B.sbuf(f"{tag}_gb{u}", [128, TQ], F32)
    cst = B.sbuf(f"{tag}_cs{u}", [128, TQ], F32)
    esel = B.sbuf(f"{tag}_es{u}", [8, 8 * 128], F32)
    B.dma("pool", esel[:, :], C.esel_d[:, :], tag=f"{tag}lde")
    bank = C.bank
    w1v = w1_d.rearrange("e (kc kp) n -> e kc kp n", kp=128)
    w2v = w2_d.rearrange("e (kc kp) n -> e kc kp n", kp=128)
    for e in range(nexp):
        for kc in range(NFC):
            B.dma("pool", ws1[:, kc * dff:(kc + 1) * dff], w1v[e, kc, :, 0:dff],
                  tag=f"{tag}lw1")
        for kc in range(ndf):
            B.dma("pool", ws2[:, kc * D:(kc + 1) * D], w2v[e, kc, :, 0:D],
                  tag=f"{tag}lw2")
        for tq in range(T // TQ):
            t0 = tq * TQ
            for kc in range(NFC):
                B.dma("sp", hs[:, kc * TQ:(kc + 1) * TQ], h_d[kc, :, t0:t0 + TQ],
                      tag=f"{tag}ldh")
            for jj in range(TQ // 128):
                B.dma("sp", gt[:, 0:8], G_d[t0 + jj * 128: t0 + (jj + 1) * 128, 0:8],
                      tag=f"{tag}ldg")
                pt_ = C.bank[6][0:8, 0:128]
                B.transpose(pt_, gt[:, 0:8], C.identf[:, :], tag=f"{tag}gtr")
                B.op("dve", "tensor_copy", (gtr[0:8, jj * 128:(jj + 1) * 128], pt_), {},
                     reads=(pt_,), writes=(gtr[0:8, jj * 128:(jj + 1) * 128],),
                     tag=f"{tag}gc")
            pgb = bank[6][:, 0:TQ]
            B.mm(pgb, esel[0:8, e * 128:(e + 1) * 128], gtr[0:8, 0:TQ],
                 start=True, stop=True, tag=f"{tag}gbm")
            B.op("dve", "tensor_copy", (gbc[:, 0:TQ], pgb), {},
                 reads=(pgb,), writes=(gbc[:, 0:TQ],), tag=f"{tag}gbe")
            for fu in range(ndf):
                pb = bank[4 + fu % 2][:, 0:TQ]
                for kc in range(NFC):
                    B.mm(pb, ws1[:, kc * dff + fu * 128: kc * dff + (fu + 1) * 128],
                         hs[:, kc * TQ:(kc + 1) * TQ],
                         start=(kc == 0), stop=(kc == NFC - 1), tag=f"{tag}up")
                B.act(he[:, fu * TQ:(fu + 1) * TQ], pb, AF.Gelu,
                      bias=C.pv(b1cols[e] + fu), tag=f"{tag}gl")
            for oc in range(NFC):
                pb = bank[oc % 4][:, 0:TQ]
                for kc in range(ndf):
                    B.mm(pb, ws2[:, kc * D + oc * 128: kc * D + (oc + 1) * 128],
                         he[:, kc * TQ:(kc + 1) * TQ],
                         start=(kc == 0), stop=(kc == ndf - 1),
                         tag=f"{tag}dn")
                B.op("dve", "scalar_tensor_tensor",
                     (cst[:, 0:TQ], pb, C.pv(b2cols[e] + oc), gbc[:, 0:TQ],
                      ALU.add, ALU.mult), {},
                     reads=(pb, C.pv(b2cols[e] + oc), gbc[:, 0:TQ]),
                     writes=(cst[:, 0:TQ],), tag=f"{tag}ev")
                B.dma("pool", x2_d[oc, :, t0:t0 + TQ], cst[:, 0:TQ],
                      accum_op=ALU.add, tag=f"{tag}ac")


def phase_l0ffn_qkv(B, C, Tn, cr):
    ln_full(B, C, Tn["x1"], Tn["u1"], "p4ln")
    ln_full(B, C, Tn["u1"], Tn["h1"], "p4l2", g_rowcol=cr.row_index["l0f_g1"],
            b_rowcol=cr.row_index["l0f_b1"], src_dtype=BF16)
    proj_fm(B, C, Tn["h1"], Tn["L0_ffn_w_qkv"], Tn["Q2"], D, "p4q",
            scale=0.125, col_off=0)
    proj_fm(B, C, Tn["h1"], Tn["L0_ffn_w_qkv"], Tn["K2"], D, "p4k", col_off=D)
    proj_tm_v(B, C, Tn["h1"], Tn["L0_ffn_w_qkv"], Tn["V2"], D, 16, 64, "p4v",
              col_off=2 * D)


def phase_l0ffn_av(B, C, Tn, cr):
    attention(B, C, Tn["Q2"], Tn["K2"], Tn["V2"], Tn["O2"], 16, 64, "p5")


def phase_l0ffn_mid(B, C, Tn, cr):
    attn_out_proj(B, C, Tn["O2"], Tn["L0_ffn_w_o"], Tn["u1"], Tn["xr"], "p6",
                  xin_dtype=BF16, out_dtype=BF16)
    ln_full(B, C, Tn["xr"], Tn["h2"], "p6l2", g_rowcol=cr.row_index["l0f_g2"],
            b_rowcol=cr.row_index["l0f_b2"], src_dtype=BF16)
    moe_gate(B, C, Tn["h2"], Tn["L0_ffn_w_gate"], Tn["G1"], "p6g")


def phase_l0ffn_moe(B, C, Tn, cr):
    add_streams(B, C, Tn["x1"], Tn["xr"], Tn["x2"], "p7i")
    b1cols = [cr.pvt_index[f"l0f_b1_{e}"] for e in range(8)]
    b2cols = [cr.pvt_index[f"l0f_b2_{e}"] for e in range(8)]
    moe_dense_l0(B, C, Tn["h2"], Tn["L0_ffn_w1"], Tn["L0_ffn_w2"], Tn["G1"],
                 Tn["x2"], "p7", b1cols, b2cols)





def proj_tm(B, C, h_d, w_d, out_d, dout, tag, col_off=0, scale=None, wT=False):
    u = B.uid()
    ws = B.sbuf(f"{tag}_w{u}", [128, NFC * dout], BF16)
    hs = B.sbuf(f"{tag}_h{u}", [128, NFC * 128], BF16)
    os_ = B.sbuf(f"{tag}_o{u}", [128, dout], BF16)
    w_view = w_d.rearrange("(kc kp) n -> kc kp n", kp=128)
    for kc in range(NFC):
        B.dma("pool", ws[:, kc * dout:(kc + 1) * dout], w_view[kc, :, col_off:col_off + dout],
              tag=f"{tag}ldw")
    bank = C.bank
    nnc = (dout + 511) // 512
    for tt in range(T // 128):
        t0 = tt * 128
        for kc in range(NFC):
            B.dma("sp", hs[:, kc * 128:(kc + 1) * 128], h_d[kc, :, t0:t0 + 128],
                  tag=f"{tag}ldh")
        for nc_ in range(nnc):
            n0 = nc_ * 512
            nw = min(512, dout - n0)
            pb = bank[nc_ % 4][:, 0:nw]
            for kc in range(NFC):
                B.mm(pb, hs[:, kc * 128:(kc + 1) * 128], ws[:, kc * dout + n0: kc * dout + n0 + nw],
                     start=(kc == 0), stop=(kc == NFC - 1), tag=f"{tag}mm")
            if scale is not None:
                B.op("dve", "tensor_scalar", (os_[:, n0:n0 + nw], pb, scale, None, ALU.mult),
                     {}, reads=(pb,), writes=(os_[:, n0:n0 + nw],), tag=f"{tag}ev")
            else:
                B.op("dve", "tensor_copy", (os_[:, n0:n0 + nw], pb), {},
                     reads=(pb,), writes=(os_[:, n0:n0 + nw],), tag=f"{tag}ev")
        B.dma("act", out_d[t0:t0 + 128, 0:dout], os_[:, 0:dout], tag=f"{tag}st")



def rope_tm(B, C, x_d, cos_d, sin_d, width, ranges, nfreq, tag):
    u = B.uid()
    xt = B.sbuf(f"{tag}_x{u}", [128, width], BF16)
    cs = B.sbuf(f"{tag}_c{u}", [128, nfreq], F32)
    sn = B.sbuf(f"{tag}_s{u}", [128, nfreq], F32)
    t1 = B.sbuf(f"{tag}_t1{u}", [128, nfreq], F32)
    t2 = B.sbuf(f"{tag}_t2{u}", [128, nfreq], F32)
    t3 = B.sbuf(f"{tag}_t3{u}", [128, nfreq], F32)
    for tt in range(T // 128):
        t0 = tt * 128
        B.dma("sp", xt[:, 0:width], x_d[t0:t0 + 128, 0:width], tag=f"{tag}ldx")
        B.dma("sp", cs[:, :], cos_d[t0:t0 + 128, 0:nfreq], tag=f"{tag}ldc")
        B.dma("sp", sn[:, :], sin_d[t0:t0 + 128, 0:nfreq], tag=f"{tag}lds")
        for c0 in ranges:
            ev = xt[:, c0: c0 + 2 * nfreq: 2]
            od = xt[:, c0 + 1: c0 + 2 * nfreq: 2]
            rr = xt[:, c0:c0 + 2 * nfreq]
            B.op("dve", "tensor_mul", (t1[:, :], ev, cs[:, :]), {},
                 reads=(rr, cs[:, :]), writes=(t1[:, :],), tag=f"{tag}a")
            B.op("dve", "tensor_mul", (t2[:, :], od, sn[:, :]), {},
                 reads=(rr, sn[:, :]), writes=(t2[:, :],), tag=f"{tag}b")
            B.op("dve", "tensor_sub", (t1[:, :], t1[:, :], t2[:, :]), {},
                 reads=(t1[:, :], t2[:, :]), writes=(t1[:, :],), tag=f"{tag}c")
            B.op("dve", "tensor_mul", (t2[:, :], od, cs[:, :]), {},
                 reads=(rr, cs[:, :]), writes=(t2[:, :],), tag=f"{tag}d")
            B.op("dve", "tensor_mul", (t3[:, :], ev, sn[:, :]), {},
                 reads=(rr, sn[:, :]), writes=(t3[:, :],), tag=f"{tag}e")
            B.op("dve", "tensor_add", (t2[:, :], t2[:, :], t3[:, :]), {},
                 reads=(t2[:, :], t3[:, :]), writes=(t2[:, :],), tag=f"{tag}f")
            B.op("dve", "tensor_copy", (ev, t1[:, :]), {},
                 reads=(t1[:, :],), writes=(rr,), tag=f"{tag}g")
            B.op("dve", "tensor_copy", (od, t2[:, :]), {},
                 reads=(t2[:, :],), writes=(rr,), tag=f"{tag}h")
        B.dma("act", x_d[t0:t0 + 128, 0:width], xt[:, 0:width], tag=f"{tag}st")



def tm_to_fm(B, C, x_d, out_d, w, tag, col_groups=None):
    u = B.uid()
    nch = (w // 128) if col_groups is None else len(col_groups)
    xt = B.sbuf(f"{tag}_x{u}", [128, w], BF16)
    st = B.sbuf(f"{tag}_s{u}", [128, 128], BF16)
    bkb = C.bankb
    for tt in range(T // 128):
        t0 = tt * 128
        B.dma("sp", xt[:, 0:w], x_d[t0:t0 + 128, 0:w], tag=f"{tag}ldx")
        for ch in range(nch):
            if col_groups is None:
                segs = [(ch * 128, 128)]
            else:
                segs = col_groups[ch]
            pt_ = bkb[:, 0:128]
            p0 = 0
            for (s0, span) in segs:
                B.transpose(bkb[p0:p0 + span, 0:128], xt[:, s0:s0 + span],
                            C.identb[:, :], tag=f"{tag}tr")
                p0 += span
            B.op("dve", "tensor_copy", (st[:, 0:128], pt_), {},
                 reads=(pt_,), writes=(st[:, 0:128],), tag=f"{tag}cp")
            B.dma("act", out_d[ch, :, t0:t0 + 128], st[:, 0:128], tag=f"{tag}st")



LAMBDA_INIT = 0.8 - 0.6 * math.exp(-0.3 * 12)


def l1attn_combine(B, C, o1_d, o2_d, lam_d, lnw_d, lnb_d, y_d, tag):
    u = B.uid()
    o1 = B.sbuf(f"{tag}_o1{u}", [128, TB], BF16)
    o2 = B.sbuf(f"{tag}_o2{u}", [128, TB], BF16)
    oc = B.sbuf(f"{tag}_oc{u}", [128, TB], F32)
    sq = B.sbuf(f"{tag}_sq{u}", [128, TB], F32)
    bcf = B.sbuf(f"{tag}_bc{u}", [128, TB], F32)
    yt = B.sbuf(f"{tag}_y{u}", [128, TB], BF16)
    rrow = B.sbuf(f"{tag}_rr{u}", [1, TB], F32)
    lv = B.sbuf(f"{tag}_lv{u}", [1, 4 * 64], F32)
    dots = B.sbuf(f"{tag}_dt{u}", [1, 4], F32)
    lamr = B.sbuf(f"{tag}_lm{u}", [1, 1], F32)
    nlam = B.sbuf(f"{tag}_nl{u}", [128, 1], F32)
    lnw = B.sbuf(f"{tag}_lw{u}", [128, 2], F32)
    onesc = B.sbuf(f"{tag}_1c{u}", [128, 1], F32)
    B.op("pool", "memset", (onesc[:, :], 1.0), {}, writes=(onesc[:, :],))
    B.dma("pool", lnw[:, 0:1], lnw_d[:, 0:1], tag=f"{tag}lw")
    B.dma("pool", lnw[:, 1:2], lnb_d[:, 0:1], tag=f"{tag}lb")
    B.dma("pool", lv[0:1, :], lam_d[0:1, :], tag=f"{tag}ll")
    B.op("dve", "tensor_scalar", (lnw[:, 0:2], lnw[:, 0:2], 1.0 - LAMBDA_INIT, None,
                                  ALU.mult), {},
         reads=(lnw[:, 0:2],), writes=(lnw[:, 0:2],), tag=f"{tag}sw")
    for i, (a, b) in enumerate(((0, 64), (128, 192))):
        B.op("dve", "tensor_mul", (lv[0:1, a:a + 64], lv[0:1, a:a + 64], lv[0:1, b:b + 64]),
             {}, reads=(lv[0:1, :],), writes=(lv[0:1, a:a + 64],), tag=f"{tag}m{i}")
        B.op("dve", "reduce_sum", (dots[0:1, i:i + 1], lv[0:1, a:a + 64]), dict(axis=AX.X),
             reads=(lv[0:1, a:a + 64],), writes=(dots[0:1, i:i + 1],), tag=f"{tag}r{i}")
    B.act(dots[0:1, 0:2], dots[0:1, 0:2], AF.Exp, tag=f"{tag}ex")
    B.op("dve", "tensor_sub", (lamr[0:1, 0:1], dots[0:1, 0:1], dots[0:1, 1:2]), {},
         reads=(dots[0:1, 0:2],), writes=(lamr[0:1, 0:1],), tag=f"{tag}sub")
    B.op("dve", "tensor_scalar", (lamr[0:1, 0:1], lamr[0:1, 0:1], LAMBDA_INIT, -1.0,
                                  ALU.add, ALU.mult), {},
         reads=(lamr[0:1, 0:1],), writes=(lamr[0:1, 0:1],), tag=f"{tag}neg")
    pb0 = C.bank[5][:, 0:1]
    B.mm(pb0, C.onesrowf[0:1, :], lamr[0:1, 0:1], start=True, stop=True, tag=f"{tag}rep")
    B.op("dve", "tensor_copy", (nlam[:, 0:1], pb0), {}, reads=(pb0,),
         writes=(nlam[:, 0:1],), tag=f"{tag}cp")
    inv128 = 1.0 / 128.0
    bank = C.bank
    for h in range(8):
        for tb in range(NTB):
            t0 = tb * TB
            B.dma("sp", o1[:, 0:TB], o1_d[h, :, t0:t0 + TB], tag=f"{tag}l1")
            B.dma("sp", o2[:, 0:TB], o2_d[h, :, t0:t0 + TB], tag=f"{tag}l2")
            B.op("dve", "scalar_tensor_tensor", (oc[:, 0:TB], o2[:, 0:TB], nlam[:, 0:1],
                                                 o1[:, 0:TB], ALU.mult, ALU.add), {},
                 reads=(o2[:, 0:TB], nlam[:, 0:1], o1[:, 0:TB]), writes=(oc[:, 0:TB],),
                 tag=f"{tag}cmb")
            B.act(sq[:, 0:TB], oc[:, 0:TB], AF.Square, tag=f"{tag}sq")
            ps = bank[tb % 2][0:1, 0:TB]
            B.mm(ps, onesc[:, 0:1], sq[:, 0:TB], start=True, stop=True, tag=f"{tag}ss")
            B.act(rrow[0:1, 0:TB], ps, AF.Ln, bias=C.pv(C.cr.pvt_index["eps_a8"])[0:1, 0:1],
                  scale=inv128, tag=f"{tag}ln")
            B.act(rrow[0:1, 0:TB], rrow[0:1, 0:TB], AF.Exp, scale=-0.5, tag=f"{tag}ep")
            pbc = bank[2 + tb % 2][:, 0:TB]
            B.mm(pbc, C.onesrowf[0:1, :], rrow[0:1, 0:TB], start=True, stop=True,
                 tag=f"{tag}bm")
            B.op("dve", "tensor_copy", (bcf[:, 0:TB], pbc), {},
                 reads=(pbc,), writes=(bcf[:, 0:TB],), tag=f"{tag}be")
            B.op("dve", "tensor_mul", (oc[:, 0:TB], oc[:, 0:TB], bcf[:, 0:TB]), {},
                 reads=(oc[:, 0:TB], bcf[:, 0:TB]), writes=(oc[:, 0:TB],), tag=f"{tag}n1")
            B.op("dve", "tensor_scalar", (yt[:, 0:TB], oc[:, 0:TB], lnw[:, 0:1],
                                          lnw[:, 1:2], ALU.mult, ALU.add), {},
                 reads=(oc[:, 0:TB], lnw[:, 0:2]), writes=(yt[:, 0:TB],), tag=f"{tag}n2")
            B.dma("act", y_d[h, :, t0:t0 + TB], yt[:, 0:TB], tag=f"{tag}st")



def moe_l1(B, C, h_d, Wg_d, Wu_d, Wd_d, sWg_d, sWu_d, sWd_d, G_d, x4_d, tag,
           dint=512):
    u = B.uid()
    TQ = 512
    ndi = dint // 128
    wsg = B.sbuf(f"{tag}_wg{u}", [128, NFC * dint], BF16)
    wsu = B.sbuf(f"{tag}_wu{u}", [128, NFC * dint], BF16)
    wsd = B.sbuf(f"{tag}_wd{u}", [128, ndi * D], BF16)
    hs = B.sbuf(f"{tag}_h{u}", [128, NFC * TQ], BF16)
    gu = B.sbuf(f"{tag}_gu{u}", [128, ndi * TQ], BF16)
    us = B.sbuf(f"{tag}_us{u}", [128, TQ], F32)
    gt = B.sbuf(f"{tag}_gt{u}", [128, 8], F32)
    gtr = B.sbuf(f"{tag}_gr{u}", [8, TQ], F32)
    gbc = B.sbuf(f"{tag}_gb{u}", [128, TQ], F32)
    cst = B.sbuf(f"{tag}_cs{u}", [128, TQ], F32)
    esel = B.sbuf(f"{tag}_es{u}", [8, 8 * 128], F32)
    B.dma("pool", esel[:, :], C.esel_d[:, :], tag=f"{tag}lde")
    bank = C.bank
    Wgv = Wg_d.rearrange("e (kc kp) n -> e kc kp n", kp=128)
    Wuv = Wu_d.rearrange("e (kc kp) n -> e kc kp n", kp=128)
    Wdv = Wd_d.rearrange("e (kc kp) n -> e kc kp n", kp=128)
    sWgv = sWg_d.rearrange("e (kc kp) n -> e kc kp n", kp=128)
    sWuv = sWu_d.rearrange("e (kc kp) n -> e kc kp n", kp=128)
    sWdv = sWd_d.rearrange("e (kc kp) n -> e kc kp n", kp=128)
    for e in range(-1, 8):
        gv, uv, dv_ = (sWgv, sWuv, sWdv) if e < 0 else (Wgv, Wuv, Wdv)
        ei = 0 if e < 0 else e
        for kc in range(NFC):
            B.dma("pool", wsg[:, kc * dint:(kc + 1) * dint], gv[ei, kc, :, 0:dint],
                  tag=f"{tag}lwg")
            B.dma("pool", wsu[:, kc * dint:(kc + 1) * dint], uv[ei, kc, :, 0:dint],
                  tag=f"{tag}lwu")
        for kc in range(ndi):
            B.dma("pool", wsd[:, kc * D:(kc + 1) * D], dv_[ei, kc, :, 0:D],
                  tag=f"{tag}lwd")
        for tq in range(T // TQ):
            t0 = tq * TQ
            for kc in range(NFC):
                B.dma("sp", hs[:, kc * TQ:(kc + 1) * TQ], h_d[kc, :, t0:t0 + TQ],
                      tag=f"{tag}ldh")
            if e >= 0:
                for jj in range(TQ // 128):
                    B.dma("sp", gt[:, 0:8], G_d[t0 + jj * 128: t0 + (jj + 1) * 128, 0:8],
                          tag=f"{tag}ldg")
                    pt_ = bank[6][0:8, 0:128]
                    B.transpose(pt_, gt[:, 0:8], C.identf[:, :], tag=f"{tag}gtr")
                    B.op("dve", "tensor_copy", (gtr[0:8, jj * 128:(jj + 1) * 128], pt_),
                         {}, reads=(pt_,), writes=(gtr[0:8, jj * 128:(jj + 1) * 128],),
                         tag=f"{tag}gc")
                pgb = bank[6][:, 0:TQ]
                B.mm(pgb, esel[0:8, e * 128:(e + 1) * 128], gtr[0:8, 0:TQ],
                     start=True, stop=True, tag=f"{tag}gbm")
                B.op("dve", "tensor_copy", (gbc[:, 0:TQ], pgb), {},
                     reads=(pgb,), writes=(gbc[:, 0:TQ],), tag=f"{tag}gbe")
            for fu in range(ndi):
                pg = bank[4 + 2 * (fu % 2)][:, 0:TQ] if False else bank[(4, 0)[fu % 2]][:, 0:TQ]
                pu = bank[(5, 1)[fu % 2]][:, 0:TQ]
                for kc in range(NFC):
                    B.mm(pg, wsg[:, kc * dint + fu * 128: kc * dint + (fu + 1) * 128],
                         hs[:, kc * TQ:(kc + 1) * TQ],
                         start=(kc == 0), stop=(kc == NFC - 1), tag=f"{tag}ug")
                for kc in range(NFC):
                    B.mm(pu, wsu[:, kc * dint + fu * 128: kc * dint + (fu + 1) * 128],
                         hs[:, kc * TQ:(kc + 1) * TQ],
                         start=(kc == 0), stop=(kc == NFC - 1), tag=f"{tag}uu")
                B.op("dve", "tensor_copy", (us[:, 0:TQ], pu), {},
                     reads=(pu,), writes=(us[:, 0:TQ],), tag=f"{tag}ue")
                B.act(cst[:, 0:TQ], pg, AF.Silu, tag=f"{tag}si")
                B.op("dve", "tensor_mul", (gu[:, fu * TQ:(fu + 1) * TQ], cst[:, 0:TQ],
                                           us[:, 0:TQ]), {},
                     reads=(cst[:, 0:TQ], us[:, 0:TQ]),
                     writes=(gu[:, fu * TQ:(fu + 1) * TQ],), tag=f"{tag}gm")
            for oc in range(NFC):
                pb = bank[oc % 4][:, 0:TQ]
                for kc in range(ndi):
                    B.mm(pb, wsd[:, kc * D + oc * 128: kc * D + (oc + 1) * 128],
                         gu[:, kc * TQ:(kc + 1) * TQ],
                         start=(kc == 0), stop=(kc == ndi - 1), tag=f"{tag}dn")
                if e >= 0:
                    B.op("dve", "tensor_mul", (cst[:, 0:TQ], pb, gbc[:, 0:TQ]), {},
                         reads=(pb, gbc[:, 0:TQ]), writes=(cst[:, 0:TQ],), tag=f"{tag}ev")
                else:
                    B.op("dve", "tensor_copy", (cst[:, 0:TQ], pb), {},
                         reads=(pb,), writes=(cst[:, 0:TQ],), tag=f"{tag}ev")
                B.dma("pool", x4_d[oc, :, t0:t0 + TQ], cst[:, 0:TQ],
                      accum_op=ALU.add, tag=f"{tag}ac")


def phase_l1attn_qkv(B, C, Tn, cr):
    ln_full(B, C, Tn["x2"], Tn["u2"], "p8ln")
    sc = 64.0 ** (-0.25)
    ranges = [64 * s for s in range(16)]
    ga = [[(256 * p, 64), (256 * p + 128, 64)] for p in range(4)]
    gb = [[(256 * p + 64, 64), (256 * p + 192, 64)] for p in range(4)]
    proj_rope_fm(B, C, Tn["u2"], Tn["L1_attn_WqT"],
                 [(Tn["Q3a"], ga), (Tn["Q3b"], gb)],
                 Tn["rope64_cos"], Tn["rope64_sin"], ranges, 32, "p8q", scale=sc)
    proj_rope_fm(B, C, Tn["u2"], Tn["L1_attn_WkT"],
                 [(Tn["K3a"], ga), (Tn["K3b"], gb)],
                 Tn["rope64_cos"], Tn["rope64_sin"], ranges, 32, "p8k")
    proj_tm_v(B, C, Tn["u2"], Tn["L1_attn_WvT"], Tn["V3"], D, 8, 128, "p8v")


def phase_l1attn_av(B, C, Tn, cr):
    attention(B, C, Tn["Q3a"], Tn["K3a"], Tn["V3"], Tn["O3a"], 8, 128, "p9a",
              causal=False)


def phase_l1attn_av2(B, C, Tn, cr):
    attention(B, C, Tn["Q3b"], Tn["K3b"], Tn["V3"], Tn["O3b"], 8, 128, "p9b",
              causal=False)


def phase_l1attn_out(B, C, Tn, cr):
    l1attn_combine(B, C, Tn["O3a"], Tn["O3b"], Tn["lam_cat"], Tn["lnw_col"],
                   Tn["lnb_col"], Tn["Y3"], "pA")
    attn_out_proj(B, C, Tn["Y3"], Tn["L1_attn_WoT"], Tn["x2"], Tn["x3"], "pAo")


def phase_l1ffn_qkv(B, C, Tn, cr):
    ln_full(B, C, Tn["x3"], Tn["u3"], "pBln")
    ln_full(B, C, Tn["u3"], Tn["h3"], "pBl2", g_rowcol=cr.row_index["l1f_g1"],
            rms=True, src_dtype=BF16)
    ga = [[(128 * p, 128)] for p in range(8)]
    proj_rope_fm(B, C, Tn["h3"], Tn["L1_ffn_Wq"], [(Tn["Q4"], ga)],
                 Tn["rope32_cos"], Tn["rope32_sin"],
                 [64 * h + 32 for h in range(16)], 16, "pBq", scale=0.125)
    proj_tm(B, C, Tn["h3"], Tn["L1_ffn_Wkva"], Tn["ckvt"], 160, "pBkv")
    rope_tm(B, C, Tn["ckvt"], Tn["rope32_cos"], Tn["rope32_sin"], 160, [128], 16, "pBrk")


def phase_l1ffn_kv(B, C, Tn, cr):
    B_ = B
    u = B.uid()
    tag = "pC"
    ck = B.sbuf(f"{tag}_ck{u}", [128, 160], BF16)
    lat = B.sbuf(f"{tag}_lt{u}", [128, 128], BF16)
    wkvb = B.sbuf(f"{tag}_w{u}", [128, 512], BF16)
    kvsb = B.sbuf(f"{tag}_kv{u}", [128, 512], BF16)
    trsb = B.sbuf(f"{tag}_tr{u}", [128, 128], BF16)
    krsb = B.sbuf(f"{tag}_kr{u}", [128, 128], BF16)
    v4sb = B.sbuf(f"{tag}_v4{u}", [128, 16 * 33], BF16)
    B.dma("pool", wkvb[:, :], Tn["L1_ffn_Wkvb"][:, :], tag=f"{tag}lw")
    bank = C.bank
    bkb = C.bankb
    for tt in range(T // 128):
        t0 = tt * 128
        B.dma("sp", ck[:, 0:160], Tn["ckvt"][t0:t0 + 128, 0:160], tag=f"{tag}ldc")
        ptb = bkb[:, 0:128]
        B.transpose(ptb, ck[:, 0:128], C.identb[:, :], tag=f"{tag}t1")
        B.op("dve", "tensor_copy", (lat[:, :], ptb), {}, reads=(ptb,),
             writes=(lat[:, :],), tag=f"{tag}c1")
        pkv = bank[4][:, 0:512]
        B.mm(pkv, lat[:, :], wkvb[:, 0:512], start=True, stop=True, tag=f"{tag}mm")
        B.op("dve", "tensor_copy", (kvsb[:, :], pkv), {}, reads=(pkv,),
             writes=(kvsb[:, :],), tag=f"{tag}c2")
        for h in range(16):
            B.op("dve", "tensor_copy",
                 (v4sb[:, h * 33: h * 33 + 32], kvsb[:, h * 32:(h + 1) * 32]), {},
                 reads=(kvsb[:, h * 32:(h + 1) * 32],),
                 writes=(v4sb[:, h * 33: h * 33 + 32],), tag=f"{tag}v")
        ones_view = v4sb.rearrange("p (h e) -> p h e", e=33)[:, :, 32:33]
        B.op("dve", "memset", (ones_view, 1.0), {}, writes=(v4sb[:, 0:16 * 33],),
             tag=f"{tag}on")
        B.dma("act", Tn["V4"][t0:t0 + 128, 0:16 * 33], v4sb[:, 0:16 * 33], tag=f"{tag}sv")
        for j in range(4):
            B.transpose(bkb[:, 0:128], kvsb[:, j * 128:(j + 1) * 128], C.identb[:, :],
                        tag=f"{tag}t2")
            B.op("dve", "tensor_copy", (trsb[:, :], bkb[:, 0:128]), {},
                 reads=(bkb[:, 0:128],), writes=(trsb[:, :],), tag=f"{tag}c3")
            for i in range(4):
                h = 4 * j + i
                B.dma("act", Tn["K4"][h // 2, (h % 2) * 64:(h % 2) * 64 + 32, t0:t0 + 128],
                      trsb[32 * i:32 * i + 32, :], tag=f"{tag}sk")
        B.transpose(bkb[0:32, 0:128], ck[:, 128:160], C.identb[:, :], tag=f"{tag}t3")
        B.op("dve", "tensor_copy", (krsb[0:32, :], bkb[0:32, 0:128]), {},
             reads=(bkb[0:32, 0:128],), writes=(krsb[0:32, :],), tag=f"{tag}c4")
        for p in range(8):
            B.dma("act", Tn["K4"][p, 32:64, t0:t0 + 128], krsb[0:32, :], tag=f"{tag}skr")
            B.dma("act", Tn["K4"][p, 96:128, t0:t0 + 128], krsb[0:32, :], tag=f"{tag}skr2")


def phase_l1ffn_av(B, C, Tn, cr):
    attention(B, C, Tn["Q4"], Tn["K4"], Tn["V4"], Tn["O4"], 16, 32, "pD")


def phase_l1ffn_mid(B, C, Tn, cr):
    attn_out_proj(B, C, Tn["O4"], Tn["L1f_Wo_sub"], Tn["u3"], Tn["xi"], "pE",
                  xin_dtype=BF16, out_dtype=BF16, ncol=512)
    ln_full(B, C, Tn["xi"], Tn["h4"], "pEl2", g_rowcol=cr.row_index["l1f_g2"],
            rms=True, src_dtype=BF16)
    moe_gate(B, C, Tn["h4"], Tn["L1_ffn_Wgate"], Tn["G2"], "pEg",
             ebias_bc_d=Tn["ebias_bc"])


def phase_l1ffn_moe(B, C, Tn, cr):
    add_streams(B, C, Tn["x3"], Tn["xi"], Tn["x4"], "pFi")
    moe_l1(B, C, Tn["h4"], Tn["L1_ffn_Wg"], Tn["L1_ffn_Wu"], Tn["L1_ffn_Wd"],
           Tn["L1_ffn_sWg"], Tn["L1_ffn_sWu"], Tn["L1_ffn_sWd"], Tn["G2"],
           Tn["x4"], "pF")


PHASES[:] = [
    ("l0attn_qkv", ["x0"], ["u_l0", "Q1", "K1", "V1"],
     ["L0_attn_W_qkv"], ["vbias_l0"], phase_l0attn_qkv),
    ("l0attn_av", ["Q1", "K1", "V1"], ["O1"], [], [], phase_l0attn_av),
    ("l0attn_out", ["O1", "x0"], ["x1"], ["L0_attn_W_out"], [], phase_l0attn_out),
    ("l0ffn_qkv", ["x1"], ["u1", "h1", "Q2", "K2", "V2"], ["L0_ffn_w_qkv"], [], phase_l0ffn_qkv),
    ("l0ffn_av", ["Q2", "K2", "V2"], ["O2"], [], [], phase_l0ffn_av),
    ("l0ffn_mid", ["O2", "u1"], ["xr", "h2", "G1"], ["L0_ffn_w_o", "L0_ffn_w_gate"], [], phase_l0ffn_mid),
    ("l0ffn_moe", ["x1", "xr", "h2", "G1"], ["x2"], ["L0_ffn_w1", "L0_ffn_w2"], [], phase_l0ffn_moe),
    ("l1attn_qkv", ["x2"], ["u2", "q3t", "k3t", "Q3a", "Q3b", "K3a", "K3b", "V3"],
     ["L1_attn_WqT", "L1_attn_WkT", "L1_attn_WvT"],
     ["rope64_cos", "rope64_sin"], phase_l1attn_qkv),
    ("l1attn_av", ["Q3a", "K3a", "V3"], ["O3a"], [], [], phase_l1attn_av),
    ("l1attn_av2", ["Q3b", "K3b", "V3"], ["O3b"], [], [], phase_l1attn_av2),
    ("l1attn_out", ["O3a", "O3b", "x2"], ["Y3", "x3"], ["L1_attn_WoT"],
     ["lam_cat", "lnw_col", "lnb_col"], phase_l1attn_out),
    ("l1ffn_qkv", ["x3"], ["u3", "h3", "q4t", "ckvt", "Q4"],
     ["L1_ffn_Wq", "L1_ffn_Wkva"], ["rope32_cos", "rope32_sin"], phase_l1ffn_qkv),
    ("l1ffn_kv", ["ckvt"], ["K4", "V4"], ["L1_ffn_Wkvb"], [], phase_l1ffn_kv),
    ("l1ffn_av", ["Q4", "K4", "V4"], ["O4"], [], [], phase_l1ffn_av),
    ("l1ffn_mid", ["O4", "u3"], ["xi", "h4", "G2"], ["L1f_Wo_sub", "L1_ffn_Wgate"],
     ["ebias_bc"], phase_l1ffn_mid),
    ("l1ffn_moe", ["x3", "xi", "h4", "G2"], ["x4"],
     ["L1_ffn_Wg", "L1_ffn_Wu", "L1_ffn_Wd", "L1_ffn_sWg", "L1_ffn_sWu", "L1_ffn_sWd"],
     [], phase_l1ffn_moe),
]



def proj_rope_fm(B, C, h_d, w_d, outs_fm, cos_d, sin_d, ranges, nfreq, tag,
                 dout=D, scale=None, wT_note=None):
    u = B.uid()
    ws = B.sbuf(f"{tag}_w{u}", [128, NFC * dout], BF16)
    hs = B.sbuf(f"{tag}_h{u}", [128, NFC * 128], BF16)
    os_ = B.sbuf(f"{tag}_o{u}", [128, dout], BF16)
    cs = B.sbuf(f"{tag}_c{u}", [128, nfreq], F32)
    sn = B.sbuf(f"{tag}_s{u}", [128, nfreq], F32)
    t1 = B.sbuf(f"{tag}_t1{u}", [128, nfreq], F32)
    t2 = B.sbuf(f"{tag}_t2{u}", [128, nfreq], F32)
    t3 = B.sbuf(f"{tag}_t3{u}", [128, nfreq], F32)
    st = B.sbuf(f"{tag}_st{u}", [128, 128], BF16)
    w_view = w_d.rearrange("(kc kp) n -> kc kp n", kp=128)
    for kc in range(NFC):
        B.dma("pool", ws[:, kc * dout:(kc + 1) * dout], w_view[kc, :, 0:dout],
              tag=f"{tag}ldw")
    bank = C.bank
    bkb = C.bankb
    nnc = (dout + 511) // 512
    for tt in range(T // 128):
        t0 = tt * 128
        for kc in range(NFC):
            B.dma("sp", hs[:, kc * 128:(kc + 1) * 128], h_d[kc, :, t0:t0 + 128],
                  tag=f"{tag}ldh")
        B.dma("sp", cs[:, :], cos_d[t0:t0 + 128, 0:nfreq], tag=f"{tag}ldc")
        B.dma("sp", sn[:, :], sin_d[t0:t0 + 128, 0:nfreq], tag=f"{tag}lds")
        for nc_ in range(nnc):
            n0 = nc_ * 512
            nw = min(512, dout - n0)
            pb = bank[nc_ % 4][:, 0:nw]
            for kc in range(NFC):
                B.mm(pb, hs[:, kc * 128:(kc + 1) * 128], ws[:, kc * dout + n0: kc * dout + n0 + nw],
                     start=(kc == 0), stop=(kc == NFC - 1), tag=f"{tag}mm")
            if scale is not None:
                B.op("dve", "tensor_scalar", (os_[:, n0:n0 + nw], pb, scale, None, ALU.mult),
                     {}, reads=(pb,), writes=(os_[:, n0:n0 + nw],), tag=f"{tag}ev")
            else:
                B.op("dve", "tensor_copy", (os_[:, n0:n0 + nw], pb), {},
                     reads=(pb,), writes=(os_[:, n0:n0 + nw],), tag=f"{tag}ev")
        for c0 in ranges:
            ev = os_[:, c0: c0 + 2 * nfreq: 2]
            od = os_[:, c0 + 1: c0 + 2 * nfreq: 2]
            rr = os_[:, c0:c0 + 2 * nfreq]
            B.op("dve", "tensor_mul", (t1[:, :], ev, cs[:, :]), {},
                 reads=(rr, cs[:, :]), writes=(t1[:, :],), tag=f"{tag}ra")
            B.op("dve", "tensor_mul", (t2[:, :], od, sn[:, :]), {},
                 reads=(rr, sn[:, :]), writes=(t2[:, :],), tag=f"{tag}rb")
            B.op("dve", "tensor_sub", (t1[:, :], t1[:, :], t2[:, :]), {},
                 reads=(t1[:, :], t2[:, :]), writes=(t1[:, :],), tag=f"{tag}rc")
            B.op("dve", "tensor_mul", (t2[:, :], od, cs[:, :]), {},
                 reads=(rr, cs[:, :]), writes=(t2[:, :],), tag=f"{tag}rd")
            B.op("dve", "tensor_mul", (t3[:, :], ev, sn[:, :]), {},
                 reads=(rr, sn[:, :]), writes=(t3[:, :],), tag=f"{tag}re")
            B.op("dve", "tensor_add", (t2[:, :], t2[:, :], t3[:, :]), {},
                 reads=(t2[:, :], t3[:, :]), writes=(t2[:, :],), tag=f"{tag}rf")
            B.op("dve", "tensor_copy", (ev, t1[:, :]), {},
                 reads=(t1[:, :],), writes=(rr,), tag=f"{tag}rg")
            B.op("dve", "tensor_copy", (od, t2[:, :]), {},
                 reads=(t2[:, :],), writes=(rr,), tag=f"{tag}rh")
        for (out_d, col_groups) in outs_fm:
            nch = len(col_groups)
            for ch in range(nch):
                segs = col_groups[ch]
                p0 = 0
                for (s0, span) in segs:
                    B.transpose(bkb[p0:p0 + span, 0:128], os_[:, s0:s0 + span],
                                C.identb[:, :], tag=f"{tag}tr")
                    p0 += span
                B.op("dve", "tensor_copy", (st[:, 0:128], bkb[:, 0:128]), {},
                     reads=(bkb[:, 0:128],), writes=(st[:, 0:128],), tag=f"{tag}cp")
                B.dma("act", out_d[ch, :, t0:t0 + 128], st[:, 0:128], tag=f"{tag}sf")


# version 09
# speedup vs baseline: 1.3415x; 1.3415x over previous
import math
from contextlib import ExitStack

import numpy as np

import concourse.bass as bass
import concourse.mybir as mybir
from concourse.bass_utils import run_bass_kernel_spmd

F32 = mybir.dt.float32
BF16 = mybir.dt.bfloat16
AF = mybir.ActivationFunctionType
ALU = mybir.AluOpType
AX = mybir.AxisListType

T = 4096
D = 1024
TB = 512
NTB = T // TB
NFC = D // 128
EPS_LN = 1e-5
NEG = -30000.0


ANNOTATE = False
DMA_Q = {"pool": "q_pool", "sp": "q_sp", "act": "q_act", "dve": "q_dve"}


class Reg:
    __slots__ = ("key", "lo", "hi")

    def __init__(self, key, lo, hi):
        self.key, self.lo, self.hi = key, lo, hi


def _ap_region(ap):
    t = ap.tensor
    name = t.name
    if type(t).__name__.startswith("PSum"):
        return Reg(name, 0, 1 << 60)
    shape = list(t.shape)
    pairs = list(ap.ap)
    off = ap.offset
    if len(shape) == 2 and shape[0] > 1:
        n = shape[1]
        free_span = sum(s * (c - 1) for (s, c) in pairs if s != n)
        col = off % n
        return Reg(name, col, col + free_span + 1)
    span = sum(s * (c - 1) for (s, c) in pairs)
    return Reg(name, off, off + span + 1)


class Op:
    __slots__ = ("idx", "veng", "ieng", "method", "args", "kwargs", "waits",
                 "incs", "seq", "tag")

    def __init__(self, idx, veng, ieng, method, args, kwargs, tag=""):
        self.idx = idx
        self.veng = veng
        self.ieng = ieng
        self.method = method
        self.args = args
        self.kwargs = kwargs
        self.waits = []
        self.incs = 0
        self.seq = 0
        self.tag = tag




def op_wave_target(dep_op):
    return dep_op.kwargs.get("_wave_cum", dep_op.seq + 1)

class Builder:
    def __init__(self, name):
        self.nc = bass.Bass("TRN2", target_bir_lowering=False, debug=False)
        self.name = name
        self.ops = []
        self.per_veng = {}
        self.marks = {}
        self.records = {}
        self.qstate = {}
        self.scopes = []
        self.ctx = ExitStack()
        self._uid = 0

    def dram(self, name, shape, dtype, kind="Internal"):
        return self.nc.dram_tensor(name, list(shape), dtype, kind=kind).ap()

    def sbuf(self, name, shape, dtype):
        tgt = self.scopes[-1] if self.scopes else self.ctx
        return tgt.enter_context(self.nc.sbuf_tensor(name, list(shape), dtype))

    def push_scope(self):
        self.scopes.append(ExitStack())

    def pop_scope(self):
        self.barrier()
        st = self.scopes.pop()
        st.close()

    def psum(self, name, shape, dtype):
        return self.ctx.enter_context(self.nc.psum_tensor(name, list(shape), dtype))

    def uid(self):
        self._uid += 1
        return self._uid

    SAME_ENG_SKIP = False
    WATERMARK = True

    def _dep(self, op, dep_op):
        if dep_op is op:
            return
        if self.SAME_ENG_SKIP:
            if dep_op.veng == dep_op.ieng == op.ieng:
                return
        elif dep_op.ieng == "pe" and op.ieng == "pe":
            return
        veng = dep_op.veng
        if veng.startswith("q_") or veng.startswith("sq_"):
            self._close_wave(veng)
            val = 16 * op_wave_target(dep_op)
            for i, (v, w) in enumerate(op.waits):
                if v == veng:
                    if val > w:
                        op.waits[i] = (veng, val)
                    return
            op.waits.append((veng, val))
            return
        marks = self.marks.setdefault(veng, [])
        pos = None
        for i in range(len(marks) - 1, -1, -1):
            if marks[i] >= dep_op.seq:
                pos = i
            else:
                break
        if pos is None:
            marks.append(dep_op.seq)
            self.per_veng[veng][dep_op.seq].incs += 1
            pos = len(marks) - 1
        val = pos + 1
        for i, (v, w) in enumerate(op.waits):
            if v == veng:
                if val > w:
                    op.waits[i] = (veng, val)
                return
        op.waits.append((veng, val))

    def _access(self, op, regs, is_write):
        for r in regs:
            recs = self.records.setdefault(r.key, [])
            if is_write:
                newrecs = []
                for rec in recs:
                    if rec[0] < r.hi and r.lo < rec[1]:
                        if rec[2] is not None:
                            self._dep(op, rec[2])
                        for rd in rec[3]:
                            self._dep(op, rd)
                        if rec[0] < r.lo or rec[1] > r.hi:
                            newrecs.append(rec)
                    else:
                        newrecs.append(rec)
                newrecs.append([r.lo, r.hi, op, []])
                self.records[r.key] = newrecs
            else:
                for rec in recs:
                    if rec[0] < r.hi and r.lo < rec[1]:
                        if rec[2] is not None:
                            self._dep(op, rec[2])
                        rec[3].append(op)

    def _mk(self, veng, ieng, method, args, kwargs, tag):
        op = Op(len(self.ops), veng, ieng, method, args, kwargs, tag)
        lst = self.per_veng.setdefault(veng, [])
        op.seq = len(lst)
        lst.append(op)
        if veng != ieng:
            pass
        self.ops.append(op)
        return op

    def barrier(self):
        for q in list(self.qstate):
            st = self.qstate[q]
            sem_i = st["wave"] % self.R_WAVES
            self._close_wave(f"{q}#{sem_i}")
        qwaits = []
        for q, st in self.qstate.items():
            for i in range(self.R_WAVES):
                if st["cum"][i] > 0:
                    qwaits.append((f"{q}#{i}", 16 * st["cum"][i]))
        vals = {}
        for e in ("pe", "act", "dve", "pool", "sp"):
            lst = self.per_veng.get(e, [])
            last = None
            for o in reversed(lst):
                if o.method is not None:
                    last = o
                    break
            if last is not None:
                marks = self.marks.setdefault(e, [])
                if not marks or marks[-1] < last.seq:
                    marks.append(last.seq)
                    last.incs = 1
                vals[e] = len(marks)
        spop = self._mk("sp", "sp", "nop", (), {}, "barrier")
        for e, v in vals.items():
            if e != "sp":
                spop.waits.append((e, v))
        spop.waits.extend(qwaits)
        marks_sp = self.marks.setdefault("sp", [])
        marks_sp.append(spop.seq)
        spop.incs = 1
        spv = len(marks_sp)
        for e in ("pe", "act", "dve", "pool"):
            w = self._mk(e, e, None, (), {}, "barwait")
            w.waits.append(("sp", spv))

    def op(self, eng, method, args, kwargs=None, reads=(), writes=(), tag=""):
        op = self._mk(eng, eng, method, args, dict(kwargs or {}), tag)
        self._access(op, [r if isinstance(r, Reg) else _ap_region(r) for r in reads], False)
        self._access(op, [r if isinstance(r, Reg) else _ap_region(r) for r in writes], True)
        return op

    R_WAVES = 4

    def dma(self, eng, out_ap, in_ap, tag="", **kw):
        small = False
        for a in (out_ap, in_ap):
            tn = type(a.tensor).__name__
            if tn.startswith("SB") and a.tensor.shape[0] < 128:
                small = True
        kw["out"] = out_ap
        kw["in_"] = in_ap
        q = ("s" + DMA_Q[eng]) if small else DMA_Q[eng]
        st = self.qstate.setdefault(q, {"wave": 0, "open": 0, "cum": [0] * self.R_WAVES})
        sem_i = st["wave"] % self.R_WAVES
        veng = f"{q}#{sem_i}"
        if st["open"] == 0 and st["cum"][sem_i] > 0:
            gate = Op(len(self.ops), eng, eng, None, (), {}, "gate")
            glst = self.per_veng.setdefault(eng, [])
            gate.seq = len(glst)
            glst.append(gate)
            self.ops.append(gate)
            gate.waits.append((veng, 16 * st["cum"][sem_i]))
        op = self._mk(veng, eng, "dma_start", (), kw, tag)
        op.incs = 1
        st["open"] += 1
        st.setdefault("open_ops", []).append(op)
        op.kwargs["_wave_cum"] = st["cum"][sem_i] + st["open"]
        self._access(op, [_ap_region(in_ap)], False)
        self._access(op, [_ap_region(out_ap)], True)
        return op

    def _close_wave(self, veng):
        q, _, si = veng.partition("#")
        st = self.qstate.get(q)
        if st is None:
            return
        sem_i = st["wave"] % self.R_WAVES
        if f"{q}#{sem_i}" == veng and st["open"] > 0:
            st["cum"][sem_i] += st["open"]
            final = st["cum"][sem_i]
            for o in st.get("open_ops", []):
                o.kwargs["_wave_cum"] = final
            st["open_ops"] = []
            st["open"] = 0
            st["wave"] += 1

    def mm(self, out_ap, lhsT_ap, rhs_ap, start, stop, tile_position=None, tag=""):
        kw = dict(start=start, stop=stop)
        if tile_position is not None:
            kw["tile_position"] = tile_position
        return self.op("pe", "matmul", (out_ap, lhsT_ap, rhs_ap), kw,
                       reads=(lhsT_ap, rhs_ap), writes=(out_ap,), tag=tag)

    def transpose(self, out_ap, in_ap, ident_ap, tag=""):
        return self.op("pe", "transpose", (out_ap, in_ap, ident_ap), {},
                       reads=(in_ap, ident_ap), writes=(out_ap,), tag=tag)

    def act(self, out_ap, in_ap, func, bias=0.0, scale=1.0, tag=""):
        kw = dict(bias=bias, scale=scale)
        reads = [in_ap] + [b for b in (bias, scale) if hasattr(b, "tensor")]
        return self.op("act", "activation", (out_ap, in_ap, func), kw,
                       reads=reads, writes=(out_ap,), tag=tag)

    def finalize(self, final_writes=()):
        end_deps = []
        for ap in final_writes:
            r = ap if isinstance(ap, Reg) else _ap_region(ap)
            for rec in self.records.get(r.key, []):
                if rec[2] is not None:
                    end_deps.append(rec[2])
        term = self._mk("sp", "sp", None, (), {}, "terminal")
        for d in end_deps:
            self._dep(term, d)

        nc = self.nc
        with self.ctx:
            sems = {}
            for veng in self.per_veng:
                sems[veng] = self.ctx.enter_context(nc.semaphore("sem_" + veng))
            block = self.ctx.enter_context(nc.Block())

            def run_engine(eng_obj, eng_name):
                watermark = {}
                for op in self.ops:
                    if op.ieng != eng_name:
                        continue
                    for (veng, val) in op.waits:
                        if self.WATERMARK and watermark.get(veng, 0) >= val:
                            continue
                        watermark[veng] = val
                        eng_obj.wait_ge(sems[veng], val)
                    if op.method is None:
                        continue
                    kwargs = {k: v for k, v in op.kwargs.items() if not k.startswith("_")}
                    inst = getattr(eng_obj, op.method)(*op.args, **kwargs)
                    if op.tag and ANNOTATE:
                        try:
                            inst.annotate(op.tag)
                        except Exception:
                            pass
                    if op.incs:
                        amt = 16 if ("q_" in op.veng) else 1
                        inst.then_inc(sems[op.veng], amt)

            @block.tensor
            def _(t):
                run_engine(t, "pe")

            @block.scalar
            def _(s):
                run_engine(s, "act")

            @block.vector
            def _(v):
                run_engine(v, "dve")

            @block.gpsimd
            def _(g):
                run_engine(g, "pool")

            @block.sync
            def _(sy):
                run_engine(sy, "sp")

        return nc



class ConstReg:

    def __init__(self):
        self.pvt_cols = []
        self.pvt_index = {}
        self.row_vals = []
        self.row_index = {}

    def pvec(self, name, vec):
        v = np.asarray(vec, np.float32).reshape(-1)
        assert v.size % 128 == 0
        if name in self.pvt_index:
            return self.pvt_index[name]
        start = len(self.pvt_cols)
        for i in range(v.size // 128):
            self.pvt_cols.append(v[i * 128:(i + 1) * 128])
        self.pvt_index[name] = start
        return start

    def rowvec(self, name, vec):
        v = np.asarray(vec, np.float32).reshape(-1)
        if name in self.row_index:
            return self.row_index[name]
        start = len(self.row_vals)
        self.row_vals.extend(v.tolist())
        self.row_index[name] = start
        return start

    def pvt_array(self):
        if not self.pvt_cols:
            return np.zeros((128, 1), np.float32)
        return np.ascontiguousarray(np.stack(self.pvt_cols, axis=1).astype(np.float32))

    def rows_array(self):
        if not self.row_vals:
            return np.zeros((1, 4), np.float32)
        return np.asarray(self.row_vals, np.float32).reshape(1, -1)


def make_trimask_T():
    kt = np.arange(128)[:, None]
    qt = np.arange(128)[None, :]
    b = np.where(kt <= qt, 0.0, NEG).astype(np.float32)
    return np.ascontiguousarray(b.T)


def fm_layout(x):
    return np.ascontiguousarray(np.asarray(x).T.reshape(NFC, 128, -1))


def fm_unlayout(xfm):
    return np.ascontiguousarray(np.asarray(xfm).reshape(D, -1).T)



class Ctx:
    def __init__(self, B, n_pvt, n_rows, cr=None):
        self.B = B
        self.cr = cr
        self.identf_d = B.dram("c_identf", [128, 128], F32, kind="ExternalInput")
        self.pvt_d = B.dram("c_pvt", [128, max(n_pvt, 1)], F32, kind="ExternalInput")
        self.rows_d = B.dram("c_rows", [1, max(n_rows, 4)], F32, kind="ExternalInput")
        self.trimask_d = B.dram("c_trimask", [128, 128], F32, kind="ExternalInput")
        self.esel_d = B.dram("c_esel", [8, 8 * 128], F32, kind="ExternalInput")
        self.identf = B.sbuf("s_identf", [128, 128], F32)
        self.identb = B.sbuf("s_identb", [128, 128], BF16)
        self.pvt = B.sbuf("s_pvt", [128, max(n_pvt, 1)], F32)
        self.trimask = B.sbuf("s_trimask", [128, 128], BF16)
        self.onesrowf = B.sbuf("s_onesrowf", [1, 128], F32)
        self.onestb = B.sbuf("s_onestb", [1, TB], F32)
        B.dma("pool", self.identf[:, :], self.identf_d[:, :], tag="ldc")
        B.dma("pool", self.identb[:, :], self.identf_d[:, :], tag="ldc")
        B.dma("pool", self.pvt[:, :], self.pvt_d[:, :], tag="ldc")
        B.dma("pool", self.trimask[:, :], self.trimask_d[:, :], tag="ldc")
        B.op("pool", "memset", (self.onesrowf[:, :], 1.0), {}, writes=(self.onesrowf[:, :],))
        B.op("pool", "memset", (self.onestb[:, :], 1.0), {}, writes=(self.onestb[:, :],))
        self.bank = [B.psum(f"ps{i}", [128, 512], F32) for i in range(7)]
        self.bankb = B.psum("ps7b", [128, 1024], BF16)

    def pv(self, col, n=1):
        return self.pvt[:, col:col + n]

    def eps(self):
        return self.pv(self.cr.pvt_index["eps_ln"])





def ln_full(B, C, x_d, out_d, tag, g_rowcol=None, b_rowcol=None, rms=False,
            src_dtype=F32):
    u = B.uid()
    xs = B.sbuf(f"{tag}_x{u}", [128, NFC * TB], src_dtype)
    sq = B.sbuf(f"{tag}_sq{u}", [128, TB], F32)
    mrow = B.sbuf(f"{tag}_m{u}", [1, TB], F32)
    qrow = B.sbuf(f"{tag}_q{u}", [1, TB], F32)
    rrow = B.sbuf(f"{tag}_r{u}", [1, TB], F32)
    mrrow = B.sbuf(f"{tag}_mr{u}", [1, TB], F32)
    trow = B.sbuf(f"{tag}_tr{u}", [1, TB], F32)
    outs = B.sbuf(f"{tag}_o{u}", [128, NFC * TB], BF16)
    onesc = B.sbuf(f"{tag}_1c{u}", [128, 1], src_dtype)
    onescf = B.sbuf(f"{tag}_1f{u}", [128, 1], F32)
    grows = None
    brows = None
    if g_rowcol is not None:
        grows = B.sbuf(f"{tag}_gr{u}", [1, D], F32)
        B.dma("sp", grows[0:1, :], C.rows_d[0:1, g_rowcol:g_rowcol + D], tag=f"{tag}lg")
    if b_rowcol is not None:
        brows = B.sbuf(f"{tag}_br{u}", [1, D], F32)
        B.dma("sp", brows[0:1, :], C.rows_d[0:1, b_rowcol:b_rowcol + D], tag=f"{tag}lb")
    tmp = B.sbuf(f"{tag}_t{u}", [128, TB], F32)
    B.op("pool", "memset", (onesc[:, :], 1.0), {}, writes=(onesc[:, :],))
    B.op("pool", "memset", (onescf[:, :], 1.0), {}, writes=(onescf[:, :],))
    bank = C.bank
    inv_d = 1.0 / D
    for tb in range(NTB):
        t0 = tb * TB
        for fc in range(NFC):
            B.dma("sp", xs[:, fc * TB:(fc + 1) * TB], x_d[fc, :, t0:t0 + TB], tag=f"{tag}ldx")
        ps1 = bank[0][0:1, 0:TB]
        ps2 = bank[1][0:1, 0:TB]
        for fc in range(NFC):
            xc = xs[:, fc * TB:(fc + 1) * TB]
            if not rms:
                B.mm(ps1, onesc[:, 0:1], xc, start=(fc == 0), stop=(fc == NFC - 1), tag=f"{tag}s1")
            B.act(sq[:, 0:TB], xc, AF.Square, tag=f"{tag}sq")
            B.mm(ps2, onescf[:, 0:1], sq[:, 0:TB], start=(fc == 0), stop=(fc == NFC - 1), tag=f"{tag}s2")
        B.op("dve", "tensor_scalar", (qrow[0:1, :], ps2, inv_d, None, ALU.mult), {},
             reads=(ps2,), writes=(qrow[0:1, :],), tag=f"{tag}rq")
        if not rms:
            B.op("dve", "tensor_scalar", (mrow[0:1, :], ps1, inv_d, None, ALU.mult), {},
                 reads=(ps1,), writes=(mrow[0:1, :],), tag=f"{tag}rm")
            B.op("dve", "scalar_tensor_tensor",
                 (trow[0:1, :], mrow[0:1, :], 1.0, mrow[0:1, :], ALU.mult, ALU.mult), {},
                 reads=(mrow[0:1, :],), writes=(trow[0:1, :],), tag=f"{tag}m2")
            B.op("dve", "tensor_sub", (qrow[0:1, :], qrow[0:1, :], trow[0:1, :]), {},
                 reads=(qrow[0:1, :], trow[0:1, :]), writes=(qrow[0:1, :],), tag=f"{tag}var")
        B.act(rrow[0:1, :], qrow[0:1, :], AF.Ln, bias=C.eps()[0:1, 0:1], tag=f"{tag}ln")
        B.act(rrow[0:1, :], rrow[0:1, :], AF.Exp, scale=-0.5, tag=f"{tag}exp")
        if not rms:
            B.op("dve", "scalar_tensor_tensor",
                 (mrrow[0:1, :], mrow[0:1, :], -1.0, rrow[0:1, :], ALU.mult, ALU.mult), {},
                 reads=(mrow[0:1, :], rrow[0:1, :]), writes=(mrrow[0:1, :],), tag=f"{tag}mr")
        for fc in range(NFC):
            xc = xs[:, fc * TB:(fc + 1) * TB]
            oc = outs[:, fc * TB:(fc + 1) * TB]
            pg = bank[2 + (fc % 2) * 2][:, 0:TB]
            pb = bank[3 + (fc % 2) * 2][:, 0:TB]
            grow = C.onesrowf[0:1, :] if g_rowcol is None else grows[0:1, fc * 128:(fc + 1) * 128]
            B.mm(pg, grow, rrow[0:1, :], start=True, stop=True, tag=f"{tag}bg")
            use_pb = (not rms) or (b_rowcol is not None)
            if use_pb:
                first = True
                if not rms:
                    B.mm(pb, grow, mrrow[0:1, :], start=True, stop=(b_rowcol is None), tag=f"{tag}bb")
                    first = False
                if b_rowcol is not None:
                    B.mm(pb, brows[0:1, fc * 128:(fc + 1) * 128], C.onestb[0:1, 0:TB],
                         start=first, stop=True, tag=f"{tag}bb2")
            B.op("dve", "tensor_mul", (tmp[:, 0:TB], xc, pg), {},
                 reads=(xc, pg), writes=(tmp[:, 0:TB],), tag=f"{tag}ap1")
            if use_pb:
                B.op("dve", "tensor_add", (oc, tmp[:, 0:TB], pb), {},
                     reads=(tmp[:, 0:TB], pb), writes=(oc,), tag=f"{tag}ap2")
            else:
                B.op("dve", "tensor_copy", (oc, tmp[:, 0:TB]), {},
                     reads=(tmp[:, 0:TB],), writes=(oc,), tag=f"{tag}ap2")
        if out_d is not None:
            for fc in range(NFC):
                B.dma("act", out_d[fc, :, t0:t0 + TB], outs[:, fc * TB:(fc + 1) * TB],
                      tag=f"{tag}st")



def proj_fm(B, C, h_d, w_d, out_d, dout, tag, bias_pvcol=None, scale=None,
            oc_group=4, col_off=0):
    u = B.uid()
    noc = dout // 128
    hs = B.sbuf(f"{tag}_h{u}", [128, NFC * TB], BF16)
    og = min(oc_group, noc)
    ws = B.sbuf(f"{tag}_w{u}", [128, 2 * NFC * og * 128], BF16)
    os_ = B.sbuf(f"{tag}_o{u}", [128, og * TB], BF16)
    bank = C.bank
    w_view = w_d.rearrange("(kc kp) n -> kc kp n", kp=128)
    ngrp = (noc + og - 1) // og
    for g in range(ngrp):
        ocs = list(range(g * og, min((g + 1) * og, noc)))
        wbuf = (g % 2) * NFC * og * 128
        for kc in range(NFC):
            B.dma("pool",
                  ws[:, wbuf + kc * og * 128: wbuf + kc * og * 128 + len(ocs) * 128],
                  w_view[kc, :, col_off + ocs[0] * 128: col_off + ocs[0] * 128 + len(ocs) * 128],
                  tag=f"{tag}ldw")
        for tb in range(NTB):
            t0 = tb * TB
            for kc in range(NFC):
                B.dma("sp", hs[:, kc * TB:(kc + 1) * TB], h_d[kc, :, t0:t0 + TB],
                      tag=f"{tag}ldh")
            for j, oc in enumerate(ocs):
                pb = bank[j % 4][:, 0:TB]
                for kc in range(NFC):
                    B.mm(pb, ws[:, wbuf + kc * og * 128 + j * 128: wbuf + kc * og * 128 + (j + 1) * 128],
                         hs[:, kc * TB:(kc + 1) * TB],
                         start=(kc == 0), stop=(kc == NFC - 1), tag=f"{tag}mm")
                oc_sb = os_[:, j * TB:(j + 1) * TB]
                if bias_pvcol is not None and scale is not None:
                    B.op("dve", "tensor_scalar",
                         (oc_sb, pb, C.pv(bias_pvcol + oc), scale, ALU.add, ALU.mult), {},
                         reads=(pb, C.pv(bias_pvcol + oc)), writes=(oc_sb,), tag=f"{tag}ev")
                elif bias_pvcol is not None:
                    B.op("dve", "tensor_scalar",
                         (oc_sb, pb, C.pv(bias_pvcol + oc), None, ALU.add), {},
                         reads=(pb, C.pv(bias_pvcol + oc)), writes=(oc_sb,), tag=f"{tag}ev")
                elif scale is not None:
                    B.op("dve", "tensor_scalar", (oc_sb, pb, scale, None, ALU.mult), {},
                         reads=(pb,), writes=(oc_sb,), tag=f"{tag}ev")
                else:
                    B.op("dve", "tensor_copy", (oc_sb, pb), {},
                         reads=(pb,), writes=(oc_sb,), tag=f"{tag}ev")
                B.dma("act", out_d[oc, :, t0:t0 + TB], oc_sb, tag=f"{tag}st")



def proj_tm_v(B, C, h_d, w_d, out_d, dout, nheads, dv, tag, vbias_d=None,
               scale=None, col_off=0):
    u = B.uid()
    ws = B.sbuf(f"{tag}_w{u}", [128, NFC * dout], BF16)
    hs = B.sbuf(f"{tag}_h{u}", [128, NFC * 128], BF16)
    width = nheads * (dv + 1)
    os_ = B.sbuf(f"{tag}_o{u}", [128, width], BF16)
    vb = None
    if vbias_d is not None:
        vb = B.sbuf(f"{tag}_b{u}", [128, dout], F32)
        B.dma("pool", vb[:, :], vbias_d[:, :], tag=f"{tag}ldb")
    w_view = w_d.rearrange("(kc kp) n -> kc kp n", kp=128)
    for kc in range(NFC):
        B.dma("pool", ws[:, kc * dout:(kc + 1) * dout], w_view[kc, :, col_off:col_off + dout], tag=f"{tag}ldw")
    bank = C.bank
    nnc = (dout + 511) // 512
    for tt in range(T // 128):
        t0 = tt * 128
        for kc in range(NFC):
            B.dma("sp", hs[:, kc * 128:(kc + 1) * 128], h_d[kc, :, t0:t0 + 128],
                  tag=f"{tag}ldh")
        for nc_ in range(nnc):
            n0 = nc_ * 512
            nw = min(512, dout - n0)
            pb = bank[nc_ % 4][:, 0:nw]
            for kc in range(NFC):
                B.mm(pb, hs[:, kc * 128:(kc + 1) * 128], ws[:, kc * dout + n0: kc * dout + n0 + nw],
                     start=(kc == 0), stop=(kc == NFC - 1), tag=f"{tag}mm")
            h0 = n0 // dv
            nh = nw // dv
            for hh in range(nh):
                if dv > 64:
                    base = (h0 + hh) * (dv + 1)
                    for (po, do, w_) in ((0, 0, 64), (64, 65, 64)):
                        dsth = os_[:, base + do: base + do + w_]
                        srch = pb[:, hh * dv + po: hh * dv + po + w_]
                        if vb is not None:
                            B.op("dve", "tensor_add", (dsth, srch,
                                 vb[:, n0 + hh * dv + po: n0 + hh * dv + po + w_]), {},
                                 reads=(srch, vb[:, n0 + hh * dv + po: n0 + hh * dv + po + w_]),
                                 writes=(dsth,), tag=f"{tag}ev")
                        elif scale is not None:
                            B.op("dve", "tensor_scalar", (dsth, srch, scale, None, ALU.mult),
                                 {}, reads=(srch,), writes=(dsth,), tag=f"{tag}ev")
                        else:
                            B.op("dve", "tensor_copy", (dsth, srch), {},
                                 reads=(srch,), writes=(dsth,), tag=f"{tag}ev")
                    continue
                dst = os_[:, (h0 + hh) * (dv + 1): (h0 + hh) * (dv + 1) + dv]
                src = pb[:, hh * dv: (hh + 1) * dv]
                if vb is not None:
                    B.op("dve", "tensor_add", (dst, src, vb[:, n0 + hh * dv: n0 + (hh + 1) * dv]), {},
                         reads=(src, vb[:, n0 + hh * dv: n0 + (hh + 1) * dv]), writes=(dst,),
                         tag=f"{tag}ev")
                elif scale is not None:
                    B.op("dve", "tensor_scalar", (dst, src, scale, None, ALU.mult), {},
                         reads=(src,), writes=(dst,), tag=f"{tag}ev")
                else:
                    B.op("dve", "tensor_copy", (dst, src), {},
                         reads=(src,), writes=(dst,), tag=f"{tag}ev")
        one_col = 64 if dv > 64 else dv
        ones_view = os_.rearrange("p (h e) -> p h e", e=dv + 1)[:, :, one_col:one_col + 1]
        B.op("dve", "memset", (ones_view, 1.0), {}, writes=(os_[:, 0:width],), tag=f"{tag}on")
        B.dma("act", out_d[t0:t0 + 128, 0:width], os_[:, 0:width], tag=f"{tag}st")



FM = lambda: ([NFC, 128, T], F32)
FMB = lambda: ([NFC, 128, T], BF16)

TSPEC = {
    "x0": FM(),
    "u_l0": FMB(),
    "Q1": FMB(), "K1": FMB(),
    "V1": ([T, 16 * 65], BF16),
    "O1": ([8, 128, T], BF16),
    "x1": FM(),
    "u1": FMB(), "h1": FMB(),
    "Q2": FMB(), "K2": FMB(),
    "V2": ([T, 16 * 65], BF16),
    "O2": ([8, 128, T], BF16),
    "xr": FMB(), "h2": FMB(),
    "G1": ([T, 8], F32),
    "x2": FM(),
    "u2": FMB(),
    "q3t": ([T, D], BF16), "k3t": ([T, D], BF16),
    "Q3a": ([4, 128, T], BF16), "Q3b": ([4, 128, T], BF16),
    "K3a": ([4, 128, T], BF16), "K3b": ([4, 128, T], BF16),
    "V3": ([T, 8 * 129], BF16),
    "O3a": ([8, 128, T], BF16), "O3b": ([8, 128, T], BF16),
    "Y3": ([8, 128, T], BF16),
    "x3": FM(),
    "u3": FMB(), "h3": FMB(),
    "q4t": ([T, D], BF16), "ckvt": ([T, 160], BF16),
    "Q4": ([8, 128, T], BF16), "K4": ([8, 128, T], BF16),
    "V4": ([T, 16 * 33], BF16),
    "O4": ([4, 128, T], BF16),
    "xi": FMB(), "h4": FMB(),
    "G2": ([T, 8], F32),
    "x4": FM(),
}

WSPEC = {
    "L0_attn_W_qkv": ([D, 3 * D], F32),
    "L0_attn_W_out": ([D, D], F32),
    "L0_ffn_w_qkv": ([D, 3 * D], F32),
    "L0_ffn_w_o": ([D, D], F32),
    "L0_ffn_w_gate": ([D, 8], F32),
    "L0_ffn_w1": ([8, D, 4096], F32),
    "L0_ffn_w2": ([8, 4096, D], F32),
    "L1_attn_WqT": ([D, D], F32),
    "L1_attn_WkT": ([D, D], F32),
    "L1_attn_WvT": ([D, D], F32),
    "L1_attn_WoT": ([D, D], F32),
    "L1_ffn_Wq": ([D, D], F32),
    "L1_ffn_Wkva": ([D, 160], F32),
    "L1_ffn_Wkvb": ([128, 512], F32),
    "L1f_Wo_sub": ([512, D], F32),
    "L1_ffn_Wgate": ([D, 8], F32),
    "L1_ffn_Wg": ([8, D, 512], F32),
    "L1_ffn_Wu": ([8, D, 512], F32),
    "L1_ffn_Wd": ([8, 512, D], F32),
    "L1_ffn_sWg": ([1, D, 512], F32),
    "L1_ffn_sWu": ([1, D, 512], F32),
    "L1_ffn_sWd": ([1, 512, D], F32),
}

EXTRA_IN = {
    "vbias_l0": ([128, D], F32),
    "rope64_cos": ([T, 32], F32), "rope64_sin": ([T, 32], F32),
    "rope32_cos": ([T, 16], F32), "rope32_sin": ([T, 16], F32),
    "lnw_col": ([128, 1], F32), "lnb_col": ([128, 1], F32),
    "lam_cat": ([1, 256], F32),
    "ebias_bc": ([128, 8], F32),
}


def register_consts(cr: "ConstReg", inp=None):
    z = lambda n: np.zeros(n, np.float32)
    g = (lambda k: np.asarray(inp[k], np.float32)) if inp is not None else (lambda k: None)
    def pv(name, n, val):
        cr.pvec(name, z(n * 128) if val is None else val)
    def rv(name, n, val):
        cr.rowvec(name, z(n) if val is None else val)
    pv("eps_ln", 1, np.full(128, EPS_LN, np.float32))
    bqkv = g("L0_attn_b_qkv")
    pv("l0a_bq", 8, None if bqkv is None else bqkv[0:1024])
    pv("l0a_bk", 8, None if bqkv is None else bqkv[1024:2048])
    bout = g("L0_attn_b_out")
    pv("l0a_bout", 8, bout)
    rv("l0f_g1", 1024, g("L0_ffn_ln1_g"))
    rv("l0f_b1", 1024, g("L0_ffn_ln1_b"))
    rv("l0f_g2", 1024, g("L0_ffn_ln2_g"))
    rv("l0f_b2", 1024, g("L0_ffn_ln2_b"))
    b1 = g("L0_ffn_b1")
    b2 = g("L0_ffn_b2")
    for e in range(8):
        pv(f"l0f_b1_{e}", 32, None if b1 is None else b1[e])
        pv(f"l0f_b2_{e}", 8, None if b2 is None else b2[e])
    pv("eps_a8", 1, np.full(128, 1e-8, np.float32))
    rv("l1f_g1", 1024, g("L1_ffn_w_ln1"))
    rv("l1f_g2", 1024, g("L1_ffn_w_ln2"))
    return cr



def phase_l0attn_qkv(B, C, Tn, cr):
    ln_full(B, C, Tn["x0"], Tn["u_l0"], "p1ln")
    proj_fm(B, C, Tn["u_l0"], Tn["L0_attn_W_qkv"], Tn["Q1"], D, "p1q",
            bias_pvcol=cr.pvt_index["l0a_bq"], scale=0.125, col_off=0)
    proj_fm(B, C, Tn["u_l0"], Tn["L0_attn_W_qkv"], Tn["K1"], D, "p1k",
            bias_pvcol=cr.pvt_index["l0a_bk"], col_off=D)
    proj_tm_v(B, C, Tn["u_l0"], Tn["L0_attn_W_qkv"], Tn["V1"], D, 16, 64, "p1v",
              vbias_d=Tn["vbias_l0"], col_off=2 * D)


PHASES = []



def build_phase_program(idx):
    name, needs, makes, wneeds, extra, fn = PHASES[idx]
    cr = register_consts(ConstReg())
    B = Builder(name)
    Tn = {}
    for t in needs:
        shape, dt = TSPEC[t]
        Tn[t] = B.dram(t, shape, dt, kind="ExternalInput")
    for t in makes:
        shape, dt = TSPEC[t]
        Tn[t] = B.dram(t, shape, dt, kind="ExternalOutput")
    for t in wneeds:
        shape, dt = WSPEC[t]
        Tn[t] = B.dram(t, shape, dt, kind="ExternalInput")
    for t in extra:
        shape, dt = EXTRA_IN[t]
        Tn[t] = B.dram(t, shape, dt, kind="ExternalInput")
    C = Ctx(B, len(cr.pvt_cols), len(cr.row_vals), cr)
    fn(B, C, Tn, cr)
    outs = [Tn[t] for t in makes]
    nc = B.finalize(final_writes=outs)
    in_names = needs + wneeds + extra + ["c_identf", "c_pvt", "c_rows", "c_trimask", "c_esel"]
    return nc, in_names, makes


def host_all_arrays(inputs, seq):
    cr = register_consts(ConstReg(), inputs)
    arrs = {
        "x0": fm_layout(np.asarray(inputs["L0_attn_x"][seq], np.float32)),
        "c_identf": np.eye(128, dtype=np.float32),
        "c_pvt": cr.pvt_array(),
        "c_rows": cr.rows_array(),
        "c_trimask": make_trimask_T(),
        "vbias_l0": np.ascontiguousarray(
            np.broadcast_to(np.asarray(inputs["L0_attn_b_qkv"][2048:3072], np.float32)[None, :],
                            (128, 1024))),
    }
    pos = np.arange(T, dtype=np.float64)
    f64 = 1.0 / (10000.0 ** (np.arange(32, dtype=np.float64) / 32.0))
    ang = pos[:, None] * f64[None, :]
    arrs["rope64_cos"] = np.cos(ang).astype(np.float32)
    arrs["rope64_sin"] = np.sin(ang).astype(np.float32)
    f32_ = 1.0 / (100000.0 ** (np.arange(16, dtype=np.float64) / 16.0))
    ang2 = pos[:, None] * f32_[None, :]
    arrs["rope32_cos"] = np.cos(ang2).astype(np.float32)
    arrs["rope32_sin"] = np.sin(ang2).astype(np.float32)
    arrs["lnw_col"] = np.asarray(inputs["L1_attn_ln_w"], np.float32).reshape(128, 1).copy()
    arrs["lnb_col"] = np.asarray(inputs["L1_attn_ln_b"], np.float32).reshape(128, 1).copy()
    arrs["lam_cat"] = np.concatenate([
        np.asarray(inputs["L1_attn_lambda_q1"], np.float32),
        np.asarray(inputs["L1_attn_lambda_k1"], np.float32),
        np.asarray(inputs["L1_attn_lambda_q2"], np.float32),
        np.asarray(inputs["L1_attn_lambda_k2"], np.float32)]).reshape(1, 256)
    arrs["ebias_bc"] = np.ascontiguousarray(np.broadcast_to(
        np.asarray(inputs["L1_ffn_expert_bias"], np.float32)[None, :], (128, 8)))
    esel = np.zeros((8, 8 * 128), np.float32)
    for e in range(8):
        esel[e, e * 128:(e + 1) * 128] = 1.0
    arrs["c_esel"] = esel
    for w in WSPEC:
        if w.endswith("T") and w.startswith("L1_attn"):
            base = w[:-1]
            arrs[w] = np.ascontiguousarray(np.asarray(inputs[base], np.float32).T)
        elif w == "L1f_Wo_sub":
            Wo = np.asarray(inputs["L1_ffn_Wo"], np.float32)
            rows = np.concatenate([np.arange(64 * h, 64 * h + 32) for h in range(16)])
            arrs[w] = np.ascontiguousarray(Wo[rows, :])
        else:
            arrs[w] = np.ascontiguousarray(np.asarray(inputs[w], np.float32))
    return arrs


def run_phases(inputs, upto=None, state=None):
    if state is None:
        state = [host_all_arrays(inputs, b) for b in range(4)]
    n = len(PHASES) if upto is None else upto
    for i in range(n):
        if state[0].get("_done_" + PHASES[i][0]) is not None:
            continue
        import sys as _sys
        print(f"[phase {i}: {PHASES[i][0]}]", flush=True)
        nc, in_names, out_names = build_phase_program(i)
        in_maps = []
        for c in range(8):
            st = state[c // 2]
            in_maps.append({k: st[k] for k in in_names})
        res = run_bass_kernel_spmd(nc, in_maps, core_ids=list(range(8)))
        for b in range(4):
            for k in out_names:
                state[b][k] = np.asarray(res.results[2 * b][k])
            state[b]["_done_" + PHASES[i][0]] = True
    return state


FUSED = True
ANNOTATE = False


def build_fused():
    cr = register_consts(ConstReg())
    B = Builder("fused")
    Tn = {}
    ext_ins = []
    sx, dx = TSPEC["x0"]
    Tn["x0"] = B.dram("x0", sx, dx, kind="ExternalInput")
    ext_ins.append("x0")
    s4, d4 = TSPEC["x4"]
    Tn["x4"] = B.dram("x4", s4, d4, kind="ExternalOutput")
    for t, (s, d) in TSPEC.items():
        if t in ("x0", "x4"):
            continue
        Tn[t] = B.dram(t, s, d, kind="Internal")
    for t, (s, d) in WSPEC.items():
        Tn[t] = B.dram(t, s, d, kind="ExternalInput")
        ext_ins.append(t)
    for t, (s, d) in EXTRA_IN.items():
        Tn[t] = B.dram(t, s, d, kind="ExternalInput")
        ext_ins.append(t)
    C = Ctx(B, len(cr.pvt_cols), len(cr.row_vals), cr)
    for (name, needs, makes, wneeds, extra, fn) in PHASES:
        B.push_scope()
        fn(B, C, Tn, cr)
        B.pop_scope()
    nc = B.finalize(final_writes=[Tn["x4"]])
    ext_ins += ["c_identf", "c_pvt", "c_rows", "c_trimask", "c_esel"]
    return nc, ext_ins


def kernel(**inputs):
    if not FUSED:
        state = run_phases(inputs)
        out = np.stack([fm_unlayout(np.asarray(state[b]["x4"], np.float32))
                        for b in range(4)])
        return out.reshape(4, T, D)
    nc, ext_ins = build_fused()
    arrs = [host_all_arrays(inputs, b) for b in range(4)]
    in_maps = [{k: arrs[c // 2][k] for k in ext_ins} for c in range(8)]
    res = run_bass_kernel_spmd(nc, in_maps, core_ids=list(range(8)))
    out = np.stack([fm_unlayout(np.asarray(res.results[2 * b]["x4"], np.float32))
                    for b in range(4)])
    return out.reshape(4, T, D)



def attention(B, C, Q_d, K_d, V_d, O_d, nheads, dv, tag, causal=True,
              margin=0.5):
    u = B.uid()
    npair = nheads // 2
    vstride = dv + 1
    big_v = dv > 64
    ks = B.sbuf(f"{tag}_k{u}", [128, npair * T], BF16)
    vs = B.sbuf(f"{tag}_v{u}", [128, (T // 128) * nheads * vstride], BF16)
    qs = B.sbuf(f"{tag}_q{u}", [128, TB], BF16)
    qb_sq = B.sbuf(f"{tag}_sq{u}", [128, TB], BF16)
    pt = B.sbuf(f"{tag}_pt{u}", [128, 4 * TB], BF16)
    maxr = B.sbuf(f"{tag}_mx{u}", [1, 2 * nheads + 4], F32)
    tmr = B.sbuf(f"{tag}_tm{u}", [1, 4], F32)
    negc = B.sbuf(f"{tag}_nc{u}", [128, nheads], F32)
    ones64 = B.sbuf(f"{tag}_o64{u}", [128, 1], BF16)
    bcf = B.sbuf(f"{tag}_bc{u}", [128, TB], F32)
    oraw = B.sbuf(f"{tag}_or{u}", [128, TB], F32)
    oraw2 = B.sbuf(f"{tag}_o2{u}", [64, TB], F32)
    ost = B.sbuf(f"{tag}_os{u}", [128, TB], BF16)
    ost2 = B.sbuf(f"{tag}_ot{u}", [64, TB], BF16)
    onesq = B.sbuf(f"{tag}_oq{u}", [128, 128], F32)
    B.op("pool", "memset", (onesq[:, :], 1.0), {}, writes=(onesq[:, :],))
    B.op("pool", "memset", (ones64[:, :], 1.0), {}, writes=(ones64[:, :],))
    B.op("pool", "memset", (maxr[0:1, :], -1.0), {}, writes=(maxr[0:1, :],))
    bank = C.bank
    NKT = T // 128

    def sbank_of(hh, kc):
        if big_v:
            return bank[kc % 2] if hh == 0 else bank[2]
        return bank[hh * 2 + kc % 2]

    def obanks_of(hh):
        if big_v:
            return [(bank[3 + 2 * hh], 65), (bank[4 + 2 * hh], 64)]
        return [(bank[4 + hh], dv + 1)]

    for p in range(npair):
        for tb in range(NTB):
            B.dma("sp", ks[:, p * T + tb * TB: p * T + (tb + 1) * TB],
                  K_d[p, :, tb * TB:(tb + 1) * TB], tag=f"{tag}ldk")
    vw = nheads * vstride
    for tt in range(NKT):
        B.dma("sp", vs[:, tt * vw:(tt + 1) * vw],
              V_d[tt * 128:(tt + 1) * 128, 0:vw], tag=f"{tag}ldv")

    def bounds_for(src_d, out_col0):
        for p in range(npair):
            for tb in range(NTB):
                B.dma("sp", qs[:, 0:TB], src_d[p, :, tb * TB:(tb + 1) * TB],
                      tag=f"{tag}ldb")
                B.act(qb_sq[:, 0:TB], qs[:, 0:TB], AF.Square, tag=f"{tag}bsq")
                pa = bank[0][0:1, 0:TB]
                pb = bank[1][0:1, 0:TB]
                B.mm(pa, ones64[0:64, 0:1], qb_sq[0:64, 0:TB], start=True, stop=True,
                     tile_position=(0, 0), tag=f"{tag}bn")
                B.mm(pb, ones64[64:128, 0:1], qb_sq[64:128, 0:TB], start=True, stop=True,
                     tile_position=(64, 0), tag=f"{tag}bn")
                for hh, pp in ((0, pa), (1, pb)):
                    col = out_col0 + 2 * p + hh
                    B.op("dve", "reduce_max", (tmr[0:1, hh:hh + 1], pp), dict(axis=AX.X),
                         reads=(pp,), writes=(tmr[0:1, hh:hh + 1],), tag=f"{tag}brd")
                    B.op("dve", "tensor_max",
                         (maxr[0:1, col:col + 1], maxr[0:1, col:col + 1], tmr[0:1, hh:hh + 1]),
                         {}, reads=(maxr[0:1, col:col + 1], tmr[0:1, hh:hh + 1]),
                         writes=(maxr[0:1, col:col + 1],), tag=f"{tag}bmx")

    bounds_for(Q_d, 0)
    bounds_for(K_d, nheads)
    B.act(maxr[0:1, 0:2 * nheads], maxr[0:1, 0:2 * nheads], AF.Sqrt, tag=f"{tag}bsqrt")
    B.op("dve", "tensor_mul",
         (maxr[0:1, 0:nheads], maxr[0:1, 0:nheads], maxr[0:1, nheads:2 * nheads]),
         {}, reads=(maxr[0:1, 0:2 * nheads],), writes=(maxr[0:1, 0:nheads],), tag=f"{tag}bc")
    B.op("dve", "tensor_scalar",
         (maxr[0:1, 0:nheads], maxr[0:1, 0:nheads], margin, -1.0, ALU.add, ALU.mult),
         {}, reads=(maxr[0:1, 0:nheads],), writes=(maxr[0:1, 0:nheads],), tag=f"{tag}bneg")
    for h in range(nheads):
        pb = bank[0][:, 0:1]
        B.mm(pb, C.onesrowf[0:1, :], maxr[0:1, h:h + 1], start=True, stop=True,
             tag=f"{tag}brep")
        B.op("dve", "tensor_copy", (negc[:, h:h + 1], pb), {},
             reads=(pb,), writes=(negc[:, h:h + 1],), tag=f"{tag}bcp")

    for p in range(npair):
        for qb in range(NTB):
            nkc = (qb + 1) * 4 if causal else NKT
            diag0 = qb * 4 if causal else None
            B.dma("sp", qs[:, 0:TB], Q_d[p, :, qb * TB:(qb + 1) * TB], tag=f"{tag}ldq")
            started = set()

            def emit_scores(kc):
                a = (kc - diag0) if (causal and kc >= diag0) else -1
                q0 = a * 128 if a > 0 else 0
                nq = TB - q0
                sA = sbank_of(0, kc)[:, 0:nq]
                sB = sbank_of(1, kc)[:, 0:nq]
                kslice = ks[:, p * T + kc * 128: p * T + (kc + 1) * 128]
                B.mm(sA, kslice[0:64, :], qs[0:64, q0:TB], start=True, stop=(a < 0),
                     tile_position=(0, 0), tag=f"{tag}sA")
                B.mm(sB, kslice[64:128, :], qs[64:128, q0:TB], start=True, stop=(a < 0),
                     tile_position=(64, 0), tag=f"{tag}sB")
                if a >= 0:
                    B.mm(sbank_of(0, kc)[:, 0:128], C.trimask[:, :], C.identb[:, :],
                         start=False, stop=True, tag=f"{tag}mk")
                    B.mm(sbank_of(1, kc)[:, 0:128], C.trimask[:, :], C.identb[:, :],
                         start=False, stop=True, tag=f"{tag}mk")

            def emit_exp(kc):
                a = (kc - diag0) if (causal and kc >= diag0) else -1
                q0 = a * 128 if a > 0 else 0
                nq = TB - q0
                for hh in range(2):
                    h = 2 * p + hh
                    sbank = sbank_of(hh, kc)[:, 0:nq]
                    slot = pt[:, (kc % 2) * 2 * TB + hh * TB: (kc % 2) * 2 * TB + hh * TB + nq]
                    B.act(slot, sbank, AF.Exp, bias=negc[:, h:h + 1], tag=f"{tag}exp")

            def emit_av(kc):
                a = (kc - diag0) if (causal and kc >= diag0) else -1
                q0 = a * 128 if a > 0 else 0
                nq = TB - q0
                for hh in range(2):
                    h = 2 * p + hh
                    slot = pt[:, (kc % 2) * 2 * TB + hh * TB: (kc % 2) * 2 * TB + hh * TB + nq]
                    vbase = kc * vw + h * vstride
                    c0 = 0
                    for (obk, M) in obanks_of(hh):
                        st_ = id(obk) not in started
                        started.add(id(obk))
                        B.mm(obk[0:M, q0:TB], vs[:, vbase + c0: vbase + c0 + M], slot,
                             start=st_, stop=(kc == nkc - 1), tag=f"{tag}av")
                        c0 += M

            emit_scores(0)
            emit_exp(0)
            for kc in range(1, nkc):
                emit_scores(kc)
                emit_exp(kc)
                emit_av(kc - 1)
            emit_av(nkc - 1)
            for hh in range(2):
                h = 2 * p + hh
                obs = obanks_of(hh)
                drow_p = 64 if big_v else dv
                if big_v:
                    B.op("dve", "tensor_copy", (oraw[0:65, 0:TB], obs[0][0][0:65, 0:TB]),
                         {}, reads=(obs[0][0][0:65, 0:TB],), writes=(oraw[0:65, 0:TB],),
                         tag=f"{tag}ev1")
                    B.op("dve", "tensor_copy", (oraw2[0:64, 0:TB], obs[1][0][0:64, 0:TB]),
                         {}, reads=(obs[1][0][0:64, 0:TB],), writes=(oraw2[0:64, 0:TB],),
                         tag=f"{tag}ev2")
                else:
                    B.op("dve", "tensor_copy", (oraw[0:dv + 1, 0:TB],
                                                obs[0][0][0:dv + 1, 0:TB]), {},
                         reads=(obs[0][0][0:dv + 1, 0:TB],),
                         writes=(oraw[0:dv + 1, 0:TB],), tag=f"{tag}ev1")
                pbc = obs[0][0][:, 0:TB]
                B.mm(pbc, onesq[drow_p:drow_p + 1, 0:128],
                     oraw[drow_p:drow_p + 1, 0:TB], start=True, stop=True,
                     tile_position=(drow_p, 0), tag=f"{tag}bcm")
                B.op("dve", "reciprocal", (bcf[:, 0:TB], pbc), {},
                     reads=(pbc,), writes=(bcf[:, 0:TB],), tag=f"{tag}bce")
                if big_v:
                    B.op("dve", "tensor_mul", (ost[0:64, 0:TB], oraw[0:64, 0:TB],
                                               bcf[0:64, 0:TB]), {},
                         reads=(oraw[0:64, 0:TB], bcf[0:64, 0:TB]),
                         writes=(ost[0:64, 0:TB],), tag=f"{tag}nr1")
                    B.op("dve", "tensor_mul", (ost2[0:64, 0:TB], oraw2[0:64, 0:TB],
                                               bcf[0:64, 0:TB]), {},
                         reads=(oraw2[0:64, 0:TB], bcf[0:64, 0:TB]),
                         writes=(ost2[0:64, 0:TB],), tag=f"{tag}nr2")
                    B.dma("act", O_d[h, 0:64, qb * TB:(qb + 1) * TB], ost[0:64, 0:TB],
                          tag=f"{tag}sto")
                    B.dma("act", O_d[h, 64:128, qb * TB:(qb + 1) * TB], ost2[0:64, 0:TB],
                          tag=f"{tag}sto2")
                else:
                    r0 = (h * dv) % 128
                    B.op("dve", "tensor_mul", (ost[0:dv, 0:TB],
                                               oraw[0:dv, 0:TB], bcf[0:dv, 0:TB]),
                         {}, reads=(oraw[0:dv, 0:TB], bcf[0:dv, 0:TB]),
                         writes=(ost[0:dv, 0:TB],), tag=f"{tag}nr")
                    B.dma("act", O_d[(h * dv) // 128, r0:r0 + dv, qb * TB:(qb + 1) * TB],
                          ost[0:dv, 0:TB], tag=f"{tag}sto")



def attn_out_proj(B, C, O_d, W_d, xin_d, xout_d, tag, bias_pvcol=None,
                  extra_add_d=None, xin_dtype=F32, out_dtype=F32, ncol=D):
    u = B.uid()
    nic = ncol // 128
    ofm = B.sbuf(f"{tag}_of{u}", [128, nic * TB], BF16)
    ws = B.sbuf(f"{tag}_w{u}", [128, nic * D], BF16)
    xs = B.sbuf(f"{tag}_xs{u}", [128, TB], xin_dtype)
    es = B.sbuf(f"{tag}_es{u}", [128, TB], BF16) if extra_add_d is not None else None
    outs = B.sbuf(f"{tag}_os{u}", [128, TB], out_dtype)
    w_view = W_d.rearrange("(kc kp) n -> kc kp n", kp=128)
    for kc in range(nic):
        B.dma("pool", ws[:, kc * D:(kc + 1) * D], w_view[kc, :, 0:D], tag=f"{tag}ldw")
    bank = C.bank
    bkb = C.bankb
    for tb in range(NTB):
        t0 = tb * TB
        for fc in range(nic):
            B.dma("sp", ofm[:, fc * TB:(fc + 1) * TB], O_d[fc, :, t0:t0 + TB],
                  tag=f"{tag}ldo")
        for oc in range(NFC):
            pb = bank[oc % 4][:, 0:TB]
            for kc in range(nic):
                B.mm(pb, ws[:, kc * D + oc * 128: kc * D + (oc + 1) * 128],
                     ofm[:, kc * TB:(kc + 1) * TB],
                     start=(kc == 0), stop=(kc == nic - 1), tag=f"{tag}mm")
            B.dma("sp", xs[:, 0:TB], xin_d[oc, :, t0:t0 + TB], tag=f"{tag}ldx")
            if bias_pvcol is not None:
                B.op("dve", "scalar_tensor_tensor",
                     (outs[:, 0:TB], pb, C.pv(bias_pvcol + oc), xs[:, 0:TB],
                      ALU.add, ALU.add), {},
                     reads=(pb, C.pv(bias_pvcol + oc), xs[:, 0:TB]),
                     writes=(outs[:, 0:TB],), tag=f"{tag}ev")
            else:
                B.op("dve", "tensor_add", (outs[:, 0:TB], pb, xs[:, 0:TB]), {},
                     reads=(pb, xs[:, 0:TB]), writes=(outs[:, 0:TB],), tag=f"{tag}ev")
            if extra_add_d is not None:
                B.dma("sp", es[:, 0:TB], extra_add_d[oc, :, t0:t0 + TB], tag=f"{tag}lde")
                B.op("dve", "tensor_add", (outs[:, 0:TB], outs[:, 0:TB], es[:, 0:TB]), {},
                     reads=(outs[:, 0:TB], es[:, 0:TB]), writes=(outs[:, 0:TB],),
                     tag=f"{tag}ev2")
            B.dma("act", xout_d[oc, :, t0:t0 + TB], outs[:, 0:TB], tag=f"{tag}st")


def phase_l0attn_av(B, C, Tn, cr):
    attention(B, C, Tn["Q1"], Tn["K1"], Tn["V1"], Tn["O1"], 16, 64, "p2")


def phase_l0attn_out(B, C, Tn, cr):
    attn_out_proj(B, C, Tn["O1"], Tn["L0_attn_W_out"], Tn["x0"], Tn["x1"], "p3",
                  bias_pvcol=cr.pvt_index["l0a_bout"])





def moe_gate(B, C, h_d, wgate_d, G_d, tag, ebias_bc_d=None):
    u = B.uid()
    ws = B.sbuf(f"{tag}_w{u}", [128, NFC * 8], BF16)
    hs = B.sbuf(f"{tag}_h{u}", [128, NFC * 128], BF16)
    pr = B.sbuf(f"{tag}_p{u}", [128, 8], F32)
    q2 = B.sbuf(f"{tag}_q2{u}", [128, 8], F32)
    eq1 = B.sbuf(f"{tag}_e1{u}", [128, 8], F32)
    eq2 = B.sbuf(f"{tag}_e2{u}", [128, 8], F32)
    gout = B.sbuf(f"{tag}_g{u}", [128, 8], F32)
    col = B.sbuf(f"{tag}_c{u}", [128, 6], F32)
    eb = None
    if ebias_bc_d is not None:
        eb = B.sbuf(f"{tag}_eb{u}", [128, 8], F32)
        B.dma("pool", eb[:, :], ebias_bc_d[:, :], tag=f"{tag}ldeb")
    lg = B.sbuf(f"{tag}_lg{u}", [128, 8], F32)
    w_view = wgate_d.rearrange("(kc kp) n -> kc kp n", kp=128)
    for kc in range(NFC):
        B.dma("pool", ws[:, kc * 8:(kc + 1) * 8], w_view[kc, :, 0:8], tag=f"{tag}ldw")
    bank = C.bank
    for tt in range(T // 128):
        t0 = tt * 128
        for kc in range(NFC):
            B.dma("sp", hs[:, kc * 128:(kc + 1) * 128], h_d[kc, :, t0:t0 + 128],
                  tag=f"{tag}ldh")
        pb = bank[tt % 2][:, 0:8]
        for kc in range(NFC):
            B.mm(pb, hs[:, kc * 128:(kc + 1) * 128], ws[:, kc * 8:(kc + 1) * 8],
                 start=(kc == 0), stop=(kc == NFC - 1), tag=f"{tag}mm")
        if eb is not None:
            B.op("dve", "tensor_add", (lg[:, :], pb, eb[:, :]), {},
                 reads=(pb, eb[:, :]), writes=(lg[:, :],), tag=f"{tag}eb")
            lsrc = lg[:, :]
        else:
            B.op("dve", "tensor_copy", (lg[:, :], pb), {},
                 reads=(pb,), writes=(lg[:, :],), tag=f"{tag}cp")
            lsrc = lg[:, :]
        B.op("dve", "reduce_max", (col[:, 0:1], lsrc), dict(axis=AX.X, negate=True),
             reads=(lsrc,), writes=(col[:, 0:1],), tag=f"{tag}mx")
        B.act(pr[:, :], lsrc, AF.Exp, bias=col[:, 0:1], tag=f"{tag}ex")
        B.op("dve", "reduce_sum", (col[:, 1:2], pr[:, :]), dict(axis=AX.X),
             reads=(pr[:, :],), writes=(col[:, 1:2],), tag=f"{tag}sm")
        B.op("dve", "reciprocal", (col[:, 2:3], col[:, 1:2]), {},
             reads=(col[:, 1:2],), writes=(col[:, 2:3],), tag=f"{tag}rc")
        B.op("dve", "tensor_scalar", (pr[:, :], pr[:, :], col[:, 2:3], None, ALU.mult),
             {}, reads=(pr[:, :], col[:, 2:3]), writes=(pr[:, :],), tag=f"{tag}pn")
        B.op("dve", "reduce_max", (col[:, 3:4], pr[:, :]), dict(axis=AX.X),
             reads=(pr[:, :],), writes=(col[:, 3:4],), tag=f"{tag}m1")
        B.op("dve", "tensor_scalar", (eq1[:, :], pr[:, :], col[:, 3:4], None, ALU.is_equal),
             {}, reads=(pr[:, :], col[:, 3:4]), writes=(eq1[:, :],), tag=f"{tag}e1")
        B.op("dve", "scalar_tensor_tensor", (q2[:, :], eq1[:, :], -2.0, pr[:, :],
                                             ALU.mult, ALU.add), {},
             reads=(eq1[:, :], pr[:, :]), writes=(q2[:, :],), tag=f"{tag}q2")
        B.op("dve", "reduce_max", (col[:, 4:5], q2[:, :]), dict(axis=AX.X),
             reads=(q2[:, :],), writes=(col[:, 4:5],), tag=f"{tag}m2")
        B.op("dve", "tensor_scalar", (eq2[:, :], q2[:, :], col[:, 4:5], None, ALU.is_equal),
             {}, reads=(q2[:, :], col[:, 4:5]), writes=(eq2[:, :],), tag=f"{tag}e2")
        B.op("dve", "tensor_add", (col[:, 5:6], col[:, 3:4], col[:, 4:5]), {},
             reads=(col[:, 3:5],), writes=(col[:, 5:6],), tag=f"{tag}dn")
        B.op("dve", "reciprocal", (col[:, 5:6], col[:, 5:6]), {},
             reads=(col[:, 5:6],), writes=(col[:, 5:6],), tag=f"{tag}rd")
        B.op("dve", "tensor_mul", (col[:, 3:4], col[:, 3:4], col[:, 5:6]), {},
             reads=(col[:, 3:4], col[:, 5:6]), writes=(col[:, 3:4],), tag=f"{tag}a1")
        B.op("dve", "tensor_mul", (col[:, 4:5], col[:, 4:5], col[:, 5:6]), {},
             reads=(col[:, 4:5], col[:, 5:6]), writes=(col[:, 4:5],), tag=f"{tag}a2")
        B.op("dve", "tensor_scalar", (gout[:, :], eq1[:, :], col[:, 3:4], None, ALU.mult),
             {}, reads=(eq1[:, :], col[:, 3:4]), writes=(gout[:, :],), tag=f"{tag}g1")
        B.op("dve", "scalar_tensor_tensor", (gout[:, :], eq2[:, :], col[:, 4:5],
                                             gout[:, :], ALU.mult, ALU.add), {},
             reads=(eq2[:, :], col[:, 4:5], gout[:, :]), writes=(gout[:, :],), tag=f"{tag}g2")
        B.dma("act", G_d[t0:t0 + 128, 0:8], gout[:, :], tag=f"{tag}st")



def add_streams(B, C, a_d, b_d, out_d, tag, a_dtype=F32):
    u = B.uid()
    xa = B.sbuf(f"{tag}_a{u}", [128, TB], a_dtype)
    xb = B.sbuf(f"{tag}_b{u}", [128, TB], BF16)
    xo = B.sbuf(f"{tag}_o{u}", [128, TB], F32)
    for fc in range(NFC):
        for tb in range(NTB):
            t0 = tb * TB
            B.dma("sp", xa[:, 0:TB], a_d[fc, :, t0:t0 + TB], tag=f"{tag}la")
            B.dma("sp", xb[:, 0:TB], b_d[fc, :, t0:t0 + TB], tag=f"{tag}lb")
            B.op("dve", "tensor_add", (xo[:, 0:TB], xa[:, 0:TB], xb[:, 0:TB]), {},
                 reads=(xa[:, 0:TB], xb[:, 0:TB]), writes=(xo[:, 0:TB],), tag=f"{tag}ad")
            B.dma("act", out_d[fc, :, t0:t0 + TB], xo[:, 0:TB], tag=f"{tag}st")



def moe_dense_l0(B, C, h_d, w1_d, w2_d, G_d, x2_d, tag, b1cols, b2cols,
                 dff=4096, nexp=8):
    u = B.uid()
    TQ = 512
    ndf = dff // 128
    ws1 = B.sbuf(f"{tag}_w1{u}", [128, NFC * dff], BF16)
    ws2 = B.sbuf(f"{tag}_w2{u}", [128, ndf * D], BF16)
    hs = B.sbuf(f"{tag}_h{u}", [128, NFC * TQ], BF16)
    he = B.sbuf(f"{tag}_he{u}", [128, ndf * TQ], BF16)
    gt = B.sbuf(f"{tag}_gt{u}", [128, 8], F32)
    gtr = B.sbuf(f"{tag}_gr{u}", [8, T], F32)
    gbc = B.sbuf(f"{tag}_gb{u}", [128, TQ], F32)
    cst = B.sbuf(f"{tag}_cs{u}", [128, TQ], F32)
    esel = B.sbuf(f"{tag}_es{u}", [8, 8 * 128], F32)
    B.dma("pool", esel[:, :], C.esel_d[:, :], tag=f"{tag}lde")
    bank = C.bank
    for jj in range(T // 128):
        B.dma("sp", gt[:, 0:8], G_d[jj * 128:(jj + 1) * 128, 0:8], tag=f"{tag}ldg")
        pt_ = bank[6][0:8, 0:128]
        B.transpose(pt_, gt[:, 0:8], C.identf[:, :], tag=f"{tag}gtr")
        B.op("dve", "tensor_copy", (gtr[0:8, jj * 128:(jj + 1) * 128], pt_), {},
             reads=(pt_,), writes=(gtr[0:8, jj * 128:(jj + 1) * 128],),
             tag=f"{tag}gc")
    w1v = w1_d.rearrange("e (kc kp) n -> e kc kp n", kp=128)
    w2v = w2_d.rearrange("e (kc kp) n -> e kc kp n", kp=128)
    for e in range(nexp):
        for kc in range(NFC):
            B.dma("pool", ws1[:, kc * dff:(kc + 1) * dff], w1v[e, kc, :, 0:dff],
                  tag=f"{tag}lw1")
        for kc in range(ndf):
            B.dma("pool", ws2[:, kc * D:(kc + 1) * D], w2v[e, kc, :, 0:D],
                  tag=f"{tag}lw2")
        for tq in range(T // TQ):
            t0 = tq * TQ
            for kc in range(NFC):
                B.dma("sp", hs[:, kc * TQ:(kc + 1) * TQ], h_d[kc, :, t0:t0 + TQ],
                      tag=f"{tag}ldh")
            pgb = bank[6][:, 0:TQ]
            B.mm(pgb, esel[0:8, e * 128:(e + 1) * 128], gtr[0:8, t0:t0 + TQ],
                 start=True, stop=True, tag=f"{tag}gbm")
            B.op("dve", "tensor_copy", (gbc[:, 0:TQ], pgb), {},
                 reads=(pgb,), writes=(gbc[:, 0:TQ],), tag=f"{tag}gbe")
            for fu in range(ndf):
                pb = bank[4 + fu % 2][:, 0:TQ]
                for kc in range(NFC):
                    B.mm(pb, ws1[:, kc * dff + fu * 128: kc * dff + (fu + 1) * 128],
                         hs[:, kc * TQ:(kc + 1) * TQ],
                         start=(kc == 0), stop=(kc == NFC - 1), tag=f"{tag}up")
                B.act(he[:, fu * TQ:(fu + 1) * TQ], pb, AF.Gelu,
                      bias=C.pv(b1cols[e] + fu), tag=f"{tag}gl")
            for oc in range(NFC):
                pb = bank[oc % 4][:, 0:TQ]
                for kc in range(ndf):
                    B.mm(pb, ws2[:, kc * D + oc * 128: kc * D + (oc + 1) * 128],
                         he[:, kc * TQ:(kc + 1) * TQ],
                         start=(kc == 0), stop=(kc == ndf - 1),
                         tag=f"{tag}dn")
                B.op("dve", "scalar_tensor_tensor",
                     (cst[:, 0:TQ], pb, C.pv(b2cols[e] + oc), gbc[:, 0:TQ],
                      ALU.add, ALU.mult), {},
                     reads=(pb, C.pv(b2cols[e] + oc), gbc[:, 0:TQ]),
                     writes=(cst[:, 0:TQ],), tag=f"{tag}ev")
                B.dma("pool", x2_d[oc, :, t0:t0 + TQ], cst[:, 0:TQ],
                      accum_op=ALU.add, tag=f"{tag}ac")


def phase_l0ffn_qkv(B, C, Tn, cr):
    ln_full(B, C, Tn["x1"], Tn["u1"], "p4ln")
    ln_full(B, C, Tn["u1"], Tn["h1"], "p4l2", g_rowcol=cr.row_index["l0f_g1"],
            b_rowcol=cr.row_index["l0f_b1"], src_dtype=BF16)
    proj_fm(B, C, Tn["h1"], Tn["L0_ffn_w_qkv"], Tn["Q2"], D, "p4q",
            scale=0.125, col_off=0)
    proj_fm(B, C, Tn["h1"], Tn["L0_ffn_w_qkv"], Tn["K2"], D, "p4k", col_off=D)
    proj_tm_v(B, C, Tn["h1"], Tn["L0_ffn_w_qkv"], Tn["V2"], D, 16, 64, "p4v",
              col_off=2 * D)


def phase_l0ffn_av(B, C, Tn, cr):
    attention(B, C, Tn["Q2"], Tn["K2"], Tn["V2"], Tn["O2"], 16, 64, "p5")


def phase_l0ffn_mid(B, C, Tn, cr):
    attn_out_proj(B, C, Tn["O2"], Tn["L0_ffn_w_o"], Tn["u1"], Tn["xr"], "p6",
                  xin_dtype=BF16, out_dtype=BF16)
    ln_full(B, C, Tn["xr"], Tn["h2"], "p6l2", g_rowcol=cr.row_index["l0f_g2"],
            b_rowcol=cr.row_index["l0f_b2"], src_dtype=BF16)
    moe_gate(B, C, Tn["h2"], Tn["L0_ffn_w_gate"], Tn["G1"], "p6g")


def phase_l0ffn_moe(B, C, Tn, cr):
    add_streams(B, C, Tn["x1"], Tn["xr"], Tn["x2"], "p7i")
    b1cols = [cr.pvt_index[f"l0f_b1_{e}"] for e in range(8)]
    b2cols = [cr.pvt_index[f"l0f_b2_{e}"] for e in range(8)]
    moe_dense_l0(B, C, Tn["h2"], Tn["L0_ffn_w1"], Tn["L0_ffn_w2"], Tn["G1"],
                 Tn["x2"], "p7", b1cols, b2cols)





def proj_tm(B, C, h_d, w_d, out_d, dout, tag, col_off=0, scale=None, wT=False):
    u = B.uid()
    ws = B.sbuf(f"{tag}_w{u}", [128, NFC * dout], BF16)
    hs = B.sbuf(f"{tag}_h{u}", [128, NFC * 128], BF16)
    os_ = B.sbuf(f"{tag}_o{u}", [128, dout], BF16)
    w_view = w_d.rearrange("(kc kp) n -> kc kp n", kp=128)
    for kc in range(NFC):
        B.dma("pool", ws[:, kc * dout:(kc + 1) * dout], w_view[kc, :, col_off:col_off + dout],
              tag=f"{tag}ldw")
    bank = C.bank
    nnc = (dout + 511) // 512
    for tt in range(T // 128):
        t0 = tt * 128
        for kc in range(NFC):
            B.dma("sp", hs[:, kc * 128:(kc + 1) * 128], h_d[kc, :, t0:t0 + 128],
                  tag=f"{tag}ldh")
        for nc_ in range(nnc):
            n0 = nc_ * 512
            nw = min(512, dout - n0)
            pb = bank[nc_ % 4][:, 0:nw]
            for kc in range(NFC):
                B.mm(pb, hs[:, kc * 128:(kc + 1) * 128], ws[:, kc * dout + n0: kc * dout + n0 + nw],
                     start=(kc == 0), stop=(kc == NFC - 1), tag=f"{tag}mm")
            if scale is not None:
                B.op("dve", "tensor_scalar", (os_[:, n0:n0 + nw], pb, scale, None, ALU.mult),
                     {}, reads=(pb,), writes=(os_[:, n0:n0 + nw],), tag=f"{tag}ev")
            else:
                B.op("dve", "tensor_copy", (os_[:, n0:n0 + nw], pb), {},
                     reads=(pb,), writes=(os_[:, n0:n0 + nw],), tag=f"{tag}ev")
        B.dma("act", out_d[t0:t0 + 128, 0:dout], os_[:, 0:dout], tag=f"{tag}st")



def rope_tm(B, C, x_d, cos_d, sin_d, width, ranges, nfreq, tag):
    u = B.uid()
    xt = B.sbuf(f"{tag}_x{u}", [128, width], BF16)
    cs = B.sbuf(f"{tag}_c{u}", [128, nfreq], F32)
    sn = B.sbuf(f"{tag}_s{u}", [128, nfreq], F32)
    t1 = B.sbuf(f"{tag}_t1{u}", [128, nfreq], F32)
    t2 = B.sbuf(f"{tag}_t2{u}", [128, nfreq], F32)
    t3 = B.sbuf(f"{tag}_t3{u}", [128, nfreq], F32)
    for tt in range(T // 128):
        t0 = tt * 128
        B.dma("sp", xt[:, 0:width], x_d[t0:t0 + 128, 0:width], tag=f"{tag}ldx")
        B.dma("sp", cs[:, :], cos_d[t0:t0 + 128, 0:nfreq], tag=f"{tag}ldc")
        B.dma("sp", sn[:, :], sin_d[t0:t0 + 128, 0:nfreq], tag=f"{tag}lds")
        for c0 in ranges:
            ev = xt[:, c0: c0 + 2 * nfreq: 2]
            od = xt[:, c0 + 1: c0 + 2 * nfreq: 2]
            rr = xt[:, c0:c0 + 2 * nfreq]
            B.op("dve", "tensor_mul", (t1[:, :], ev, cs[:, :]), {},
                 reads=(rr, cs[:, :]), writes=(t1[:, :],), tag=f"{tag}a")
            B.op("dve", "tensor_mul", (t2[:, :], od, sn[:, :]), {},
                 reads=(rr, sn[:, :]), writes=(t2[:, :],), tag=f"{tag}b")
            B.op("dve", "tensor_sub", (t1[:, :], t1[:, :], t2[:, :]), {},
                 reads=(t1[:, :], t2[:, :]), writes=(t1[:, :],), tag=f"{tag}c")
            B.op("dve", "tensor_mul", (t2[:, :], od, cs[:, :]), {},
                 reads=(rr, cs[:, :]), writes=(t2[:, :],), tag=f"{tag}d")
            B.op("dve", "tensor_mul", (t3[:, :], ev, sn[:, :]), {},
                 reads=(rr, sn[:, :]), writes=(t3[:, :],), tag=f"{tag}e")
            B.op("dve", "tensor_add", (t2[:, :], t2[:, :], t3[:, :]), {},
                 reads=(t2[:, :], t3[:, :]), writes=(t2[:, :],), tag=f"{tag}f")
            B.op("dve", "tensor_copy", (ev, t1[:, :]), {},
                 reads=(t1[:, :],), writes=(rr,), tag=f"{tag}g")
            B.op("dve", "tensor_copy", (od, t2[:, :]), {},
                 reads=(t2[:, :],), writes=(rr,), tag=f"{tag}h")
        B.dma("act", x_d[t0:t0 + 128, 0:width], xt[:, 0:width], tag=f"{tag}st")



def tm_to_fm(B, C, x_d, out_d, w, tag, col_groups=None):
    u = B.uid()
    nch = (w // 128) if col_groups is None else len(col_groups)
    xt = B.sbuf(f"{tag}_x{u}", [128, w], BF16)
    st = B.sbuf(f"{tag}_s{u}", [128, 128], BF16)
    bkb = C.bankb
    for tt in range(T // 128):
        t0 = tt * 128
        B.dma("sp", xt[:, 0:w], x_d[t0:t0 + 128, 0:w], tag=f"{tag}ldx")
        for ch in range(nch):
            if col_groups is None:
                segs = [(ch * 128, 128)]
            else:
                segs = col_groups[ch]
            pt_ = bkb[:, 0:128]
            p0 = 0
            for (s0, span) in segs:
                B.transpose(bkb[p0:p0 + span, 0:128], xt[:, s0:s0 + span],
                            C.identb[:, :], tag=f"{tag}tr")
                p0 += span
            B.op("dve", "tensor_copy", (st[:, 0:128], pt_), {},
                 reads=(pt_,), writes=(st[:, 0:128],), tag=f"{tag}cp")
            B.dma("act", out_d[ch, :, t0:t0 + 128], st[:, 0:128], tag=f"{tag}st")



LAMBDA_INIT = 0.8 - 0.6 * math.exp(-0.3 * 12)


def l1attn_combine(B, C, o1_d, o2_d, lam_d, lnw_d, lnb_d, y_d, tag):
    u = B.uid()
    o1 = B.sbuf(f"{tag}_o1{u}", [128, TB], BF16)
    o2 = B.sbuf(f"{tag}_o2{u}", [128, TB], BF16)
    oc = B.sbuf(f"{tag}_oc{u}", [128, TB], F32)
    sq = B.sbuf(f"{tag}_sq{u}", [128, TB], F32)
    bcf = B.sbuf(f"{tag}_bc{u}", [128, TB], F32)
    yt = B.sbuf(f"{tag}_y{u}", [128, TB], BF16)
    rrow = B.sbuf(f"{tag}_rr{u}", [1, TB], F32)
    lv = B.sbuf(f"{tag}_lv{u}", [1, 4 * 64], F32)
    dots = B.sbuf(f"{tag}_dt{u}", [1, 4], F32)
    lamr = B.sbuf(f"{tag}_lm{u}", [1, 1], F32)
    nlam = B.sbuf(f"{tag}_nl{u}", [128, 1], F32)
    lnw = B.sbuf(f"{tag}_lw{u}", [128, 2], F32)
    onesc = B.sbuf(f"{tag}_1c{u}", [128, 1], F32)
    B.op("pool", "memset", (onesc[:, :], 1.0), {}, writes=(onesc[:, :],))
    B.dma("pool", lnw[:, 0:1], lnw_d[:, 0:1], tag=f"{tag}lw")
    B.dma("pool", lnw[:, 1:2], lnb_d[:, 0:1], tag=f"{tag}lb")
    B.dma("pool", lv[0:1, :], lam_d[0:1, :], tag=f"{tag}ll")
    B.op("dve", "tensor_scalar", (lnw[:, 0:2], lnw[:, 0:2], 1.0 - LAMBDA_INIT, None,
                                  ALU.mult), {},
         reads=(lnw[:, 0:2],), writes=(lnw[:, 0:2],), tag=f"{tag}sw")
    for i, (a, b) in enumerate(((0, 64), (128, 192))):
        B.op("dve", "tensor_mul", (lv[0:1, a:a + 64], lv[0:1, a:a + 64], lv[0:1, b:b + 64]),
             {}, reads=(lv[0:1, :],), writes=(lv[0:1, a:a + 64],), tag=f"{tag}m{i}")
        B.op("dve", "reduce_sum", (dots[0:1, i:i + 1], lv[0:1, a:a + 64]), dict(axis=AX.X),
             reads=(lv[0:1, a:a + 64],), writes=(dots[0:1, i:i + 1],), tag=f"{tag}r{i}")
    B.act(dots[0:1, 0:2], dots[0:1, 0:2], AF.Exp, tag=f"{tag}ex")
    B.op("dve", "tensor_sub", (lamr[0:1, 0:1], dots[0:1, 0:1], dots[0:1, 1:2]), {},
         reads=(dots[0:1, 0:2],), writes=(lamr[0:1, 0:1],), tag=f"{tag}sub")
    B.op("dve", "tensor_scalar", (lamr[0:1, 0:1], lamr[0:1, 0:1], LAMBDA_INIT, -1.0,
                                  ALU.add, ALU.mult), {},
         reads=(lamr[0:1, 0:1],), writes=(lamr[0:1, 0:1],), tag=f"{tag}neg")
    pb0 = C.bank[5][:, 0:1]
    B.mm(pb0, C.onesrowf[0:1, :], lamr[0:1, 0:1], start=True, stop=True, tag=f"{tag}rep")
    B.op("dve", "tensor_copy", (nlam[:, 0:1], pb0), {}, reads=(pb0,),
         writes=(nlam[:, 0:1],), tag=f"{tag}cp")
    inv128 = 1.0 / 128.0
    bank = C.bank
    for h in range(8):
        for tb in range(NTB):
            t0 = tb * TB
            B.dma("sp", o1[:, 0:TB], o1_d[h, :, t0:t0 + TB], tag=f"{tag}l1")
            B.dma("sp", o2[:, 0:TB], o2_d[h, :, t0:t0 + TB], tag=f"{tag}l2")
            B.op("dve", "scalar_tensor_tensor", (oc[:, 0:TB], o2[:, 0:TB], nlam[:, 0:1],
                                                 o1[:, 0:TB], ALU.mult, ALU.add), {},
                 reads=(o2[:, 0:TB], nlam[:, 0:1], o1[:, 0:TB]), writes=(oc[:, 0:TB],),
                 tag=f"{tag}cmb")
            B.act(sq[:, 0:TB], oc[:, 0:TB], AF.Square, tag=f"{tag}sq")
            ps = bank[tb % 2][0:1, 0:TB]
            B.mm(ps, onesc[:, 0:1], sq[:, 0:TB], start=True, stop=True, tag=f"{tag}ss")
            B.act(rrow[0:1, 0:TB], ps, AF.Ln, bias=C.pv(C.cr.pvt_index["eps_a8"])[0:1, 0:1],
                  scale=inv128, tag=f"{tag}ln")
            B.act(rrow[0:1, 0:TB], rrow[0:1, 0:TB], AF.Exp, scale=-0.5, tag=f"{tag}ep")
            pbc = bank[2 + tb % 2][:, 0:TB]
            B.mm(pbc, C.onesrowf[0:1, :], rrow[0:1, 0:TB], start=True, stop=True,
                 tag=f"{tag}bm")
            B.op("dve", "tensor_copy", (bcf[:, 0:TB], pbc), {},
                 reads=(pbc,), writes=(bcf[:, 0:TB],), tag=f"{tag}be")
            B.op("dve", "tensor_mul", (oc[:, 0:TB], oc[:, 0:TB], bcf[:, 0:TB]), {},
                 reads=(oc[:, 0:TB], bcf[:, 0:TB]), writes=(oc[:, 0:TB],), tag=f"{tag}n1")
            B.op("dve", "tensor_scalar", (yt[:, 0:TB], oc[:, 0:TB], lnw[:, 0:1],
                                          lnw[:, 1:2], ALU.mult, ALU.add), {},
                 reads=(oc[:, 0:TB], lnw[:, 0:2]), writes=(yt[:, 0:TB],), tag=f"{tag}n2")
            B.dma("act", y_d[h, :, t0:t0 + TB], yt[:, 0:TB], tag=f"{tag}st")



def moe_l1(B, C, h_d, Wg_d, Wu_d, Wd_d, sWg_d, sWu_d, sWd_d, G_d, x4_d, tag,
           dint=512):
    u = B.uid()
    TQ = 512
    ndi = dint // 128
    wsg = B.sbuf(f"{tag}_wg{u}", [128, NFC * dint], BF16)
    wsu = B.sbuf(f"{tag}_wu{u}", [128, NFC * dint], BF16)
    wsd = B.sbuf(f"{tag}_wd{u}", [128, ndi * D], BF16)
    hs = B.sbuf(f"{tag}_h{u}", [128, NFC * TQ], BF16)
    gu = B.sbuf(f"{tag}_gu{u}", [128, ndi * TQ], BF16)
    us = B.sbuf(f"{tag}_us{u}", [128, TQ], F32)
    gt = B.sbuf(f"{tag}_gt{u}", [128, 8], F32)
    gtr = B.sbuf(f"{tag}_gr{u}", [8, T], F32)
    gbc = B.sbuf(f"{tag}_gb{u}", [128, TQ], F32)
    cst = B.sbuf(f"{tag}_cs{u}", [128, TQ], F32)
    esel = B.sbuf(f"{tag}_es{u}", [8, 8 * 128], F32)
    B.dma("pool", esel[:, :], C.esel_d[:, :], tag=f"{tag}lde")
    bank = C.bank
    for jj in range(T // 128):
        B.dma("sp", gt[:, 0:8], G_d[jj * 128:(jj + 1) * 128, 0:8], tag=f"{tag}ldg")
        pt_ = bank[6][0:8, 0:128]
        B.transpose(pt_, gt[:, 0:8], C.identf[:, :], tag=f"{tag}gtr")
        B.op("dve", "tensor_copy", (gtr[0:8, jj * 128:(jj + 1) * 128], pt_), {},
             reads=(pt_,), writes=(gtr[0:8, jj * 128:(jj + 1) * 128],),
             tag=f"{tag}gc")
    Wgv = Wg_d.rearrange("e (kc kp) n -> e kc kp n", kp=128)
    Wuv = Wu_d.rearrange("e (kc kp) n -> e kc kp n", kp=128)
    Wdv = Wd_d.rearrange("e (kc kp) n -> e kc kp n", kp=128)
    sWgv = sWg_d.rearrange("e (kc kp) n -> e kc kp n", kp=128)
    sWuv = sWu_d.rearrange("e (kc kp) n -> e kc kp n", kp=128)
    sWdv = sWd_d.rearrange("e (kc kp) n -> e kc kp n", kp=128)
    for e in range(-1, 8):
        gv, uv, dv_ = (sWgv, sWuv, sWdv) if e < 0 else (Wgv, Wuv, Wdv)
        ei = 0 if e < 0 else e
        for kc in range(NFC):
            B.dma("pool", wsg[:, kc * dint:(kc + 1) * dint], gv[ei, kc, :, 0:dint],
                  tag=f"{tag}lwg")
            B.dma("pool", wsu[:, kc * dint:(kc + 1) * dint], uv[ei, kc, :, 0:dint],
                  tag=f"{tag}lwu")
        for kc in range(ndi):
            B.dma("pool", wsd[:, kc * D:(kc + 1) * D], dv_[ei, kc, :, 0:D],
                  tag=f"{tag}lwd")
        for tq in range(T // TQ):
            t0 = tq * TQ
            for kc in range(NFC):
                B.dma("sp", hs[:, kc * TQ:(kc + 1) * TQ], h_d[kc, :, t0:t0 + TQ],
                      tag=f"{tag}ldh")
            if e >= 0:
                pgb = bank[6][:, 0:TQ]
                B.mm(pgb, esel[0:8, e * 128:(e + 1) * 128], gtr[0:8, t0:t0 + TQ],
                     start=True, stop=True, tag=f"{tag}gbm")
                B.op("dve", "tensor_copy", (gbc[:, 0:TQ], pgb), {},
                     reads=(pgb,), writes=(gbc[:, 0:TQ],), tag=f"{tag}gbe")
            for fu in range(ndi):
                pg = bank[4 + 2 * (fu % 2)][:, 0:TQ] if False else bank[(4, 0)[fu % 2]][:, 0:TQ]
                pu = bank[(5, 1)[fu % 2]][:, 0:TQ]
                for kc in range(NFC):
                    B.mm(pg, wsg[:, kc * dint + fu * 128: kc * dint + (fu + 1) * 128],
                         hs[:, kc * TQ:(kc + 1) * TQ],
                         start=(kc == 0), stop=(kc == NFC - 1), tag=f"{tag}ug")
                for kc in range(NFC):
                    B.mm(pu, wsu[:, kc * dint + fu * 128: kc * dint + (fu + 1) * 128],
                         hs[:, kc * TQ:(kc + 1) * TQ],
                         start=(kc == 0), stop=(kc == NFC - 1), tag=f"{tag}uu")
                B.op("dve", "tensor_copy", (us[:, 0:TQ], pu), {},
                     reads=(pu,), writes=(us[:, 0:TQ],), tag=f"{tag}ue")
                B.act(cst[:, 0:TQ], pg, AF.Silu, tag=f"{tag}si")
                B.op("dve", "tensor_mul", (gu[:, fu * TQ:(fu + 1) * TQ], cst[:, 0:TQ],
                                           us[:, 0:TQ]), {},
                     reads=(cst[:, 0:TQ], us[:, 0:TQ]),
                     writes=(gu[:, fu * TQ:(fu + 1) * TQ],), tag=f"{tag}gm")
            for oc in range(NFC):
                pb = bank[oc % 4][:, 0:TQ]
                for kc in range(ndi):
                    B.mm(pb, wsd[:, kc * D + oc * 128: kc * D + (oc + 1) * 128],
                         gu[:, kc * TQ:(kc + 1) * TQ],
                         start=(kc == 0), stop=(kc == ndi - 1), tag=f"{tag}dn")
                if e >= 0:
                    B.op("dve", "tensor_mul", (cst[:, 0:TQ], pb, gbc[:, 0:TQ]), {},
                         reads=(pb, gbc[:, 0:TQ]), writes=(cst[:, 0:TQ],), tag=f"{tag}ev")
                else:
                    B.op("dve", "tensor_copy", (cst[:, 0:TQ], pb), {},
                         reads=(pb,), writes=(cst[:, 0:TQ],), tag=f"{tag}ev")
                B.dma("pool", x4_d[oc, :, t0:t0 + TQ], cst[:, 0:TQ],
                      accum_op=ALU.add, tag=f"{tag}ac")


def phase_l1attn_qkv(B, C, Tn, cr):
    ln_full(B, C, Tn["x2"], Tn["u2"], "p8ln")
    sc = 64.0 ** (-0.25)
    ranges = [64 * s for s in range(16)]
    ga = [[(256 * p, 64), (256 * p + 128, 64)] for p in range(4)]
    gb = [[(256 * p + 64, 64), (256 * p + 192, 64)] for p in range(4)]
    proj_rope_fm(B, C, Tn["u2"], Tn["L1_attn_WqT"],
                 [(Tn["Q3a"], ga), (Tn["Q3b"], gb)],
                 Tn["rope64_cos"], Tn["rope64_sin"], ranges, 32, "p8q", scale=sc)
    proj_rope_fm(B, C, Tn["u2"], Tn["L1_attn_WkT"],
                 [(Tn["K3a"], ga), (Tn["K3b"], gb)],
                 Tn["rope64_cos"], Tn["rope64_sin"], ranges, 32, "p8k")
    proj_tm_v(B, C, Tn["u2"], Tn["L1_attn_WvT"], Tn["V3"], D, 8, 128, "p8v")


def phase_l1attn_av(B, C, Tn, cr):
    attention(B, C, Tn["Q3a"], Tn["K3a"], Tn["V3"], Tn["O3a"], 8, 128, "p9a",
              causal=False)


def phase_l1attn_av2(B, C, Tn, cr):
    attention(B, C, Tn["Q3b"], Tn["K3b"], Tn["V3"], Tn["O3b"], 8, 128, "p9b",
              causal=False)


def phase_l1attn_out(B, C, Tn, cr):
    l1attn_combine(B, C, Tn["O3a"], Tn["O3b"], Tn["lam_cat"], Tn["lnw_col"],
                   Tn["lnb_col"], Tn["Y3"], "pA")
    attn_out_proj(B, C, Tn["Y3"], Tn["L1_attn_WoT"], Tn["x2"], Tn["x3"], "pAo")


def phase_l1ffn_qkv(B, C, Tn, cr):
    ln_full(B, C, Tn["x3"], Tn["u3"], "pBln")
    ln_full(B, C, Tn["u3"], Tn["h3"], "pBl2", g_rowcol=cr.row_index["l1f_g1"],
            rms=True, src_dtype=BF16)
    ga = [[(128 * p, 128)] for p in range(8)]
    proj_rope_fm(B, C, Tn["h3"], Tn["L1_ffn_Wq"], [(Tn["Q4"], ga)],
                 Tn["rope32_cos"], Tn["rope32_sin"],
                 [64 * h + 32 for h in range(16)], 16, "pBq", scale=0.125)
    proj_tm(B, C, Tn["h3"], Tn["L1_ffn_Wkva"], Tn["ckvt"], 160, "pBkv")
    rope_tm(B, C, Tn["ckvt"], Tn["rope32_cos"], Tn["rope32_sin"], 160, [128], 16, "pBrk")


def phase_l1ffn_kv(B, C, Tn, cr):
    B_ = B
    u = B.uid()
    tag = "pC"
    ck = B.sbuf(f"{tag}_ck{u}", [128, 160], BF16)
    lat = B.sbuf(f"{tag}_lt{u}", [128, 128], BF16)
    wkvb = B.sbuf(f"{tag}_w{u}", [128, 512], BF16)
    kvsb = B.sbuf(f"{tag}_kv{u}", [128, 512], BF16)
    trsb = B.sbuf(f"{tag}_tr{u}", [128, 128], BF16)
    krsb = B.sbuf(f"{tag}_kr{u}", [128, 128], BF16)
    v4sb = B.sbuf(f"{tag}_v4{u}", [128, 16 * 33], BF16)
    B.dma("pool", wkvb[:, :], Tn["L1_ffn_Wkvb"][:, :], tag=f"{tag}lw")
    bank = C.bank
    bkb = C.bankb
    for tt in range(T // 128):
        t0 = tt * 128
        B.dma("sp", ck[:, 0:160], Tn["ckvt"][t0:t0 + 128, 0:160], tag=f"{tag}ldc")
        ptb = bkb[:, 0:128]
        B.transpose(ptb, ck[:, 0:128], C.identb[:, :], tag=f"{tag}t1")
        B.op("dve", "tensor_copy", (lat[:, :], ptb), {}, reads=(ptb,),
             writes=(lat[:, :],), tag=f"{tag}c1")
        pkv = bank[4][:, 0:512]
        B.mm(pkv, lat[:, :], wkvb[:, 0:512], start=True, stop=True, tag=f"{tag}mm")
        B.op("dve", "tensor_copy", (kvsb[:, :], pkv), {}, reads=(pkv,),
             writes=(kvsb[:, :],), tag=f"{tag}c2")
        for h in range(16):
            B.op("dve", "tensor_copy",
                 (v4sb[:, h * 33: h * 33 + 32], kvsb[:, h * 32:(h + 1) * 32]), {},
                 reads=(kvsb[:, h * 32:(h + 1) * 32],),
                 writes=(v4sb[:, h * 33: h * 33 + 32],), tag=f"{tag}v")
        ones_view = v4sb.rearrange("p (h e) -> p h e", e=33)[:, :, 32:33]
        B.op("dve", "memset", (ones_view, 1.0), {}, writes=(v4sb[:, 0:16 * 33],),
             tag=f"{tag}on")
        B.dma("act", Tn["V4"][t0:t0 + 128, 0:16 * 33], v4sb[:, 0:16 * 33], tag=f"{tag}sv")
        for j in range(4):
            B.transpose(bkb[:, 0:128], kvsb[:, j * 128:(j + 1) * 128], C.identb[:, :],
                        tag=f"{tag}t2")
            B.op("dve", "tensor_copy", (trsb[:, :], bkb[:, 0:128]), {},
                 reads=(bkb[:, 0:128],), writes=(trsb[:, :],), tag=f"{tag}c3")
            for i in range(4):
                h = 4 * j + i
                B.dma("act", Tn["K4"][h // 2, (h % 2) * 64:(h % 2) * 64 + 32, t0:t0 + 128],
                      trsb[32 * i:32 * i + 32, :], tag=f"{tag}sk")
        B.transpose(bkb[0:32, 0:128], ck[:, 128:160], C.identb[:, :], tag=f"{tag}t3")
        B.op("dve", "tensor_copy", (krsb[0:32, :], bkb[0:32, 0:128]), {},
             reads=(bkb[0:32, 0:128],), writes=(krsb[0:32, :],), tag=f"{tag}c4")
        for p in range(8):
            B.dma("act", Tn["K4"][p, 32:64, t0:t0 + 128], krsb[0:32, :], tag=f"{tag}skr")
            B.dma("act", Tn["K4"][p, 96:128, t0:t0 + 128], krsb[0:32, :], tag=f"{tag}skr2")


def phase_l1ffn_av(B, C, Tn, cr):
    attention(B, C, Tn["Q4"], Tn["K4"], Tn["V4"], Tn["O4"], 16, 32, "pD")


def phase_l1ffn_mid(B, C, Tn, cr):
    attn_out_proj(B, C, Tn["O4"], Tn["L1f_Wo_sub"], Tn["u3"], Tn["xi"], "pE",
                  xin_dtype=BF16, out_dtype=BF16, ncol=512)
    ln_full(B, C, Tn["xi"], Tn["h4"], "pEl2", g_rowcol=cr.row_index["l1f_g2"],
            rms=True, src_dtype=BF16)
    moe_gate(B, C, Tn["h4"], Tn["L1_ffn_Wgate"], Tn["G2"], "pEg",
             ebias_bc_d=Tn["ebias_bc"])


def phase_l1ffn_moe(B, C, Tn, cr):
    add_streams(B, C, Tn["x3"], Tn["xi"], Tn["x4"], "pFi")
    moe_l1(B, C, Tn["h4"], Tn["L1_ffn_Wg"], Tn["L1_ffn_Wu"], Tn["L1_ffn_Wd"],
           Tn["L1_ffn_sWg"], Tn["L1_ffn_sWu"], Tn["L1_ffn_sWd"], Tn["G2"],
           Tn["x4"], "pF")


PHASES[:] = [
    ("l0attn_qkv", ["x0"], ["u_l0", "Q1", "K1", "V1"],
     ["L0_attn_W_qkv"], ["vbias_l0"], phase_l0attn_qkv),
    ("l0attn_av", ["Q1", "K1", "V1"], ["O1"], [], [], phase_l0attn_av),
    ("l0attn_out", ["O1", "x0"], ["x1"], ["L0_attn_W_out"], [], phase_l0attn_out),
    ("l0ffn_qkv", ["x1"], ["u1", "h1", "Q2", "K2", "V2"], ["L0_ffn_w_qkv"], [], phase_l0ffn_qkv),
    ("l0ffn_av", ["Q2", "K2", "V2"], ["O2"], [], [], phase_l0ffn_av),
    ("l0ffn_mid", ["O2", "u1"], ["xr", "h2", "G1"], ["L0_ffn_w_o", "L0_ffn_w_gate"], [], phase_l0ffn_mid),
    ("l0ffn_moe", ["x1", "xr", "h2", "G1"], ["x2"], ["L0_ffn_w1", "L0_ffn_w2"], [], phase_l0ffn_moe),
    ("l1attn_qkv", ["x2"], ["u2", "q3t", "k3t", "Q3a", "Q3b", "K3a", "K3b", "V3"],
     ["L1_attn_WqT", "L1_attn_WkT", "L1_attn_WvT"],
     ["rope64_cos", "rope64_sin"], phase_l1attn_qkv),
    ("l1attn_av", ["Q3a", "K3a", "V3"], ["O3a"], [], [], phase_l1attn_av),
    ("l1attn_av2", ["Q3b", "K3b", "V3"], ["O3b"], [], [], phase_l1attn_av2),
    ("l1attn_out", ["O3a", "O3b", "x2"], ["Y3", "x3"], ["L1_attn_WoT"],
     ["lam_cat", "lnw_col", "lnb_col"], phase_l1attn_out),
    ("l1ffn_qkv", ["x3"], ["u3", "h3", "q4t", "ckvt", "Q4"],
     ["L1_ffn_Wq", "L1_ffn_Wkva"], ["rope32_cos", "rope32_sin"], phase_l1ffn_qkv),
    ("l1ffn_kv", ["ckvt"], ["K4", "V4"], ["L1_ffn_Wkvb"], [], phase_l1ffn_kv),
    ("l1ffn_av", ["Q4", "K4", "V4"], ["O4"], [], [], phase_l1ffn_av),
    ("l1ffn_mid", ["O4", "u3"], ["xi", "h4", "G2"], ["L1f_Wo_sub", "L1_ffn_Wgate"],
     ["ebias_bc"], phase_l1ffn_mid),
    ("l1ffn_moe", ["x3", "xi", "h4", "G2"], ["x4"],
     ["L1_ffn_Wg", "L1_ffn_Wu", "L1_ffn_Wd", "L1_ffn_sWg", "L1_ffn_sWu", "L1_ffn_sWd"],
     [], phase_l1ffn_moe),
]



def proj_rope_fm(B, C, h_d, w_d, outs_fm, cos_d, sin_d, ranges, nfreq, tag,
                 dout=D, scale=None, wT_note=None):
    u = B.uid()
    ws = B.sbuf(f"{tag}_w{u}", [128, NFC * dout], BF16)
    hs = B.sbuf(f"{tag}_h{u}", [128, NFC * 128], BF16)
    os_ = B.sbuf(f"{tag}_o{u}", [128, dout], BF16)
    cs = B.sbuf(f"{tag}_c{u}", [128, nfreq], F32)
    sn = B.sbuf(f"{tag}_s{u}", [128, nfreq], F32)
    t1 = B.sbuf(f"{tag}_t1{u}", [128, nfreq], F32)
    t2 = B.sbuf(f"{tag}_t2{u}", [128, nfreq], F32)
    t3 = B.sbuf(f"{tag}_t3{u}", [128, nfreq], F32)
    st = B.sbuf(f"{tag}_st{u}", [128, 128], BF16)
    w_view = w_d.rearrange("(kc kp) n -> kc kp n", kp=128)
    for kc in range(NFC):
        B.dma("pool", ws[:, kc * dout:(kc + 1) * dout], w_view[kc, :, 0:dout],
              tag=f"{tag}ldw")
    bank = C.bank
    bkb = C.bankb
    nnc = (dout + 511) // 512
    for tt in range(T // 128):
        t0 = tt * 128
        for kc in range(NFC):
            B.dma("sp", hs[:, kc * 128:(kc + 1) * 128], h_d[kc, :, t0:t0 + 128],
                  tag=f"{tag}ldh")
        B.dma("sp", cs[:, :], cos_d[t0:t0 + 128, 0:nfreq], tag=f"{tag}ldc")
        B.dma("sp", sn[:, :], sin_d[t0:t0 + 128, 0:nfreq], tag=f"{tag}lds")
        for nc_ in range(nnc):
            n0 = nc_ * 512
            nw = min(512, dout - n0)
            pb = bank[nc_ % 4][:, 0:nw]
            for kc in range(NFC):
                B.mm(pb, hs[:, kc * 128:(kc + 1) * 128], ws[:, kc * dout + n0: kc * dout + n0 + nw],
                     start=(kc == 0), stop=(kc == NFC - 1), tag=f"{tag}mm")
            if scale is not None:
                B.op("dve", "tensor_scalar", (os_[:, n0:n0 + nw], pb, scale, None, ALU.mult),
                     {}, reads=(pb,), writes=(os_[:, n0:n0 + nw],), tag=f"{tag}ev")
            else:
                B.op("dve", "tensor_copy", (os_[:, n0:n0 + nw], pb), {},
                     reads=(pb,), writes=(os_[:, n0:n0 + nw],), tag=f"{tag}ev")
        for c0 in ranges:
            ev = os_[:, c0: c0 + 2 * nfreq: 2]
            od = os_[:, c0 + 1: c0 + 2 * nfreq: 2]
            rr = os_[:, c0:c0 + 2 * nfreq]
            B.op("dve", "tensor_mul", (t1[:, :], ev, cs[:, :]), {},
                 reads=(rr, cs[:, :]), writes=(t1[:, :],), tag=f"{tag}ra")
            B.op("dve", "tensor_mul", (t2[:, :], od, sn[:, :]), {},
                 reads=(rr, sn[:, :]), writes=(t2[:, :],), tag=f"{tag}rb")
            B.op("dve", "tensor_sub", (t1[:, :], t1[:, :], t2[:, :]), {},
                 reads=(t1[:, :], t2[:, :]), writes=(t1[:, :],), tag=f"{tag}rc")
            B.op("dve", "tensor_mul", (t2[:, :], od, cs[:, :]), {},
                 reads=(rr, cs[:, :]), writes=(t2[:, :],), tag=f"{tag}rd")
            B.op("dve", "tensor_mul", (t3[:, :], ev, sn[:, :]), {},
                 reads=(rr, sn[:, :]), writes=(t3[:, :],), tag=f"{tag}re")
            B.op("dve", "tensor_add", (t2[:, :], t2[:, :], t3[:, :]), {},
                 reads=(t2[:, :], t3[:, :]), writes=(t2[:, :],), tag=f"{tag}rf")
            B.op("dve", "tensor_copy", (ev, t1[:, :]), {},
                 reads=(t1[:, :],), writes=(rr,), tag=f"{tag}rg")
            B.op("dve", "tensor_copy", (od, t2[:, :]), {},
                 reads=(t2[:, :],), writes=(rr,), tag=f"{tag}rh")
        for (out_d, col_groups) in outs_fm:
            nch = len(col_groups)
            for ch in range(nch):
                segs = col_groups[ch]
                p0 = 0
                for (s0, span) in segs:
                    B.transpose(bkb[p0:p0 + span, 0:128], os_[:, s0:s0 + span],
                                C.identb[:, :], tag=f"{tag}tr")
                    p0 += span
                B.op("dve", "tensor_copy", (st[:, 0:128], bkb[:, 0:128]), {},
                     reads=(bkb[:, 0:128],), writes=(st[:, 0:128],), tag=f"{tag}cp")
                B.dma("act", out_d[ch, :, t0:t0 + 128], st[:, 0:128], tag=f"{tag}sf")


# version 10
# speedup vs baseline: 1.4088x; 1.0502x over previous
import math
from contextlib import ExitStack

import numpy as np

import concourse.bass as bass
import concourse.mybir as mybir
from concourse.bass_utils import run_bass_kernel_spmd

F32 = mybir.dt.float32
BF16 = mybir.dt.bfloat16
AF = mybir.ActivationFunctionType
ALU = mybir.AluOpType
AX = mybir.AxisListType

T = 4096
D = 1024
TB = 512
NTB = T // TB
NFC = D // 128
EPS_LN = 1e-5
NEG = -30000.0


ANNOTATE = False
DMA_Q = {"pool": "q_pool", "sp": "q_sp", "act": "q_act", "dve": "q_dve"}


class Reg:
    __slots__ = ("key", "lo", "hi")

    def __init__(self, key, lo, hi):
        self.key, self.lo, self.hi = key, lo, hi


def _ap_region(ap):
    t = ap.tensor
    name = t.name
    if type(t).__name__.startswith("PSum"):
        return Reg(name, 0, 1 << 60)
    shape = list(t.shape)
    pairs = list(ap.ap)
    off = ap.offset
    if len(shape) == 2 and shape[0] > 1:
        n = shape[1]
        free_span = sum(s * (c - 1) for (s, c) in pairs if s != n)
        col = off % n
        return Reg(name, col, col + free_span + 1)
    span = sum(s * (c - 1) for (s, c) in pairs)
    return Reg(name, off, off + span + 1)


class Op:
    __slots__ = ("idx", "veng", "ieng", "method", "args", "kwargs", "waits",
                 "incs", "seq", "tag")

    def __init__(self, idx, veng, ieng, method, args, kwargs, tag=""):
        self.idx = idx
        self.veng = veng
        self.ieng = ieng
        self.method = method
        self.args = args
        self.kwargs = kwargs
        self.waits = []
        self.incs = 0
        self.seq = 0
        self.tag = tag




def op_wave_target(dep_op):
    return dep_op.kwargs.get("_wave_cum", dep_op.seq + 1)

class Builder:
    def __init__(self, name):
        self.nc = bass.Bass("TRN2", target_bir_lowering=False, debug=False)
        self.name = name
        self.ops = []
        self.per_veng = {}
        self.marks = {}
        self.records = {}
        self.qstate = {}
        self.scopes = []
        self.ctx = ExitStack()
        self._uid = 0

    def dram(self, name, shape, dtype, kind="Internal"):
        return self.nc.dram_tensor(name, list(shape), dtype, kind=kind).ap()

    def sbuf(self, name, shape, dtype):
        tgt = self.scopes[-1] if self.scopes else self.ctx
        return tgt.enter_context(self.nc.sbuf_tensor(name, list(shape), dtype))

    def push_scope(self):
        self.scopes.append(ExitStack())

    def pop_scope(self):
        self.barrier()
        st = self.scopes.pop()
        st.close()

    def psum(self, name, shape, dtype):
        return self.ctx.enter_context(self.nc.psum_tensor(name, list(shape), dtype))

    def uid(self):
        self._uid += 1
        return self._uid

    SAME_ENG_SKIP = False
    WATERMARK = True

    def _dep(self, op, dep_op):
        if dep_op is op:
            return
        if self.SAME_ENG_SKIP:
            if dep_op.veng == dep_op.ieng == op.ieng:
                return
        elif dep_op.ieng == "pe" and op.ieng == "pe":
            return
        veng = dep_op.veng
        if veng.startswith("q_") or veng.startswith("sq_"):
            self._close_wave(veng)
            val = 16 * op_wave_target(dep_op)
            for i, (v, w) in enumerate(op.waits):
                if v == veng:
                    if val > w:
                        op.waits[i] = (veng, val)
                    return
            op.waits.append((veng, val))
            return
        marks = self.marks.setdefault(veng, [])
        pos = None
        for i in range(len(marks) - 1, -1, -1):
            if marks[i] >= dep_op.seq:
                pos = i
            else:
                break
        if pos is None:
            marks.append(dep_op.seq)
            self.per_veng[veng][dep_op.seq].incs += 1
            pos = len(marks) - 1
        val = pos + 1
        for i, (v, w) in enumerate(op.waits):
            if v == veng:
                if val > w:
                    op.waits[i] = (veng, val)
                return
        op.waits.append((veng, val))

    def _access(self, op, regs, is_write):
        for r in regs:
            recs = self.records.setdefault(r.key, [])
            if is_write:
                newrecs = []
                for rec in recs:
                    if rec[0] < r.hi and r.lo < rec[1]:
                        if rec[2] is not None:
                            self._dep(op, rec[2])
                        for rd in rec[3]:
                            self._dep(op, rd)
                        if rec[0] < r.lo or rec[1] > r.hi:
                            newrecs.append(rec)
                    else:
                        newrecs.append(rec)
                newrecs.append([r.lo, r.hi, op, []])
                self.records[r.key] = newrecs
            else:
                for rec in recs:
                    if rec[0] < r.hi and r.lo < rec[1]:
                        if rec[2] is not None:
                            self._dep(op, rec[2])
                        rec[3].append(op)

    def _mk(self, veng, ieng, method, args, kwargs, tag):
        op = Op(len(self.ops), veng, ieng, method, args, kwargs, tag)
        lst = self.per_veng.setdefault(veng, [])
        op.seq = len(lst)
        lst.append(op)
        if veng != ieng:
            pass
        self.ops.append(op)
        return op

    def barrier(self):
        for q in list(self.qstate):
            st = self.qstate[q]
            sem_i = st["wave"] % self.R_WAVES
            self._close_wave(f"{q}#{sem_i}")
        qwaits = []
        for q, st in self.qstate.items():
            for i in range(self.R_WAVES):
                if st["cum"][i] > 0:
                    qwaits.append((f"{q}#{i}", 16 * st["cum"][i]))
        vals = {}
        for e in ("pe", "act", "dve", "pool", "sp"):
            lst = self.per_veng.get(e, [])
            last = None
            for o in reversed(lst):
                if o.method is not None:
                    last = o
                    break
            if last is not None:
                marks = self.marks.setdefault(e, [])
                if not marks or marks[-1] < last.seq:
                    marks.append(last.seq)
                    last.incs = 1
                vals[e] = len(marks)
        spop = self._mk("sp", "sp", "nop", (), {}, "barrier")
        for e, v in vals.items():
            if e != "sp":
                spop.waits.append((e, v))
        spop.waits.extend(qwaits)
        marks_sp = self.marks.setdefault("sp", [])
        marks_sp.append(spop.seq)
        spop.incs = 1
        spv = len(marks_sp)
        for e in ("pe", "act", "dve", "pool"):
            w = self._mk(e, e, None, (), {}, "barwait")
            w.waits.append(("sp", spv))

    def op(self, eng, method, args, kwargs=None, reads=(), writes=(), tag=""):
        op = self._mk(eng, eng, method, args, dict(kwargs or {}), tag)
        self._access(op, [r if isinstance(r, Reg) else _ap_region(r) for r in reads], False)
        self._access(op, [r if isinstance(r, Reg) else _ap_region(r) for r in writes], True)
        return op

    R_WAVES = 4

    def dma(self, eng, out_ap, in_ap, tag="", **kw):
        small = False
        for a in (out_ap, in_ap):
            tn = type(a.tensor).__name__
            if tn.startswith("SB") and a.tensor.shape[0] < 128:
                small = True
        kw["out"] = out_ap
        kw["in_"] = in_ap
        q = ("s" + DMA_Q[eng]) if small else DMA_Q[eng]
        st = self.qstate.setdefault(q, {"wave": 0, "open": 0, "cum": [0] * self.R_WAVES})
        sem_i = st["wave"] % self.R_WAVES
        veng = f"{q}#{sem_i}"
        if st["open"] == 0 and st["cum"][sem_i] > 0:
            gate = Op(len(self.ops), eng, eng, None, (), {}, "gate")
            glst = self.per_veng.setdefault(eng, [])
            gate.seq = len(glst)
            glst.append(gate)
            self.ops.append(gate)
            gate.waits.append((veng, 16 * st["cum"][sem_i]))
        op = self._mk(veng, eng, "dma_start", (), kw, tag)
        op.incs = 1
        st["open"] += 1
        st.setdefault("open_ops", []).append(op)
        op.kwargs["_wave_cum"] = st["cum"][sem_i] + st["open"]
        self._access(op, [_ap_region(in_ap)], False)
        self._access(op, [_ap_region(out_ap)], True)
        return op

    def _close_wave(self, veng):
        q, _, si = veng.partition("#")
        st = self.qstate.get(q)
        if st is None:
            return
        sem_i = st["wave"] % self.R_WAVES
        if f"{q}#{sem_i}" == veng and st["open"] > 0:
            st["cum"][sem_i] += st["open"]
            final = st["cum"][sem_i]
            for o in st.get("open_ops", []):
                o.kwargs["_wave_cum"] = final
            st["open_ops"] = []
            st["open"] = 0
            st["wave"] += 1

    def mm(self, out_ap, lhsT_ap, rhs_ap, start, stop, tile_position=None, tag=""):
        kw = dict(start=start, stop=stop)
        if tile_position is not None:
            kw["tile_position"] = tile_position
        return self.op("pe", "matmul", (out_ap, lhsT_ap, rhs_ap), kw,
                       reads=(lhsT_ap, rhs_ap), writes=(out_ap,), tag=tag)

    def transpose(self, out_ap, in_ap, ident_ap, tag=""):
        return self.op("pe", "transpose", (out_ap, in_ap, ident_ap), {},
                       reads=(in_ap, ident_ap), writes=(out_ap,), tag=tag)

    def act(self, out_ap, in_ap, func, bias=0.0, scale=1.0, tag=""):
        kw = dict(bias=bias, scale=scale)
        reads = [in_ap] + [b for b in (bias, scale) if hasattr(b, "tensor")]
        return self.op("act", "activation", (out_ap, in_ap, func), kw,
                       reads=reads, writes=(out_ap,), tag=tag)

    def finalize(self, final_writes=()):
        end_deps = []
        for ap in final_writes:
            r = ap if isinstance(ap, Reg) else _ap_region(ap)
            for rec in self.records.get(r.key, []):
                if rec[2] is not None:
                    end_deps.append(rec[2])
        term = self._mk("sp", "sp", None, (), {}, "terminal")
        for d in end_deps:
            self._dep(term, d)

        nc = self.nc
        with self.ctx:
            sems = {}
            for veng in self.per_veng:
                sems[veng] = self.ctx.enter_context(nc.semaphore("sem_" + veng))
            block = self.ctx.enter_context(nc.Block())

            def run_engine(eng_obj, eng_name):
                watermark = {}
                for op in self.ops:
                    if op.ieng != eng_name:
                        continue
                    for (veng, val) in op.waits:
                        if self.WATERMARK and watermark.get(veng, 0) >= val:
                            continue
                        watermark[veng] = val
                        eng_obj.wait_ge(sems[veng], val)
                    if op.method is None:
                        continue
                    kwargs = {k: v for k, v in op.kwargs.items() if not k.startswith("_")}
                    inst = getattr(eng_obj, op.method)(*op.args, **kwargs)
                    if op.tag and ANNOTATE:
                        try:
                            inst.annotate(op.tag)
                        except Exception:
                            pass
                    if op.incs:
                        amt = 16 if ("q_" in op.veng) else 1
                        inst.then_inc(sems[op.veng], amt)

            @block.tensor
            def _(t):
                run_engine(t, "pe")

            @block.scalar
            def _(s):
                run_engine(s, "act")

            @block.vector
            def _(v):
                run_engine(v, "dve")

            @block.gpsimd
            def _(g):
                run_engine(g, "pool")

            @block.sync
            def _(sy):
                run_engine(sy, "sp")

        return nc



class ConstReg:

    def __init__(self):
        self.pvt_cols = []
        self.pvt_index = {}
        self.row_vals = []
        self.row_index = {}

    def pvec(self, name, vec):
        v = np.asarray(vec, np.float32).reshape(-1)
        assert v.size % 128 == 0
        if name in self.pvt_index:
            return self.pvt_index[name]
        start = len(self.pvt_cols)
        for i in range(v.size // 128):
            self.pvt_cols.append(v[i * 128:(i + 1) * 128])
        self.pvt_index[name] = start
        return start

    def rowvec(self, name, vec):
        v = np.asarray(vec, np.float32).reshape(-1)
        if name in self.row_index:
            return self.row_index[name]
        start = len(self.row_vals)
        self.row_vals.extend(v.tolist())
        self.row_index[name] = start
        return start

    def pvt_array(self):
        if not self.pvt_cols:
            return np.zeros((128, 1), np.float32)
        return np.ascontiguousarray(np.stack(self.pvt_cols, axis=1).astype(np.float32))

    def rows_array(self):
        if not self.row_vals:
            return np.zeros((1, 4), np.float32)
        return np.asarray(self.row_vals, np.float32).reshape(1, -1)


def make_trimask_T():
    kt = np.arange(128)[:, None]
    qt = np.arange(128)[None, :]
    b = np.where(kt <= qt, 0.0, NEG).astype(np.float32)
    return np.ascontiguousarray(b.T)


def fm_layout(x):
    return np.ascontiguousarray(np.asarray(x).T.reshape(NFC, 128, -1))


def fm_unlayout(xfm):
    return np.ascontiguousarray(np.asarray(xfm).reshape(D, -1).T)



class Ctx:
    def __init__(self, B, n_pvt, n_rows, cr=None):
        self.B = B
        self.cr = cr
        self.identf_d = B.dram("c_identf", [128, 128], F32, kind="ExternalInput")
        self.pvt_d = B.dram("c_pvt", [128, max(n_pvt, 1)], F32, kind="ExternalInput")
        self.rows_d = B.dram("c_rows", [1, max(n_rows, 4)], F32, kind="ExternalInput")
        self.trimask_d = B.dram("c_trimask", [128, 128], F32, kind="ExternalInput")
        self.esel_d = B.dram("c_esel", [8, 8 * 128], F32, kind="ExternalInput")
        self.identf = B.sbuf("s_identf", [128, 128], F32)
        self.identb = B.sbuf("s_identb", [128, 128], BF16)
        self.pvt = B.sbuf("s_pvt", [128, max(n_pvt, 1)], F32)
        self.trimask = B.sbuf("s_trimask", [128, 128], BF16)
        self.onesrowf = B.sbuf("s_onesrowf", [1, 128], F32)
        self.onestb = B.sbuf("s_onestb", [1, TB], F32)
        B.dma("pool", self.identf[:, :], self.identf_d[:, :], tag="ldc")
        B.dma("pool", self.identb[:, :], self.identf_d[:, :], tag="ldc")
        B.dma("pool", self.pvt[:, :], self.pvt_d[:, :], tag="ldc")
        B.dma("pool", self.trimask[:, :], self.trimask_d[:, :], tag="ldc")
        B.op("pool", "memset", (self.onesrowf[:, :], 1.0), {}, writes=(self.onesrowf[:, :],))
        B.op("pool", "memset", (self.onestb[:, :], 1.0), {}, writes=(self.onestb[:, :],))
        self.bank = [B.psum(f"ps{i}", [128, 512], F32) for i in range(7)]
        self.bankb = B.psum("ps7b", [128, 1024], BF16)

    def pv(self, col, n=1):
        return self.pvt[:, col:col + n]

    def eps(self):
        return self.pv(self.cr.pvt_index["eps_ln"])





def ln_full(B, C, x_d, out_d, tag, g_rowcol=None, b_rowcol=None, rms=False,
            src_dtype=F32):
    u = B.uid()
    xs = B.sbuf(f"{tag}_x{u}", [128, NFC * TB], src_dtype)
    sq = B.sbuf(f"{tag}_sq{u}", [128, TB], F32)
    mrow = B.sbuf(f"{tag}_m{u}", [1, TB], F32)
    qrow = B.sbuf(f"{tag}_q{u}", [1, TB], F32)
    rrow = B.sbuf(f"{tag}_r{u}", [1, TB], F32)
    mrrow = B.sbuf(f"{tag}_mr{u}", [1, TB], F32)
    trow = B.sbuf(f"{tag}_tr{u}", [1, TB], F32)
    outs = B.sbuf(f"{tag}_o{u}", [128, NFC * TB], BF16)
    onesc = B.sbuf(f"{tag}_1c{u}", [128, 1], src_dtype)
    onescf = B.sbuf(f"{tag}_1f{u}", [128, 1], F32)
    grows = None
    brows = None
    if g_rowcol is not None:
        grows = B.sbuf(f"{tag}_gr{u}", [1, D], F32)
        B.dma("sp", grows[0:1, :], C.rows_d[0:1, g_rowcol:g_rowcol + D], tag=f"{tag}lg")
    if b_rowcol is not None:
        brows = B.sbuf(f"{tag}_br{u}", [1, D], F32)
        B.dma("sp", brows[0:1, :], C.rows_d[0:1, b_rowcol:b_rowcol + D], tag=f"{tag}lb")
    tmp = B.sbuf(f"{tag}_t{u}", [128, TB], F32)
    B.op("pool", "memset", (onesc[:, :], 1.0), {}, writes=(onesc[:, :],))
    B.op("pool", "memset", (onescf[:, :], 1.0), {}, writes=(onescf[:, :],))
    bank = C.bank
    inv_d = 1.0 / D
    for tb in range(NTB):
        t0 = tb * TB
        for fc in range(NFC):
            B.dma("sp", xs[:, fc * TB:(fc + 1) * TB], x_d[fc, :, t0:t0 + TB], tag=f"{tag}ldx")
        ps1 = bank[0][0:1, 0:TB]
        ps2 = bank[1][0:1, 0:TB]
        for fc in range(NFC):
            xc = xs[:, fc * TB:(fc + 1) * TB]
            if not rms:
                B.mm(ps1, onesc[:, 0:1], xc, start=(fc == 0), stop=(fc == NFC - 1), tag=f"{tag}s1")
            B.act(sq[:, 0:TB], xc, AF.Square, tag=f"{tag}sq")
            B.mm(ps2, onescf[:, 0:1], sq[:, 0:TB], start=(fc == 0), stop=(fc == NFC - 1), tag=f"{tag}s2")
        B.op("dve", "tensor_scalar", (qrow[0:1, :], ps2, inv_d, None, ALU.mult), {},
             reads=(ps2,), writes=(qrow[0:1, :],), tag=f"{tag}rq")
        if not rms:
            B.op("dve", "tensor_scalar", (mrow[0:1, :], ps1, inv_d, None, ALU.mult), {},
                 reads=(ps1,), writes=(mrow[0:1, :],), tag=f"{tag}rm")
            B.op("dve", "scalar_tensor_tensor",
                 (trow[0:1, :], mrow[0:1, :], 1.0, mrow[0:1, :], ALU.mult, ALU.mult), {},
                 reads=(mrow[0:1, :],), writes=(trow[0:1, :],), tag=f"{tag}m2")
            B.op("dve", "tensor_sub", (qrow[0:1, :], qrow[0:1, :], trow[0:1, :]), {},
                 reads=(qrow[0:1, :], trow[0:1, :]), writes=(qrow[0:1, :],), tag=f"{tag}var")
        B.act(rrow[0:1, :], qrow[0:1, :], AF.Ln, bias=C.eps()[0:1, 0:1], tag=f"{tag}ln")
        B.act(rrow[0:1, :], rrow[0:1, :], AF.Exp, scale=-0.5, tag=f"{tag}exp")
        if not rms:
            B.op("dve", "scalar_tensor_tensor",
                 (mrrow[0:1, :], mrow[0:1, :], -1.0, rrow[0:1, :], ALU.mult, ALU.mult), {},
                 reads=(mrow[0:1, :], rrow[0:1, :]), writes=(mrrow[0:1, :],), tag=f"{tag}mr")
        for fc in range(NFC):
            xc = xs[:, fc * TB:(fc + 1) * TB]
            oc = outs[:, fc * TB:(fc + 1) * TB]
            pg = bank[2 + (fc % 2) * 2][:, 0:TB]
            pb = bank[3 + (fc % 2) * 2][:, 0:TB]
            grow = C.onesrowf[0:1, :] if g_rowcol is None else grows[0:1, fc * 128:(fc + 1) * 128]
            B.mm(pg, grow, rrow[0:1, :], start=True, stop=True, tag=f"{tag}bg")
            use_pb = (not rms) or (b_rowcol is not None)
            if use_pb:
                first = True
                if not rms:
                    B.mm(pb, grow, mrrow[0:1, :], start=True, stop=(b_rowcol is None), tag=f"{tag}bb")
                    first = False
                if b_rowcol is not None:
                    B.mm(pb, brows[0:1, fc * 128:(fc + 1) * 128], C.onestb[0:1, 0:TB],
                         start=first, stop=True, tag=f"{tag}bb2")
            B.op("dve", "tensor_mul", (tmp[:, 0:TB], xc, pg), {},
                 reads=(xc, pg), writes=(tmp[:, 0:TB],), tag=f"{tag}ap1")
            if use_pb:
                B.op("dve", "tensor_add", (oc, tmp[:, 0:TB], pb), {},
                     reads=(tmp[:, 0:TB], pb), writes=(oc,), tag=f"{tag}ap2")
            else:
                B.op("dve", "tensor_copy", (oc, tmp[:, 0:TB]), {},
                     reads=(tmp[:, 0:TB],), writes=(oc,), tag=f"{tag}ap2")
        if out_d is not None:
            for fc in range(NFC):
                B.dma("act", out_d[fc, :, t0:t0 + TB], outs[:, fc * TB:(fc + 1) * TB],
                      tag=f"{tag}st")



def proj_fm(B, C, h_d, w_d, out_d, dout, tag, bias_pvcol=None, scale=None,
            oc_group=4, col_off=0):
    u = B.uid()
    noc = dout // 128
    hs = B.sbuf(f"{tag}_h{u}", [128, 2 * NFC * TB], BF16)
    og = min(oc_group, noc)
    ws = B.sbuf(f"{tag}_w{u}", [128, 2 * NFC * og * 128], BF16)
    os_ = B.sbuf(f"{tag}_o{u}", [128, og * TB], BF16)
    bank = C.bank
    w_view = w_d.rearrange("(kc kp) n -> kc kp n", kp=128)
    ngrp = (noc + og - 1) // og
    for g in range(ngrp):
        ocs = list(range(g * og, min((g + 1) * og, noc)))
        wbuf = (g % 2) * NFC * og * 128
        for kc in range(NFC):
            B.dma("pool",
                  ws[:, wbuf + kc * og * 128: wbuf + kc * og * 128 + len(ocs) * 128],
                  w_view[kc, :, col_off + ocs[0] * 128: col_off + ocs[0] * 128 + len(ocs) * 128],
                  tag=f"{tag}ldw")
        for tb in range(NTB):
            t0 = tb * TB
            hb = (tb % 2) * NFC * TB
            for kc in range(NFC):
                B.dma("sp", hs[:, hb + kc * TB: hb + (kc + 1) * TB], h_d[kc, :, t0:t0 + TB],
                      tag=f"{tag}ldh")
            for j, oc in enumerate(ocs):
                pb = bank[j % 4][:, 0:TB]
                for kc in range(NFC):
                    B.mm(pb, ws[:, wbuf + kc * og * 128 + j * 128: wbuf + kc * og * 128 + (j + 1) * 128],
                         hs[:, hb + kc * TB: hb + (kc + 1) * TB],
                         start=(kc == 0), stop=(kc == NFC - 1), tag=f"{tag}mm")
                oc_sb = os_[:, j * TB:(j + 1) * TB]
                if bias_pvcol is not None and scale is not None:
                    B.op("dve", "tensor_scalar",
                         (oc_sb, pb, C.pv(bias_pvcol + oc), scale, ALU.add, ALU.mult), {},
                         reads=(pb, C.pv(bias_pvcol + oc)), writes=(oc_sb,), tag=f"{tag}ev")
                elif bias_pvcol is not None:
                    B.op("dve", "tensor_scalar",
                         (oc_sb, pb, C.pv(bias_pvcol + oc), None, ALU.add), {},
                         reads=(pb, C.pv(bias_pvcol + oc)), writes=(oc_sb,), tag=f"{tag}ev")
                elif scale is not None:
                    B.op("dve", "tensor_scalar", (oc_sb, pb, scale, None, ALU.mult), {},
                         reads=(pb,), writes=(oc_sb,), tag=f"{tag}ev")
                else:
                    B.op("dve", "tensor_copy", (oc_sb, pb), {},
                         reads=(pb,), writes=(oc_sb,), tag=f"{tag}ev")
                B.dma("act", out_d[oc, :, t0:t0 + TB], oc_sb, tag=f"{tag}st")



def proj_tm_v(B, C, h_d, w_d, out_d, dout, nheads, dv, tag, vbias_d=None,
               scale=None, col_off=0):
    u = B.uid()
    ws = B.sbuf(f"{tag}_w{u}", [128, NFC * dout], BF16)
    hs = B.sbuf(f"{tag}_h{u}", [128, NFC * 128], BF16)
    width = nheads * (dv + 1)
    os_ = B.sbuf(f"{tag}_o{u}", [128, width], BF16)
    vb = None
    if vbias_d is not None:
        vb = B.sbuf(f"{tag}_b{u}", [128, dout], F32)
        B.dma("pool", vb[:, :], vbias_d[:, :], tag=f"{tag}ldb")
    w_view = w_d.rearrange("(kc kp) n -> kc kp n", kp=128)
    for kc in range(NFC):
        B.dma("pool", ws[:, kc * dout:(kc + 1) * dout], w_view[kc, :, col_off:col_off + dout], tag=f"{tag}ldw")
    bank = C.bank
    nnc = (dout + 511) // 512
    for tt in range(T // 128):
        t0 = tt * 128
        for kc in range(NFC):
            B.dma("sp", hs[:, kc * 128:(kc + 1) * 128], h_d[kc, :, t0:t0 + 128],
                  tag=f"{tag}ldh")
        for nc_ in range(nnc):
            n0 = nc_ * 512
            nw = min(512, dout - n0)
            pb = bank[nc_ % 4][:, 0:nw]
            for kc in range(NFC):
                B.mm(pb, hs[:, kc * 128:(kc + 1) * 128], ws[:, kc * dout + n0: kc * dout + n0 + nw],
                     start=(kc == 0), stop=(kc == NFC - 1), tag=f"{tag}mm")
            h0 = n0 // dv
            nh = nw // dv
            for hh in range(nh):
                if dv > 64:
                    base = (h0 + hh) * (dv + 1)
                    for (po, do, w_) in ((0, 0, 64), (64, 65, 64)):
                        dsth = os_[:, base + do: base + do + w_]
                        srch = pb[:, hh * dv + po: hh * dv + po + w_]
                        if vb is not None:
                            B.op("dve", "tensor_add", (dsth, srch,
                                 vb[:, n0 + hh * dv + po: n0 + hh * dv + po + w_]), {},
                                 reads=(srch, vb[:, n0 + hh * dv + po: n0 + hh * dv + po + w_]),
                                 writes=(dsth,), tag=f"{tag}ev")
                        elif scale is not None:
                            B.op("dve", "tensor_scalar", (dsth, srch, scale, None, ALU.mult),
                                 {}, reads=(srch,), writes=(dsth,), tag=f"{tag}ev")
                        else:
                            B.op("dve", "tensor_copy", (dsth, srch), {},
                                 reads=(srch,), writes=(dsth,), tag=f"{tag}ev")
                    continue
                dst = os_[:, (h0 + hh) * (dv + 1): (h0 + hh) * (dv + 1) + dv]
                src = pb[:, hh * dv: (hh + 1) * dv]
                if vb is not None:
                    B.op("dve", "tensor_add", (dst, src, vb[:, n0 + hh * dv: n0 + (hh + 1) * dv]), {},
                         reads=(src, vb[:, n0 + hh * dv: n0 + (hh + 1) * dv]), writes=(dst,),
                         tag=f"{tag}ev")
                elif scale is not None:
                    B.op("dve", "tensor_scalar", (dst, src, scale, None, ALU.mult), {},
                         reads=(src,), writes=(dst,), tag=f"{tag}ev")
                else:
                    B.op("dve", "tensor_copy", (dst, src), {},
                         reads=(src,), writes=(dst,), tag=f"{tag}ev")
        one_col = 64 if dv > 64 else dv
        ones_view = os_.rearrange("p (h e) -> p h e", e=dv + 1)[:, :, one_col:one_col + 1]
        B.op("dve", "memset", (ones_view, 1.0), {}, writes=(os_[:, 0:width],), tag=f"{tag}on")
        B.dma("act", out_d[t0:t0 + 128, 0:width], os_[:, 0:width], tag=f"{tag}st")



FM = lambda: ([NFC, 128, T], F32)
FMB = lambda: ([NFC, 128, T], BF16)

TSPEC = {
    "x0": FM(),
    "u_l0": FMB(),
    "Q1": FMB(), "K1": FMB(),
    "V1": ([T, 16 * 65], BF16),
    "O1": ([8, 128, T], BF16),
    "x1": FM(),
    "u1": FMB(), "h1": FMB(),
    "Q2": FMB(), "K2": FMB(),
    "V2": ([T, 16 * 65], BF16),
    "O2": ([8, 128, T], BF16),
    "xr": FMB(), "h2": FMB(),
    "G1": ([T, 8], F32),
    "x2": FM(),
    "u2": FMB(),
    "q3t": ([T, D], BF16), "k3t": ([T, D], BF16),
    "Q3a": ([4, 128, T], BF16), "Q3b": ([4, 128, T], BF16),
    "K3a": ([4, 128, T], BF16), "K3b": ([4, 128, T], BF16),
    "V3": ([T, 8 * 129], BF16),
    "O3a": ([8, 128, T], BF16), "O3b": ([8, 128, T], BF16),
    "Y3": ([8, 128, T], BF16),
    "x3": FM(),
    "u3": FMB(), "h3": FMB(),
    "q4t": ([T, D], BF16), "ckvt": ([T, 160], BF16),
    "Q4": ([8, 128, T], BF16), "K4": ([8, 128, T], BF16),
    "V4": ([T, 16 * 33], BF16),
    "O4": ([4, 128, T], BF16),
    "xi": FMB(), "h4": FMB(),
    "G2": ([T, 8], F32),
    "x4": FM(),
}

WSPEC = {
    "L0_attn_W_qkv": ([D, 3 * D], F32),
    "L0_attn_W_out": ([D, D], F32),
    "L0_ffn_w_qkv": ([D, 3 * D], F32),
    "L0_ffn_w_o": ([D, D], F32),
    "L0_ffn_w_gate": ([D, 8], F32),
    "L0_ffn_w1": ([8, D, 4096], F32),
    "L0_ffn_w2": ([8, 4096, D], F32),
    "L1_attn_WqT": ([D, D], F32),
    "L1_attn_WkT": ([D, D], F32),
    "L1_attn_WvT": ([D, D], F32),
    "L1_attn_WoT": ([D, D], F32),
    "L1_ffn_Wq": ([D, D], F32),
    "L1_ffn_Wkva": ([D, 160], F32),
    "L1_ffn_Wkvb": ([128, 512], F32),
    "L1f_Wo_sub": ([512, D], F32),
    "L1_ffn_Wgate": ([D, 8], F32),
    "L1_ffn_Wg": ([8, D, 512], F32),
    "L1_ffn_Wu": ([8, D, 512], F32),
    "L1_ffn_Wd": ([8, 512, D], F32),
    "L1_ffn_sWg": ([1, D, 512], F32),
    "L1_ffn_sWu": ([1, D, 512], F32),
    "L1_ffn_sWd": ([1, 512, D], F32),
}

EXTRA_IN = {
    "vbias_l0": ([128, D], F32),
    "rope64_cos": ([T, 32], F32), "rope64_sin": ([T, 32], F32),
    "rope32_cos": ([T, 16], F32), "rope32_sin": ([T, 16], F32),
    "rope64w_cos": ([T, 512], F32), "rope64w_sin": ([T, 512], F32),
    "rope32w_cos": ([T, 256], F32), "rope32w_sin": ([T, 256], F32),
    "lnw_col": ([128, 1], F32), "lnb_col": ([128, 1], F32),
    "lam_cat": ([1, 256], F32),
    "ebias_bc": ([128, 8], F32),
}


def register_consts(cr: "ConstReg", inp=None):
    z = lambda n: np.zeros(n, np.float32)
    g = (lambda k: np.asarray(inp[k], np.float32)) if inp is not None else (lambda k: None)
    def pv(name, n, val):
        cr.pvec(name, z(n * 128) if val is None else val)
    def rv(name, n, val):
        cr.rowvec(name, z(n) if val is None else val)
    pv("eps_ln", 1, np.full(128, EPS_LN, np.float32))
    bqkv = g("L0_attn_b_qkv")
    pv("l0a_bq", 8, None if bqkv is None else bqkv[0:1024])
    pv("l0a_bk", 8, None if bqkv is None else bqkv[1024:2048])
    bout = g("L0_attn_b_out")
    pv("l0a_bout", 8, bout)
    rv("l0f_g1", 1024, g("L0_ffn_ln1_g"))
    rv("l0f_b1", 1024, g("L0_ffn_ln1_b"))
    rv("l0f_g2", 1024, g("L0_ffn_ln2_g"))
    rv("l0f_b2", 1024, g("L0_ffn_ln2_b"))
    b1 = g("L0_ffn_b1")
    b2 = g("L0_ffn_b2")
    for e in range(8):
        pv(f"l0f_b1_{e}", 32, None if b1 is None else b1[e])
        pv(f"l0f_b2_{e}", 8, None if b2 is None else b2[e])
    pv("eps_a8", 1, np.full(128, 1e-8, np.float32))
    rv("l1f_g1", 1024, g("L1_ffn_w_ln1"))
    rv("l1f_g2", 1024, g("L1_ffn_w_ln2"))
    return cr



def phase_l0attn_qkv(B, C, Tn, cr):
    ln_full(B, C, Tn["x0"], Tn["u_l0"], "p1ln")
    proj_fm(B, C, Tn["u_l0"], Tn["L0_attn_W_qkv"], Tn["Q1"], D, "p1q",
            bias_pvcol=cr.pvt_index["l0a_bq"], scale=0.125, col_off=0)
    proj_fm(B, C, Tn["u_l0"], Tn["L0_attn_W_qkv"], Tn["K1"], D, "p1k",
            bias_pvcol=cr.pvt_index["l0a_bk"], col_off=D)
    proj_tm_v(B, C, Tn["u_l0"], Tn["L0_attn_W_qkv"], Tn["V1"], D, 16, 64, "p1v",
              vbias_d=Tn["vbias_l0"], col_off=2 * D)


PHASES = []



def build_phase_program(idx):
    name, needs, makes, wneeds, extra, fn = PHASES[idx]
    cr = register_consts(ConstReg())
    B = Builder(name)
    Tn = {}
    for t in needs:
        shape, dt = TSPEC[t]
        Tn[t] = B.dram(t, shape, dt, kind="ExternalInput")
    for t in makes:
        shape, dt = TSPEC[t]
        Tn[t] = B.dram(t, shape, dt, kind="ExternalOutput")
    for t in wneeds:
        shape, dt = WSPEC[t]
        Tn[t] = B.dram(t, shape, dt, kind="ExternalInput")
    for t in extra:
        shape, dt = EXTRA_IN[t]
        Tn[t] = B.dram(t, shape, dt, kind="ExternalInput")
    C = Ctx(B, len(cr.pvt_cols), len(cr.row_vals), cr)
    fn(B, C, Tn, cr)
    outs = [Tn[t] for t in makes]
    nc = B.finalize(final_writes=outs)
    in_names = needs + wneeds + extra + ["c_identf", "c_pvt", "c_rows", "c_trimask", "c_esel"]
    return nc, in_names, makes


def host_all_arrays(inputs, seq):
    cr = register_consts(ConstReg(), inputs)
    arrs = {
        "x0": fm_layout(np.asarray(inputs["L0_attn_x"][seq], np.float32)),
        "c_identf": np.eye(128, dtype=np.float32),
        "c_pvt": cr.pvt_array(),
        "c_rows": cr.rows_array(),
        "c_trimask": make_trimask_T(),
        "vbias_l0": np.ascontiguousarray(
            np.broadcast_to(np.asarray(inputs["L0_attn_b_qkv"][2048:3072], np.float32)[None, :],
                            (128, 1024))),
    }
    pos = np.arange(T, dtype=np.float64)
    f64 = 1.0 / (10000.0 ** (np.arange(32, dtype=np.float64) / 32.0))
    ang = pos[:, None] * f64[None, :]
    arrs["rope64_cos"] = np.cos(ang).astype(np.float32)
    arrs["rope64_sin"] = np.sin(ang).astype(np.float32)
    f32_ = 1.0 / (100000.0 ** (np.arange(16, dtype=np.float64) / 16.0))
    ang2 = pos[:, None] * f32_[None, :]
    arrs["rope32_cos"] = np.cos(ang2).astype(np.float32)
    arrs["rope32_sin"] = np.sin(ang2).astype(np.float32)
    arrs["rope64w_cos"] = np.ascontiguousarray(np.tile(arrs["rope64_cos"], (1, 16)))
    arrs["rope64w_sin"] = np.ascontiguousarray(np.tile(arrs["rope64_sin"], (1, 16)))
    arrs["rope32w_cos"] = np.ascontiguousarray(np.tile(arrs["rope32_cos"], (1, 16)))
    arrs["rope32w_sin"] = np.ascontiguousarray(np.tile(arrs["rope32_sin"], (1, 16)))
    arrs["lnw_col"] = np.asarray(inputs["L1_attn_ln_w"], np.float32).reshape(128, 1).copy()
    arrs["lnb_col"] = np.asarray(inputs["L1_attn_ln_b"], np.float32).reshape(128, 1).copy()
    arrs["lam_cat"] = np.concatenate([
        np.asarray(inputs["L1_attn_lambda_q1"], np.float32),
        np.asarray(inputs["L1_attn_lambda_k1"], np.float32),
        np.asarray(inputs["L1_attn_lambda_q2"], np.float32),
        np.asarray(inputs["L1_attn_lambda_k2"], np.float32)]).reshape(1, 256)
    arrs["ebias_bc"] = np.ascontiguousarray(np.broadcast_to(
        np.asarray(inputs["L1_ffn_expert_bias"], np.float32)[None, :], (128, 8)))
    esel = np.zeros((8, 8 * 128), np.float32)
    for e in range(8):
        esel[e, e * 128:(e + 1) * 128] = 1.0
    arrs["c_esel"] = esel
    for w in WSPEC:
        if w.endswith("T") and w.startswith("L1_attn"):
            base = w[:-1]
            arrs[w] = np.ascontiguousarray(np.asarray(inputs[base], np.float32).T)
        elif w == "L1f_Wo_sub":
            Wo = np.asarray(inputs["L1_ffn_Wo"], np.float32)
            rows = np.concatenate([np.arange(64 * h, 64 * h + 32) for h in range(16)])
            arrs[w] = np.ascontiguousarray(Wo[rows, :])
        else:
            arrs[w] = np.ascontiguousarray(np.asarray(inputs[w], np.float32))
    return arrs


def run_phases(inputs, upto=None, state=None):
    if state is None:
        state = [host_all_arrays(inputs, b) for b in range(4)]
    n = len(PHASES) if upto is None else upto
    for i in range(n):
        if state[0].get("_done_" + PHASES[i][0]) is not None:
            continue
        import sys as _sys
        print(f"[phase {i}: {PHASES[i][0]}]", flush=True)
        nc, in_names, out_names = build_phase_program(i)
        in_maps = []
        for c in range(8):
            st = state[c // 2]
            in_maps.append({k: st[k] for k in in_names})
        res = run_bass_kernel_spmd(nc, in_maps, core_ids=list(range(8)))
        for b in range(4):
            for k in out_names:
                state[b][k] = np.asarray(res.results[2 * b][k])
            state[b]["_done_" + PHASES[i][0]] = True
    return state


FUSED = True
ANNOTATE = False


def build_fused():
    cr = register_consts(ConstReg())
    B = Builder("fused")
    Tn = {}
    ext_ins = []
    sx, dx = TSPEC["x0"]
    Tn["x0"] = B.dram("x0", sx, dx, kind="ExternalInput")
    ext_ins.append("x0")
    s4, d4 = TSPEC["x4"]
    Tn["x4"] = B.dram("x4", s4, d4, kind="ExternalOutput")
    for t, (s, d) in TSPEC.items():
        if t in ("x0", "x4"):
            continue
        Tn[t] = B.dram(t, s, d, kind="Internal")
    for t, (s, d) in WSPEC.items():
        Tn[t] = B.dram(t, s, d, kind="ExternalInput")
        ext_ins.append(t)
    for t, (s, d) in EXTRA_IN.items():
        Tn[t] = B.dram(t, s, d, kind="ExternalInput")
        ext_ins.append(t)
    C = Ctx(B, len(cr.pvt_cols), len(cr.row_vals), cr)
    for (name, needs, makes, wneeds, extra, fn) in PHASES:
        B.push_scope()
        fn(B, C, Tn, cr)
        B.pop_scope()
    nc = B.finalize(final_writes=[Tn["x4"]])
    ext_ins += ["c_identf", "c_pvt", "c_rows", "c_trimask", "c_esel"]
    return nc, ext_ins


def kernel(**inputs):
    if not FUSED:
        state = run_phases(inputs)
        out = np.stack([fm_unlayout(np.asarray(state[b]["x4"], np.float32))
                        for b in range(4)])
        return out.reshape(4, T, D)
    nc, ext_ins = build_fused()
    arrs = [host_all_arrays(inputs, b) for b in range(4)]
    in_maps = [{k: arrs[c // 2][k] for k in ext_ins} for c in range(8)]
    res = run_bass_kernel_spmd(nc, in_maps, core_ids=list(range(8)))
    out = np.stack([fm_unlayout(np.asarray(res.results[2 * b]["x4"], np.float32))
                    for b in range(4)])
    return out.reshape(4, T, D)



def attention(B, C, Q_d, K_d, V_d, O_d, nheads, dv, tag, causal=True,
              margin=0.5):
    u = B.uid()
    npair = nheads // 2
    vstride = dv + 1
    big_v = dv > 64
    ks = B.sbuf(f"{tag}_k{u}", [128, npair * T], BF16)
    vs = B.sbuf(f"{tag}_v{u}", [128, (T // 128) * nheads * vstride], BF16)
    qs = B.sbuf(f"{tag}_q{u}", [128, TB], BF16)
    qb_sq = B.sbuf(f"{tag}_sq{u}", [128, TB], BF16)
    pt = B.sbuf(f"{tag}_pt{u}", [128, 4 * TB], BF16)
    maxr = B.sbuf(f"{tag}_mx{u}", [1, 2 * nheads + 4], F32)
    tmr = B.sbuf(f"{tag}_tm{u}", [1, 4], F32)
    negc = B.sbuf(f"{tag}_nc{u}", [128, nheads], F32)
    ones64 = B.sbuf(f"{tag}_o64{u}", [128, 1], BF16)
    bcf = B.sbuf(f"{tag}_bc{u}", [128, TB], F32)
    oraw = B.sbuf(f"{tag}_or{u}", [128, TB], F32)
    oraw2 = B.sbuf(f"{tag}_o2{u}", [64, TB], F32)
    ost = B.sbuf(f"{tag}_os{u}", [128, TB], BF16)
    ost2 = B.sbuf(f"{tag}_ot{u}", [64, TB], BF16)
    onesq = B.sbuf(f"{tag}_oq{u}", [128, 128], F32)
    B.op("pool", "memset", (onesq[:, :], 1.0), {}, writes=(onesq[:, :],))
    B.op("pool", "memset", (ones64[:, :], 1.0), {}, writes=(ones64[:, :],))
    B.op("pool", "memset", (maxr[0:1, :], -1.0), {}, writes=(maxr[0:1, :],))
    bank = C.bank
    NKT = T // 128

    def sbank_of(hh, kc):
        if big_v:
            return bank[kc % 2] if hh == 0 else bank[2]
        return bank[hh * 2 + kc % 2]

    def obanks_of(hh):
        if big_v:
            return [(bank[3 + 2 * hh], 65), (bank[4 + 2 * hh], 64)]
        return [(bank[4 + hh], dv + 1)]

    for p in range(npair):
        for tb in range(NTB):
            B.dma("sp", ks[:, p * T + tb * TB: p * T + (tb + 1) * TB],
                  K_d[p, :, tb * TB:(tb + 1) * TB], tag=f"{tag}ldk")
    vw = nheads * vstride
    for tt in range(NKT):
        B.dma("sp", vs[:, tt * vw:(tt + 1) * vw],
              V_d[tt * 128:(tt + 1) * 128, 0:vw], tag=f"{tag}ldv")

    def bounds_for(src_d, out_col0):
        for p in range(npair):
            for tb in range(NTB):
                B.dma("sp", qs[:, 0:TB], src_d[p, :, tb * TB:(tb + 1) * TB],
                      tag=f"{tag}ldb")
                B.act(qb_sq[:, 0:TB], qs[:, 0:TB], AF.Square, tag=f"{tag}bsq")
                pa = bank[0][0:1, 0:TB]
                pb = bank[1][0:1, 0:TB]
                B.mm(pa, ones64[0:64, 0:1], qb_sq[0:64, 0:TB], start=True, stop=True,
                     tile_position=(0, 0), tag=f"{tag}bn")
                B.mm(pb, ones64[64:128, 0:1], qb_sq[64:128, 0:TB], start=True, stop=True,
                     tile_position=(64, 0), tag=f"{tag}bn")
                for hh, pp in ((0, pa), (1, pb)):
                    col = out_col0 + 2 * p + hh
                    B.op("dve", "reduce_max", (tmr[0:1, hh:hh + 1], pp), dict(axis=AX.X),
                         reads=(pp,), writes=(tmr[0:1, hh:hh + 1],), tag=f"{tag}brd")
                    B.op("dve", "tensor_max",
                         (maxr[0:1, col:col + 1], maxr[0:1, col:col + 1], tmr[0:1, hh:hh + 1]),
                         {}, reads=(maxr[0:1, col:col + 1], tmr[0:1, hh:hh + 1]),
                         writes=(maxr[0:1, col:col + 1],), tag=f"{tag}bmx")

    bounds_for(Q_d, 0)
    bounds_for(K_d, nheads)
    B.act(maxr[0:1, 0:2 * nheads], maxr[0:1, 0:2 * nheads], AF.Sqrt, tag=f"{tag}bsqrt")
    B.op("dve", "tensor_mul",
         (maxr[0:1, 0:nheads], maxr[0:1, 0:nheads], maxr[0:1, nheads:2 * nheads]),
         {}, reads=(maxr[0:1, 0:2 * nheads],), writes=(maxr[0:1, 0:nheads],), tag=f"{tag}bc")
    B.op("dve", "tensor_scalar",
         (maxr[0:1, 0:nheads], maxr[0:1, 0:nheads], margin, -1.0, ALU.add, ALU.mult),
         {}, reads=(maxr[0:1, 0:nheads],), writes=(maxr[0:1, 0:nheads],), tag=f"{tag}bneg")
    for h in range(nheads):
        pb = bank[0][:, 0:1]
        B.mm(pb, C.onesrowf[0:1, :], maxr[0:1, h:h + 1], start=True, stop=True,
             tag=f"{tag}brep")
        B.op("dve", "tensor_copy", (negc[:, h:h + 1], pb), {},
             reads=(pb,), writes=(negc[:, h:h + 1],), tag=f"{tag}bcp")

    for p in range(npair):
        for qb in range(NTB):
            nkc = (qb + 1) * 4 if causal else NKT
            diag0 = qb * 4 if causal else None
            B.dma("sp", qs[:, 0:TB], Q_d[p, :, qb * TB:(qb + 1) * TB], tag=f"{tag}ldq")
            started = set()

            def emit_scores(kc):
                a = (kc - diag0) if (causal and kc >= diag0) else -1
                q0 = a * 128 if a > 0 else 0
                nq = TB - q0
                sA = sbank_of(0, kc)[:, 0:nq]
                sB = sbank_of(1, kc)[:, 0:nq]
                kslice = ks[:, p * T + kc * 128: p * T + (kc + 1) * 128]
                B.mm(sA, kslice[0:64, :], qs[0:64, q0:TB], start=True, stop=(a < 0),
                     tile_position=(0, 0), tag=f"{tag}sA")
                B.mm(sB, kslice[64:128, :], qs[64:128, q0:TB], start=True, stop=(a < 0),
                     tile_position=(64, 0), tag=f"{tag}sB")
                if a >= 0:
                    B.mm(sbank_of(0, kc)[:, 0:128], C.trimask[:, :], C.identb[:, :],
                         start=False, stop=True, tag=f"{tag}mk")
                    B.mm(sbank_of(1, kc)[:, 0:128], C.trimask[:, :], C.identb[:, :],
                         start=False, stop=True, tag=f"{tag}mk")

            def emit_exp(kc):
                a = (kc - diag0) if (causal and kc >= diag0) else -1
                q0 = a * 128 if a > 0 else 0
                nq = TB - q0
                for hh in range(2):
                    h = 2 * p + hh
                    sbank = sbank_of(hh, kc)[:, 0:nq]
                    slot = pt[:, (kc % 2) * 2 * TB + hh * TB: (kc % 2) * 2 * TB + hh * TB + nq]
                    B.act(slot, sbank, AF.Exp, bias=negc[:, h:h + 1], tag=f"{tag}exp")

            def emit_av(kc):
                a = (kc - diag0) if (causal and kc >= diag0) else -1
                q0 = a * 128 if a > 0 else 0
                nq = TB - q0
                for hh in range(2):
                    h = 2 * p + hh
                    slot = pt[:, (kc % 2) * 2 * TB + hh * TB: (kc % 2) * 2 * TB + hh * TB + nq]
                    vbase = kc * vw + h * vstride
                    c0 = 0
                    for (obk, M) in obanks_of(hh):
                        st_ = id(obk) not in started
                        started.add(id(obk))
                        B.mm(obk[0:M, q0:TB], vs[:, vbase + c0: vbase + c0 + M], slot,
                             start=st_, stop=(kc == nkc - 1), tag=f"{tag}av")
                        c0 += M

            emit_scores(0)
            emit_exp(0)
            for kc in range(1, nkc):
                emit_scores(kc)
                emit_exp(kc)
                emit_av(kc - 1)
            emit_av(nkc - 1)
            for hh in range(2):
                h = 2 * p + hh
                obs = obanks_of(hh)
                drow_p = 64 if big_v else dv
                if big_v:
                    B.op("dve", "tensor_copy", (oraw[0:65, 0:TB], obs[0][0][0:65, 0:TB]),
                         {}, reads=(obs[0][0][0:65, 0:TB],), writes=(oraw[0:65, 0:TB],),
                         tag=f"{tag}ev1")
                    B.op("dve", "tensor_copy", (oraw2[0:64, 0:TB], obs[1][0][0:64, 0:TB]),
                         {}, reads=(obs[1][0][0:64, 0:TB],), writes=(oraw2[0:64, 0:TB],),
                         tag=f"{tag}ev2")
                else:
                    B.op("dve", "tensor_copy", (oraw[0:dv + 1, 0:TB],
                                                obs[0][0][0:dv + 1, 0:TB]), {},
                         reads=(obs[0][0][0:dv + 1, 0:TB],),
                         writes=(oraw[0:dv + 1, 0:TB],), tag=f"{tag}ev1")
                pbc = obs[0][0][:, 0:TB]
                B.mm(pbc, onesq[drow_p:drow_p + 1, 0:128],
                     oraw[drow_p:drow_p + 1, 0:TB], start=True, stop=True,
                     tile_position=(drow_p, 0), tag=f"{tag}bcm")
                B.op("dve", "reciprocal", (bcf[:, 0:TB], pbc), {},
                     reads=(pbc,), writes=(bcf[:, 0:TB],), tag=f"{tag}bce")
                if big_v:
                    B.op("dve", "tensor_mul", (ost[0:64, 0:TB], oraw[0:64, 0:TB],
                                               bcf[0:64, 0:TB]), {},
                         reads=(oraw[0:64, 0:TB], bcf[0:64, 0:TB]),
                         writes=(ost[0:64, 0:TB],), tag=f"{tag}nr1")
                    B.op("dve", "tensor_mul", (ost2[0:64, 0:TB], oraw2[0:64, 0:TB],
                                               bcf[0:64, 0:TB]), {},
                         reads=(oraw2[0:64, 0:TB], bcf[0:64, 0:TB]),
                         writes=(ost2[0:64, 0:TB],), tag=f"{tag}nr2")
                    B.dma("act", O_d[h, 0:64, qb * TB:(qb + 1) * TB], ost[0:64, 0:TB],
                          tag=f"{tag}sto")
                    B.dma("act", O_d[h, 64:128, qb * TB:(qb + 1) * TB], ost2[0:64, 0:TB],
                          tag=f"{tag}sto2")
                else:
                    r0 = (h * dv) % 128
                    B.op("dve", "tensor_mul", (ost[0:dv, 0:TB],
                                               oraw[0:dv, 0:TB], bcf[0:dv, 0:TB]),
                         {}, reads=(oraw[0:dv, 0:TB], bcf[0:dv, 0:TB]),
                         writes=(ost[0:dv, 0:TB],), tag=f"{tag}nr")
                    B.dma("act", O_d[(h * dv) // 128, r0:r0 + dv, qb * TB:(qb + 1) * TB],
                          ost[0:dv, 0:TB], tag=f"{tag}sto")



def attn_out_proj(B, C, O_d, W_d, xin_d, xout_d, tag, bias_pvcol=None,
                  extra_add_d=None, xin_dtype=F32, out_dtype=F32, ncol=D):
    u = B.uid()
    nic = ncol // 128
    ofm = B.sbuf(f"{tag}_of{u}", [128, 2 * nic * TB], BF16)
    ws = B.sbuf(f"{tag}_w{u}", [128, nic * D], BF16)
    xs = B.sbuf(f"{tag}_xs{u}", [128, 2 * TB], xin_dtype)
    es = B.sbuf(f"{tag}_es{u}", [128, TB], BF16) if extra_add_d is not None else None
    outs = B.sbuf(f"{tag}_os{u}", [128, TB], out_dtype)
    w_view = W_d.rearrange("(kc kp) n -> kc kp n", kp=128)
    for kc in range(nic):
        B.dma("pool", ws[:, kc * D:(kc + 1) * D], w_view[kc, :, 0:D], tag=f"{tag}ldw")
    bank = C.bank
    bkb = C.bankb
    for tb in range(NTB):
        t0 = tb * TB
        ob = (tb % 2) * nic * TB
        for fc in range(nic):
            B.dma("sp", ofm[:, ob + fc * TB: ob + (fc + 1) * TB], O_d[fc, :, t0:t0 + TB],
                  tag=f"{tag}ldo")
        for oc in range(NFC):
            pb = bank[oc % 4][:, 0:TB]
            for kc in range(nic):
                B.mm(pb, ws[:, kc * D + oc * 128: kc * D + (oc + 1) * 128],
                     ofm[:, ob + kc * TB: ob + (kc + 1) * TB],
                     start=(kc == 0), stop=(kc == nic - 1), tag=f"{tag}mm")
            xb = (oc % 2) * TB
            B.dma("sp", xs[:, xb:xb + TB], xin_d[oc, :, t0:t0 + TB], tag=f"{tag}ldx")
            if bias_pvcol is not None:
                B.op("dve", "scalar_tensor_tensor",
                     (outs[:, 0:TB], pb, C.pv(bias_pvcol + oc), xs[:, xb:xb + TB],
                      ALU.add, ALU.add), {},
                     reads=(pb, C.pv(bias_pvcol + oc), xs[:, xb:xb + TB]),
                     writes=(outs[:, 0:TB],), tag=f"{tag}ev")
            else:
                B.op("dve", "tensor_add", (outs[:, 0:TB], pb, xs[:, xb:xb + TB]), {},
                     reads=(pb, xs[:, xb:xb + TB]), writes=(outs[:, 0:TB],), tag=f"{tag}ev")
            if extra_add_d is not None:
                B.dma("sp", es[:, 0:TB], extra_add_d[oc, :, t0:t0 + TB], tag=f"{tag}lde")
                B.op("dve", "tensor_add", (outs[:, 0:TB], outs[:, 0:TB], es[:, 0:TB]), {},
                     reads=(outs[:, 0:TB], es[:, 0:TB]), writes=(outs[:, 0:TB],),
                     tag=f"{tag}ev2")
            B.dma("act", xout_d[oc, :, t0:t0 + TB], outs[:, 0:TB], tag=f"{tag}st")


def phase_l0attn_av(B, C, Tn, cr):
    attention(B, C, Tn["Q1"], Tn["K1"], Tn["V1"], Tn["O1"], 16, 64, "p2")


def phase_l0attn_out(B, C, Tn, cr):
    attn_out_proj(B, C, Tn["O1"], Tn["L0_attn_W_out"], Tn["x0"], Tn["x1"], "p3",
                  bias_pvcol=cr.pvt_index["l0a_bout"])





def moe_gate(B, C, h_d, wgate_d, G_d, tag, ebias_bc_d=None):
    u = B.uid()
    ws = B.sbuf(f"{tag}_w{u}", [128, NFC * 8], BF16)
    hs = B.sbuf(f"{tag}_h{u}", [128, NFC * 128], BF16)
    pr = B.sbuf(f"{tag}_p{u}", [128, 8], F32)
    q2 = B.sbuf(f"{tag}_q2{u}", [128, 8], F32)
    eq1 = B.sbuf(f"{tag}_e1{u}", [128, 8], F32)
    eq2 = B.sbuf(f"{tag}_e2{u}", [128, 8], F32)
    gout = B.sbuf(f"{tag}_g{u}", [128, 8], F32)
    col = B.sbuf(f"{tag}_c{u}", [128, 6], F32)
    eb = None
    if ebias_bc_d is not None:
        eb = B.sbuf(f"{tag}_eb{u}", [128, 8], F32)
        B.dma("pool", eb[:, :], ebias_bc_d[:, :], tag=f"{tag}ldeb")
    lg = B.sbuf(f"{tag}_lg{u}", [128, 8], F32)
    w_view = wgate_d.rearrange("(kc kp) n -> kc kp n", kp=128)
    for kc in range(NFC):
        B.dma("pool", ws[:, kc * 8:(kc + 1) * 8], w_view[kc, :, 0:8], tag=f"{tag}ldw")
    bank = C.bank
    for tt in range(T // 128):
        t0 = tt * 128
        for kc in range(NFC):
            B.dma("sp", hs[:, kc * 128:(kc + 1) * 128], h_d[kc, :, t0:t0 + 128],
                  tag=f"{tag}ldh")
        pb = bank[tt % 2][:, 0:8]
        for kc in range(NFC):
            B.mm(pb, hs[:, kc * 128:(kc + 1) * 128], ws[:, kc * 8:(kc + 1) * 8],
                 start=(kc == 0), stop=(kc == NFC - 1), tag=f"{tag}mm")
        if eb is not None:
            B.op("dve", "tensor_add", (lg[:, :], pb, eb[:, :]), {},
                 reads=(pb, eb[:, :]), writes=(lg[:, :],), tag=f"{tag}eb")
            lsrc = lg[:, :]
        else:
            B.op("dve", "tensor_copy", (lg[:, :], pb), {},
                 reads=(pb,), writes=(lg[:, :],), tag=f"{tag}cp")
            lsrc = lg[:, :]
        B.op("dve", "reduce_max", (col[:, 0:1], lsrc), dict(axis=AX.X, negate=True),
             reads=(lsrc,), writes=(col[:, 0:1],), tag=f"{tag}mx")
        B.act(pr[:, :], lsrc, AF.Exp, bias=col[:, 0:1], tag=f"{tag}ex")
        B.op("dve", "reduce_sum", (col[:, 1:2], pr[:, :]), dict(axis=AX.X),
             reads=(pr[:, :],), writes=(col[:, 1:2],), tag=f"{tag}sm")
        B.op("dve", "reciprocal", (col[:, 2:3], col[:, 1:2]), {},
             reads=(col[:, 1:2],), writes=(col[:, 2:3],), tag=f"{tag}rc")
        B.op("dve", "tensor_scalar", (pr[:, :], pr[:, :], col[:, 2:3], None, ALU.mult),
             {}, reads=(pr[:, :], col[:, 2:3]), writes=(pr[:, :],), tag=f"{tag}pn")
        B.op("dve", "reduce_max", (col[:, 3:4], pr[:, :]), dict(axis=AX.X),
             reads=(pr[:, :],), writes=(col[:, 3:4],), tag=f"{tag}m1")
        B.op("dve", "tensor_scalar", (eq1[:, :], pr[:, :], col[:, 3:4], None, ALU.is_equal),
             {}, reads=(pr[:, :], col[:, 3:4]), writes=(eq1[:, :],), tag=f"{tag}e1")
        B.op("dve", "scalar_tensor_tensor", (q2[:, :], eq1[:, :], -2.0, pr[:, :],
                                             ALU.mult, ALU.add), {},
             reads=(eq1[:, :], pr[:, :]), writes=(q2[:, :],), tag=f"{tag}q2")
        B.op("dve", "reduce_max", (col[:, 4:5], q2[:, :]), dict(axis=AX.X),
             reads=(q2[:, :],), writes=(col[:, 4:5],), tag=f"{tag}m2")
        B.op("dve", "tensor_scalar", (eq2[:, :], q2[:, :], col[:, 4:5], None, ALU.is_equal),
             {}, reads=(q2[:, :], col[:, 4:5]), writes=(eq2[:, :],), tag=f"{tag}e2")
        B.op("dve", "tensor_add", (col[:, 5:6], col[:, 3:4], col[:, 4:5]), {},
             reads=(col[:, 3:5],), writes=(col[:, 5:6],), tag=f"{tag}dn")
        B.op("dve", "reciprocal", (col[:, 5:6], col[:, 5:6]), {},
             reads=(col[:, 5:6],), writes=(col[:, 5:6],), tag=f"{tag}rd")
        B.op("dve", "tensor_mul", (col[:, 3:4], col[:, 3:4], col[:, 5:6]), {},
             reads=(col[:, 3:4], col[:, 5:6]), writes=(col[:, 3:4],), tag=f"{tag}a1")
        B.op("dve", "tensor_mul", (col[:, 4:5], col[:, 4:5], col[:, 5:6]), {},
             reads=(col[:, 4:5], col[:, 5:6]), writes=(col[:, 4:5],), tag=f"{tag}a2")
        B.op("dve", "tensor_scalar", (gout[:, :], eq1[:, :], col[:, 3:4], None, ALU.mult),
             {}, reads=(eq1[:, :], col[:, 3:4]), writes=(gout[:, :],), tag=f"{tag}g1")
        B.op("dve", "scalar_tensor_tensor", (gout[:, :], eq2[:, :], col[:, 4:5],
                                             gout[:, :], ALU.mult, ALU.add), {},
             reads=(eq2[:, :], col[:, 4:5], gout[:, :]), writes=(gout[:, :],), tag=f"{tag}g2")
        B.dma("act", G_d[t0:t0 + 128, 0:8], gout[:, :], tag=f"{tag}st")



def add_streams(B, C, a_d, b_d, out_d, tag, a_dtype=F32):
    u = B.uid()
    xa = B.sbuf(f"{tag}_a{u}", [128, TB], a_dtype)
    xb = B.sbuf(f"{tag}_b{u}", [128, TB], BF16)
    xo = B.sbuf(f"{tag}_o{u}", [128, TB], F32)
    for fc in range(NFC):
        for tb in range(NTB):
            t0 = tb * TB
            B.dma("sp", xa[:, 0:TB], a_d[fc, :, t0:t0 + TB], tag=f"{tag}la")
            B.dma("sp", xb[:, 0:TB], b_d[fc, :, t0:t0 + TB], tag=f"{tag}lb")
            B.op("dve", "tensor_add", (xo[:, 0:TB], xa[:, 0:TB], xb[:, 0:TB]), {},
                 reads=(xa[:, 0:TB], xb[:, 0:TB]), writes=(xo[:, 0:TB],), tag=f"{tag}ad")
            B.dma("act", out_d[fc, :, t0:t0 + TB], xo[:, 0:TB], tag=f"{tag}st")



def moe_dense_l0(B, C, h_d, w1_d, w2_d, G_d, x2_d, tag, b1cols, b2cols,
                 dff=4096, nexp=8):
    u = B.uid()
    TQ = 512
    ndf = dff // 128
    ws1 = B.sbuf(f"{tag}_w1{u}", [128, NFC * dff], BF16)
    ws2 = B.sbuf(f"{tag}_w2{u}", [128, ndf * D], BF16)
    hs = B.sbuf(f"{tag}_h{u}", [128, 2 * NFC * TQ], BF16)
    he = B.sbuf(f"{tag}_he{u}", [128, ndf * TQ], BF16)
    gt = B.sbuf(f"{tag}_gt{u}", [128, 8], F32)
    gtr = B.sbuf(f"{tag}_gr{u}", [8, T], BF16)
    gbc = B.sbuf(f"{tag}_gb{u}", [128, TQ], F32)
    cst = B.sbuf(f"{tag}_cs{u}", [128, TQ], F32)
    esel = B.sbuf(f"{tag}_es{u}", [8, 8 * 128], BF16)
    B.dma("pool", esel[:, :], C.esel_d[:, :], tag=f"{tag}lde")
    bank = C.bank
    for jj in range(T // 128):
        B.dma("sp", gt[:, 0:8], G_d[jj * 128:(jj + 1) * 128, 0:8], tag=f"{tag}ldg")
        pt_ = bank[6][0:8, 0:128]
        B.transpose(pt_, gt[:, 0:8], C.identf[:, :], tag=f"{tag}gtr")
        B.op("dve", "tensor_copy", (gtr[0:8, jj * 128:(jj + 1) * 128], pt_), {},
             reads=(pt_,), writes=(gtr[0:8, jj * 128:(jj + 1) * 128],),
             tag=f"{tag}gc")
    w1v = w1_d.rearrange("e (kc kp) n -> e kc kp n", kp=128)
    w2v = w2_d.rearrange("e (kc kp) n -> e kc kp n", kp=128)
    for e in range(nexp):
        for kc in range(NFC):
            B.dma("pool", ws1[:, kc * dff:(kc + 1) * dff], w1v[e, kc, :, 0:dff],
                  tag=f"{tag}lw1")
        for kc in range(ndf):
            B.dma("pool", ws2[:, kc * D:(kc + 1) * D], w2v[e, kc, :, 0:D],
                  tag=f"{tag}lw2")
        for tq in range(T // TQ):
            t0 = tq * TQ
            hb = (tq % 2) * NFC * TQ
            for kc in range(NFC):
                B.dma("sp", hs[:, hb + kc * TQ: hb + (kc + 1) * TQ], h_d[kc, :, t0:t0 + TQ],
                      tag=f"{tag}ldh")
            pgb = bank[6][:, 0:TQ]
            B.mm(pgb, esel[0:8, e * 128:(e + 1) * 128], gtr[0:8, t0:t0 + TQ],
                 start=True, stop=True, tag=f"{tag}gbm")
            B.op("dve", "tensor_copy", (gbc[:, 0:TQ], pgb), {},
                 reads=(pgb,), writes=(gbc[:, 0:TQ],), tag=f"{tag}gbe")
            for fu in range(ndf):
                pb = bank[4 + fu % 2][:, 0:TQ]
                for kc in range(NFC):
                    B.mm(pb, ws1[:, kc * dff + fu * 128: kc * dff + (fu + 1) * 128],
                         hs[:, hb + kc * TQ: hb + (kc + 1) * TQ],
                         start=(kc == 0), stop=(kc == NFC - 1), tag=f"{tag}up")
                B.act(he[:, fu * TQ:(fu + 1) * TQ], pb, AF.Gelu,
                      bias=C.pv(b1cols[e] + fu), tag=f"{tag}gl")
            for oc in range(NFC):
                pb = bank[oc % 4][:, 0:TQ]
                for kc in range(ndf):
                    B.mm(pb, ws2[:, kc * D + oc * 128: kc * D + (oc + 1) * 128],
                         he[:, kc * TQ:(kc + 1) * TQ],
                         start=(kc == 0), stop=(kc == ndf - 1),
                         tag=f"{tag}dn")
                B.op("dve", "scalar_tensor_tensor",
                     (cst[:, 0:TQ], pb, C.pv(b2cols[e] + oc), gbc[:, 0:TQ],
                      ALU.add, ALU.mult), {},
                     reads=(pb, C.pv(b2cols[e] + oc), gbc[:, 0:TQ]),
                     writes=(cst[:, 0:TQ],), tag=f"{tag}ev")
                B.dma("pool", x2_d[oc, :, t0:t0 + TQ], cst[:, 0:TQ],
                      accum_op=ALU.add, tag=f"{tag}ac")


def phase_l0ffn_qkv(B, C, Tn, cr):
    ln_full(B, C, Tn["x1"], Tn["u1"], "p4ln")
    ln_full(B, C, Tn["u1"], Tn["h1"], "p4l2", g_rowcol=cr.row_index["l0f_g1"],
            b_rowcol=cr.row_index["l0f_b1"], src_dtype=BF16)
    proj_fm(B, C, Tn["h1"], Tn["L0_ffn_w_qkv"], Tn["Q2"], D, "p4q",
            scale=0.125, col_off=0)
    proj_fm(B, C, Tn["h1"], Tn["L0_ffn_w_qkv"], Tn["K2"], D, "p4k", col_off=D)
    proj_tm_v(B, C, Tn["h1"], Tn["L0_ffn_w_qkv"], Tn["V2"], D, 16, 64, "p4v",
              col_off=2 * D)


def phase_l0ffn_av(B, C, Tn, cr):
    attention(B, C, Tn["Q2"], Tn["K2"], Tn["V2"], Tn["O2"], 16, 64, "p5")


def phase_l0ffn_mid(B, C, Tn, cr):
    attn_out_proj(B, C, Tn["O2"], Tn["L0_ffn_w_o"], Tn["u1"], Tn["xr"], "p6",
                  xin_dtype=BF16, out_dtype=BF16)
    ln_full(B, C, Tn["xr"], Tn["h2"], "p6l2", g_rowcol=cr.row_index["l0f_g2"],
            b_rowcol=cr.row_index["l0f_b2"], src_dtype=BF16)
    moe_gate(B, C, Tn["h2"], Tn["L0_ffn_w_gate"], Tn["G1"], "p6g")


def phase_l0ffn_moe(B, C, Tn, cr):
    add_streams(B, C, Tn["x1"], Tn["xr"], Tn["x2"], "p7i")
    b1cols = [cr.pvt_index[f"l0f_b1_{e}"] for e in range(8)]
    b2cols = [cr.pvt_index[f"l0f_b2_{e}"] for e in range(8)]
    moe_dense_l0(B, C, Tn["h2"], Tn["L0_ffn_w1"], Tn["L0_ffn_w2"], Tn["G1"],
                 Tn["x2"], "p7", b1cols, b2cols)





def proj_tm(B, C, h_d, w_d, out_d, dout, tag, col_off=0, scale=None, wT=False):
    u = B.uid()
    ws = B.sbuf(f"{tag}_w{u}", [128, NFC * dout], BF16)
    hs = B.sbuf(f"{tag}_h{u}", [128, NFC * 128], BF16)
    os_ = B.sbuf(f"{tag}_o{u}", [128, dout], BF16)
    w_view = w_d.rearrange("(kc kp) n -> kc kp n", kp=128)
    for kc in range(NFC):
        B.dma("pool", ws[:, kc * dout:(kc + 1) * dout], w_view[kc, :, col_off:col_off + dout],
              tag=f"{tag}ldw")
    bank = C.bank
    nnc = (dout + 511) // 512
    for tt in range(T // 128):
        t0 = tt * 128
        for kc in range(NFC):
            B.dma("sp", hs[:, kc * 128:(kc + 1) * 128], h_d[kc, :, t0:t0 + 128],
                  tag=f"{tag}ldh")
        for nc_ in range(nnc):
            n0 = nc_ * 512
            nw = min(512, dout - n0)
            pb = bank[nc_ % 4][:, 0:nw]
            for kc in range(NFC):
                B.mm(pb, hs[:, kc * 128:(kc + 1) * 128], ws[:, kc * dout + n0: kc * dout + n0 + nw],
                     start=(kc == 0), stop=(kc == NFC - 1), tag=f"{tag}mm")
            if scale is not None:
                B.op("dve", "tensor_scalar", (os_[:, n0:n0 + nw], pb, scale, None, ALU.mult),
                     {}, reads=(pb,), writes=(os_[:, n0:n0 + nw],), tag=f"{tag}ev")
            else:
                B.op("dve", "tensor_copy", (os_[:, n0:n0 + nw], pb), {},
                     reads=(pb,), writes=(os_[:, n0:n0 + nw],), tag=f"{tag}ev")
        B.dma("act", out_d[t0:t0 + 128, 0:dout], os_[:, 0:dout], tag=f"{tag}st")



def rope_tm(B, C, x_d, cos_d, sin_d, width, ranges, nfreq, tag):
    u = B.uid()
    xt = B.sbuf(f"{tag}_x{u}", [128, width], BF16)
    cs = B.sbuf(f"{tag}_c{u}", [128, nfreq], F32)
    sn = B.sbuf(f"{tag}_s{u}", [128, nfreq], F32)
    t1 = B.sbuf(f"{tag}_t1{u}", [128, nfreq], F32)
    t2 = B.sbuf(f"{tag}_t2{u}", [128, nfreq], F32)
    t3 = B.sbuf(f"{tag}_t3{u}", [128, nfreq], F32)
    for tt in range(T // 128):
        t0 = tt * 128
        B.dma("sp", xt[:, 0:width], x_d[t0:t0 + 128, 0:width], tag=f"{tag}ldx")
        B.dma("sp", cs[:, :], cos_d[t0:t0 + 128, 0:nfreq], tag=f"{tag}ldc")
        B.dma("sp", sn[:, :], sin_d[t0:t0 + 128, 0:nfreq], tag=f"{tag}lds")
        for c0 in ranges:
            ev = xt[:, c0: c0 + 2 * nfreq: 2]
            od = xt[:, c0 + 1: c0 + 2 * nfreq: 2]
            rr = xt[:, c0:c0 + 2 * nfreq]
            B.op("dve", "tensor_mul", (t1[:, :], ev, cs[:, :]), {},
                 reads=(rr, cs[:, :]), writes=(t1[:, :],), tag=f"{tag}a")
            B.op("dve", "tensor_mul", (t2[:, :], od, sn[:, :]), {},
                 reads=(rr, sn[:, :]), writes=(t2[:, :],), tag=f"{tag}b")
            B.op("dve", "tensor_sub", (t1[:, :], t1[:, :], t2[:, :]), {},
                 reads=(t1[:, :], t2[:, :]), writes=(t1[:, :],), tag=f"{tag}c")
            B.op("dve", "tensor_mul", (t2[:, :], od, cs[:, :]), {},
                 reads=(rr, cs[:, :]), writes=(t2[:, :],), tag=f"{tag}d")
            B.op("dve", "tensor_mul", (t3[:, :], ev, sn[:, :]), {},
                 reads=(rr, sn[:, :]), writes=(t3[:, :],), tag=f"{tag}e")
            B.op("dve", "tensor_add", (t2[:, :], t2[:, :], t3[:, :]), {},
                 reads=(t2[:, :], t3[:, :]), writes=(t2[:, :],), tag=f"{tag}f")
            B.op("dve", "tensor_copy", (ev, t1[:, :]), {},
                 reads=(t1[:, :],), writes=(rr,), tag=f"{tag}g")
            B.op("dve", "tensor_copy", (od, t2[:, :]), {},
                 reads=(t2[:, :],), writes=(rr,), tag=f"{tag}h")
        B.dma("act", x_d[t0:t0 + 128, 0:width], xt[:, 0:width], tag=f"{tag}st")



def tm_to_fm(B, C, x_d, out_d, w, tag, col_groups=None):
    u = B.uid()
    nch = (w // 128) if col_groups is None else len(col_groups)
    xt = B.sbuf(f"{tag}_x{u}", [128, w], BF16)
    st = B.sbuf(f"{tag}_s{u}", [128, 128], BF16)
    bkb = C.bankb
    for tt in range(T // 128):
        t0 = tt * 128
        B.dma("sp", xt[:, 0:w], x_d[t0:t0 + 128, 0:w], tag=f"{tag}ldx")
        for ch in range(nch):
            if col_groups is None:
                segs = [(ch * 128, 128)]
            else:
                segs = col_groups[ch]
            pt_ = bkb[:, 0:128]
            p0 = 0
            for (s0, span) in segs:
                B.transpose(bkb[p0:p0 + span, 0:128], xt[:, s0:s0 + span],
                            C.identb[:, :], tag=f"{tag}tr")
                p0 += span
            B.op("dve", "tensor_copy", (st[:, 0:128], pt_), {},
                 reads=(pt_,), writes=(st[:, 0:128],), tag=f"{tag}cp")
            B.dma("act", out_d[ch, :, t0:t0 + 128], st[:, 0:128], tag=f"{tag}st")



LAMBDA_INIT = 0.8 - 0.6 * math.exp(-0.3 * 12)


def l1attn_combine(B, C, o1_d, o2_d, lam_d, lnw_d, lnb_d, y_d, tag):
    u = B.uid()
    o1 = B.sbuf(f"{tag}_o1{u}", [128, TB], BF16)
    o2 = B.sbuf(f"{tag}_o2{u}", [128, TB], BF16)
    oc = B.sbuf(f"{tag}_oc{u}", [128, TB], F32)
    sq = B.sbuf(f"{tag}_sq{u}", [128, TB], F32)
    bcf = B.sbuf(f"{tag}_bc{u}", [128, TB], F32)
    yt = B.sbuf(f"{tag}_y{u}", [128, TB], BF16)
    rrow = B.sbuf(f"{tag}_rr{u}", [1, TB], F32)
    lv = B.sbuf(f"{tag}_lv{u}", [1, 4 * 64], F32)
    dots = B.sbuf(f"{tag}_dt{u}", [1, 4], F32)
    lamr = B.sbuf(f"{tag}_lm{u}", [1, 1], F32)
    nlam = B.sbuf(f"{tag}_nl{u}", [128, 1], F32)
    lnw = B.sbuf(f"{tag}_lw{u}", [128, 2], F32)
    onesc = B.sbuf(f"{tag}_1c{u}", [128, 1], F32)
    B.op("pool", "memset", (onesc[:, :], 1.0), {}, writes=(onesc[:, :],))
    B.dma("pool", lnw[:, 0:1], lnw_d[:, 0:1], tag=f"{tag}lw")
    B.dma("pool", lnw[:, 1:2], lnb_d[:, 0:1], tag=f"{tag}lb")
    B.dma("pool", lv[0:1, :], lam_d[0:1, :], tag=f"{tag}ll")
    B.op("dve", "tensor_scalar", (lnw[:, 0:2], lnw[:, 0:2], 1.0 - LAMBDA_INIT, None,
                                  ALU.mult), {},
         reads=(lnw[:, 0:2],), writes=(lnw[:, 0:2],), tag=f"{tag}sw")
    for i, (a, b) in enumerate(((0, 64), (128, 192))):
        B.op("dve", "tensor_mul", (lv[0:1, a:a + 64], lv[0:1, a:a + 64], lv[0:1, b:b + 64]),
             {}, reads=(lv[0:1, :],), writes=(lv[0:1, a:a + 64],), tag=f"{tag}m{i}")
        B.op("dve", "reduce_sum", (dots[0:1, i:i + 1], lv[0:1, a:a + 64]), dict(axis=AX.X),
             reads=(lv[0:1, a:a + 64],), writes=(dots[0:1, i:i + 1],), tag=f"{tag}r{i}")
    B.act(dots[0:1, 0:2], dots[0:1, 0:2], AF.Exp, tag=f"{tag}ex")
    B.op("dve", "tensor_sub", (lamr[0:1, 0:1], dots[0:1, 0:1], dots[0:1, 1:2]), {},
         reads=(dots[0:1, 0:2],), writes=(lamr[0:1, 0:1],), tag=f"{tag}sub")
    B.op("dve", "tensor_scalar", (lamr[0:1, 0:1], lamr[0:1, 0:1], LAMBDA_INIT, -1.0,
                                  ALU.add, ALU.mult), {},
         reads=(lamr[0:1, 0:1],), writes=(lamr[0:1, 0:1],), tag=f"{tag}neg")
    pb0 = C.bank[5][:, 0:1]
    B.mm(pb0, C.onesrowf[0:1, :], lamr[0:1, 0:1], start=True, stop=True, tag=f"{tag}rep")
    B.op("dve", "tensor_copy", (nlam[:, 0:1], pb0), {}, reads=(pb0,),
         writes=(nlam[:, 0:1],), tag=f"{tag}cp")
    inv128 = 1.0 / 128.0
    bank = C.bank
    for h in range(8):
        for tb in range(NTB):
            t0 = tb * TB
            B.dma("sp", o1[:, 0:TB], o1_d[h, :, t0:t0 + TB], tag=f"{tag}l1")
            B.dma("sp", o2[:, 0:TB], o2_d[h, :, t0:t0 + TB], tag=f"{tag}l2")
            B.op("dve", "scalar_tensor_tensor", (oc[:, 0:TB], o2[:, 0:TB], nlam[:, 0:1],
                                                 o1[:, 0:TB], ALU.mult, ALU.add), {},
                 reads=(o2[:, 0:TB], nlam[:, 0:1], o1[:, 0:TB]), writes=(oc[:, 0:TB],),
                 tag=f"{tag}cmb")
            B.act(sq[:, 0:TB], oc[:, 0:TB], AF.Square, tag=f"{tag}sq")
            ps = bank[tb % 2][0:1, 0:TB]
            B.mm(ps, onesc[:, 0:1], sq[:, 0:TB], start=True, stop=True, tag=f"{tag}ss")
            B.act(rrow[0:1, 0:TB], ps, AF.Ln, bias=C.pv(C.cr.pvt_index["eps_a8"])[0:1, 0:1],
                  scale=inv128, tag=f"{tag}ln")
            B.act(rrow[0:1, 0:TB], rrow[0:1, 0:TB], AF.Exp, scale=-0.5, tag=f"{tag}ep")
            pbc = bank[2 + tb % 2][:, 0:TB]
            B.mm(pbc, C.onesrowf[0:1, :], rrow[0:1, 0:TB], start=True, stop=True,
                 tag=f"{tag}bm")
            B.op("dve", "tensor_copy", (bcf[:, 0:TB], pbc), {},
                 reads=(pbc,), writes=(bcf[:, 0:TB],), tag=f"{tag}be")
            B.op("dve", "tensor_mul", (oc[:, 0:TB], oc[:, 0:TB], bcf[:, 0:TB]), {},
                 reads=(oc[:, 0:TB], bcf[:, 0:TB]), writes=(oc[:, 0:TB],), tag=f"{tag}n1")
            B.op("dve", "tensor_scalar", (yt[:, 0:TB], oc[:, 0:TB], lnw[:, 0:1],
                                          lnw[:, 1:2], ALU.mult, ALU.add), {},
                 reads=(oc[:, 0:TB], lnw[:, 0:2]), writes=(yt[:, 0:TB],), tag=f"{tag}n2")
            B.dma("act", y_d[h, :, t0:t0 + TB], yt[:, 0:TB], tag=f"{tag}st")



def moe_l1(B, C, h_d, Wg_d, Wu_d, Wd_d, sWg_d, sWu_d, sWd_d, G_d, x4_d, tag,
           dint=512):
    u = B.uid()
    TQ = 512
    ndi = dint // 128
    wsg = B.sbuf(f"{tag}_wg{u}", [128, NFC * dint], BF16)
    wsu = B.sbuf(f"{tag}_wu{u}", [128, NFC * dint], BF16)
    wsd = B.sbuf(f"{tag}_wd{u}", [128, ndi * D], BF16)
    hs = B.sbuf(f"{tag}_h{u}", [128, 2 * NFC * TQ], BF16)
    gu = B.sbuf(f"{tag}_gu{u}", [128, ndi * TQ], BF16)
    us = B.sbuf(f"{tag}_us{u}", [128, TQ], F32)
    gt = B.sbuf(f"{tag}_gt{u}", [128, 8], F32)
    gtr = B.sbuf(f"{tag}_gr{u}", [8, T], BF16)
    gbc = B.sbuf(f"{tag}_gb{u}", [128, TQ], F32)
    cst = B.sbuf(f"{tag}_cs{u}", [128, TQ], F32)
    esel = B.sbuf(f"{tag}_es{u}", [8, 8 * 128], BF16)
    B.dma("pool", esel[:, :], C.esel_d[:, :], tag=f"{tag}lde")
    bank = C.bank
    for jj in range(T // 128):
        B.dma("sp", gt[:, 0:8], G_d[jj * 128:(jj + 1) * 128, 0:8], tag=f"{tag}ldg")
        pt_ = bank[6][0:8, 0:128]
        B.transpose(pt_, gt[:, 0:8], C.identf[:, :], tag=f"{tag}gtr")
        B.op("dve", "tensor_copy", (gtr[0:8, jj * 128:(jj + 1) * 128], pt_), {},
             reads=(pt_,), writes=(gtr[0:8, jj * 128:(jj + 1) * 128],),
             tag=f"{tag}gc")
    Wgv = Wg_d.rearrange("e (kc kp) n -> e kc kp n", kp=128)
    Wuv = Wu_d.rearrange("e (kc kp) n -> e kc kp n", kp=128)
    Wdv = Wd_d.rearrange("e (kc kp) n -> e kc kp n", kp=128)
    sWgv = sWg_d.rearrange("e (kc kp) n -> e kc kp n", kp=128)
    sWuv = sWu_d.rearrange("e (kc kp) n -> e kc kp n", kp=128)
    sWdv = sWd_d.rearrange("e (kc kp) n -> e kc kp n", kp=128)
    for e in range(-1, 8):
        gv, uv, dv_ = (sWgv, sWuv, sWdv) if e < 0 else (Wgv, Wuv, Wdv)
        ei = 0 if e < 0 else e
        for kc in range(NFC):
            B.dma("pool", wsg[:, kc * dint:(kc + 1) * dint], gv[ei, kc, :, 0:dint],
                  tag=f"{tag}lwg")
            B.dma("pool", wsu[:, kc * dint:(kc + 1) * dint], uv[ei, kc, :, 0:dint],
                  tag=f"{tag}lwu")
        for kc in range(ndi):
            B.dma("pool", wsd[:, kc * D:(kc + 1) * D], dv_[ei, kc, :, 0:D],
                  tag=f"{tag}lwd")
        for tq in range(T // TQ):
            t0 = tq * TQ
            hb = (tq % 2) * NFC * TQ
            for kc in range(NFC):
                B.dma("sp", hs[:, hb + kc * TQ: hb + (kc + 1) * TQ], h_d[kc, :, t0:t0 + TQ],
                      tag=f"{tag}ldh")
            if e >= 0:
                pgb = bank[6][:, 0:TQ]
                B.mm(pgb, esel[0:8, e * 128:(e + 1) * 128], gtr[0:8, t0:t0 + TQ],
                     start=True, stop=True, tag=f"{tag}gbm")
                B.op("dve", "tensor_copy", (gbc[:, 0:TQ], pgb), {},
                     reads=(pgb,), writes=(gbc[:, 0:TQ],), tag=f"{tag}gbe")
            for fu in range(ndi):
                pg = bank[4 + 2 * (fu % 2)][:, 0:TQ] if False else bank[(4, 0)[fu % 2]][:, 0:TQ]
                pu = bank[(5, 1)[fu % 2]][:, 0:TQ]
                for kc in range(NFC):
                    B.mm(pg, wsg[:, kc * dint + fu * 128: kc * dint + (fu + 1) * 128],
                         hs[:, hb + kc * TQ: hb + (kc + 1) * TQ],
                         start=(kc == 0), stop=(kc == NFC - 1), tag=f"{tag}ug")
                for kc in range(NFC):
                    B.mm(pu, wsu[:, kc * dint + fu * 128: kc * dint + (fu + 1) * 128],
                         hs[:, hb + kc * TQ: hb + (kc + 1) * TQ],
                         start=(kc == 0), stop=(kc == NFC - 1), tag=f"{tag}uu")
                B.op("dve", "tensor_copy", (us[:, 0:TQ], pu), {},
                     reads=(pu,), writes=(us[:, 0:TQ],), tag=f"{tag}ue")
                B.act(cst[:, 0:TQ], pg, AF.Silu, tag=f"{tag}si")
                B.op("dve", "tensor_mul", (gu[:, fu * TQ:(fu + 1) * TQ], cst[:, 0:TQ],
                                           us[:, 0:TQ]), {},
                     reads=(cst[:, 0:TQ], us[:, 0:TQ]),
                     writes=(gu[:, fu * TQ:(fu + 1) * TQ],), tag=f"{tag}gm")
            for oc in range(NFC):
                pb = bank[oc % 4][:, 0:TQ]
                for kc in range(ndi):
                    B.mm(pb, wsd[:, kc * D + oc * 128: kc * D + (oc + 1) * 128],
                         gu[:, kc * TQ:(kc + 1) * TQ],
                         start=(kc == 0), stop=(kc == ndi - 1), tag=f"{tag}dn")
                if e >= 0:
                    B.op("dve", "tensor_mul", (cst[:, 0:TQ], pb, gbc[:, 0:TQ]), {},
                         reads=(pb, gbc[:, 0:TQ]), writes=(cst[:, 0:TQ],), tag=f"{tag}ev")
                else:
                    B.op("dve", "tensor_copy", (cst[:, 0:TQ], pb), {},
                         reads=(pb,), writes=(cst[:, 0:TQ],), tag=f"{tag}ev")
                B.dma("pool", x4_d[oc, :, t0:t0 + TQ], cst[:, 0:TQ],
                      accum_op=ALU.add, tag=f"{tag}ac")


def phase_l1attn_qkv(B, C, Tn, cr):
    ln_full(B, C, Tn["x2"], Tn["u2"], "p8ln")
    sc = 64.0 ** (-0.25)
    ga = [[(256 * p, 64), (256 * p + 128, 64)] for p in range(4)]
    gb = [[(256 * p + 64, 64), (256 * p + 192, 64)] for p in range(4)]
    proj_rope_fm(B, C, Tn["u2"], Tn["L1_attn_WqT"],
                 [(Tn["Q3a"], ga), (Tn["Q3b"], gb)],
                 Tn["rope64w_cos"], Tn["rope64w_sin"], 0, 32, "p8q", scale=sc)
    proj_rope_fm(B, C, Tn["u2"], Tn["L1_attn_WkT"],
                 [(Tn["K3a"], ga), (Tn["K3b"], gb)],
                 Tn["rope64w_cos"], Tn["rope64w_sin"], 0, 32, "p8k")
    proj_tm_v(B, C, Tn["u2"], Tn["L1_attn_WvT"], Tn["V3"], D, 8, 128, "p8v")


def phase_l1attn_av(B, C, Tn, cr):
    attention(B, C, Tn["Q3a"], Tn["K3a"], Tn["V3"], Tn["O3a"], 8, 128, "p9a",
              causal=False)


def phase_l1attn_av2(B, C, Tn, cr):
    attention(B, C, Tn["Q3b"], Tn["K3b"], Tn["V3"], Tn["O3b"], 8, 128, "p9b",
              causal=False)


def phase_l1attn_out(B, C, Tn, cr):
    l1attn_combine(B, C, Tn["O3a"], Tn["O3b"], Tn["lam_cat"], Tn["lnw_col"],
                   Tn["lnb_col"], Tn["Y3"], "pA")
    attn_out_proj(B, C, Tn["Y3"], Tn["L1_attn_WoT"], Tn["x2"], Tn["x3"], "pAo")


def phase_l1ffn_qkv(B, C, Tn, cr):
    ln_full(B, C, Tn["x3"], Tn["u3"], "pBln")
    ln_full(B, C, Tn["u3"], Tn["h3"], "pBl2", g_rowcol=cr.row_index["l1f_g1"],
            rms=True, src_dtype=BF16)
    ga = [[(128 * p, 128)] for p in range(8)]
    proj_rope_fm(B, C, Tn["h3"], Tn["L1_ffn_Wq"], [(Tn["Q4"], ga)],
                 Tn["rope32w_cos"], Tn["rope32w_sin"], 32, 16, "pBq", scale=0.125)
    proj_tm(B, C, Tn["h3"], Tn["L1_ffn_Wkva"], Tn["ckvt"], 160, "pBkv")
    rope_tm(B, C, Tn["ckvt"], Tn["rope32_cos"], Tn["rope32_sin"], 160, [128], 16, "pBrk")


def phase_l1ffn_kv(B, C, Tn, cr):
    B_ = B
    u = B.uid()
    tag = "pC"
    ck = B.sbuf(f"{tag}_ck{u}", [128, 160], BF16)
    lat = B.sbuf(f"{tag}_lt{u}", [128, 128], BF16)
    wkvb = B.sbuf(f"{tag}_w{u}", [128, 512], BF16)
    kvsb = B.sbuf(f"{tag}_kv{u}", [128, 512], BF16)
    trsb = B.sbuf(f"{tag}_tr{u}", [128, 128], BF16)
    krsb = B.sbuf(f"{tag}_kr{u}", [128, 128], BF16)
    v4sb = B.sbuf(f"{tag}_v4{u}", [128, 16 * 33], BF16)
    B.dma("pool", wkvb[:, :], Tn["L1_ffn_Wkvb"][:, :], tag=f"{tag}lw")
    bank = C.bank
    bkb = C.bankb
    for tt in range(T // 128):
        t0 = tt * 128
        B.dma("sp", ck[:, 0:160], Tn["ckvt"][t0:t0 + 128, 0:160], tag=f"{tag}ldc")
        ptb = bkb[:, 0:128]
        B.transpose(ptb, ck[:, 0:128], C.identb[:, :], tag=f"{tag}t1")
        B.op("dve", "tensor_copy", (lat[:, :], ptb), {}, reads=(ptb,),
             writes=(lat[:, :],), tag=f"{tag}c1")
        pkv = bank[4][:, 0:512]
        B.mm(pkv, lat[:, :], wkvb[:, 0:512], start=True, stop=True, tag=f"{tag}mm")
        B.op("dve", "tensor_copy", (kvsb[:, :], pkv), {}, reads=(pkv,),
             writes=(kvsb[:, :],), tag=f"{tag}c2")
        for h in range(16):
            B.op("dve", "tensor_copy",
                 (v4sb[:, h * 33: h * 33 + 32], kvsb[:, h * 32:(h + 1) * 32]), {},
                 reads=(kvsb[:, h * 32:(h + 1) * 32],),
                 writes=(v4sb[:, h * 33: h * 33 + 32],), tag=f"{tag}v")
        ones_view = v4sb.rearrange("p (h e) -> p h e", e=33)[:, :, 32:33]
        B.op("dve", "memset", (ones_view, 1.0), {}, writes=(v4sb[:, 0:16 * 33],),
             tag=f"{tag}on")
        B.dma("act", Tn["V4"][t0:t0 + 128, 0:16 * 33], v4sb[:, 0:16 * 33], tag=f"{tag}sv")
        for j in range(4):
            B.transpose(bkb[:, 0:128], kvsb[:, j * 128:(j + 1) * 128], C.identb[:, :],
                        tag=f"{tag}t2")
            B.op("dve", "tensor_copy", (trsb[:, :], bkb[:, 0:128]), {},
                 reads=(bkb[:, 0:128],), writes=(trsb[:, :],), tag=f"{tag}c3")
            for i in range(4):
                h = 4 * j + i
                B.dma("act", Tn["K4"][h // 2, (h % 2) * 64:(h % 2) * 64 + 32, t0:t0 + 128],
                      trsb[32 * i:32 * i + 32, :], tag=f"{tag}sk")
        B.transpose(bkb[0:32, 0:128], ck[:, 128:160], C.identb[:, :], tag=f"{tag}t3")
        B.op("dve", "tensor_copy", (krsb[0:32, :], bkb[0:32, 0:128]), {},
             reads=(bkb[0:32, 0:128],), writes=(krsb[0:32, :],), tag=f"{tag}c4")
        for p in range(8):
            B.dma("act", Tn["K4"][p, 32:64, t0:t0 + 128], krsb[0:32, :], tag=f"{tag}skr")
            B.dma("act", Tn["K4"][p, 96:128, t0:t0 + 128], krsb[0:32, :], tag=f"{tag}skr2")


def phase_l1ffn_av(B, C, Tn, cr):
    attention(B, C, Tn["Q4"], Tn["K4"], Tn["V4"], Tn["O4"], 16, 32, "pD")


def phase_l1ffn_mid(B, C, Tn, cr):
    attn_out_proj(B, C, Tn["O4"], Tn["L1f_Wo_sub"], Tn["u3"], Tn["xi"], "pE",
                  xin_dtype=BF16, out_dtype=BF16, ncol=512)
    ln_full(B, C, Tn["xi"], Tn["h4"], "pEl2", g_rowcol=cr.row_index["l1f_g2"],
            rms=True, src_dtype=BF16)
    moe_gate(B, C, Tn["h4"], Tn["L1_ffn_Wgate"], Tn["G2"], "pEg",
             ebias_bc_d=Tn["ebias_bc"])


def phase_l1ffn_moe(B, C, Tn, cr):
    add_streams(B, C, Tn["x3"], Tn["xi"], Tn["x4"], "pFi")
    moe_l1(B, C, Tn["h4"], Tn["L1_ffn_Wg"], Tn["L1_ffn_Wu"], Tn["L1_ffn_Wd"],
           Tn["L1_ffn_sWg"], Tn["L1_ffn_sWu"], Tn["L1_ffn_sWd"], Tn["G2"],
           Tn["x4"], "pF")


PHASES[:] = [
    ("l0attn_qkv", ["x0"], ["u_l0", "Q1", "K1", "V1"],
     ["L0_attn_W_qkv"], ["vbias_l0"], phase_l0attn_qkv),
    ("l0attn_av", ["Q1", "K1", "V1"], ["O1"], [], [], phase_l0attn_av),
    ("l0attn_out", ["O1", "x0"], ["x1"], ["L0_attn_W_out"], [], phase_l0attn_out),
    ("l0ffn_qkv", ["x1"], ["u1", "h1", "Q2", "K2", "V2"], ["L0_ffn_w_qkv"], [], phase_l0ffn_qkv),
    ("l0ffn_av", ["Q2", "K2", "V2"], ["O2"], [], [], phase_l0ffn_av),
    ("l0ffn_mid", ["O2", "u1"], ["xr", "h2", "G1"], ["L0_ffn_w_o", "L0_ffn_w_gate"], [], phase_l0ffn_mid),
    ("l0ffn_moe", ["x1", "xr", "h2", "G1"], ["x2"], ["L0_ffn_w1", "L0_ffn_w2"], [], phase_l0ffn_moe),
    ("l1attn_qkv", ["x2"], ["u2", "q3t", "k3t", "Q3a", "Q3b", "K3a", "K3b", "V3"],
     ["L1_attn_WqT", "L1_attn_WkT", "L1_attn_WvT"],
     ["rope64w_cos", "rope64w_sin"], phase_l1attn_qkv),
    ("l1attn_av", ["Q3a", "K3a", "V3"], ["O3a"], [], [], phase_l1attn_av),
    ("l1attn_av2", ["Q3b", "K3b", "V3"], ["O3b"], [], [], phase_l1attn_av2),
    ("l1attn_out", ["O3a", "O3b", "x2"], ["Y3", "x3"], ["L1_attn_WoT"],
     ["lam_cat", "lnw_col", "lnb_col"], phase_l1attn_out),
    ("l1ffn_qkv", ["x3"], ["u3", "h3", "q4t", "ckvt", "Q4"],
     ["L1_ffn_Wq", "L1_ffn_Wkva"],
     ["rope32_cos", "rope32_sin", "rope32w_cos", "rope32w_sin"], phase_l1ffn_qkv),
    ("l1ffn_kv", ["ckvt"], ["K4", "V4"], ["L1_ffn_Wkvb"], [], phase_l1ffn_kv),
    ("l1ffn_av", ["Q4", "K4", "V4"], ["O4"], [], [], phase_l1ffn_av),
    ("l1ffn_mid", ["O4", "u3"], ["xi", "h4", "G2"], ["L1f_Wo_sub", "L1_ffn_Wgate"],
     ["ebias_bc"], phase_l1ffn_mid),
    ("l1ffn_moe", ["x3", "xi", "h4", "G2"], ["x4"],
     ["L1_ffn_Wg", "L1_ffn_Wu", "L1_ffn_Wd", "L1_ffn_sWg", "L1_ffn_sWu", "L1_ffn_sWd"],
     [], phase_l1ffn_moe),
]



def proj_rope_fm(B, C, h_d, w_d, outs_fm, cosw_d, sinw_d, off, nfreq, tag,
                 dout=D, scale=None):
    u = B.uid()
    wcols = 16 * nfreq
    ws = B.sbuf(f"{tag}_w{u}", [128, NFC * dout], BF16)
    hs = B.sbuf(f"{tag}_h{u}", [128, NFC * 128], BF16)
    os_ = B.sbuf(f"{tag}_o{u}", [128, dout], BF16)
    cs = B.sbuf(f"{tag}_c{u}", [128, wcols], F32)
    sn = B.sbuf(f"{tag}_s{u}", [128, wcols], F32)
    t1 = B.sbuf(f"{tag}_t1{u}", [128, wcols], F32)
    t2 = B.sbuf(f"{tag}_t2{u}", [128, wcols], F32)
    t3 = B.sbuf(f"{tag}_t3{u}", [128, wcols], F32)
    st = B.sbuf(f"{tag}_st{u}", [128, 128], BF16)
    w_view = w_d.rearrange("(kc kp) n -> kc kp n", kp=128)
    for kc in range(NFC):
        B.dma("pool", ws[:, kc * dout:(kc + 1) * dout], w_view[kc, :, 0:dout],
              tag=f"{tag}ldw")
    bank = C.bank
    bkb = C.bankb
    nnc = (dout + 511) // 512
    for tt in range(T // 128):
        t0 = tt * 128
        for kc in range(NFC):
            B.dma("sp", hs[:, kc * 128:(kc + 1) * 128], h_d[kc, :, t0:t0 + 128],
                  tag=f"{tag}ldh")
        B.dma("sp", cs[:, :], cosw_d[t0:t0 + 128, 0:wcols], tag=f"{tag}ldc")
        B.dma("sp", sn[:, :], sinw_d[t0:t0 + 128, 0:wcols], tag=f"{tag}lds")
        for nc_ in range(nnc):
            n0 = nc_ * 512
            nw = min(512, dout - n0)
            pb = bank[nc_ % 4][:, 0:nw]
            for kc in range(NFC):
                B.mm(pb, hs[:, kc * 128:(kc + 1) * 128], ws[:, kc * dout + n0: kc * dout + n0 + nw],
                     start=(kc == 0), stop=(kc == NFC - 1), tag=f"{tag}mm")
            if scale is not None:
                B.op("dve", "tensor_scalar", (os_[:, n0:n0 + nw], pb, scale, None, ALU.mult),
                     {}, reads=(pb,), writes=(os_[:, n0:n0 + nw],), tag=f"{tag}ev")
            else:
                B.op("dve", "tensor_copy", (os_[:, n0:n0 + nw], pb), {},
                     reads=(pb,), writes=(os_[:, n0:n0 + nw],), tag=f"{tag}ev")
        osv = os_.rearrange("p (h c) -> p h c", c=64)
        ev = osv[:, :, off: off + 2 * nfreq: 2]
        od = osv[:, :, off + 1: off + 2 * nfreq: 2]
        rr = os_[:, 0:dout]
        csv = cs.rearrange("p (h f) -> p h f", f=nfreq)[:, :, :]
        snv = sn.rearrange("p (h f) -> p h f", f=nfreq)[:, :, :]
        t1v = t1.rearrange("p (h f) -> p h f", f=nfreq)[:, :, :]
        t2v = t2.rearrange("p (h f) -> p h f", f=nfreq)[:, :, :]
        t3v = t3.rearrange("p (h f) -> p h f", f=nfreq)[:, :, :]
        B.op("dve", "tensor_mul", (t1v, ev, csv), {},
             reads=(rr, cs[:, :]), writes=(t1[:, :],), tag=f"{tag}ra")
        B.op("dve", "tensor_mul", (t2v, od, snv), {},
             reads=(rr, sn[:, :]), writes=(t2[:, :],), tag=f"{tag}rb")
        B.op("dve", "tensor_sub", (t1[:, :], t1[:, :], t2[:, :]), {},
             reads=(t1[:, :], t2[:, :]), writes=(t1[:, :],), tag=f"{tag}rc")
        B.op("dve", "tensor_mul", (t2v, od, csv), {},
             reads=(rr, cs[:, :]), writes=(t2[:, :],), tag=f"{tag}rd")
        B.op("dve", "tensor_mul", (t3v, ev, snv), {},
             reads=(rr, sn[:, :]), writes=(t3[:, :],), tag=f"{tag}re")
        B.op("dve", "tensor_add", (t2[:, :], t2[:, :], t3[:, :]), {},
             reads=(t2[:, :], t3[:, :]), writes=(t2[:, :],), tag=f"{tag}rf")
        B.op("dve", "tensor_copy", (ev, t1v), {},
             reads=(t1[:, :],), writes=(rr,), tag=f"{tag}rg")
        B.op("dve", "tensor_copy", (od, t2v), {},
             reads=(t2[:, :],), writes=(rr,), tag=f"{tag}rh")
        for (out_d, col_groups) in outs_fm:
            nch = len(col_groups)
            for ch in range(nch):
                segs = col_groups[ch]
                p0 = 0
                for (s0, span) in segs:
                    B.transpose(bkb[p0:p0 + span, 0:128], os_[:, s0:s0 + span],
                                C.identb[:, :], tag=f"{tag}tr")
                    p0 += span
                B.op("dve", "tensor_copy", (st[:, 0:128], bkb[:, 0:128]), {},
                     reads=(bkb[:, 0:128],), writes=(st[:, 0:128],), tag=f"{tag}cp")
                B.dma("act", out_d[ch, :, t0:t0 + 128], st[:, 0:128], tag=f"{tag}sf")


# version 11 (final)
# speedup vs baseline: 1.4411x; 1.0229x over previous
import math
from contextlib import ExitStack

import numpy as np

import concourse.bass as bass
import concourse.mybir as mybir
from concourse.bass_utils import run_bass_kernel_spmd

F32 = mybir.dt.float32
BF16 = mybir.dt.bfloat16
AF = mybir.ActivationFunctionType
ALU = mybir.AluOpType
AX = mybir.AxisListType

T = 4096
D = 1024
TB = 512
NTB = T // TB
NFC = D // 128
EPS_LN = 1e-5
NEG = -30000.0


ANNOTATE = False
DMA_Q = {"pool": "q_pool", "sp": "q_sp", "act": "q_act", "dve": "q_dve"}


class Reg:
    __slots__ = ("key", "lo", "hi")

    def __init__(self, key, lo, hi):
        self.key, self.lo, self.hi = key, lo, hi


def _ap_region(ap):
    t = ap.tensor
    name = t.name
    if type(t).__name__.startswith("PSum"):
        return Reg(name, 0, 1 << 60)
    shape = list(t.shape)
    pairs = list(ap.ap)
    off = ap.offset
    if len(shape) == 2 and shape[0] > 1:
        n = shape[1]
        free_span = sum(s * (c - 1) for (s, c) in pairs if s != n)
        col = off % n
        return Reg(name, col, col + free_span + 1)
    span = sum(s * (c - 1) for (s, c) in pairs)
    return Reg(name, off, off + span + 1)


class Op:
    __slots__ = ("idx", "veng", "ieng", "method", "args", "kwargs", "waits",
                 "incs", "seq", "tag")

    def __init__(self, idx, veng, ieng, method, args, kwargs, tag=""):
        self.idx = idx
        self.veng = veng
        self.ieng = ieng
        self.method = method
        self.args = args
        self.kwargs = kwargs
        self.waits = []
        self.incs = 0
        self.seq = 0
        self.tag = tag




def op_wave_target(dep_op):
    return dep_op.kwargs.get("_wave_cum", dep_op.seq + 1)

class Builder:
    def __init__(self, name):
        self.nc = bass.Bass("TRN2", target_bir_lowering=False, debug=False)
        self.name = name
        self.ops = []
        self.per_veng = {}
        self.marks = {}
        self.records = {}
        self.qstate = {}
        self.scopes = []
        self.ctx = ExitStack()
        self._uid = 0

    def dram(self, name, shape, dtype, kind="Internal"):
        return self.nc.dram_tensor(name, list(shape), dtype, kind=kind).ap()

    def sbuf(self, name, shape, dtype):
        tgt = self.scopes[-1] if self.scopes else self.ctx
        return tgt.enter_context(self.nc.sbuf_tensor(name, list(shape), dtype))

    def push_scope(self):
        self.scopes.append(ExitStack())

    def pop_scope(self):
        self.barrier()
        st = self.scopes.pop()
        st.close()

    def psum(self, name, shape, dtype):
        return self.ctx.enter_context(self.nc.psum_tensor(name, list(shape), dtype))

    def uid(self):
        self._uid += 1
        return self._uid

    SAME_ENG_SKIP = False
    WATERMARK = True

    def _dep(self, op, dep_op):
        if dep_op is op:
            return
        if self.SAME_ENG_SKIP:
            if dep_op.veng == dep_op.ieng == op.ieng:
                return
        elif dep_op.ieng == "pe" and op.ieng == "pe":
            return
        veng = dep_op.veng
        if veng.startswith("q_") or veng.startswith("sq_"):
            self._close_wave(veng)
            val = 16 * op_wave_target(dep_op)
            for i, (v, w) in enumerate(op.waits):
                if v == veng:
                    if val > w:
                        op.waits[i] = (veng, val)
                    return
            op.waits.append((veng, val))
            return
        marks = self.marks.setdefault(veng, [])
        pos = None
        for i in range(len(marks) - 1, -1, -1):
            if marks[i] >= dep_op.seq:
                pos = i
            else:
                break
        if pos is None:
            marks.append(dep_op.seq)
            self.per_veng[veng][dep_op.seq].incs += 1
            pos = len(marks) - 1
        val = pos + 1
        for i, (v, w) in enumerate(op.waits):
            if v == veng:
                if val > w:
                    op.waits[i] = (veng, val)
                return
        op.waits.append((veng, val))

    def _access(self, op, regs, is_write):
        for r in regs:
            recs = self.records.setdefault(r.key, [])
            if is_write:
                newrecs = []
                for rec in recs:
                    if rec[0] < r.hi and r.lo < rec[1]:
                        if rec[2] is not None:
                            self._dep(op, rec[2])
                        for rd in rec[3]:
                            self._dep(op, rd)
                        if rec[0] < r.lo or rec[1] > r.hi:
                            newrecs.append(rec)
                    else:
                        newrecs.append(rec)
                newrecs.append([r.lo, r.hi, op, []])
                self.records[r.key] = newrecs
            else:
                for rec in recs:
                    if rec[0] < r.hi and r.lo < rec[1]:
                        if rec[2] is not None:
                            self._dep(op, rec[2])
                        rec[3].append(op)

    def _mk(self, veng, ieng, method, args, kwargs, tag):
        op = Op(len(self.ops), veng, ieng, method, args, kwargs, tag)
        lst = self.per_veng.setdefault(veng, [])
        op.seq = len(lst)
        lst.append(op)
        if veng != ieng:
            pass
        self.ops.append(op)
        return op

    def barrier(self):
        for q in list(self.qstate):
            st = self.qstate[q]
            sem_i = st["wave"] % self.R_WAVES
            self._close_wave(f"{q}#{sem_i}")
        qwaits = []
        for q, st in self.qstate.items():
            for i in range(self.R_WAVES):
                if st["cum"][i] > 0:
                    qwaits.append((f"{q}#{i}", 16 * st["cum"][i]))
        vals = {}
        for e in ("pe", "act", "dve", "pool", "sp"):
            lst = self.per_veng.get(e, [])
            last = None
            for o in reversed(lst):
                if o.method is not None:
                    last = o
                    break
            if last is not None:
                marks = self.marks.setdefault(e, [])
                if not marks or marks[-1] < last.seq:
                    marks.append(last.seq)
                    last.incs = 1
                vals[e] = len(marks)
        spop = self._mk("sp", "sp", "nop", (), {}, "barrier")
        for e, v in vals.items():
            if e != "sp":
                spop.waits.append((e, v))
        spop.waits.extend(qwaits)
        marks_sp = self.marks.setdefault("sp", [])
        marks_sp.append(spop.seq)
        spop.incs = 1
        spv = len(marks_sp)
        for e in ("pe", "act", "dve", "pool"):
            w = self._mk(e, e, None, (), {}, "barwait")
            w.waits.append(("sp", spv))

    def op(self, eng, method, args, kwargs=None, reads=(), writes=(), tag=""):
        op = self._mk(eng, eng, method, args, dict(kwargs or {}), tag)
        self._access(op, [r if isinstance(r, Reg) else _ap_region(r) for r in reads], False)
        self._access(op, [r if isinstance(r, Reg) else _ap_region(r) for r in writes], True)
        return op

    R_WAVES = 4

    def dma(self, eng, out_ap, in_ap, tag="", **kw):
        small = False
        for a in (out_ap, in_ap):
            tn = type(a.tensor).__name__
            if tn.startswith("SB") and a.tensor.shape[0] < 128:
                small = True
        kw["out"] = out_ap
        kw["in_"] = in_ap
        q = ("s" + DMA_Q[eng]) if small else DMA_Q[eng]
        st = self.qstate.setdefault(q, {"wave": 0, "open": 0, "cum": [0] * self.R_WAVES})
        sem_i = st["wave"] % self.R_WAVES
        veng = f"{q}#{sem_i}"
        if st["open"] == 0 and st["cum"][sem_i] > 0:
            gate = Op(len(self.ops), eng, eng, None, (), {}, "gate")
            glst = self.per_veng.setdefault(eng, [])
            gate.seq = len(glst)
            glst.append(gate)
            self.ops.append(gate)
            gate.waits.append((veng, 16 * st["cum"][sem_i]))
        op = self._mk(veng, eng, "dma_start", (), kw, tag)
        op.incs = 1
        st["open"] += 1
        st.setdefault("open_ops", []).append(op)
        op.kwargs["_wave_cum"] = st["cum"][sem_i] + st["open"]
        self._access(op, [_ap_region(in_ap)], False)
        self._access(op, [_ap_region(out_ap)], True)
        return op

    def _close_wave(self, veng):
        q, _, si = veng.partition("#")
        st = self.qstate.get(q)
        if st is None:
            return
        sem_i = st["wave"] % self.R_WAVES
        if f"{q}#{sem_i}" == veng and st["open"] > 0:
            st["cum"][sem_i] += st["open"]
            final = st["cum"][sem_i]
            for o in st.get("open_ops", []):
                o.kwargs["_wave_cum"] = final
            st["open_ops"] = []
            st["open"] = 0
            st["wave"] += 1

    def mm(self, out_ap, lhsT_ap, rhs_ap, start, stop, tile_position=None, tag=""):
        kw = dict(start=start, stop=stop)
        if tile_position is not None:
            kw["tile_position"] = tile_position
        return self.op("pe", "matmul", (out_ap, lhsT_ap, rhs_ap), kw,
                       reads=(lhsT_ap, rhs_ap), writes=(out_ap,), tag=tag)

    def transpose(self, out_ap, in_ap, ident_ap, tag=""):
        return self.op("pe", "transpose", (out_ap, in_ap, ident_ap), {},
                       reads=(in_ap, ident_ap), writes=(out_ap,), tag=tag)

    def act(self, out_ap, in_ap, func, bias=0.0, scale=1.0, tag=""):
        kw = dict(bias=bias, scale=scale)
        reads = [in_ap] + [b for b in (bias, scale) if hasattr(b, "tensor")]
        return self.op("act", "activation", (out_ap, in_ap, func), kw,
                       reads=reads, writes=(out_ap,), tag=tag)

    def finalize(self, final_writes=()):
        end_deps = []
        for ap in final_writes:
            r = ap if isinstance(ap, Reg) else _ap_region(ap)
            for rec in self.records.get(r.key, []):
                if rec[2] is not None:
                    end_deps.append(rec[2])
        term = self._mk("sp", "sp", None, (), {}, "terminal")
        for d in end_deps:
            self._dep(term, d)

        nc = self.nc
        with self.ctx:
            sems = {}
            for veng in self.per_veng:
                sems[veng] = self.ctx.enter_context(nc.semaphore("sem_" + veng))
            block = self.ctx.enter_context(nc.Block())

            def run_engine(eng_obj, eng_name):
                watermark = {}
                for op in self.ops:
                    if op.ieng != eng_name:
                        continue
                    for (veng, val) in op.waits:
                        if self.WATERMARK and watermark.get(veng, 0) >= val:
                            continue
                        watermark[veng] = val
                        eng_obj.wait_ge(sems[veng], val)
                    if op.method is None:
                        continue
                    kwargs = {k: v for k, v in op.kwargs.items() if not k.startswith("_")}
                    inst = getattr(eng_obj, op.method)(*op.args, **kwargs)
                    if op.tag and ANNOTATE:
                        try:
                            inst.annotate(op.tag)
                        except Exception:
                            pass
                    if op.incs:
                        amt = 16 if ("q_" in op.veng) else 1
                        inst.then_inc(sems[op.veng], amt)

            @block.tensor
            def _(t):
                run_engine(t, "pe")

            @block.scalar
            def _(s):
                run_engine(s, "act")

            @block.vector
            def _(v):
                run_engine(v, "dve")

            @block.gpsimd
            def _(g):
                run_engine(g, "pool")

            @block.sync
            def _(sy):
                run_engine(sy, "sp")

        return nc



class ConstReg:

    def __init__(self):
        self.pvt_cols = []
        self.pvt_index = {}
        self.row_vals = []
        self.row_index = {}

    def pvec(self, name, vec):
        v = np.asarray(vec, np.float32).reshape(-1)
        assert v.size % 128 == 0
        if name in self.pvt_index:
            return self.pvt_index[name]
        start = len(self.pvt_cols)
        for i in range(v.size // 128):
            self.pvt_cols.append(v[i * 128:(i + 1) * 128])
        self.pvt_index[name] = start
        return start

    def rowvec(self, name, vec):
        v = np.asarray(vec, np.float32).reshape(-1)
        if name in self.row_index:
            return self.row_index[name]
        start = len(self.row_vals)
        self.row_vals.extend(v.tolist())
        self.row_index[name] = start
        return start

    def pvt_array(self):
        if not self.pvt_cols:
            return np.zeros((128, 1), np.float32)
        return np.ascontiguousarray(np.stack(self.pvt_cols, axis=1).astype(np.float32))

    def rows_array(self):
        if not self.row_vals:
            return np.zeros((1, 4), np.float32)
        return np.asarray(self.row_vals, np.float32).reshape(1, -1)


def make_trimask_T():
    kt = np.arange(128)[:, None]
    qt = np.arange(128)[None, :]
    b = np.where(kt <= qt, 0.0, NEG).astype(np.float32)
    return np.ascontiguousarray(b.T)


def fm_layout(x):
    return np.ascontiguousarray(np.asarray(x).T.reshape(NFC, 128, -1))


def fm_unlayout(xfm):
    return np.ascontiguousarray(np.asarray(xfm).reshape(D, -1).T)



class Ctx:
    def __init__(self, B, n_pvt, n_rows, cr=None):
        self.B = B
        self.cr = cr
        self.identf_d = B.dram("c_identf", [128, 128], F32, kind="ExternalInput")
        self.pvt_d = B.dram("c_pvt", [128, max(n_pvt, 1)], F32, kind="ExternalInput")
        self.rows_d = B.dram("c_rows", [1, max(n_rows, 4)], F32, kind="ExternalInput")
        self.trimask_d = B.dram("c_trimask", [128, 128], F32, kind="ExternalInput")
        self.esel_d = B.dram("c_esel", [8, 8 * 128], F32, kind="ExternalInput")
        self.identf = B.sbuf("s_identf", [128, 128], F32)
        self.identb = B.sbuf("s_identb", [128, 128], BF16)
        self.pvt = B.sbuf("s_pvt", [128, max(n_pvt, 1)], F32)
        self.trimask = B.sbuf("s_trimask", [128, 128], BF16)
        self.onesrowf = B.sbuf("s_onesrowf", [1, 128], F32)
        self.onestb = B.sbuf("s_onestb", [1, TB], F32)
        B.dma("pool", self.identf[:, :], self.identf_d[:, :], tag="ldc")
        B.dma("pool", self.identb[:, :], self.identf_d[:, :], tag="ldc")
        B.dma("pool", self.pvt[:, :], self.pvt_d[:, :], tag="ldc")
        B.dma("pool", self.trimask[:, :], self.trimask_d[:, :], tag="ldc")
        B.op("pool", "memset", (self.onesrowf[:, :], 1.0), {}, writes=(self.onesrowf[:, :],))
        B.op("pool", "memset", (self.onestb[:, :], 1.0), {}, writes=(self.onestb[:, :],))
        self.bank = [B.psum(f"ps{i}", [128, 512], F32) for i in range(7)]
        self.bankb = B.psum("ps7b", [128, 1024], BF16)

    def pv(self, col, n=1):
        return self.pvt[:, col:col + n]

    def eps(self):
        return self.pv(self.cr.pvt_index["eps_ln"])





def ln_full(B, C, x_d, out_d, tag, g_rowcol=None, b_rowcol=None, rms=False,
            src_dtype=F32):
    u = B.uid()
    xs = B.sbuf(f"{tag}_x{u}", [128, NFC * TB], src_dtype)
    sq = B.sbuf(f"{tag}_sq{u}", [128, TB], BF16)
    mrow = B.sbuf(f"{tag}_m{u}", [1, TB], F32)
    qrow = B.sbuf(f"{tag}_q{u}", [1, TB], F32)
    rrow = B.sbuf(f"{tag}_r{u}", [1, TB], F32)
    mrrow = B.sbuf(f"{tag}_mr{u}", [1, TB], F32)
    trow = B.sbuf(f"{tag}_tr{u}", [1, TB], F32)
    outs = B.sbuf(f"{tag}_o{u}", [128, NFC * TB], BF16)
    onesc = B.sbuf(f"{tag}_1c{u}", [128, 1], src_dtype)
    onescf = B.sbuf(f"{tag}_1f{u}", [128, 1], BF16)
    grows = None
    brows = None
    if g_rowcol is not None:
        grows = B.sbuf(f"{tag}_gr{u}", [1, D], F32)
        B.dma("sp", grows[0:1, :], C.rows_d[0:1, g_rowcol:g_rowcol + D], tag=f"{tag}lg")
    if b_rowcol is not None:
        brows = B.sbuf(f"{tag}_br{u}", [1, D], F32)
        B.dma("sp", brows[0:1, :], C.rows_d[0:1, b_rowcol:b_rowcol + D], tag=f"{tag}lb")
    tmp = B.sbuf(f"{tag}_t{u}", [128, TB], F32)
    B.op("pool", "memset", (onesc[:, :], 1.0), {}, writes=(onesc[:, :],))
    B.op("pool", "memset", (onescf[:, :], 1.0), {}, writes=(onescf[:, :],))
    bank = C.bank
    inv_d = 1.0 / D
    for tb in range(NTB):
        t0 = tb * TB
        for fc in range(NFC):
            B.dma("sp", xs[:, fc * TB:(fc + 1) * TB], x_d[fc, :, t0:t0 + TB], tag=f"{tag}ldx")
        ps1 = bank[0][0:1, 0:TB]
        ps2 = bank[1][0:1, 0:TB]
        for fc in range(NFC):
            xc = xs[:, fc * TB:(fc + 1) * TB]
            if not rms:
                B.mm(ps1, onesc[:, 0:1], xc, start=(fc == 0), stop=(fc == NFC - 1), tag=f"{tag}s1")
            B.act(sq[:, 0:TB], xc, AF.Square, tag=f"{tag}sq")
            B.mm(ps2, onescf[:, 0:1], sq[:, 0:TB], start=(fc == 0), stop=(fc == NFC - 1), tag=f"{tag}s2")
        B.op("dve", "tensor_scalar", (qrow[0:1, :], ps2, inv_d, None, ALU.mult), {},
             reads=(ps2,), writes=(qrow[0:1, :],), tag=f"{tag}rq")
        if not rms:
            B.op("dve", "tensor_scalar", (mrow[0:1, :], ps1, inv_d, None, ALU.mult), {},
                 reads=(ps1,), writes=(mrow[0:1, :],), tag=f"{tag}rm")
            B.op("dve", "scalar_tensor_tensor",
                 (trow[0:1, :], mrow[0:1, :], 1.0, mrow[0:1, :], ALU.mult, ALU.mult), {},
                 reads=(mrow[0:1, :],), writes=(trow[0:1, :],), tag=f"{tag}m2")
            B.op("dve", "tensor_sub", (qrow[0:1, :], qrow[0:1, :], trow[0:1, :]), {},
                 reads=(qrow[0:1, :], trow[0:1, :]), writes=(qrow[0:1, :],), tag=f"{tag}var")
        B.act(rrow[0:1, :], qrow[0:1, :], AF.Ln, bias=C.eps()[0:1, 0:1], tag=f"{tag}ln")
        B.act(rrow[0:1, :], rrow[0:1, :], AF.Exp, scale=-0.5, tag=f"{tag}exp")
        if not rms:
            B.op("dve", "scalar_tensor_tensor",
                 (mrrow[0:1, :], mrow[0:1, :], -1.0, rrow[0:1, :], ALU.mult, ALU.mult), {},
                 reads=(mrow[0:1, :], rrow[0:1, :]), writes=(mrrow[0:1, :],), tag=f"{tag}mr")
        for fc in range(NFC):
            xc = xs[:, fc * TB:(fc + 1) * TB]
            oc = outs[:, fc * TB:(fc + 1) * TB]
            pg = bank[2 + (fc % 2) * 2][:, 0:TB]
            pb = bank[3 + (fc % 2) * 2][:, 0:TB]
            grow = C.onesrowf[0:1, :] if g_rowcol is None else grows[0:1, fc * 128:(fc + 1) * 128]
            B.mm(pg, grow, rrow[0:1, :], start=True, stop=True, tag=f"{tag}bg")
            use_pb = (not rms) or (b_rowcol is not None)
            if use_pb:
                first = True
                if not rms:
                    B.mm(pb, grow, mrrow[0:1, :], start=True, stop=(b_rowcol is None), tag=f"{tag}bb")
                    first = False
                if b_rowcol is not None:
                    B.mm(pb, brows[0:1, fc * 128:(fc + 1) * 128], C.onestb[0:1, 0:TB],
                         start=first, stop=True, tag=f"{tag}bb2")
            B.op("dve", "tensor_mul", (tmp[:, 0:TB], xc, pg), {},
                 reads=(xc, pg), writes=(tmp[:, 0:TB],), tag=f"{tag}ap1")
            if use_pb:
                B.op("dve", "tensor_add", (oc, tmp[:, 0:TB], pb), {},
                     reads=(tmp[:, 0:TB], pb), writes=(oc,), tag=f"{tag}ap2")
            else:
                B.op("dve", "tensor_copy", (oc, tmp[:, 0:TB]), {},
                     reads=(tmp[:, 0:TB],), writes=(oc,), tag=f"{tag}ap2")
        if out_d is not None:
            for fc in range(NFC):
                B.dma("act", out_d[fc, :, t0:t0 + TB], outs[:, fc * TB:(fc + 1) * TB],
                      tag=f"{tag}st")



def proj_fm(B, C, h_d, w_d, out_d, dout, tag, bias_pvcol=None, scale=None,
            oc_group=4, col_off=0):
    u = B.uid()
    noc = dout // 128
    hs = B.sbuf(f"{tag}_h{u}", [128, 2 * NFC * TB], BF16)
    og = min(oc_group, noc)
    ws = B.sbuf(f"{tag}_w{u}", [128, 2 * NFC * og * 128], BF16)
    os_ = B.sbuf(f"{tag}_o{u}", [128, og * TB], BF16)
    bank = C.bank
    w_view = w_d.rearrange("(kc kp) n -> kc kp n", kp=128)
    ngrp = (noc + og - 1) // og
    for g in range(ngrp):
        ocs = list(range(g * og, min((g + 1) * og, noc)))
        wbuf = (g % 2) * NFC * og * 128
        for kc in range(NFC):
            B.dma("pool",
                  ws[:, wbuf + kc * og * 128: wbuf + kc * og * 128 + len(ocs) * 128],
                  w_view[kc, :, col_off + ocs[0] * 128: col_off + ocs[0] * 128 + len(ocs) * 128],
                  tag=f"{tag}ldw")
        for tb in range(NTB):
            t0 = tb * TB
            hb = (tb % 2) * NFC * TB
            for kc in range(NFC):
                B.dma("sp", hs[:, hb + kc * TB: hb + (kc + 1) * TB], h_d[kc, :, t0:t0 + TB],
                      tag=f"{tag}ldh")
            for j, oc in enumerate(ocs):
                pb = bank[j % 4][:, 0:TB]
                for kc in range(NFC):
                    B.mm(pb, ws[:, wbuf + kc * og * 128 + j * 128: wbuf + kc * og * 128 + (j + 1) * 128],
                         hs[:, hb + kc * TB: hb + (kc + 1) * TB],
                         start=(kc == 0), stop=(kc == NFC - 1), tag=f"{tag}mm")
                oc_sb = os_[:, j * TB:(j + 1) * TB]
                if bias_pvcol is not None and scale is not None:
                    B.op("dve", "tensor_scalar",
                         (oc_sb, pb, C.pv(bias_pvcol + oc), scale, ALU.add, ALU.mult), {},
                         reads=(pb, C.pv(bias_pvcol + oc)), writes=(oc_sb,), tag=f"{tag}ev")
                elif bias_pvcol is not None:
                    B.op("dve", "tensor_scalar",
                         (oc_sb, pb, C.pv(bias_pvcol + oc), None, ALU.add), {},
                         reads=(pb, C.pv(bias_pvcol + oc)), writes=(oc_sb,), tag=f"{tag}ev")
                elif scale is not None:
                    B.op("dve", "tensor_scalar", (oc_sb, pb, scale, None, ALU.mult), {},
                         reads=(pb,), writes=(oc_sb,), tag=f"{tag}ev")
                else:
                    B.op("dve", "tensor_copy", (oc_sb, pb), {},
                         reads=(pb,), writes=(oc_sb,), tag=f"{tag}ev")
                B.dma("act", out_d[oc, :, t0:t0 + TB], oc_sb, tag=f"{tag}st")



def proj_tm_v(B, C, h_d, w_d, out_d, dout, nheads, dv, tag, vbias_d=None,
               scale=None, col_off=0):
    u = B.uid()
    ws = B.sbuf(f"{tag}_w{u}", [128, NFC * dout], BF16)
    hs = B.sbuf(f"{tag}_h{u}", [128, NFC * 128], BF16)
    width = nheads * (dv + 1)
    os_ = B.sbuf(f"{tag}_o{u}", [128, width], BF16)
    vb = None
    if vbias_d is not None:
        vb = B.sbuf(f"{tag}_b{u}", [128, dout], F32)
        B.dma("pool", vb[:, :], vbias_d[:, :], tag=f"{tag}ldb")
    w_view = w_d.rearrange("(kc kp) n -> kc kp n", kp=128)
    for kc in range(NFC):
        B.dma("pool", ws[:, kc * dout:(kc + 1) * dout], w_view[kc, :, col_off:col_off + dout], tag=f"{tag}ldw")
    bank = C.bank
    nnc = (dout + 511) // 512
    for tt in range(T // 128):
        t0 = tt * 128
        for kc in range(NFC):
            B.dma("sp", hs[:, kc * 128:(kc + 1) * 128], h_d[kc, :, t0:t0 + 128],
                  tag=f"{tag}ldh")
        for nc_ in range(nnc):
            n0 = nc_ * 512
            nw = min(512, dout - n0)
            pb = bank[nc_ % 4][:, 0:nw]
            for kc in range(NFC):
                B.mm(pb, hs[:, kc * 128:(kc + 1) * 128], ws[:, kc * dout + n0: kc * dout + n0 + nw],
                     start=(kc == 0), stop=(kc == NFC - 1), tag=f"{tag}mm")
            h0 = n0 // dv
            nh = nw // dv
            for hh in range(nh):
                if dv > 64:
                    base = (h0 + hh) * (dv + 1)
                    for (po, do, w_) in ((0, 0, 64), (64, 65, 64)):
                        dsth = os_[:, base + do: base + do + w_]
                        srch = pb[:, hh * dv + po: hh * dv + po + w_]
                        if vb is not None:
                            B.op("dve", "tensor_add", (dsth, srch,
                                 vb[:, n0 + hh * dv + po: n0 + hh * dv + po + w_]), {},
                                 reads=(srch, vb[:, n0 + hh * dv + po: n0 + hh * dv + po + w_]),
                                 writes=(dsth,), tag=f"{tag}ev")
                        elif scale is not None:
                            B.op("dve", "tensor_scalar", (dsth, srch, scale, None, ALU.mult),
                                 {}, reads=(srch,), writes=(dsth,), tag=f"{tag}ev")
                        else:
                            B.op("dve", "tensor_copy", (dsth, srch), {},
                                 reads=(srch,), writes=(dsth,), tag=f"{tag}ev")
                    continue
                dst = os_[:, (h0 + hh) * (dv + 1): (h0 + hh) * (dv + 1) + dv]
                src = pb[:, hh * dv: (hh + 1) * dv]
                if vb is not None:
                    B.op("dve", "tensor_add", (dst, src, vb[:, n0 + hh * dv: n0 + (hh + 1) * dv]), {},
                         reads=(src, vb[:, n0 + hh * dv: n0 + (hh + 1) * dv]), writes=(dst,),
                         tag=f"{tag}ev")
                elif scale is not None:
                    B.op("dve", "tensor_scalar", (dst, src, scale, None, ALU.mult), {},
                         reads=(src,), writes=(dst,), tag=f"{tag}ev")
                else:
                    B.op("dve", "tensor_copy", (dst, src), {},
                         reads=(src,), writes=(dst,), tag=f"{tag}ev")
        one_col = 64 if dv > 64 else dv
        ones_view = os_.rearrange("p (h e) -> p h e", e=dv + 1)[:, :, one_col:one_col + 1]
        B.op("dve", "memset", (ones_view, 1.0), {}, writes=(os_[:, 0:width],), tag=f"{tag}on")
        B.dma("act", out_d[t0:t0 + 128, 0:width], os_[:, 0:width], tag=f"{tag}st")



FM = lambda: ([NFC, 128, T], F32)
FMB = lambda: ([NFC, 128, T], BF16)

TSPEC = {
    "x0": FM(),
    "u_l0": FMB(),
    "Q1": FMB(), "K1": FMB(),
    "V1": ([T, 16 * 65], BF16),
    "O1": ([8, 128, T], BF16),
    "x1": FM(),
    "u1": FMB(), "h1": FMB(),
    "Q2": FMB(), "K2": FMB(),
    "V2": ([T, 16 * 65], BF16),
    "O2": ([8, 128, T], BF16),
    "xr": FMB(), "h2": FMB(),
    "G1": ([T, 8], F32),
    "x2": FM(),
    "u2": FMB(),
    "q3t": ([T, D], BF16), "k3t": ([T, D], BF16),
    "Q3a": ([4, 128, T], BF16), "Q3b": ([4, 128, T], BF16),
    "K3a": ([4, 128, T], BF16), "K3b": ([4, 128, T], BF16),
    "V3": ([T, 8 * 129], BF16),
    "O3a": ([8, 128, T], BF16), "O3b": ([8, 128, T], BF16),
    "Y3": ([8, 128, T], BF16),
    "x3": FM(),
    "u3": FMB(), "h3": FMB(),
    "q4t": ([T, D], BF16), "ckvt": ([T, 160], BF16),
    "Q4": ([8, 128, T], BF16), "K4": ([8, 128, T], BF16),
    "V4": ([T, 16 * 33], BF16),
    "O4": ([4, 128, T], BF16),
    "xi": FMB(), "h4": FMB(),
    "G2": ([T, 8], F32),
    "x4": FM(),
}

WSPEC = {
    "L0_attn_W_qkv": ([D, 3 * D], F32),
    "L0_attn_W_out": ([D, D], F32),
    "L0_ffn_w_qkv": ([D, 3 * D], F32),
    "L0_ffn_w_o": ([D, D], F32),
    "L0_ffn_w_gate": ([D, 8], F32),
    "L0_ffn_w1": ([8, D, 4096], F32),
    "L0_ffn_w2": ([8, 4096, D], F32),
    "L1_attn_WqT": ([D, D], F32),
    "L1_attn_WkT": ([D, D], F32),
    "L1_attn_WvT": ([D, D], F32),
    "L1_attn_WoT": ([D, D], F32),
    "L1_ffn_Wq": ([D, D], F32),
    "L1_ffn_Wkva": ([D, 160], F32),
    "L1_ffn_Wkvb": ([128, 512], F32),
    "L1f_Wo_sub": ([512, D], F32),
    "L1_ffn_Wgate": ([D, 8], F32),
    "L1_ffn_Wg": ([8, D, 512], F32),
    "L1_ffn_Wu": ([8, D, 512], F32),
    "L1_ffn_Wd": ([8, 512, D], F32),
    "L1_ffn_sWg": ([1, D, 512], F32),
    "L1_ffn_sWu": ([1, D, 512], F32),
    "L1_ffn_sWd": ([1, 512, D], F32),
}

EXTRA_IN = {
    "vbias_l0": ([128, D], F32),
    "rope64_cos": ([T, 32], F32), "rope64_sin": ([T, 32], F32),
    "rope32_cos": ([T, 16], F32), "rope32_sin": ([T, 16], F32),
    "rope64w_cos": ([T, 512], F32), "rope64w_sin": ([T, 512], F32),
    "rope32w_cos": ([T, 256], F32), "rope32w_sin": ([T, 256], F32),
    "lnw_col": ([128, 1], F32), "lnb_col": ([128, 1], F32),
    "lam_cat": ([1, 256], F32),
    "ebias_bc": ([128, 8], F32),
}


def register_consts(cr: "ConstReg", inp=None):
    z = lambda n: np.zeros(n, np.float32)
    g = (lambda k: np.asarray(inp[k], np.float32)) if inp is not None else (lambda k: None)
    def pv(name, n, val):
        cr.pvec(name, z(n * 128) if val is None else val)
    def rv(name, n, val):
        cr.rowvec(name, z(n) if val is None else val)
    pv("eps_ln", 1, np.full(128, EPS_LN, np.float32))
    bqkv = g("L0_attn_b_qkv")
    pv("l0a_bq", 8, None if bqkv is None else bqkv[0:1024])
    pv("l0a_bk", 8, None if bqkv is None else bqkv[1024:2048])
    bout = g("L0_attn_b_out")
    pv("l0a_bout", 8, bout)
    rv("l0f_g1", 1024, g("L0_ffn_ln1_g"))
    rv("l0f_b1", 1024, g("L0_ffn_ln1_b"))
    rv("l0f_g2", 1024, g("L0_ffn_ln2_g"))
    rv("l0f_b2", 1024, g("L0_ffn_ln2_b"))
    b1 = g("L0_ffn_b1")
    b2 = g("L0_ffn_b2")
    for e in range(8):
        pv(f"l0f_b1_{e}", 32, None if b1 is None else b1[e])
        pv(f"l0f_b2_{e}", 8, None if b2 is None else b2[e])
    pv("eps_a8", 1, np.full(128, 1e-8, np.float32))
    rv("l1f_g1", 1024, g("L1_ffn_w_ln1"))
    rv("l1f_g2", 1024, g("L1_ffn_w_ln2"))
    return cr



def phase_l0attn_qkv(B, C, Tn, cr):
    ln_full(B, C, Tn["x0"], Tn["u_l0"], "p1ln")
    proj_fm(B, C, Tn["u_l0"], Tn["L0_attn_W_qkv"], Tn["Q1"], D, "p1q",
            bias_pvcol=cr.pvt_index["l0a_bq"], scale=0.125, col_off=0)
    proj_fm(B, C, Tn["u_l0"], Tn["L0_attn_W_qkv"], Tn["K1"], D, "p1k",
            bias_pvcol=cr.pvt_index["l0a_bk"], col_off=D)
    proj_tm_v(B, C, Tn["u_l0"], Tn["L0_attn_W_qkv"], Tn["V1"], D, 16, 64, "p1v",
              vbias_d=Tn["vbias_l0"], col_off=2 * D)


PHASES = []



def build_phase_program(idx):
    name, needs, makes, wneeds, extra, fn = PHASES[idx]
    cr = register_consts(ConstReg())
    B = Builder(name)
    Tn = {}
    for t in needs:
        shape, dt = TSPEC[t]
        Tn[t] = B.dram(t, shape, dt, kind="ExternalInput")
    for t in makes:
        shape, dt = TSPEC[t]
        Tn[t] = B.dram(t, shape, dt, kind="ExternalOutput")
    for t in wneeds:
        shape, dt = WSPEC[t]
        Tn[t] = B.dram(t, shape, dt, kind="ExternalInput")
    for t in extra:
        shape, dt = EXTRA_IN[t]
        Tn[t] = B.dram(t, shape, dt, kind="ExternalInput")
    C = Ctx(B, len(cr.pvt_cols), len(cr.row_vals), cr)
    fn(B, C, Tn, cr)
    outs = [Tn[t] for t in makes]
    nc = B.finalize(final_writes=outs)
    in_names = needs + wneeds + extra + ["c_identf", "c_pvt", "c_rows", "c_trimask", "c_esel"]
    return nc, in_names, makes


def host_all_arrays(inputs, seq):
    cr = register_consts(ConstReg(), inputs)
    arrs = {
        "x0": fm_layout(np.asarray(inputs["L0_attn_x"][seq], np.float32)),
        "c_identf": np.eye(128, dtype=np.float32),
        "c_pvt": cr.pvt_array(),
        "c_rows": cr.rows_array(),
        "c_trimask": make_trimask_T(),
        "vbias_l0": np.ascontiguousarray(
            np.broadcast_to(np.asarray(inputs["L0_attn_b_qkv"][2048:3072], np.float32)[None, :],
                            (128, 1024))),
    }
    pos = np.arange(T, dtype=np.float64)
    f64 = 1.0 / (10000.0 ** (np.arange(32, dtype=np.float64) / 32.0))
    ang = pos[:, None] * f64[None, :]
    arrs["rope64_cos"] = np.cos(ang).astype(np.float32)
    arrs["rope64_sin"] = np.sin(ang).astype(np.float32)
    f32_ = 1.0 / (100000.0 ** (np.arange(16, dtype=np.float64) / 16.0))
    ang2 = pos[:, None] * f32_[None, :]
    arrs["rope32_cos"] = np.cos(ang2).astype(np.float32)
    arrs["rope32_sin"] = np.sin(ang2).astype(np.float32)
    arrs["rope64w_cos"] = np.ascontiguousarray(np.tile(arrs["rope64_cos"], (1, 16)))
    arrs["rope64w_sin"] = np.ascontiguousarray(np.tile(arrs["rope64_sin"], (1, 16)))
    arrs["rope32w_cos"] = np.ascontiguousarray(np.tile(arrs["rope32_cos"], (1, 16)))
    arrs["rope32w_sin"] = np.ascontiguousarray(np.tile(arrs["rope32_sin"], (1, 16)))
    arrs["lnw_col"] = np.asarray(inputs["L1_attn_ln_w"], np.float32).reshape(128, 1).copy()
    arrs["lnb_col"] = np.asarray(inputs["L1_attn_ln_b"], np.float32).reshape(128, 1).copy()
    arrs["lam_cat"] = np.concatenate([
        np.asarray(inputs["L1_attn_lambda_q1"], np.float32),
        np.asarray(inputs["L1_attn_lambda_k1"], np.float32),
        np.asarray(inputs["L1_attn_lambda_q2"], np.float32),
        np.asarray(inputs["L1_attn_lambda_k2"], np.float32)]).reshape(1, 256)
    arrs["ebias_bc"] = np.ascontiguousarray(np.broadcast_to(
        np.asarray(inputs["L1_ffn_expert_bias"], np.float32)[None, :], (128, 8)))
    esel = np.zeros((8, 8 * 128), np.float32)
    for e in range(8):
        esel[e, e * 128:(e + 1) * 128] = 1.0
    arrs["c_esel"] = esel
    for w in WSPEC:
        if w.endswith("T") and w.startswith("L1_attn"):
            base = w[:-1]
            arrs[w] = np.ascontiguousarray(np.asarray(inputs[base], np.float32).T)
        elif w == "L1f_Wo_sub":
            Wo = np.asarray(inputs["L1_ffn_Wo"], np.float32)
            rows = np.concatenate([np.arange(64 * h, 64 * h + 32) for h in range(16)])
            arrs[w] = np.ascontiguousarray(Wo[rows, :])
        else:
            arrs[w] = np.ascontiguousarray(np.asarray(inputs[w], np.float32))
    return arrs


def run_phases(inputs, upto=None, state=None):
    if state is None:
        state = [host_all_arrays(inputs, b) for b in range(4)]
    n = len(PHASES) if upto is None else upto
    for i in range(n):
        if state[0].get("_done_" + PHASES[i][0]) is not None:
            continue
        import sys as _sys
        print(f"[phase {i}: {PHASES[i][0]}]", flush=True)
        nc, in_names, out_names = build_phase_program(i)
        in_maps = []
        for c in range(8):
            st = state[c // 2]
            in_maps.append({k: st[k] for k in in_names})
        res = run_bass_kernel_spmd(nc, in_maps, core_ids=list(range(8)))
        for b in range(4):
            for k in out_names:
                state[b][k] = np.asarray(res.results[2 * b][k])
            state[b]["_done_" + PHASES[i][0]] = True
    return state


FUSED = True
ANNOTATE = False


def build_fused():
    cr = register_consts(ConstReg())
    B = Builder("fused")
    Tn = {}
    ext_ins = []
    sx, dx = TSPEC["x0"]
    Tn["x0"] = B.dram("x0", sx, dx, kind="ExternalInput")
    ext_ins.append("x0")
    s4, d4 = TSPEC["x4"]
    Tn["x4"] = B.dram("x4", s4, d4, kind="ExternalOutput")
    for t, (s, d) in TSPEC.items():
        if t in ("x0", "x4"):
            continue
        Tn[t] = B.dram(t, s, d, kind="Internal")
    for t, (s, d) in WSPEC.items():
        Tn[t] = B.dram(t, s, d, kind="ExternalInput")
        ext_ins.append(t)
    for t, (s, d) in EXTRA_IN.items():
        Tn[t] = B.dram(t, s, d, kind="ExternalInput")
        ext_ins.append(t)
    C = Ctx(B, len(cr.pvt_cols), len(cr.row_vals), cr)
    for (name, needs, makes, wneeds, extra, fn) in PHASES:
        B.push_scope()
        fn(B, C, Tn, cr)
        B.pop_scope()
    nc = B.finalize(final_writes=[Tn["x4"]])
    ext_ins += ["c_identf", "c_pvt", "c_rows", "c_trimask", "c_esel"]
    return nc, ext_ins


def kernel(**inputs):
    if not FUSED:
        state = run_phases(inputs)
        out = np.stack([fm_unlayout(np.asarray(state[b]["x4"], np.float32))
                        for b in range(4)])
        return out.reshape(4, T, D)
    nc, ext_ins = build_fused()
    arrs = [host_all_arrays(inputs, b) for b in range(4)]
    in_maps = [{k: arrs[c // 2][k] for k in ext_ins} for c in range(8)]
    res = run_bass_kernel_spmd(nc, in_maps, core_ids=list(range(8)))
    out = np.stack([fm_unlayout(np.asarray(res.results[2 * b]["x4"], np.float32))
                    for b in range(4)])
    return out.reshape(4, T, D)



def attention(B, C, Q_d, K_d, V_d, O_d, nheads, dv, tag, causal=True,
              margin=0.5):
    u = B.uid()
    npair = nheads // 2
    vstride = dv + 1
    big_v = dv > 64
    ks = B.sbuf(f"{tag}_k{u}", [128, npair * T], BF16)
    vs = B.sbuf(f"{tag}_v{u}", [128, (T // 128) * nheads * vstride], BF16)
    qs = B.sbuf(f"{tag}_q{u}", [128, TB], BF16)
    qb_sq = B.sbuf(f"{tag}_sq{u}", [128, TB], BF16)
    pt = B.sbuf(f"{tag}_pt{u}", [128, 4 * TB], BF16)
    maxr = B.sbuf(f"{tag}_mx{u}", [1, 2 * nheads + 4], F32)
    tmr = B.sbuf(f"{tag}_tm{u}", [1, 4], F32)
    negc = B.sbuf(f"{tag}_nc{u}", [128, nheads], F32)
    ones64 = B.sbuf(f"{tag}_o64{u}", [128, 1], BF16)
    bcf = B.sbuf(f"{tag}_bc{u}", [128, TB], F32)
    oraw = B.sbuf(f"{tag}_or{u}", [128, TB], BF16)
    oraw2 = B.sbuf(f"{tag}_o2{u}", [64, TB], BF16)
    ost = B.sbuf(f"{tag}_os{u}", [128, TB], BF16)
    ost2 = B.sbuf(f"{tag}_ot{u}", [64, TB], BF16)
    onesq = B.sbuf(f"{tag}_oq{u}", [128, 128], BF16)
    B.op("pool", "memset", (onesq[:, :], 1.0), {}, writes=(onesq[:, :],))
    B.op("pool", "memset", (ones64[:, :], 1.0), {}, writes=(ones64[:, :],))
    B.op("pool", "memset", (maxr[0:1, :], -1.0), {}, writes=(maxr[0:1, :],))
    bank = C.bank
    NKT = T // 128

    def sbank_of(hh, kc):
        if big_v:
            return bank[kc % 2] if hh == 0 else bank[2]
        return bank[hh * 2 + kc % 2]

    def obanks_of(hh):
        if big_v:
            return [(bank[3 + 2 * hh], 65), (bank[4 + 2 * hh], 64)]
        return [(bank[4 + hh], dv + 1)]

    for p in range(npair):
        for tb in range(NTB):
            B.dma("sp", ks[:, p * T + tb * TB: p * T + (tb + 1) * TB],
                  K_d[p, :, tb * TB:(tb + 1) * TB], tag=f"{tag}ldk")
    vw = nheads * vstride
    for tt in range(NKT):
        B.dma("sp", vs[:, tt * vw:(tt + 1) * vw],
              V_d[tt * 128:(tt + 1) * 128, 0:vw], tag=f"{tag}ldv")

    def bounds_for(src_d, out_col0):
        for p in range(npair):
            for tb in range(NTB):
                B.dma("sp", qs[:, 0:TB], src_d[p, :, tb * TB:(tb + 1) * TB],
                      tag=f"{tag}ldb")
                B.act(qb_sq[:, 0:TB], qs[:, 0:TB], AF.Square, tag=f"{tag}bsq")
                pa = bank[0][0:1, 0:TB]
                pb = bank[1][0:1, 0:TB]
                B.mm(pa, ones64[0:64, 0:1], qb_sq[0:64, 0:TB], start=True, stop=True,
                     tile_position=(0, 0), tag=f"{tag}bn")
                B.mm(pb, ones64[64:128, 0:1], qb_sq[64:128, 0:TB], start=True, stop=True,
                     tile_position=(64, 0), tag=f"{tag}bn")
                for hh, pp in ((0, pa), (1, pb)):
                    col = out_col0 + 2 * p + hh
                    B.op("dve", "reduce_max", (tmr[0:1, hh:hh + 1], pp), dict(axis=AX.X),
                         reads=(pp,), writes=(tmr[0:1, hh:hh + 1],), tag=f"{tag}brd")
                    B.op("dve", "tensor_max",
                         (maxr[0:1, col:col + 1], maxr[0:1, col:col + 1], tmr[0:1, hh:hh + 1]),
                         {}, reads=(maxr[0:1, col:col + 1], tmr[0:1, hh:hh + 1]),
                         writes=(maxr[0:1, col:col + 1],), tag=f"{tag}bmx")

    bounds_for(Q_d, 0)
    bounds_for(K_d, nheads)
    B.act(maxr[0:1, 0:2 * nheads], maxr[0:1, 0:2 * nheads], AF.Sqrt, tag=f"{tag}bsqrt")
    B.op("dve", "tensor_mul",
         (maxr[0:1, 0:nheads], maxr[0:1, 0:nheads], maxr[0:1, nheads:2 * nheads]),
         {}, reads=(maxr[0:1, 0:2 * nheads],), writes=(maxr[0:1, 0:nheads],), tag=f"{tag}bc")
    B.op("dve", "tensor_scalar",
         (maxr[0:1, 0:nheads], maxr[0:1, 0:nheads], margin, -1.0, ALU.add, ALU.mult),
         {}, reads=(maxr[0:1, 0:nheads],), writes=(maxr[0:1, 0:nheads],), tag=f"{tag}bneg")
    for h in range(nheads):
        pb = bank[0][:, 0:1]
        B.mm(pb, C.onesrowf[0:1, :], maxr[0:1, h:h + 1], start=True, stop=True,
             tag=f"{tag}brep")
        B.op("dve", "tensor_copy", (negc[:, h:h + 1], pb), {},
             reads=(pb,), writes=(negc[:, h:h + 1],), tag=f"{tag}bcp")

    for p in range(npair):
        for qb in range(NTB):
            nkc = (qb + 1) * 4 if causal else NKT
            diag0 = qb * 4 if causal else None
            B.dma("sp", qs[:, 0:TB], Q_d[p, :, qb * TB:(qb + 1) * TB], tag=f"{tag}ldq")
            started = set()

            def emit_scores(kc):
                a = (kc - diag0) if (causal and kc >= diag0) else -1
                q0 = a * 128 if a > 0 else 0
                nq = TB - q0
                sA = sbank_of(0, kc)[:, 0:nq]
                sB = sbank_of(1, kc)[:, 0:nq]
                kslice = ks[:, p * T + kc * 128: p * T + (kc + 1) * 128]
                B.mm(sA, kslice[0:64, :], qs[0:64, q0:TB], start=True, stop=(a < 0),
                     tile_position=(0, 0), tag=f"{tag}sA")
                B.mm(sB, kslice[64:128, :], qs[64:128, q0:TB], start=True, stop=(a < 0),
                     tile_position=(64, 0), tag=f"{tag}sB")
                if a >= 0:
                    B.mm(sbank_of(0, kc)[:, 0:128], C.trimask[:, :], C.identb[:, :],
                         start=False, stop=True, tag=f"{tag}mk")
                    B.mm(sbank_of(1, kc)[:, 0:128], C.trimask[:, :], C.identb[:, :],
                         start=False, stop=True, tag=f"{tag}mk")

            def emit_exp(kc):
                a = (kc - diag0) if (causal and kc >= diag0) else -1
                q0 = a * 128 if a > 0 else 0
                nq = TB - q0
                for hh in range(2):
                    h = 2 * p + hh
                    sbank = sbank_of(hh, kc)[:, 0:nq]
                    slot = pt[:, (kc % 2) * 2 * TB + hh * TB: (kc % 2) * 2 * TB + hh * TB + nq]
                    B.act(slot, sbank, AF.Exp, bias=negc[:, h:h + 1], tag=f"{tag}exp")

            def emit_av(kc):
                a = (kc - diag0) if (causal and kc >= diag0) else -1
                q0 = a * 128 if a > 0 else 0
                nq = TB - q0
                for hh in range(2):
                    h = 2 * p + hh
                    slot = pt[:, (kc % 2) * 2 * TB + hh * TB: (kc % 2) * 2 * TB + hh * TB + nq]
                    vbase = kc * vw + h * vstride
                    c0 = 0
                    for (obk, M) in obanks_of(hh):
                        st_ = id(obk) not in started
                        started.add(id(obk))
                        B.mm(obk[0:M, q0:TB], vs[:, vbase + c0: vbase + c0 + M], slot,
                             start=st_, stop=(kc == nkc - 1), tag=f"{tag}av")
                        c0 += M

            emit_scores(0)
            emit_exp(0)
            for kc in range(1, nkc):
                emit_scores(kc)
                emit_exp(kc)
                emit_av(kc - 1)
            emit_av(nkc - 1)
            for hh in range(2):
                h = 2 * p + hh
                obs = obanks_of(hh)
                drow_p = 64 if big_v else dv
                if big_v:
                    B.op("dve", "tensor_copy", (oraw[0:65, 0:TB], obs[0][0][0:65, 0:TB]),
                         {}, reads=(obs[0][0][0:65, 0:TB],), writes=(oraw[0:65, 0:TB],),
                         tag=f"{tag}ev1")
                    B.op("dve", "tensor_copy", (oraw2[0:64, 0:TB], obs[1][0][0:64, 0:TB]),
                         {}, reads=(obs[1][0][0:64, 0:TB],), writes=(oraw2[0:64, 0:TB],),
                         tag=f"{tag}ev2")
                else:
                    B.op("dve", "tensor_copy", (oraw[0:dv + 1, 0:TB],
                                                obs[0][0][0:dv + 1, 0:TB]), {},
                         reads=(obs[0][0][0:dv + 1, 0:TB],),
                         writes=(oraw[0:dv + 1, 0:TB],), tag=f"{tag}ev1")
                pbc = obs[0][0][:, 0:TB]
                B.mm(pbc, onesq[drow_p:drow_p + 1, 0:128],
                     oraw[drow_p:drow_p + 1, 0:TB], start=True, stop=True,
                     tile_position=(drow_p, 0), tag=f"{tag}bcm")
                B.op("dve", "reciprocal", (bcf[:, 0:TB], pbc), {},
                     reads=(pbc,), writes=(bcf[:, 0:TB],), tag=f"{tag}bce")
                if big_v:
                    B.op("dve", "tensor_mul", (ost[0:64, 0:TB], oraw[0:64, 0:TB],
                                               bcf[0:64, 0:TB]), {},
                         reads=(oraw[0:64, 0:TB], bcf[0:64, 0:TB]),
                         writes=(ost[0:64, 0:TB],), tag=f"{tag}nr1")
                    B.op("dve", "tensor_mul", (ost2[0:64, 0:TB], oraw2[0:64, 0:TB],
                                               bcf[0:64, 0:TB]), {},
                         reads=(oraw2[0:64, 0:TB], bcf[0:64, 0:TB]),
                         writes=(ost2[0:64, 0:TB],), tag=f"{tag}nr2")
                    B.dma("act", O_d[h, 0:64, qb * TB:(qb + 1) * TB], ost[0:64, 0:TB],
                          tag=f"{tag}sto")
                    B.dma("act", O_d[h, 64:128, qb * TB:(qb + 1) * TB], ost2[0:64, 0:TB],
                          tag=f"{tag}sto2")
                else:
                    r0 = (h * dv) % 128
                    B.op("dve", "tensor_mul", (ost[0:dv, 0:TB],
                                               oraw[0:dv, 0:TB], bcf[0:dv, 0:TB]),
                         {}, reads=(oraw[0:dv, 0:TB], bcf[0:dv, 0:TB]),
                         writes=(ost[0:dv, 0:TB],), tag=f"{tag}nr")
                    B.dma("act", O_d[(h * dv) // 128, r0:r0 + dv, qb * TB:(qb + 1) * TB],
                          ost[0:dv, 0:TB], tag=f"{tag}sto")



def attn_out_proj(B, C, O_d, W_d, xin_d, xout_d, tag, bias_pvcol=None,
                  extra_add_d=None, xin_dtype=F32, out_dtype=F32, ncol=D):
    u = B.uid()
    nic = ncol // 128
    ofm = B.sbuf(f"{tag}_of{u}", [128, 2 * nic * TB], BF16)
    ws = B.sbuf(f"{tag}_w{u}", [128, nic * D], BF16)
    xs = B.sbuf(f"{tag}_xs{u}", [128, 2 * TB], xin_dtype)
    es = B.sbuf(f"{tag}_es{u}", [128, TB], BF16) if extra_add_d is not None else None
    outs = B.sbuf(f"{tag}_os{u}", [128, TB], out_dtype)
    w_view = W_d.rearrange("(kc kp) n -> kc kp n", kp=128)
    for kc in range(nic):
        B.dma("pool", ws[:, kc * D:(kc + 1) * D], w_view[kc, :, 0:D], tag=f"{tag}ldw")
    bank = C.bank
    bkb = C.bankb
    for tb in range(NTB):
        t0 = tb * TB
        ob = (tb % 2) * nic * TB
        for fc in range(nic):
            B.dma("sp", ofm[:, ob + fc * TB: ob + (fc + 1) * TB], O_d[fc, :, t0:t0 + TB],
                  tag=f"{tag}ldo")
        for oc in range(NFC):
            pb = bank[oc % 4][:, 0:TB]
            for kc in range(nic):
                B.mm(pb, ws[:, kc * D + oc * 128: kc * D + (oc + 1) * 128],
                     ofm[:, ob + kc * TB: ob + (kc + 1) * TB],
                     start=(kc == 0), stop=(kc == nic - 1), tag=f"{tag}mm")
            xb = (oc % 2) * TB
            B.dma("sp", xs[:, xb:xb + TB], xin_d[oc, :, t0:t0 + TB], tag=f"{tag}ldx")
            if bias_pvcol is not None:
                B.op("dve", "scalar_tensor_tensor",
                     (outs[:, 0:TB], pb, C.pv(bias_pvcol + oc), xs[:, xb:xb + TB],
                      ALU.add, ALU.add), {},
                     reads=(pb, C.pv(bias_pvcol + oc), xs[:, xb:xb + TB]),
                     writes=(outs[:, 0:TB],), tag=f"{tag}ev")
            else:
                B.op("dve", "tensor_add", (outs[:, 0:TB], pb, xs[:, xb:xb + TB]), {},
                     reads=(pb, xs[:, xb:xb + TB]), writes=(outs[:, 0:TB],), tag=f"{tag}ev")
            if extra_add_d is not None:
                B.dma("sp", es[:, 0:TB], extra_add_d[oc, :, t0:t0 + TB], tag=f"{tag}lde")
                B.op("dve", "tensor_add", (outs[:, 0:TB], outs[:, 0:TB], es[:, 0:TB]), {},
                     reads=(outs[:, 0:TB], es[:, 0:TB]), writes=(outs[:, 0:TB],),
                     tag=f"{tag}ev2")
            B.dma("act", xout_d[oc, :, t0:t0 + TB], outs[:, 0:TB], tag=f"{tag}st")


def phase_l0attn_av(B, C, Tn, cr):
    attention(B, C, Tn["Q1"], Tn["K1"], Tn["V1"], Tn["O1"], 16, 64, "p2")


def phase_l0attn_out(B, C, Tn, cr):
    attn_out_proj(B, C, Tn["O1"], Tn["L0_attn_W_out"], Tn["x0"], Tn["x1"], "p3",
                  bias_pvcol=cr.pvt_index["l0a_bout"])





def moe_gate(B, C, h_d, wgate_d, G_d, tag, ebias_bc_d=None):
    u = B.uid()
    ws = B.sbuf(f"{tag}_w{u}", [128, NFC * 8], BF16)
    hs = B.sbuf(f"{tag}_h{u}", [128, NFC * 128], BF16)
    pr = B.sbuf(f"{tag}_p{u}", [128, 8], F32)
    q2 = B.sbuf(f"{tag}_q2{u}", [128, 8], F32)
    eq1 = B.sbuf(f"{tag}_e1{u}", [128, 8], F32)
    eq2 = B.sbuf(f"{tag}_e2{u}", [128, 8], F32)
    gout = B.sbuf(f"{tag}_g{u}", [128, 8], F32)
    col = B.sbuf(f"{tag}_c{u}", [128, 6], F32)
    eb = None
    if ebias_bc_d is not None:
        eb = B.sbuf(f"{tag}_eb{u}", [128, 8], F32)
        B.dma("pool", eb[:, :], ebias_bc_d[:, :], tag=f"{tag}ldeb")
    lg = B.sbuf(f"{tag}_lg{u}", [128, 8], F32)
    w_view = wgate_d.rearrange("(kc kp) n -> kc kp n", kp=128)
    for kc in range(NFC):
        B.dma("pool", ws[:, kc * 8:(kc + 1) * 8], w_view[kc, :, 0:8], tag=f"{tag}ldw")
    bank = C.bank
    for tt in range(T // 128):
        t0 = tt * 128
        for kc in range(NFC):
            B.dma("sp", hs[:, kc * 128:(kc + 1) * 128], h_d[kc, :, t0:t0 + 128],
                  tag=f"{tag}ldh")
        pb = bank[tt % 2][:, 0:8]
        for kc in range(NFC):
            B.mm(pb, hs[:, kc * 128:(kc + 1) * 128], ws[:, kc * 8:(kc + 1) * 8],
                 start=(kc == 0), stop=(kc == NFC - 1), tag=f"{tag}mm")
        if eb is not None:
            B.op("dve", "tensor_add", (lg[:, :], pb, eb[:, :]), {},
                 reads=(pb, eb[:, :]), writes=(lg[:, :],), tag=f"{tag}eb")
            lsrc = lg[:, :]
        else:
            B.op("dve", "tensor_copy", (lg[:, :], pb), {},
                 reads=(pb,), writes=(lg[:, :],), tag=f"{tag}cp")
            lsrc = lg[:, :]
        B.op("dve", "reduce_max", (col[:, 0:1], lsrc), dict(axis=AX.X, negate=True),
             reads=(lsrc,), writes=(col[:, 0:1],), tag=f"{tag}mx")
        B.act(pr[:, :], lsrc, AF.Exp, bias=col[:, 0:1], tag=f"{tag}ex")
        B.op("dve", "reduce_sum", (col[:, 1:2], pr[:, :]), dict(axis=AX.X),
             reads=(pr[:, :],), writes=(col[:, 1:2],), tag=f"{tag}sm")
        B.op("dve", "reciprocal", (col[:, 2:3], col[:, 1:2]), {},
             reads=(col[:, 1:2],), writes=(col[:, 2:3],), tag=f"{tag}rc")
        B.op("dve", "tensor_scalar", (pr[:, :], pr[:, :], col[:, 2:3], None, ALU.mult),
             {}, reads=(pr[:, :], col[:, 2:3]), writes=(pr[:, :],), tag=f"{tag}pn")
        B.op("dve", "reduce_max", (col[:, 3:4], pr[:, :]), dict(axis=AX.X),
             reads=(pr[:, :],), writes=(col[:, 3:4],), tag=f"{tag}m1")
        B.op("dve", "tensor_scalar", (eq1[:, :], pr[:, :], col[:, 3:4], None, ALU.is_equal),
             {}, reads=(pr[:, :], col[:, 3:4]), writes=(eq1[:, :],), tag=f"{tag}e1")
        B.op("dve", "scalar_tensor_tensor", (q2[:, :], eq1[:, :], -2.0, pr[:, :],
                                             ALU.mult, ALU.add), {},
             reads=(eq1[:, :], pr[:, :]), writes=(q2[:, :],), tag=f"{tag}q2")
        B.op("dve", "reduce_max", (col[:, 4:5], q2[:, :]), dict(axis=AX.X),
             reads=(q2[:, :],), writes=(col[:, 4:5],), tag=f"{tag}m2")
        B.op("dve", "tensor_scalar", (eq2[:, :], q2[:, :], col[:, 4:5], None, ALU.is_equal),
             {}, reads=(q2[:, :], col[:, 4:5]), writes=(eq2[:, :],), tag=f"{tag}e2")
        B.op("dve", "tensor_add", (col[:, 5:6], col[:, 3:4], col[:, 4:5]), {},
             reads=(col[:, 3:5],), writes=(col[:, 5:6],), tag=f"{tag}dn")
        B.op("dve", "reciprocal", (col[:, 5:6], col[:, 5:6]), {},
             reads=(col[:, 5:6],), writes=(col[:, 5:6],), tag=f"{tag}rd")
        B.op("dve", "tensor_mul", (col[:, 3:4], col[:, 3:4], col[:, 5:6]), {},
             reads=(col[:, 3:4], col[:, 5:6]), writes=(col[:, 3:4],), tag=f"{tag}a1")
        B.op("dve", "tensor_mul", (col[:, 4:5], col[:, 4:5], col[:, 5:6]), {},
             reads=(col[:, 4:5], col[:, 5:6]), writes=(col[:, 4:5],), tag=f"{tag}a2")
        B.op("dve", "tensor_scalar", (gout[:, :], eq1[:, :], col[:, 3:4], None, ALU.mult),
             {}, reads=(eq1[:, :], col[:, 3:4]), writes=(gout[:, :],), tag=f"{tag}g1")
        B.op("dve", "scalar_tensor_tensor", (gout[:, :], eq2[:, :], col[:, 4:5],
                                             gout[:, :], ALU.mult, ALU.add), {},
             reads=(eq2[:, :], col[:, 4:5], gout[:, :]), writes=(gout[:, :],), tag=f"{tag}g2")
        B.dma("act", G_d[t0:t0 + 128, 0:8], gout[:, :], tag=f"{tag}st")



def add_streams(B, C, a_d, b_d, out_d, tag, a_dtype=F32):
    u = B.uid()
    xa = B.sbuf(f"{tag}_a{u}", [128, TB], a_dtype)
    xb = B.sbuf(f"{tag}_b{u}", [128, TB], BF16)
    xo = B.sbuf(f"{tag}_o{u}", [128, TB], F32)
    for fc in range(NFC):
        for tb in range(NTB):
            t0 = tb * TB
            B.dma("sp", xa[:, 0:TB], a_d[fc, :, t0:t0 + TB], tag=f"{tag}la")
            B.dma("sp", xb[:, 0:TB], b_d[fc, :, t0:t0 + TB], tag=f"{tag}lb")
            B.op("dve", "tensor_add", (xo[:, 0:TB], xa[:, 0:TB], xb[:, 0:TB]), {},
                 reads=(xa[:, 0:TB], xb[:, 0:TB]), writes=(xo[:, 0:TB],), tag=f"{tag}ad")
            B.dma("act", out_d[fc, :, t0:t0 + TB], xo[:, 0:TB], tag=f"{tag}st")



def moe_dense_l0(B, C, h_d, w1_d, w2_d, G_d, x2_d, tag, b1cols, b2cols,
                 dff=4096, nexp=8):
    u = B.uid()
    TQ = 512
    ndf = dff // 128
    ws1 = B.sbuf(f"{tag}_w1{u}", [128, NFC * dff], BF16)
    ws2 = B.sbuf(f"{tag}_w2{u}", [128, ndf * D], BF16)
    hs = B.sbuf(f"{tag}_h{u}", [128, 2 * NFC * TQ], BF16)
    he = B.sbuf(f"{tag}_he{u}", [128, ndf * TQ], BF16)
    gt = B.sbuf(f"{tag}_gt{u}", [128, 8], F32)
    gtr = B.sbuf(f"{tag}_gr{u}", [8, T], BF16)
    gbc = B.sbuf(f"{tag}_gb{u}", [128, TQ], F32)
    cst = B.sbuf(f"{tag}_cs{u}", [128, TQ], F32)
    esel = B.sbuf(f"{tag}_es{u}", [8, 8 * 128], BF16)
    B.dma("pool", esel[:, :], C.esel_d[:, :], tag=f"{tag}lde")
    bank = C.bank
    for jj in range(T // 128):
        B.dma("sp", gt[:, 0:8], G_d[jj * 128:(jj + 1) * 128, 0:8], tag=f"{tag}ldg")
        pt_ = bank[6][0:8, 0:128]
        B.transpose(pt_, gt[:, 0:8], C.identf[:, :], tag=f"{tag}gtr")
        B.op("dve", "tensor_copy", (gtr[0:8, jj * 128:(jj + 1) * 128], pt_), {},
             reads=(pt_,), writes=(gtr[0:8, jj * 128:(jj + 1) * 128],),
             tag=f"{tag}gc")
    w1v = w1_d.rearrange("e (kc kp) n -> e kc kp n", kp=128)
    w2v = w2_d.rearrange("e (kc kp) n -> e kc kp n", kp=128)
    for e in range(nexp):
        for kc in range(NFC):
            B.dma("pool", ws1[:, kc * dff:(kc + 1) * dff], w1v[e, kc, :, 0:dff],
                  tag=f"{tag}lw1")
        for kc in range(ndf):
            B.dma("pool", ws2[:, kc * D:(kc + 1) * D], w2v[e, kc, :, 0:D],
                  tag=f"{tag}lw2")
        for tq in range(T // TQ):
            t0 = tq * TQ
            hb = (tq % 2) * NFC * TQ
            for kc in range(NFC):
                B.dma("sp", hs[:, hb + kc * TQ: hb + (kc + 1) * TQ], h_d[kc, :, t0:t0 + TQ],
                      tag=f"{tag}ldh")
            pgb = bank[6][:, 0:TQ]
            B.mm(pgb, esel[0:8, e * 128:(e + 1) * 128], gtr[0:8, t0:t0 + TQ],
                 start=True, stop=True, tag=f"{tag}gbm")
            B.op("dve", "tensor_copy", (gbc[:, 0:TQ], pgb), {},
                 reads=(pgb,), writes=(gbc[:, 0:TQ],), tag=f"{tag}gbe")
            for fu in range(ndf):
                pb = bank[4 + fu % 2][:, 0:TQ]
                for kc in range(NFC):
                    B.mm(pb, ws1[:, kc * dff + fu * 128: kc * dff + (fu + 1) * 128],
                         hs[:, hb + kc * TQ: hb + (kc + 1) * TQ],
                         start=(kc == 0), stop=(kc == NFC - 1), tag=f"{tag}up")
                B.act(he[:, fu * TQ:(fu + 1) * TQ], pb, AF.Gelu,
                      bias=C.pv(b1cols[e] + fu), tag=f"{tag}gl")
            for oc in range(NFC):
                pb = bank[oc % 4][:, 0:TQ]
                for kc in range(ndf):
                    B.mm(pb, ws2[:, kc * D + oc * 128: kc * D + (oc + 1) * 128],
                         he[:, kc * TQ:(kc + 1) * TQ],
                         start=(kc == 0), stop=(kc == ndf - 1),
                         tag=f"{tag}dn")
                B.op("dve", "scalar_tensor_tensor",
                     (cst[:, 0:TQ], pb, C.pv(b2cols[e] + oc), gbc[:, 0:TQ],
                      ALU.add, ALU.mult), {},
                     reads=(pb, C.pv(b2cols[e] + oc), gbc[:, 0:TQ]),
                     writes=(cst[:, 0:TQ],), tag=f"{tag}ev")
                B.dma("pool", x2_d[oc, :, t0:t0 + TQ], cst[:, 0:TQ],
                      accum_op=ALU.add, tag=f"{tag}ac")


def phase_l0ffn_qkv(B, C, Tn, cr):
    ln_full(B, C, Tn["x1"], Tn["u1"], "p4ln")
    ln_full(B, C, Tn["u1"], Tn["h1"], "p4l2", g_rowcol=cr.row_index["l0f_g1"],
            b_rowcol=cr.row_index["l0f_b1"], src_dtype=BF16)
    proj_fm(B, C, Tn["h1"], Tn["L0_ffn_w_qkv"], Tn["Q2"], D, "p4q",
            scale=0.125, col_off=0)
    proj_fm(B, C, Tn["h1"], Tn["L0_ffn_w_qkv"], Tn["K2"], D, "p4k", col_off=D)
    proj_tm_v(B, C, Tn["h1"], Tn["L0_ffn_w_qkv"], Tn["V2"], D, 16, 64, "p4v",
              col_off=2 * D)


def phase_l0ffn_av(B, C, Tn, cr):
    attention(B, C, Tn["Q2"], Tn["K2"], Tn["V2"], Tn["O2"], 16, 64, "p5")


def phase_l0ffn_mid(B, C, Tn, cr):
    attn_out_proj(B, C, Tn["O2"], Tn["L0_ffn_w_o"], Tn["u1"], Tn["xr"], "p6",
                  xin_dtype=BF16, out_dtype=BF16)
    ln_full(B, C, Tn["xr"], Tn["h2"], "p6l2", g_rowcol=cr.row_index["l0f_g2"],
            b_rowcol=cr.row_index["l0f_b2"], src_dtype=BF16)
    moe_gate(B, C, Tn["h2"], Tn["L0_ffn_w_gate"], Tn["G1"], "p6g")


def phase_l0ffn_moe(B, C, Tn, cr):
    add_streams(B, C, Tn["x1"], Tn["xr"], Tn["x2"], "p7i")
    b1cols = [cr.pvt_index[f"l0f_b1_{e}"] for e in range(8)]
    b2cols = [cr.pvt_index[f"l0f_b2_{e}"] for e in range(8)]
    moe_dense_l0(B, C, Tn["h2"], Tn["L0_ffn_w1"], Tn["L0_ffn_w2"], Tn["G1"],
                 Tn["x2"], "p7", b1cols, b2cols)





def proj_tm(B, C, h_d, w_d, out_d, dout, tag, col_off=0, scale=None, wT=False):
    u = B.uid()
    ws = B.sbuf(f"{tag}_w{u}", [128, NFC * dout], BF16)
    hs = B.sbuf(f"{tag}_h{u}", [128, NFC * 128], BF16)
    os_ = B.sbuf(f"{tag}_o{u}", [128, dout], BF16)
    w_view = w_d.rearrange("(kc kp) n -> kc kp n", kp=128)
    for kc in range(NFC):
        B.dma("pool", ws[:, kc * dout:(kc + 1) * dout], w_view[kc, :, col_off:col_off + dout],
              tag=f"{tag}ldw")
    bank = C.bank
    nnc = (dout + 511) // 512
    for tt in range(T // 128):
        t0 = tt * 128
        for kc in range(NFC):
            B.dma("sp", hs[:, kc * 128:(kc + 1) * 128], h_d[kc, :, t0:t0 + 128],
                  tag=f"{tag}ldh")
        for nc_ in range(nnc):
            n0 = nc_ * 512
            nw = min(512, dout - n0)
            pb = bank[nc_ % 4][:, 0:nw]
            for kc in range(NFC):
                B.mm(pb, hs[:, kc * 128:(kc + 1) * 128], ws[:, kc * dout + n0: kc * dout + n0 + nw],
                     start=(kc == 0), stop=(kc == NFC - 1), tag=f"{tag}mm")
            if scale is not None:
                B.op("dve", "tensor_scalar", (os_[:, n0:n0 + nw], pb, scale, None, ALU.mult),
                     {}, reads=(pb,), writes=(os_[:, n0:n0 + nw],), tag=f"{tag}ev")
            else:
                B.op("dve", "tensor_copy", (os_[:, n0:n0 + nw], pb), {},
                     reads=(pb,), writes=(os_[:, n0:n0 + nw],), tag=f"{tag}ev")
        B.dma("act", out_d[t0:t0 + 128, 0:dout], os_[:, 0:dout], tag=f"{tag}st")



def rope_tm(B, C, x_d, cos_d, sin_d, width, ranges, nfreq, tag):
    u = B.uid()
    xt = B.sbuf(f"{tag}_x{u}", [128, width], BF16)
    cs = B.sbuf(f"{tag}_c{u}", [128, nfreq], F32)
    sn = B.sbuf(f"{tag}_s{u}", [128, nfreq], F32)
    t1 = B.sbuf(f"{tag}_t1{u}", [128, nfreq], F32)
    t2 = B.sbuf(f"{tag}_t2{u}", [128, nfreq], F32)
    t3 = B.sbuf(f"{tag}_t3{u}", [128, nfreq], F32)
    for tt in range(T // 128):
        t0 = tt * 128
        B.dma("sp", xt[:, 0:width], x_d[t0:t0 + 128, 0:width], tag=f"{tag}ldx")
        B.dma("sp", cs[:, :], cos_d[t0:t0 + 128, 0:nfreq], tag=f"{tag}ldc")
        B.dma("sp", sn[:, :], sin_d[t0:t0 + 128, 0:nfreq], tag=f"{tag}lds")
        for c0 in ranges:
            ev = xt[:, c0: c0 + 2 * nfreq: 2]
            od = xt[:, c0 + 1: c0 + 2 * nfreq: 2]
            rr = xt[:, c0:c0 + 2 * nfreq]
            B.op("dve", "tensor_mul", (t1[:, :], ev, cs[:, :]), {},
                 reads=(rr, cs[:, :]), writes=(t1[:, :],), tag=f"{tag}a")
            B.op("dve", "tensor_mul", (t2[:, :], od, sn[:, :]), {},
                 reads=(rr, sn[:, :]), writes=(t2[:, :],), tag=f"{tag}b")
            B.op("dve", "tensor_sub", (t1[:, :], t1[:, :], t2[:, :]), {},
                 reads=(t1[:, :], t2[:, :]), writes=(t1[:, :],), tag=f"{tag}c")
            B.op("dve", "tensor_mul", (t2[:, :], od, cs[:, :]), {},
                 reads=(rr, cs[:, :]), writes=(t2[:, :],), tag=f"{tag}d")
            B.op("dve", "tensor_mul", (t3[:, :], ev, sn[:, :]), {},
                 reads=(rr, sn[:, :]), writes=(t3[:, :],), tag=f"{tag}e")
            B.op("dve", "tensor_add", (t2[:, :], t2[:, :], t3[:, :]), {},
                 reads=(t2[:, :], t3[:, :]), writes=(t2[:, :],), tag=f"{tag}f")
            B.op("dve", "tensor_copy", (ev, t1[:, :]), {},
                 reads=(t1[:, :],), writes=(rr,), tag=f"{tag}g")
            B.op("dve", "tensor_copy", (od, t2[:, :]), {},
                 reads=(t2[:, :],), writes=(rr,), tag=f"{tag}h")
        B.dma("act", x_d[t0:t0 + 128, 0:width], xt[:, 0:width], tag=f"{tag}st")



def tm_to_fm(B, C, x_d, out_d, w, tag, col_groups=None):
    u = B.uid()
    nch = (w // 128) if col_groups is None else len(col_groups)
    xt = B.sbuf(f"{tag}_x{u}", [128, w], BF16)
    st = B.sbuf(f"{tag}_s{u}", [128, 128], BF16)
    bkb = C.bankb
    for tt in range(T // 128):
        t0 = tt * 128
        B.dma("sp", xt[:, 0:w], x_d[t0:t0 + 128, 0:w], tag=f"{tag}ldx")
        for ch in range(nch):
            if col_groups is None:
                segs = [(ch * 128, 128)]
            else:
                segs = col_groups[ch]
            pt_ = bkb[:, 0:128]
            p0 = 0
            for (s0, span) in segs:
                B.transpose(bkb[p0:p0 + span, 0:128], xt[:, s0:s0 + span],
                            C.identb[:, :], tag=f"{tag}tr")
                p0 += span
            B.op("dve", "tensor_copy", (st[:, 0:128], pt_), {},
                 reads=(pt_,), writes=(st[:, 0:128],), tag=f"{tag}cp")
            B.dma("act", out_d[ch, :, t0:t0 + 128], st[:, 0:128], tag=f"{tag}st")



LAMBDA_INIT = 0.8 - 0.6 * math.exp(-0.3 * 12)


def l1attn_combine(B, C, o1_d, o2_d, lam_d, lnw_d, lnb_d, y_d, tag):
    u = B.uid()
    o1 = B.sbuf(f"{tag}_o1{u}", [128, TB], BF16)
    o2 = B.sbuf(f"{tag}_o2{u}", [128, TB], BF16)
    oc = B.sbuf(f"{tag}_oc{u}", [128, TB], F32)
    sq = B.sbuf(f"{tag}_sq{u}", [128, TB], F32)
    bcf = B.sbuf(f"{tag}_bc{u}", [128, TB], F32)
    yt = B.sbuf(f"{tag}_y{u}", [128, TB], BF16)
    rrow = B.sbuf(f"{tag}_rr{u}", [1, TB], F32)
    lv = B.sbuf(f"{tag}_lv{u}", [1, 4 * 64], F32)
    dots = B.sbuf(f"{tag}_dt{u}", [1, 4], F32)
    lamr = B.sbuf(f"{tag}_lm{u}", [1, 1], F32)
    nlam = B.sbuf(f"{tag}_nl{u}", [128, 1], F32)
    lnw = B.sbuf(f"{tag}_lw{u}", [128, 2], F32)
    onesc = B.sbuf(f"{tag}_1c{u}", [128, 1], F32)
    B.op("pool", "memset", (onesc[:, :], 1.0), {}, writes=(onesc[:, :],))
    B.dma("pool", lnw[:, 0:1], lnw_d[:, 0:1], tag=f"{tag}lw")
    B.dma("pool", lnw[:, 1:2], lnb_d[:, 0:1], tag=f"{tag}lb")
    B.dma("pool", lv[0:1, :], lam_d[0:1, :], tag=f"{tag}ll")
    B.op("dve", "tensor_scalar", (lnw[:, 0:2], lnw[:, 0:2], 1.0 - LAMBDA_INIT, None,
                                  ALU.mult), {},
         reads=(lnw[:, 0:2],), writes=(lnw[:, 0:2],), tag=f"{tag}sw")
    for i, (a, b) in enumerate(((0, 64), (128, 192))):
        B.op("dve", "tensor_mul", (lv[0:1, a:a + 64], lv[0:1, a:a + 64], lv[0:1, b:b + 64]),
             {}, reads=(lv[0:1, :],), writes=(lv[0:1, a:a + 64],), tag=f"{tag}m{i}")
        B.op("dve", "reduce_sum", (dots[0:1, i:i + 1], lv[0:1, a:a + 64]), dict(axis=AX.X),
             reads=(lv[0:1, a:a + 64],), writes=(dots[0:1, i:i + 1],), tag=f"{tag}r{i}")
    B.act(dots[0:1, 0:2], dots[0:1, 0:2], AF.Exp, tag=f"{tag}ex")
    B.op("dve", "tensor_sub", (lamr[0:1, 0:1], dots[0:1, 0:1], dots[0:1, 1:2]), {},
         reads=(dots[0:1, 0:2],), writes=(lamr[0:1, 0:1],), tag=f"{tag}sub")
    B.op("dve", "tensor_scalar", (lamr[0:1, 0:1], lamr[0:1, 0:1], LAMBDA_INIT, -1.0,
                                  ALU.add, ALU.mult), {},
         reads=(lamr[0:1, 0:1],), writes=(lamr[0:1, 0:1],), tag=f"{tag}neg")
    pb0 = C.bank[5][:, 0:1]
    B.mm(pb0, C.onesrowf[0:1, :], lamr[0:1, 0:1], start=True, stop=True, tag=f"{tag}rep")
    B.op("dve", "tensor_copy", (nlam[:, 0:1], pb0), {}, reads=(pb0,),
         writes=(nlam[:, 0:1],), tag=f"{tag}cp")
    inv128 = 1.0 / 128.0
    bank = C.bank
    for h in range(8):
        for tb in range(NTB):
            t0 = tb * TB
            B.dma("sp", o1[:, 0:TB], o1_d[h, :, t0:t0 + TB], tag=f"{tag}l1")
            B.dma("sp", o2[:, 0:TB], o2_d[h, :, t0:t0 + TB], tag=f"{tag}l2")
            B.op("dve", "scalar_tensor_tensor", (oc[:, 0:TB], o2[:, 0:TB], nlam[:, 0:1],
                                                 o1[:, 0:TB], ALU.mult, ALU.add), {},
                 reads=(o2[:, 0:TB], nlam[:, 0:1], o1[:, 0:TB]), writes=(oc[:, 0:TB],),
                 tag=f"{tag}cmb")
            B.act(sq[:, 0:TB], oc[:, 0:TB], AF.Square, tag=f"{tag}sq")
            ps = bank[tb % 2][0:1, 0:TB]
            B.mm(ps, onesc[:, 0:1], sq[:, 0:TB], start=True, stop=True, tag=f"{tag}ss")
            B.act(rrow[0:1, 0:TB], ps, AF.Ln, bias=C.pv(C.cr.pvt_index["eps_a8"])[0:1, 0:1],
                  scale=inv128, tag=f"{tag}ln")
            B.act(rrow[0:1, 0:TB], rrow[0:1, 0:TB], AF.Exp, scale=-0.5, tag=f"{tag}ep")
            pbc = bank[2 + tb % 2][:, 0:TB]
            B.mm(pbc, C.onesrowf[0:1, :], rrow[0:1, 0:TB], start=True, stop=True,
                 tag=f"{tag}bm")
            B.op("dve", "tensor_copy", (bcf[:, 0:TB], pbc), {},
                 reads=(pbc,), writes=(bcf[:, 0:TB],), tag=f"{tag}be")
            B.op("dve", "tensor_mul", (oc[:, 0:TB], oc[:, 0:TB], bcf[:, 0:TB]), {},
                 reads=(oc[:, 0:TB], bcf[:, 0:TB]), writes=(oc[:, 0:TB],), tag=f"{tag}n1")
            B.op("dve", "tensor_scalar", (yt[:, 0:TB], oc[:, 0:TB], lnw[:, 0:1],
                                          lnw[:, 1:2], ALU.mult, ALU.add), {},
                 reads=(oc[:, 0:TB], lnw[:, 0:2]), writes=(yt[:, 0:TB],), tag=f"{tag}n2")
            B.dma("act", y_d[h, :, t0:t0 + TB], yt[:, 0:TB], tag=f"{tag}st")



def moe_l1(B, C, h_d, Wg_d, Wu_d, Wd_d, sWg_d, sWu_d, sWd_d, G_d, x4_d, tag,
           dint=512):
    u = B.uid()
    TQ = 512
    ndi = dint // 128
    wsg = B.sbuf(f"{tag}_wg{u}", [128, NFC * dint], BF16)
    wsu = B.sbuf(f"{tag}_wu{u}", [128, NFC * dint], BF16)
    wsd = B.sbuf(f"{tag}_wd{u}", [128, ndi * D], BF16)
    hs = B.sbuf(f"{tag}_h{u}", [128, 2 * NFC * TQ], BF16)
    gu = B.sbuf(f"{tag}_gu{u}", [128, ndi * TQ], BF16)
    us = B.sbuf(f"{tag}_us{u}", [128, TQ], F32)
    gt = B.sbuf(f"{tag}_gt{u}", [128, 8], F32)
    gtr = B.sbuf(f"{tag}_gr{u}", [8, T], BF16)
    gbc = B.sbuf(f"{tag}_gb{u}", [128, TQ], F32)
    cst = B.sbuf(f"{tag}_cs{u}", [128, TQ], F32)
    esel = B.sbuf(f"{tag}_es{u}", [8, 8 * 128], BF16)
    B.dma("pool", esel[:, :], C.esel_d[:, :], tag=f"{tag}lde")
    bank = C.bank
    for jj in range(T // 128):
        B.dma("sp", gt[:, 0:8], G_d[jj * 128:(jj + 1) * 128, 0:8], tag=f"{tag}ldg")
        pt_ = bank[6][0:8, 0:128]
        B.transpose(pt_, gt[:, 0:8], C.identf[:, :], tag=f"{tag}gtr")
        B.op("dve", "tensor_copy", (gtr[0:8, jj * 128:(jj + 1) * 128], pt_), {},
             reads=(pt_,), writes=(gtr[0:8, jj * 128:(jj + 1) * 128],),
             tag=f"{tag}gc")
    Wgv = Wg_d.rearrange("e (kc kp) n -> e kc kp n", kp=128)
    Wuv = Wu_d.rearrange("e (kc kp) n -> e kc kp n", kp=128)
    Wdv = Wd_d.rearrange("e (kc kp) n -> e kc kp n", kp=128)
    sWgv = sWg_d.rearrange("e (kc kp) n -> e kc kp n", kp=128)
    sWuv = sWu_d.rearrange("e (kc kp) n -> e kc kp n", kp=128)
    sWdv = sWd_d.rearrange("e (kc kp) n -> e kc kp n", kp=128)
    for e in range(-1, 8):
        gv, uv, dv_ = (sWgv, sWuv, sWdv) if e < 0 else (Wgv, Wuv, Wdv)
        ei = 0 if e < 0 else e
        for kc in range(NFC):
            B.dma("pool", wsg[:, kc * dint:(kc + 1) * dint], gv[ei, kc, :, 0:dint],
                  tag=f"{tag}lwg")
            B.dma("pool", wsu[:, kc * dint:(kc + 1) * dint], uv[ei, kc, :, 0:dint],
                  tag=f"{tag}lwu")
        for kc in range(ndi):
            B.dma("pool", wsd[:, kc * D:(kc + 1) * D], dv_[ei, kc, :, 0:D],
                  tag=f"{tag}lwd")
        for tq in range(T // TQ):
            t0 = tq * TQ
            hb = (tq % 2) * NFC * TQ
            for kc in range(NFC):
                B.dma("sp", hs[:, hb + kc * TQ: hb + (kc + 1) * TQ], h_d[kc, :, t0:t0 + TQ],
                      tag=f"{tag}ldh")
            if e >= 0:
                pgb = bank[6][:, 0:TQ]
                B.mm(pgb, esel[0:8, e * 128:(e + 1) * 128], gtr[0:8, t0:t0 + TQ],
                     start=True, stop=True, tag=f"{tag}gbm")
                B.op("dve", "tensor_copy", (gbc[:, 0:TQ], pgb), {},
                     reads=(pgb,), writes=(gbc[:, 0:TQ],), tag=f"{tag}gbe")
            for fu in range(ndi):
                pg = bank[4 + 2 * (fu % 2)][:, 0:TQ] if False else bank[(4, 0)[fu % 2]][:, 0:TQ]
                pu = bank[(5, 1)[fu % 2]][:, 0:TQ]
                for kc in range(NFC):
                    B.mm(pg, wsg[:, kc * dint + fu * 128: kc * dint + (fu + 1) * 128],
                         hs[:, hb + kc * TQ: hb + (kc + 1) * TQ],
                         start=(kc == 0), stop=(kc == NFC - 1), tag=f"{tag}ug")
                for kc in range(NFC):
                    B.mm(pu, wsu[:, kc * dint + fu * 128: kc * dint + (fu + 1) * 128],
                         hs[:, hb + kc * TQ: hb + (kc + 1) * TQ],
                         start=(kc == 0), stop=(kc == NFC - 1), tag=f"{tag}uu")
                B.op("dve", "tensor_copy", (us[:, 0:TQ], pu), {},
                     reads=(pu,), writes=(us[:, 0:TQ],), tag=f"{tag}ue")
                B.act(cst[:, 0:TQ], pg, AF.Silu, tag=f"{tag}si")
                B.op("dve", "tensor_mul", (gu[:, fu * TQ:(fu + 1) * TQ], cst[:, 0:TQ],
                                           us[:, 0:TQ]), {},
                     reads=(cst[:, 0:TQ], us[:, 0:TQ]),
                     writes=(gu[:, fu * TQ:(fu + 1) * TQ],), tag=f"{tag}gm")
            for oc in range(NFC):
                pb = bank[oc % 4][:, 0:TQ]
                for kc in range(ndi):
                    B.mm(pb, wsd[:, kc * D + oc * 128: kc * D + (oc + 1) * 128],
                         gu[:, kc * TQ:(kc + 1) * TQ],
                         start=(kc == 0), stop=(kc == ndi - 1), tag=f"{tag}dn")
                if e >= 0:
                    B.op("dve", "tensor_mul", (cst[:, 0:TQ], pb, gbc[:, 0:TQ]), {},
                         reads=(pb, gbc[:, 0:TQ]), writes=(cst[:, 0:TQ],), tag=f"{tag}ev")
                else:
                    B.op("dve", "tensor_copy", (cst[:, 0:TQ], pb), {},
                         reads=(pb,), writes=(cst[:, 0:TQ],), tag=f"{tag}ev")
                B.dma("pool", x4_d[oc, :, t0:t0 + TQ], cst[:, 0:TQ],
                      accum_op=ALU.add, tag=f"{tag}ac")


def phase_l1attn_qkv(B, C, Tn, cr):
    ln_full(B, C, Tn["x2"], Tn["u2"], "p8ln")
    sc = 64.0 ** (-0.25)
    ga = [[(256 * p, 64), (256 * p + 128, 64)] for p in range(4)]
    gb = [[(256 * p + 64, 64), (256 * p + 192, 64)] for p in range(4)]
    proj_rope_fm(B, C, Tn["u2"], Tn["L1_attn_WqT"],
                 [(Tn["Q3a"], ga), (Tn["Q3b"], gb)],
                 Tn["rope64w_cos"], Tn["rope64w_sin"], 0, 32, "p8q", scale=sc)
    proj_rope_fm(B, C, Tn["u2"], Tn["L1_attn_WkT"],
                 [(Tn["K3a"], ga), (Tn["K3b"], gb)],
                 Tn["rope64w_cos"], Tn["rope64w_sin"], 0, 32, "p8k")
    proj_tm_v(B, C, Tn["u2"], Tn["L1_attn_WvT"], Tn["V3"], D, 8, 128, "p8v")


def phase_l1attn_av(B, C, Tn, cr):
    attention(B, C, Tn["Q3a"], Tn["K3a"], Tn["V3"], Tn["O3a"], 8, 128, "p9a",
              causal=False)


def phase_l1attn_av2(B, C, Tn, cr):
    attention(B, C, Tn["Q3b"], Tn["K3b"], Tn["V3"], Tn["O3b"], 8, 128, "p9b",
              causal=False)


def phase_l1attn_out(B, C, Tn, cr):
    l1attn_combine(B, C, Tn["O3a"], Tn["O3b"], Tn["lam_cat"], Tn["lnw_col"],
                   Tn["lnb_col"], Tn["Y3"], "pA")
    attn_out_proj(B, C, Tn["Y3"], Tn["L1_attn_WoT"], Tn["x2"], Tn["x3"], "pAo")


def phase_l1ffn_qkv(B, C, Tn, cr):
    ln_full(B, C, Tn["x3"], Tn["u3"], "pBln")
    ln_full(B, C, Tn["u3"], Tn["h3"], "pBl2", g_rowcol=cr.row_index["l1f_g1"],
            rms=True, src_dtype=BF16)
    ga = [[(128 * p, 128)] for p in range(8)]
    proj_rope_fm(B, C, Tn["h3"], Tn["L1_ffn_Wq"], [(Tn["Q4"], ga)],
                 Tn["rope32w_cos"], Tn["rope32w_sin"], 32, 16, "pBq", scale=0.125)
    proj_tm(B, C, Tn["h3"], Tn["L1_ffn_Wkva"], Tn["ckvt"], 160, "pBkv")
    rope_tm(B, C, Tn["ckvt"], Tn["rope32_cos"], Tn["rope32_sin"], 160, [128], 16, "pBrk")


def phase_l1ffn_kv(B, C, Tn, cr):
    B_ = B
    u = B.uid()
    tag = "pC"
    ck = B.sbuf(f"{tag}_ck{u}", [128, 160], BF16)
    lat = B.sbuf(f"{tag}_lt{u}", [128, 128], BF16)
    wkvb = B.sbuf(f"{tag}_w{u}", [128, 512], BF16)
    kvsb = B.sbuf(f"{tag}_kv{u}", [128, 512], BF16)
    trsb = B.sbuf(f"{tag}_tr{u}", [128, 128], BF16)
    krsb = B.sbuf(f"{tag}_kr{u}", [128, 128], BF16)
    v4sb = B.sbuf(f"{tag}_v4{u}", [128, 16 * 33], BF16)
    B.dma("pool", wkvb[:, :], Tn["L1_ffn_Wkvb"][:, :], tag=f"{tag}lw")
    bank = C.bank
    bkb = C.bankb
    for tt in range(T // 128):
        t0 = tt * 128
        B.dma("sp", ck[:, 0:160], Tn["ckvt"][t0:t0 + 128, 0:160], tag=f"{tag}ldc")
        ptb = bkb[:, 0:128]
        B.transpose(ptb, ck[:, 0:128], C.identb[:, :], tag=f"{tag}t1")
        B.op("dve", "tensor_copy", (lat[:, :], ptb), {}, reads=(ptb,),
             writes=(lat[:, :],), tag=f"{tag}c1")
        pkv = bank[4][:, 0:512]
        B.mm(pkv, lat[:, :], wkvb[:, 0:512], start=True, stop=True, tag=f"{tag}mm")
        B.op("dve", "tensor_copy", (kvsb[:, :], pkv), {}, reads=(pkv,),
             writes=(kvsb[:, :],), tag=f"{tag}c2")
        for h in range(16):
            B.op("dve", "tensor_copy",
                 (v4sb[:, h * 33: h * 33 + 32], kvsb[:, h * 32:(h + 1) * 32]), {},
                 reads=(kvsb[:, h * 32:(h + 1) * 32],),
                 writes=(v4sb[:, h * 33: h * 33 + 32],), tag=f"{tag}v")
        ones_view = v4sb.rearrange("p (h e) -> p h e", e=33)[:, :, 32:33]
        B.op("dve", "memset", (ones_view, 1.0), {}, writes=(v4sb[:, 0:16 * 33],),
             tag=f"{tag}on")
        B.dma("act", Tn["V4"][t0:t0 + 128, 0:16 * 33], v4sb[:, 0:16 * 33], tag=f"{tag}sv")
        for j in range(4):
            B.transpose(bkb[:, 0:128], kvsb[:, j * 128:(j + 1) * 128], C.identb[:, :],
                        tag=f"{tag}t2")
            B.op("dve", "tensor_copy", (trsb[:, :], bkb[:, 0:128]), {},
                 reads=(bkb[:, 0:128],), writes=(trsb[:, :],), tag=f"{tag}c3")
            for i in range(4):
                h = 4 * j + i
                B.dma("act", Tn["K4"][h // 2, (h % 2) * 64:(h % 2) * 64 + 32, t0:t0 + 128],
                      trsb[32 * i:32 * i + 32, :], tag=f"{tag}sk")
        B.transpose(bkb[0:32, 0:128], ck[:, 128:160], C.identb[:, :], tag=f"{tag}t3")
        B.op("dve", "tensor_copy", (krsb[0:32, :], bkb[0:32, 0:128]), {},
             reads=(bkb[0:32, 0:128],), writes=(krsb[0:32, :],), tag=f"{tag}c4")
        for p in range(8):
            B.dma("act", Tn["K4"][p, 32:64, t0:t0 + 128], krsb[0:32, :], tag=f"{tag}skr")
            B.dma("act", Tn["K4"][p, 96:128, t0:t0 + 128], krsb[0:32, :], tag=f"{tag}skr2")


def phase_l1ffn_av(B, C, Tn, cr):
    attention(B, C, Tn["Q4"], Tn["K4"], Tn["V4"], Tn["O4"], 16, 32, "pD")


def phase_l1ffn_mid(B, C, Tn, cr):
    attn_out_proj(B, C, Tn["O4"], Tn["L1f_Wo_sub"], Tn["u3"], Tn["xi"], "pE",
                  xin_dtype=BF16, out_dtype=BF16, ncol=512)
    ln_full(B, C, Tn["xi"], Tn["h4"], "pEl2", g_rowcol=cr.row_index["l1f_g2"],
            rms=True, src_dtype=BF16)
    moe_gate(B, C, Tn["h4"], Tn["L1_ffn_Wgate"], Tn["G2"], "pEg",
             ebias_bc_d=Tn["ebias_bc"])


def phase_l1ffn_moe(B, C, Tn, cr):
    add_streams(B, C, Tn["x3"], Tn["xi"], Tn["x4"], "pFi")
    moe_l1(B, C, Tn["h4"], Tn["L1_ffn_Wg"], Tn["L1_ffn_Wu"], Tn["L1_ffn_Wd"],
           Tn["L1_ffn_sWg"], Tn["L1_ffn_sWu"], Tn["L1_ffn_sWd"], Tn["G2"],
           Tn["x4"], "pF")


PHASES[:] = [
    ("l0attn_qkv", ["x0"], ["u_l0", "Q1", "K1", "V1"],
     ["L0_attn_W_qkv"], ["vbias_l0"], phase_l0attn_qkv),
    ("l0attn_av", ["Q1", "K1", "V1"], ["O1"], [], [], phase_l0attn_av),
    ("l0attn_out", ["O1", "x0"], ["x1"], ["L0_attn_W_out"], [], phase_l0attn_out),
    ("l0ffn_qkv", ["x1"], ["u1", "h1", "Q2", "K2", "V2"], ["L0_ffn_w_qkv"], [], phase_l0ffn_qkv),
    ("l0ffn_av", ["Q2", "K2", "V2"], ["O2"], [], [], phase_l0ffn_av),
    ("l0ffn_mid", ["O2", "u1"], ["xr", "h2", "G1"], ["L0_ffn_w_o", "L0_ffn_w_gate"], [], phase_l0ffn_mid),
    ("l0ffn_moe", ["x1", "xr", "h2", "G1"], ["x2"], ["L0_ffn_w1", "L0_ffn_w2"], [], phase_l0ffn_moe),
    ("l1attn_qkv", ["x2"], ["u2", "q3t", "k3t", "Q3a", "Q3b", "K3a", "K3b", "V3"],
     ["L1_attn_WqT", "L1_attn_WkT", "L1_attn_WvT"],
     ["rope64w_cos", "rope64w_sin"], phase_l1attn_qkv),
    ("l1attn_av", ["Q3a", "K3a", "V3"], ["O3a"], [], [], phase_l1attn_av),
    ("l1attn_av2", ["Q3b", "K3b", "V3"], ["O3b"], [], [], phase_l1attn_av2),
    ("l1attn_out", ["O3a", "O3b", "x2"], ["Y3", "x3"], ["L1_attn_WoT"],
     ["lam_cat", "lnw_col", "lnb_col"], phase_l1attn_out),
    ("l1ffn_qkv", ["x3"], ["u3", "h3", "q4t", "ckvt", "Q4"],
     ["L1_ffn_Wq", "L1_ffn_Wkva"],
     ["rope32_cos", "rope32_sin", "rope32w_cos", "rope32w_sin"], phase_l1ffn_qkv),
    ("l1ffn_kv", ["ckvt"], ["K4", "V4"], ["L1_ffn_Wkvb"], [], phase_l1ffn_kv),
    ("l1ffn_av", ["Q4", "K4", "V4"], ["O4"], [], [], phase_l1ffn_av),
    ("l1ffn_mid", ["O4", "u3"], ["xi", "h4", "G2"], ["L1f_Wo_sub", "L1_ffn_Wgate"],
     ["ebias_bc"], phase_l1ffn_mid),
    ("l1ffn_moe", ["x3", "xi", "h4", "G2"], ["x4"],
     ["L1_ffn_Wg", "L1_ffn_Wu", "L1_ffn_Wd", "L1_ffn_sWg", "L1_ffn_sWu", "L1_ffn_sWd"],
     [], phase_l1ffn_moe),
]



def proj_rope_fm(B, C, h_d, w_d, outs_fm, cosw_d, sinw_d, off, nfreq, tag,
                 dout=D, scale=None):
    u = B.uid()
    wcols = 16 * nfreq
    ws = B.sbuf(f"{tag}_w{u}", [128, NFC * dout], BF16)
    hs = B.sbuf(f"{tag}_h{u}", [128, NFC * 128], BF16)
    os_ = B.sbuf(f"{tag}_o{u}", [128, dout], BF16)
    cs = B.sbuf(f"{tag}_c{u}", [128, wcols], F32)
    sn = B.sbuf(f"{tag}_s{u}", [128, wcols], F32)
    t1 = B.sbuf(f"{tag}_t1{u}", [128, wcols], F32)
    t2 = B.sbuf(f"{tag}_t2{u}", [128, wcols], F32)
    t3 = B.sbuf(f"{tag}_t3{u}", [128, wcols], F32)
    st = B.sbuf(f"{tag}_st{u}", [128, 128], BF16)
    w_view = w_d.rearrange("(kc kp) n -> kc kp n", kp=128)
    for kc in range(NFC):
        B.dma("pool", ws[:, kc * dout:(kc + 1) * dout], w_view[kc, :, 0:dout],
              tag=f"{tag}ldw")
    bank = C.bank
    bkb = C.bankb
    nnc = (dout + 511) // 512
    for tt in range(T // 128):
        t0 = tt * 128
        for kc in range(NFC):
            B.dma("sp", hs[:, kc * 128:(kc + 1) * 128], h_d[kc, :, t0:t0 + 128],
                  tag=f"{tag}ldh")
        B.dma("sp", cs[:, :], cosw_d[t0:t0 + 128, 0:wcols], tag=f"{tag}ldc")
        B.dma("sp", sn[:, :], sinw_d[t0:t0 + 128, 0:wcols], tag=f"{tag}lds")
        for nc_ in range(nnc):
            n0 = nc_ * 512
            nw = min(512, dout - n0)
            pb = bank[nc_ % 4][:, 0:nw]
            for kc in range(NFC):
                B.mm(pb, hs[:, kc * 128:(kc + 1) * 128], ws[:, kc * dout + n0: kc * dout + n0 + nw],
                     start=(kc == 0), stop=(kc == NFC - 1), tag=f"{tag}mm")
            if scale is not None:
                B.op("dve", "tensor_scalar", (os_[:, n0:n0 + nw], pb, scale, None, ALU.mult),
                     {}, reads=(pb,), writes=(os_[:, n0:n0 + nw],), tag=f"{tag}ev")
            else:
                B.op("dve", "tensor_copy", (os_[:, n0:n0 + nw], pb), {},
                     reads=(pb,), writes=(os_[:, n0:n0 + nw],), tag=f"{tag}ev")
        osv = os_.rearrange("p (h c) -> p h c", c=64)
        ev = osv[:, :, off: off + 2 * nfreq: 2]
        od = osv[:, :, off + 1: off + 2 * nfreq: 2]
        rr = os_[:, 0:dout]
        csv = cs.rearrange("p (h f) -> p h f", f=nfreq)[:, :, :]
        snv = sn.rearrange("p (h f) -> p h f", f=nfreq)[:, :, :]
        t1v = t1.rearrange("p (h f) -> p h f", f=nfreq)[:, :, :]
        t2v = t2.rearrange("p (h f) -> p h f", f=nfreq)[:, :, :]
        t3v = t3.rearrange("p (h f) -> p h f", f=nfreq)[:, :, :]
        B.op("dve", "tensor_mul", (t1v, ev, csv), {},
             reads=(rr, cs[:, :]), writes=(t1[:, :],), tag=f"{tag}ra")
        B.op("dve", "tensor_mul", (t2v, od, snv), {},
             reads=(rr, sn[:, :]), writes=(t2[:, :],), tag=f"{tag}rb")
        B.op("dve", "tensor_sub", (t1[:, :], t1[:, :], t2[:, :]), {},
             reads=(t1[:, :], t2[:, :]), writes=(t1[:, :],), tag=f"{tag}rc")
        B.op("dve", "tensor_mul", (t2v, od, csv), {},
             reads=(rr, cs[:, :]), writes=(t2[:, :],), tag=f"{tag}rd")
        B.op("dve", "tensor_mul", (t3v, ev, snv), {},
             reads=(rr, sn[:, :]), writes=(t3[:, :],), tag=f"{tag}re")
        B.op("dve", "tensor_add", (t2[:, :], t2[:, :], t3[:, :]), {},
             reads=(t2[:, :], t3[:, :]), writes=(t2[:, :],), tag=f"{tag}rf")
        B.op("dve", "tensor_copy", (ev, t1v), {},
             reads=(t1[:, :],), writes=(rr,), tag=f"{tag}rg")
        B.op("dve", "tensor_copy", (od, t2v), {},
             reads=(t2[:, :],), writes=(rr,), tag=f"{tag}rh")
        for (out_d, col_groups) in outs_fm:
            nch = len(col_groups)
            for ch in range(nch):
                segs = col_groups[ch]
                p0 = 0
                for (s0, span) in segs:
                    B.transpose(bkb[p0:p0 + span, 0:128], os_[:, s0:s0 + span],
                                C.identb[:, :], tag=f"{tag}tr")
                    p0 += span
                B.op("dve", "tensor_copy", (st[:, 0:128], bkb[:, 0:128]), {},
                     reads=(bkb[:, 0:128],), writes=(st[:, 0:128],), tag=f"{tag}cp")
                B.dma("act", out_d[ch, :, t0:t0 + 128], st[:, 0:128], tag=f"{tag}sf")
